# version 14
# baseline (speedup 1.0000x reference)
import numpy as np
import ml_dtypes
import concourse.bass as bass
import concourse.mybir as mybir
from concourse.bass_utils import run_bass_kernel_spmd

F32 = mybir.dt.float32
BF16 = mybir.dt.bfloat16
FP16 = mybir.dt.float16
U8 = mybir.dt.uint8
AF = mybir.ActivationFunctionType
ALU = mybir.AluOpType
AXX = mybir.AxisListType.X

D = 1024
T = 2048
NT = 16
S = 4096
B = 4
CH = 8
ENGINES = ("tensor", "vector", "scalar", "gpsimd", "sync")

BF = ml_dtypes.bfloat16
DBG = {"l3_stop": None}


class Prog:
    def __init__(self):
        self.nc = bass.Bass("TRN2", target_bir_lowering=False)
        self.ops = {e: [] for e in ENGINES}
        self._ctx = []
        self.sems = {}
        self.cnt = {}
        self._phidx = 0
        self._cur = "ph0"
        self._pending = {e: [] for e in ENGINES}
        self.sem("ph0")
        self.pf = [self._ps(f"pbF{i}", [128, 512], F32) for i in range(6)]
        self.pb = [self._ps(f"pbB{i}", [128, 1024], BF16) for i in range(2)]

    def sem(self, name):
        if name not in self.sems:
            g = self.nc.semaphore(name)
            self.sems[name] = g.__enter__()
            self._ctx.append(g)
            self.cnt[name] = 0
        return name

    def sb(self, name, shape, dt):
        self._uid = getattr(self, "_uid", 0) + 1
        uname = f"s_{name}__{self._uid}"
        g = self.nc.sbuf_tensor(uname, shape, dt)
        h = g.__enter__()
        self._guards = getattr(self, "_guards", {})
        self._guards[uname] = g
        return h

    def free_prefix(self, pref):
        gd = getattr(self, "_guards", {})
        for n in reversed(list(gd)):
            if n.startswith("s_" + pref):
                gd.pop(n).__exit__(None, None, None)

    def _ps(self, name, shape, dt):
        g = self.nc.psum_tensor(name, shape, dt)
        h = g.__enter__()
        self._ctx.append(g)
        return h

    def dram(self, name, shape, dt, kind="Internal"):
        return self.nc.dram_tensor(name, shape, dt, kind=kind).ap()

    SEM_LIMIT = 48000

    def _rotate(self):
        old, final = self._cur, self.cnt[self._cur]
        self._phidx += 1
        self._cur = f"ph{self._phidx}"
        self.sem(self._cur)
        for e in ENGINES:
            self._pending[e].append((old, final))

    def mark(self):
        return (self._cur, self.cnt[self._cur])

    relax = True

    def op(self, eng, fn, dma=False, wait=None, inc=True):
        if not self.relax:
            wait = None
            inc = True
        amt = 16 if dma else 1
        if inc and self.cnt[self._cur] + amt > self.SEM_LIMIT:
            self._rotate()
        cur = self._cur
        if wait is None:
            wait_t = [(cur, self.cnt[cur])]
        elif wait is False:
            wait_t = []
        elif isinstance(wait, list):
            wait_t = [w for w in wait if w is not None]
            if len(wait_t) != len(wait):
                wait_t = [(cur, self.cnt[cur])]
        else:
            wait_t = [wait]
        if inc:
            inc_t = (cur, amt)
            self.cnt[cur] += amt
        else:
            inc_t = None
        pend = tuple(self._pending[eng])
        self._pending[eng] = []

        wm = {}
        for s, v in wait_t:
            wm[s] = max(wm.get(s, 0), v)
        wait_t = tuple(wm.items())

        def run(e, fn=fn, wait_t=wait_t, inc_t=inc_t, pend=pend):
            for s, v in pend:
                e.wait_ge(self.sems[s], v)
            for s, v in wait_t:
                e.wait_ge(self.sems[s], v)
            ins = fn(e)
            if inc_t is not None:
                ins.then_inc(self.sems[inc_t[0]], inc_t[1])
        self.ops[eng].append(run)

    def op_collective(self, fn):
        self.sem("cc")
        if self.cnt[self._cur] + 1 > self.SEM_LIMIT:
            self._rotate()
        cur = self._cur
        wait_val = self.cnt[cur]
        self.cnt["cc"] += 1
        cc_target = self.cnt["cc"]
        self.cnt[cur] += 1
        pend = tuple(self._pending["gpsimd"])
        self._pending["gpsimd"] = []

        def run(e, fn=fn, cur=cur, wait_val=wait_val, cc_target=cc_target, pend=pend):
            for s, v in pend:
                e.wait_ge(self.sems[s], v)
            e.wait_ge(self.sems[cur], wait_val)
            fn(e).then_inc(self.sems["cc"])
            e.wait_ge(self.sems["cc"], cc_target)
            e.engine_nop().then_inc(self.sems[cur], 1)
        self.ops["gpsimd"].append(run)

    def raw(self, eng, fn):
        self.ops[eng].append(fn)

    def final_wait(self):
        cur = self._cur
        v = self.cnt[cur]
        pend = tuple(self._pending["sync"])
        self._pending["sync"] = []

        def run(e, cur=cur, v=v, pend=pend):
            for s, vv in pend:
                e.wait_ge(self.sems[s], vv)
            e.wait_ge(self.sems[cur], v)
        self.ops["sync"].append(run)

    def finalize(self):
        blk = self.nc.Block()
        block = blk.__enter__()
        for eng in ENGINES:
            fns = self.ops[eng]
            if not fns:
                continue
            getattr(block, eng)(lambda e, fns=fns: [f(e) for f in fns])
        blk.__exit__(None, None, None)
        return self.nc


def host_constants(h):
    c = {}
    c["c_mW0"] = np.full((128, 1), float(h), dtype=np.float32)
    gp = ((np.arange(T) // 128) * 2 + h) * 128 + (np.arange(T) % 128)
    theta = 10000.0 ** (-2.0 * np.arange(32, dtype=np.float64) / 64.0)
    ang = (gp[None, :] + 1.0) * theta[(np.arange(128) % 32)][:, None]
    c["c_l3cos"] = np.cos(ang).astype(BF)
    c["c_l3sin"] = np.sin(ang).astype(BF)
    half = 512
    inv_freq = 1.0 / 10000.0 ** (np.arange(half, dtype=np.float64) / half)
    ang2 = gp[None, :] * inv_freq[:, None]
    cos2 = np.cos(ang2); sin2 = np.sin(ang2)
    tri0 = (np.arange(128)[:, None] <= np.arange(128)[None, :]).astype(np.float32)
    c["c_l0tri"] = tri0.astype(BF)
    angc = (np.pi / 2.0) * gp.astype(np.float64) / 4096.0
    cs = np.zeros((128, T), np.float32)
    cs[0:64, :] = np.cos(angc)[None, :]
    cs[64:128, :] = np.sin(angc)[None, :]
    c["c_l0cs"] = cs.astype(BF)
    c["c_l2cos"] = np.ascontiguousarray(
        cos2.reshape(4, 128, T).transpose(1, 0, 2).reshape(128, 4 * T)).astype(BF)
    c["c_l2sin"] = np.ascontiguousarray(
        sin2.reshape(4, 128, T).transpose(1, 0, 2).reshape(128, 4 * T)).astype(BF)
    tri = (np.arange(128)[:, None] <= np.arange(128)[None, :]).astype(np.float32)
    if h == 0:
        ma, mb = tri, np.zeros((128, 128), np.float32)
    else:
        ma, mb = np.ones((128, 128), np.float32), tri
    c["c_l3m_a4"] = np.tile(ma, (1, 4)).astype(BF)
    c["c_l3m_b4"] = np.tile(mb, (1, 4)).astype(BF)
    c["c_mW1"] = np.full((128, 1), float(1 - h), dtype=np.float32)
    gpos = (np.arange(T) // 128 * 2 + h) * 128 + (np.arange(T) % 128)
    c["c_ident_bf"] = np.eye(128, dtype=np.float32).astype(BF)
    c["c_ident_f"] = np.eye(128, dtype=np.float32)
    c["c_ones_col_f"] = np.ones((128, 1), dtype=np.float32)
    c["c_ones_col_bf"] = np.ones((128, 1), dtype=np.float32).astype(BF)
    c["c_ones_row_f"] = np.ones((1, 128), dtype=np.float32)
    c["c_ones_row_bf"] = np.ones((1, 128), dtype=np.float32).astype(BF)
    c["c_gpos"] = gpos.astype(np.float32).reshape(1, T)
    c["c_ones512_f"] = np.ones((1, 512), dtype=np.float32)
    ho = np.zeros((128, 2), dtype=np.float32)
    ho[0:64, 0] = 1.0
    ho[64:128, 1] = 1.0
    c["c_half_ones"] = ho
    return c


def const_decls():
    return {
        "c_ident_bf": ([128, 128], BF16),
        "c_ident_f": ([128, 128], F32),
        "c_ones_col_f": ([128, 1], F32),
        "c_ones_col_bf": ([128, 1], BF16),
        "c_ones_row_f": ([1, 128], F32),
        "c_ones_row_bf": ([1, 128], BF16),
        "c_gpos": ([1, T], F32),
        "c_ones512_f": ([1, 512], F32),
        "c_half_ones": ([128, 2], F32),
    }


class _LazyConsts:
    def __init__(self, cm):
        self.cm = cm
        self.d = {}

    def __getitem__(self, n):
        if n not in self.d:
            self.d[n] = self.cm._load_const(n)
        return self.d[n]


class Common:

    def __init__(self, p: Prog, ins):
        self.p = p
        self.ins = ins
        self.XS = None
        self.LNX = None
        self.x_in = None
        self.x_out = None
        self.SC1 = p.sb("SC1", [128, 512], F32)
        self.SC2 = p.sb("SC2", [128, 512], F32)
        self.SROW = p.sb("SROW", [1, 1024], F32)
        self._const_decls = const_decls()
        self.consts = _LazyConsts(self)
        for n in self._const_decls:
            if n != "c_gpos":
                self.consts[n]

    def _load_const(self, n):
        shape, dt = self._const_decls[n]
        t = self.p.sb(n, shape, dt)
        self.p.op("sync", lambda e, t=t, n=n: e.dma_start(out=t[:], in_=self.ins[n][:, :]), dma=True)
        return t

    def xs(self, c, t0=0, n=T):
        return self.XS[:, c * T + t0: c * T + t0 + n]

    def lnx(self, c, t0=0, n=T):
        return self.LNX[:, c * T + t0: c * T + t0 + n]

    def load_X(self, xdram):
        p = self.p
        XSB = self.XS
        for c in range(CH):
            p.op("sync", lambda e, c=c, XSB=XSB: e.dma_start(
                out=XSB[:, c * T:(c + 1) * T],
                in_=xdram[c * 128:(c + 1) * 128, :]), dma=True)

    def store_X(self, xdram):
        p = self.p
        XSB = self.XS
        for c in range(CH):
            p.op("sync", lambda e, c=c, XSB=XSB: e.dma_start(
                out=xdram[c * 128:(c + 1) * 128, :],
                in_=XSB[:, c * T:(c + 1) * T]), dma=True)

    def layernorm(self, rows=None, eps=1e-5, rms=False, src=None, fp32_dst=None):
        p = self.p
        C = self.consts
        ones_col_f = C["c_ones_col_f"]
        ones_row_f = C["c_ones_row_f"]
        XSB = self.XS
        LNXB = self.LNX
        if src is None:
            src = lambda c, t0, n, XSB=XSB: XSB[:, c * T + t0: c * T + t0 + n]
        psum = p.pf[4][0:1, :]
        psumsq = p.pf[5][0:1, :]
        pb_a = p.pf[2]
        pb_b = p.pf[3]
        for tb in range(4):
            t0 = tb * 512
            for c in range(CH):
                x = src(c, t0, 512)
                p.op("scalar", lambda e, x=x: e.activation(self.SC1[:], x, AF.Square))
                if not rms:
                    p.op("tensor", lambda e, x=x, st=(c == 0): e.matmul(
                        psum, ones_col_f[:], x, start=st, stop=(c == CH - 1)))
                p.op("tensor", lambda e, st=(c == 0): e.matmul(
                    psumsq, ones_col_f[:], self.SC1[:], start=st, stop=(c == CH - 1)))
            if rows is not None:
                rr = rows[0][:, t0:t0 + 512]
                rn = rows[1][:, t0:t0 + 512]
            else:
                rr = self.SROW[:, 0:512]
                rn = self.SROW[:, 512:1024]
            if not rms:
                p.op("vector", lambda e, rn=rn: e.tensor_scalar_mul(rn, psum, 1.0 / D))
                p.op("vector", lambda e, rr=rr: e.tensor_scalar_mul(rr, psumsq, 1.0 / D))
                p.op("vector", lambda e, rn=rn: e.scalar_tensor_tensor(
                    self.SC1[0:1, :], rn, 1.0, rn, ALU.mult, ALU.mult))
                p.op("vector", lambda e, rr=rr: e.tensor_sub(rr, rr, self.SC1[0:1, :]))
            else:
                p.op("vector", lambda e, rr=rr: e.tensor_scalar_mul(rr, psumsq, 1.0 / D))
            p.op("vector", lambda e, rr=rr, eps=eps: e.tensor_scalar_add(rr, rr, float(eps)))
            p.op("scalar", lambda e, rr=rr: e.activation(rr, rr, AF.Sqrt))
            p.op("vector", lambda e, rr=rr: e.reciprocal(rr, rr))
            if not rms:
                p.op("vector", lambda e, rr=rr, rn=rn: e.scalar_tensor_tensor(
                    rn, rn, -1.0, rr, ALU.mult, ALU.mult))
            p.op("tensor", lambda e, rr=rr: e.matmul(pb_a[:], ones_row_f[:], rr, start=True, stop=True))
            if not rms:
                p.op("tensor", lambda e, rn=rn: e.matmul(pb_b[:], ones_row_f[:], rn, start=True, stop=True))
            for c in range(CH):
                x = src(c, t0, 512)
                dst = LNXB[:, c * T + t0: c * T + t0 + 512]
                p.op("vector", lambda e, x=x: e.tensor_mul(self.SC2[:], x, pb_a[:]))
                if not rms:
                    p.op("vector", lambda e, dst=dst: e.tensor_add(dst, self.SC2[:], pb_b[:]))
                    if fp32_dst is not None:
                        hd = fp32_dst[:, c * T + t0: c * T + t0 + 512]
                        p.op("vector", lambda e, hd=hd: e.tensor_add(hd, self.SC2[:], pb_b[:]))
                else:
                    p.op("vector", lambda e, dst=dst: e.tensor_copy(out=dst, in_=self.SC2[:]))


def emit_L0_ffn(cm: Common, pref="l0f"):
    p, C, ins = cm.p, cm.consts, cm.ins
    _LNXB = cm.LNX
    _XSB = cm.XS
    def lnxs(c, t0=0, n=T, _b=_LNXB):
        return _b[:, c * T + t0: c * T + t0 + n]
    def xss(c, t0=0, n=T, _b=_XSB):
        return _b[:, c * T + t0: c * T + t0 + n]
    E, F = 8, 2048
    gw = p.sb(pref + "gw", [128, CH * 8], BF16)
    gb = p.sb(pref + "gb", [1, 8], BF16)
    WgT = p.sb(pref + "WgT", [128, NT * 8], F32)
    wrow = p.sb(pref + "wrow", [1, 512], F32)
    w1 = p.sb(pref + "w1", [128, CH * F], BF16)
    w2 = p.sb(pref + "w2", [128, 8 * D], BF16)
    b1 = p.sb(pref + "b1", [1, F], F32)
    b2 = p.sb(pref + "b2", [1, D], F32)
    hb = p.sb(pref + "hb", [128, (F // 128) * 512], BF16)
    wbb = p.sb(pref + "wb", [128, 512], BF16)
    g8 = p.sb(pref + "g8", [128, 8], F32)
    m1 = p.sb(pref + "m1", [128, 1], F32)
    nm1 = p.sb(pref + "nm1", [128, 1], F32)
    m2 = p.sb(pref + "m2", [128, 1], F32)
    sel = p.sb(pref + "sel", [128, 8], F32)
    selm = p.sb(pref + "selm", [128, 8], U8)
    neg = p.sb(pref + "neg", [128, 8], F32)
    den = p.sb(pref + "den", [128, 1], F32)

    p.op("gpsimd", lambda e: e.dma_start(
        out=gw[:].rearrange("p (c e) -> p c e", e=8),
        in_=ins["L0_ffn_gate_w"].rearrange("(c p) e -> p c e", p=128)), dma=True)
    p.op("gpsimd", lambda e: e.dma_start(out=gb[:], in_=ins["L0_ffn_gate_b"].rearrange(
        "(u e) -> u e", u=1)), dma=True)
    ones_row_bf = C["c_ones_row_bf"]
    ones_row_f = C["c_ones_row_f"]
    ones512_f = C["c_ones512_f"]
    ident_f = C["c_ident_f"]
    p.op("vector", lambda e: e.memset(neg[:], -1e30))

    ps = p.pf[0][:, 0:8]
    pr = p.pf[1][0:1, 0:128]
    for i in range(NT):
        t0 = i * 128
        p.op("tensor", lambda e: e.matmul(ps, ones_row_bf[:], gb[:],
                                          start=True, stop=False))
        for c in range(CH):
            p.op("tensor", lambda e, c=c, t0=t0: e.matmul(
                ps, lnxs(c, t0, 128), gw[:, c * 8:(c + 1) * 8],
                start=False, stop=(c == CH - 1)))
        p.op("vector", lambda e: e.tensor_copy(out=g8[:], in_=ps))
        p.op("vector", lambda e: e.reduce_max(m1[:], g8[:], axis=AXX))
        p.op("vector", lambda e: e.tensor_scalar_mul(nm1[:], m1[:], -1.0))
        p.op("vector", lambda e: e.tensor_scalar(selm[:], g8[:], m1[:, 0:1], None, ALU.is_lt))
        p.op("vector", lambda e: e.select(sel[:], selm[:], g8[:], neg[:]))
        p.op("vector", lambda e: e.reduce_max(m2[:], sel[:], axis=AXX))
        wg = WgT[:, i * 8:(i + 1) * 8]
        p.op("scalar", lambda e, wg=wg: e.activation(wg, g8[:], AF.Exp, bias=nm1[:, 0:1]))
        p.op("vector", lambda e: e.tensor_scalar(sel[:], g8[:], m2[:, 0:1], None, ALU.is_ge))
        p.op("vector", lambda e, wg=wg: e.tensor_mul(wg, wg, sel[:]))
        p.op("vector", lambda e: e.tensor_sub(den[:], m2[:], m1[:]))
        p.op("scalar", lambda e: e.activation(den[:], den[:], AF.Exp))
        p.op("vector", lambda e: e.tensor_scalar_add(den[:], den[:], 1.0))
        p.op("vector", lambda e: e.reciprocal(den[:], den[:]))
        p.op("vector", lambda e, wg=wg: e.tensor_scalar(wg, wg, den[:, 0:1], None, ALU.mult))

    w1r = ins["L0_ffn_w1"].rearrange("e (c p) f -> e p c f", p=128)
    w2r = ins["L0_ffn_w2"].rearrange("e (c p) f -> e p c f", p=128)
    b1r = ins["L0_ffn_b1"].rearrange("e (u f) -> e u f", u=1)
    NF = F // 128
    b2r = ins["L0_ffn_b2"].rearrange("e (u d) -> e u d", u=1)
    bankm = {}
    for ee in range(E):
        p.op("gpsimd", lambda e, ee=ee: e.dma_start(
            out=w1[:].rearrange("p (c f) -> p c f", f=F), in_=w1r[ee]), dma=True)
        p.op("sync", lambda e, ee=ee: e.dma_start(out=b1[:], in_=b1r[ee]), dma=True)
        p.op("sync", lambda e, ee=ee: e.dma_start(out=b2[:], in_=b2r[ee]), dma=True)
        for tb in range(4):
            t0 = tb * 512
            for j in range(4):
                i = tb * 4 + j
                p.op("tensor", lambda e, i=i, ee=ee: e.matmul(
                    pr, WgT[:, i * 8 + ee: i * 8 + ee + 1], ident_f[:],
                    start=True, stop=True))
                p.op("vector", lambda e, j=j: e.tensor_copy(
                    out=wrow[:, j * 128:(j + 1) * 128], in_=pr))
            pwb = p.pf[2]
            p.op("tensor", lambda e: e.matmul(
                pwb[:], ones_row_f[:], wrow[:], start=True, stop=True))
            p.op("vector", lambda e: e.tensor_copy(out=wbb[:], in_=pwb[:]))
            for bb in range(4):
                bankm[bb] = p.mark()
            for oc in range(NF):
                ph = p.pf[oc % 2]
                p.op("tensor", lambda e, ph=ph, oc=oc: e.matmul(
                    ph[:], b1[:, oc * 128:(oc + 1) * 128], ones512_f[:],
                    start=True, stop=False),
                    wait=(None if oc < 2 else bankm.get(oc % 2)), inc=False)
                for c in range(CH):
                    p.op("tensor", lambda e, ph=ph, oc=oc, c=c, t0=t0: e.matmul(
                        ph[:], w1[:, c * F + oc * 128: c * F + oc * 128 + 128],
                        lnxs(c, t0, 512), start=False, stop=(c == CH - 1)),
                        wait=False, inc=(c == CH - 1))
                p.op("scalar", lambda e, ph=ph, oc=oc: e.activation(
                    cm.SC1[:], ph[:], AF.Gelu))
                p.op("vector", lambda e, oc=oc: e.tensor_mul(
                    hb[:, oc * 512:(oc + 1) * 512], cm.SC1[:], wbb[:]))
                bankm[oc % 2] = p.mark()
            for half in range(2):
                p.op("gpsimd", lambda e, ee=ee, half=half: e.dma_start(
                    out=w2[:].rearrange("p (c f) -> p c f", f=D),
                    in_=w2r[ee][:, half * 8:(half + 1) * 8, :]), dma=True)
                for oc in range(CH):
                    py = p.pf[2 + (oc % 2)]
                    first = True
                    if half == 0:
                        p.op("tensor", lambda e, py=py, oc=oc: e.matmul(
                            py[:], b2[:, oc * 128:(oc + 1) * 128],
                            wrow[:], start=True, stop=False),
                            wait=(None if oc < 2 else bankm.get(2 + oc % 2)), inc=False)
                        first = False
                    for fc in range(8):
                        p.op("tensor", lambda e, py=py, oc=oc, fc=fc, half=half: e.matmul(
                            py[:], w2[:, fc * D + oc * 128: fc * D + (oc + 1) * 128],
                            hb[:, (half * 8 + fc) * 512:(half * 8 + fc + 1) * 512],
                            start=(half == 1 and fc == 0), stop=(fc == 7)),
                            wait=((None if oc < 2 else bankm.get(2 + oc % 2)) if (first and fc == 0) else False),
                            inc=(fc == 7))
                    x = xss(oc, t0, 512)
                    p.op("vector", lambda e, py=py, x=x: e.tensor_add(x, x, py[:]))
                    bankm[2 + oc % 2] = p.mark()
    p.free_prefix(pref)


def emit_L3_ffn(cm: Common, pref="l3f"):
    p, C, ins = cm.p, cm.consts, cm.ins
    _LNXB = cm.LNX
    _XSB = cm.XS
    def lnxs(c, t0=0, n=T, _b=_LNXB):
        return _b[:, c * T + t0: c * T + t0 + n]
    def xss(c, t0=0, n=T, _b=_XSB):
        return _b[:, c * T + t0: c * T + t0 + n]
    E, H = 8, 512
    NF = H // 128
    gw = p.sb(pref + "gw", [128, CH * 8], BF16)
    rs = p.sb(pref + "rs", [128, 1], F32)
    WgT = p.sb(pref + "WgT", [128, NT * 8], F32)
    wrow = p.sb(pref + "wrow", [1, 512], F32)
    w1 = p.sb(pref + "w1", [128, CH * H], BF16)
    w2 = p.sb(pref + "w2", [128, NF * D], BF16)
    b1 = p.sb(pref + "b1", [1, H], F32)
    b2 = p.sb(pref + "b2", [1, D], F32)
    hb = p.sb(pref + "hb", [128, NF * 512], BF16)
    wbb = p.sb(pref + "wb", [128, 512], BF16)
    g8 = p.sb(pref + "g8", [128, 8], F32)
    m1 = p.sb(pref + "m1", [128, 1], F32)
    m2 = p.sb(pref + "m2", [128, 1], F32)
    sel = p.sb(pref + "sel", [128, 8], F32)
    selm = p.sb(pref + "selm", [128, 8], U8)
    neg = p.sb(pref + "neg", [128, 8], F32)
    den = p.sb(pref + "den", [128, 1], F32)
    ones_row_f = C["c_ones_row_f"]
    ones512_f = C["c_ones512_f"]
    ident_f = C["c_ident_f"]

    p.op("gpsimd", lambda e: e.dma_start(
        out=gw[:].rearrange("p (c e) -> p c e", e=8),
        in_=ins["L3_ffn_gate_W"].rearrange("(c p) e -> p c e", p=128)), dma=True)
    p.op("sync", lambda e: e.dma_start(out=rs[:], in_=ins["route_scale_col"][:, :]), dma=True)
    p.op("vector", lambda e: e.memset(neg[:], -1e30))

    ps = p.pf[0][:, 0:8]
    pr = p.pf[1][0:1, 0:128]
    for i in range(NT):
        t0 = i * 128
        for c in range(CH):
            p.op("tensor", lambda e, c=c, t0=t0: e.matmul(
                ps, lnxs(c, t0, 128), gw[:, c * 8:(c + 1) * 8],
                start=(c == 0), stop=(c == CH - 1)))
        p.op("scalar", lambda e: e.activation(g8[:], ps, AF.Sigmoid))
        p.op("vector", lambda e: e.tensor_scalar(g8[:], g8[:], rs[:, 0:1], None, ALU.mult))
        p.op("vector", lambda e: e.reduce_max(m1[:], g8[:], axis=AXX))
        p.op("vector", lambda e: e.tensor_scalar(selm[:], g8[:], m1[:, 0:1], None, ALU.is_lt))
        p.op("vector", lambda e: e.select(sel[:], selm[:], g8[:], neg[:]))
        p.op("vector", lambda e: e.reduce_max(m2[:], sel[:], axis=AXX))
        wg = WgT[:, i * 8:(i + 1) * 8]
        p.op("vector", lambda e: e.tensor_add(den[:], m1[:], m2[:]))
        p.op("vector", lambda e: e.reciprocal(den[:], den[:]))
        p.op("vector", lambda e, wg=wg: e.tensor_scalar(sel[:], g8[:], m2[:, 0:1], None, ALU.is_ge))
        p.op("vector", lambda e, wg=wg: e.tensor_mul(wg, g8[:], sel[:]))
        p.op("vector", lambda e, wg=wg: e.tensor_scalar(wg, wg, den[:, 0:1], None, ALU.mult))

    w1r = ins["L3_ffn_W1"].rearrange("e (c p) f -> e p c f", p=128)
    w2r = ins["L3_ffn_W2"].rearrange("e (c p) f -> e p c f", p=128)
    b1r = ins["L3_ffn_b1"].rearrange("e (u f) -> e u f", u=1)
    b2r = ins["L3_ffn_b2"].rearrange("e (u f) -> e u f", u=1)
    ws1r = ins["L3_ffn_Ws1"].rearrange("(c p) f -> p c f", p=128)
    ws2r = ins["L3_ffn_Ws2"].rearrange("(c p) f -> p c f", p=128)
    bankm = {}
    for ee in range(E + 1):
        shared = ee == E
        if shared:
            p.op("gpsimd", lambda e: e.dma_start(
                out=w1[:].rearrange("p (c f) -> p c f", f=H), in_=ws1r), dma=True)
            p.op("gpsimd", lambda e: e.dma_start(
                out=w2[:].rearrange("p (c f) -> p c f", f=D), in_=ws2r), dma=True)
            p.op("sync", lambda e: e.dma_start(
                out=b1[:], in_=ins["L3_ffn_bs1"].rearrange("(u f) -> u f", u=1)), dma=True)
            p.op("sync", lambda e: e.dma_start(
                out=b2[:], in_=ins["L3_ffn_bs2"].rearrange("(u f) -> u f", u=1)), dma=True)
        else:
            p.op("gpsimd", lambda e, ee=ee: e.dma_start(
                out=w1[:].rearrange("p (c f) -> p c f", f=H), in_=w1r[ee]), dma=True)
            p.op("gpsimd", lambda e, ee=ee: e.dma_start(
                out=w2[:].rearrange("p (c f) -> p c f", f=D), in_=w2r[ee]), dma=True)
            p.op("sync", lambda e, ee=ee: e.dma_start(out=b1[:], in_=b1r[ee]), dma=True)
            p.op("sync", lambda e, ee=ee: e.dma_start(out=b2[:], in_=b2r[ee]), dma=True)
        for tb in range(4):
            t0 = tb * 512
            if not shared:
                for j in range(4):
                    i = tb * 4 + j
                    p.op("tensor", lambda e, i=i, ee=ee: e.matmul(
                        pr, WgT[:, i * 8 + ee: i * 8 + ee + 1], ident_f[:],
                        start=True, stop=True))
                    p.op("vector", lambda e, j=j: e.tensor_copy(
                        out=wrow[:, j * 128:(j + 1) * 128], in_=pr))
                pwb = p.pf[2]
                p.op("tensor", lambda e: e.matmul(
                    pwb[:], ones_row_f[:], wrow[:], start=True, stop=True))
                p.op("vector", lambda e: e.tensor_copy(out=wbb[:], in_=pwb[:]))
                for bb in range(4):
                    bankm[bb] = p.mark()
            for oc in range(NF):
                ph = p.pf[oc % 2]
                p.op("tensor", lambda e, ph=ph, oc=oc: e.matmul(
                    ph[:], b1[:, oc * 128:(oc + 1) * 128], ones512_f[:],
                    start=True, stop=False),
                    wait=(None if oc < 2 else bankm.get(oc % 2)), inc=False)
                for c in range(CH):
                    p.op("tensor", lambda e, ph=ph, oc=oc, c=c, t0=t0: e.matmul(
                        ph[:], w1[:, c * H + oc * 128: c * H + oc * 128 + 128],
                        lnxs(c, t0, 512), start=False, stop=(c == CH - 1)),
                        wait=False, inc=(c == CH - 1))
                p.op("scalar", lambda e, ph=ph: e.activation(cm.SC1[:], ph[:], AF.Gelu))
                if shared:
                    p.op("vector", lambda e, oc=oc: e.tensor_copy(
                        out=hb[:, oc * 512:(oc + 1) * 512], in_=cm.SC1[:]))
                else:
                    p.op("vector", lambda e, oc=oc: e.tensor_mul(
                        hb[:, oc * 512:(oc + 1) * 512], cm.SC1[:], wbb[:]))
                bankm[oc % 2] = p.mark()
            for oc in range(CH):
                py = p.pf[2 + (oc % 2)]
                if shared:
                    p.op("tensor", lambda e, py=py, oc=oc: e.matmul(
                        py[:], b2[:, oc * 128:(oc + 1) * 128], ones512_f[:],
                        start=True, stop=False),
                        wait=(None if oc < 2 else bankm.get(2 + oc % 2)), inc=False)
                else:
                    p.op("tensor", lambda e, py=py, oc=oc: e.matmul(
                        py[:], b2[:, oc * 128:(oc + 1) * 128], wrow[:],
                        start=True, stop=False),
                        wait=(None if oc < 2 else bankm.get(2 + oc % 2)), inc=False)
                for fc in range(NF):
                    p.op("tensor", lambda e, py=py, oc=oc, fc=fc: e.matmul(
                        py[:], w2[:, fc * D + oc * 128: fc * D + (oc + 1) * 128],
                        hb[:, fc * 512:(fc + 1) * 512], start=False, stop=(fc == NF - 1)),
                        wait=False, inc=(fc == NF - 1))
                x = xss(oc, t0, 512)
                p.op("vector", lambda e, py=py, x=x: e.tensor_add(x, x, py[:]))
                bankm[2 + oc % 2] = p.mark()
    p.free_prefix(pref)


def emit_L2_ffn(cm: Common, pref="l2f"):
    p, C, ins = cm.p, cm.consts, cm.ins
    _LNXB = cm.LNX
    _XSB = cm.XS
    def lnxs(c, t0=0, n=T, _b=_LNXB):
        return _b[:, c * T + t0: c * T + t0 + n]
    def xss(c, t0=0, n=T, _b=_XSB):
        return _b[:, c * T + t0: c * T + t0 + n]
    E, H = 8, 2048
    gw = p.sb(pref + "gw", [128, CH * 8], BF16)
    WgT = p.sb(pref + "WgT", [128, NT * 8], F32)
    wrow = p.sb(pref + "wrow", [1, T], F32)
    w1 = p.sb(pref + "w1", [128, CH * (H // 2)], BF16)
    w2 = p.sb(pref + "w2", [128, CH * (H // 2)], BF16)
    w3 = p.sb(pref + "w3", [128, (H // 128) * D], BF16)
    hb = p.sb(pref + "hb", [128, (H // 128) * 512], BF16)
    wbb = p.sb(pref + "wb", [128, T], BF16)
    g8 = p.sb(pref + "g8", [128, 8], F32)
    m1 = p.sb(pref + "m1", [128, 1], F32)
    nm1 = p.sb(pref + "nm1", [128, 1], F32)
    m2 = p.sb(pref + "m2", [128, 1], F32)
    sel = p.sb(pref + "sel", [128, 8], F32)
    selm = p.sb(pref + "selm", [128, 8], U8)
    neg = p.sb(pref + "neg", [128, 8], F32)
    den = p.sb(pref + "den", [128, 1], F32)
    ones_row_f = C["c_ones_row_f"]
    ident_f = C["c_ident_f"]

    p.op("gpsimd", lambda e: e.dma_start(
        out=gw[:].rearrange("p (c e) -> p c e", e=8),
        in_=ins["L2_ffn_gate_w"].rearrange("(c p) e -> p c e", p=128)), dma=True)
    p.op("vector", lambda e: e.memset(neg[:], -1e30))

    ps = p.pf[0][:, 0:8]
    pr = p.pf[1][0:1, 0:128]
    for i in range(NT):
        t0 = i * 128
        for c in range(CH):
            p.op("tensor", lambda e, c=c, t0=t0: e.matmul(
                ps, lnxs(c, t0, 128), gw[:, c * 8:(c + 1) * 8],
                start=(c == 0), stop=(c == CH - 1)))
        p.op("vector", lambda e: e.tensor_copy(out=g8[:], in_=ps))
        p.op("vector", lambda e: e.reduce_max(m1[:], g8[:], axis=AXX))
        p.op("vector", lambda e: e.tensor_scalar_mul(nm1[:], m1[:], -1.0))
        p.op("vector", lambda e: e.tensor_scalar(selm[:], g8[:], m1[:, 0:1], None, ALU.is_lt))
        p.op("vector", lambda e: e.select(sel[:], selm[:], g8[:], neg[:]))
        p.op("vector", lambda e: e.reduce_max(m2[:], sel[:], axis=AXX))
        wg = WgT[:, i * 8:(i + 1) * 8]
        p.op("scalar", lambda e, wg=wg: e.activation(wg, g8[:], AF.Exp, bias=nm1[:, 0:1]))
        p.op("vector", lambda e: e.tensor_scalar(sel[:], g8[:], m2[:, 0:1], None, ALU.is_ge))
        p.op("vector", lambda e, wg=wg: e.tensor_mul(wg, wg, sel[:]))
        p.op("vector", lambda e: e.tensor_sub(den[:], m2[:], m1[:]))
        p.op("scalar", lambda e: e.activation(den[:], den[:], AF.Exp))
        p.op("vector", lambda e: e.tensor_scalar_add(den[:], den[:], 1.0))
        p.op("vector", lambda e: e.reciprocal(den[:], den[:]))
        p.op("vector", lambda e, wg=wg: e.tensor_scalar(wg, wg, den[:, 0:1], None, ALU.mult))

    w1r = ins["L2_ffn_ew1"].rearrange("e (c p) f -> e p c f", p=128)
    w2r = ins["L2_ffn_ew2"].rearrange("e (c p) f -> e p c f", p=128)
    w3r = ins["L2_ffn_ew3"].rearrange("e (c p) f -> e p c f", p=128)
    sw1r = ins["L2_ffn_sw1"].rearrange("(c p) f -> p c f", p=128)
    sw2r = ins["L2_ffn_sw2"].rearrange("(c p) f -> p c f", p=128)
    sw3r = ins["L2_ffn_sw3"].rearrange("(c p) f -> p c f", p=128)
    NF = H // 128
    HH = H // 2
    bankm = {0: None, 1: None, 2: None, 3: None}
    for ee in range(E + 1):
        shared = ee == E
        if shared:
            p.op("gpsimd", lambda e: e.dma_start(
                out=w3[:].rearrange("p (c f) -> p c f", f=D), in_=sw3r), dma=True)
        else:
            p.op("gpsimd", lambda e, ee=ee: e.dma_start(
                out=w3[:].rearrange("p (c f) -> p c f", f=D), in_=w3r[ee]), dma=True)
            for tb in range(4):
                for j in range(4):
                    i = tb * 4 + j
                    p.op("tensor", lambda e, i=i, ee=ee: e.matmul(
                        pr, WgT[:, i * 8 + ee: i * 8 + ee + 1], ident_f[:],
                        start=True, stop=True))
                    p.op("vector", lambda e, i=i: e.tensor_copy(
                        out=wrow[:, i * 128:(i + 1) * 128], in_=pr))
            pwb = p.pf[4]
            for tb in range(4):
                p.op("tensor", lambda e, tb=tb: e.matmul(
                    pwb[:], ones_row_f[:], wrow[:, tb * 512:(tb + 1) * 512],
                    start=True, stop=True))
                p.op("vector", lambda e, tb=tb: e.tensor_copy(
                    out=wbb[:, tb * 512:(tb + 1) * 512], in_=pwb[:]))
        for fh in range(2):
            if shared:
                p.op("gpsimd", lambda e, fh=fh: e.dma_start(
                    out=w1[:].rearrange("p (c f) -> p c f", f=HH),
                    in_=sw1r[:, :, fh * HH:(fh + 1) * HH]), dma=True)
                p.op("gpsimd", lambda e, fh=fh: e.dma_start(
                    out=w2[:].rearrange("p (c f) -> p c f", f=HH),
                    in_=sw2r[:, :, fh * HH:(fh + 1) * HH]), dma=True)
            else:
                p.op("gpsimd", lambda e, ee=ee, fh=fh: e.dma_start(
                    out=w1[:].rearrange("p (c f) -> p c f", f=HH),
                    in_=w1r[ee][:, :, fh * HH:(fh + 1) * HH]), dma=True)
                p.op("gpsimd", lambda e, ee=ee, fh=fh: e.dma_start(
                    out=w2[:].rearrange("p (c f) -> p c f", f=HH),
                    in_=w2r[ee][:, :, fh * HH:(fh + 1) * HH]), dma=True)
            for tb in range(4):
                t0 = tb * 512
                for oc in range(8):
                    b1i = (oc % 2) * 2
                    ph1 = p.pf[b1i]
                    ph2 = p.pf[b1i + 1]
                    for c in range(CH):
                        p.op("tensor", lambda e, ph1=ph1, oc=oc, c=c, t0=t0: e.matmul(
                            ph1[:], w1[:, c * HH + oc * 128: c * HH + oc * 128 + 128],
                            lnxs(c, t0, 512), start=(c == 0), stop=(c == CH - 1)),
                            wait=((None if (oc == 0 and tb == 0) else bankm[b1i]) if c == 0 else False), inc=(c == CH - 1))
                    for c in range(CH):
                        p.op("tensor", lambda e, ph2=ph2, oc=oc, c=c, t0=t0: e.matmul(
                            ph2[:], w2[:, c * HH + oc * 128: c * HH + oc * 128 + 128],
                            lnxs(c, t0, 512), start=(c == 0), stop=(c == CH - 1)),
                            wait=((None if (oc == 0 and tb == 0) else bankm[b1i + 1]) if c == 0 else False), inc=(c == CH - 1))
                    p.op("scalar", lambda e, ph1=ph1: e.activation(cm.SC1[:], ph1[:], AF.Silu))
                    hdst = hb[:, oc * 512:(oc + 1) * 512]
                    if shared:
                        p.op("vector", lambda e, ph2=ph2, hdst=hdst: e.tensor_mul(hdst, cm.SC1[:], ph2[:]))
                    else:
                        p.op("vector", lambda e, ph2=ph2: e.tensor_mul(cm.SC2[:], cm.SC1[:], ph2[:]))
                        p.op("vector", lambda e, hdst=hdst, t0=t0: e.tensor_mul(
                            hdst, cm.SC2[:], wbb[:, t0:t0 + 512]))
                    bankm[b1i] = p.mark()
                    bankm[b1i + 1] = p.mark()
                for oc in range(CH):
                    b2i = (oc % 2) * 2
                    py = p.pf[b2i]
                    for fc in range(8):
                        p.op("tensor", lambda e, py=py, oc=oc, fc=fc, fh=fh: e.matmul(
                            py[:], w3[:, (fh * 8 + fc) * D + oc * 128: (fh * 8 + fc) * D + (oc + 1) * 128],
                            hb[:, fc * 512:(fc + 1) * 512],
                            start=(fc == 0), stop=(fc == 7)),
                            wait=(bankm[b2i] if fc == 0 else False), inc=(fc == 7))
                    x = xss(oc, t0, 512)
                    p.op("vector", lambda e, py=py, x=x: e.tensor_add(x, x, py[:]))
                    bankm[b2i] = p.mark()
    p.free_prefix(pref)


def emit_L1_ffn(cm: Common, pref="l1f"):
    p, C, ins = cm.p, cm.consts, cm.ins
    _LNXB = cm.LNX
    _XSB = cm.XS
    def lnxs(c, t0=0, n=T, _b=_LNXB):
        return _b[:, c * T + t0: c * T + t0 + n]
    def xss(c, t0=0, n=T, _b=_XSB):
        return _b[:, c * T + t0: c * T + t0 + n]
    E, FF, R = 4, 4096, 4
    gwT = p.sb(pref + "gwT", [128, CH * 4], BF16)
    bgr = p.sb(pref + "bgr", [1, 4], BF16)
    WgT = p.sb(pref + "WgT", [128, NT * 4], F32)
    wrow = p.sb(pref + "wrow", [1, 512], F32)
    Alh = p.sb(pref + "Alh", [128, CH * 4 * E], BF16)
    Blh = p.sb(pref + "Blh", [4, FF // 2], BF16)
    lows = p.sb(pref + "lows", [4, T], BF16)
    Wi = p.sb(pref + "Wi", [128, CH * (FF // 2)], BF16)
    Wo = p.sb(pref + "Wo", [128, (FF // 2 // 128) * D], BF16)
    hb = p.sb(pref + "hb", [128, 16 * 512], BF16)
    wbb = p.sb(pref + "wb", [128, 512], BF16)
    g4 = p.sb(pref + "g4", [128, 4], F32)
    s4 = p.sb(pref + "s4", [128, 4], F32)
    m1 = p.sb(pref + "m1", [128, 1], F32)
    nm1 = p.sb(pref + "nm1", [128, 1], F32)
    m2 = p.sb(pref + "m2", [128, 1], F32)
    sum4 = p.sb(pref + "sum4", [128, 1], F32)
    sel = p.sb(pref + "sel", [128, 4], F32)
    selm = p.sb(pref + "selm", [128, 4], U8)
    neg = p.sb(pref + "neg", [128, 4], F32)
    ones_row_bf = C["c_ones_row_bf"]
    ones_row_f = C["c_ones_row_f"]
    ident_f = C["c_ident_f"]

    p.op("gpsimd", lambda e: e.dma_start(
        out=gwT[:].rearrange("p (c e) -> p c e", e=4),
        in_=ins["L1_ffn_WgT_host"].rearrange("(c p) e -> p c e", p=128)), dma=True)
    p.op("gpsimd", lambda e: e.dma_start(
        out=bgr[:], in_=ins["L1_ffn_bg"].rearrange("(u e) -> u e", u=1)), dma=True)
    p.op("gpsimd", lambda e: e.dma_start(
        out=Alh[:].rearrange("p (c r) -> p c r", r=4 * E),
        in_=ins["L1_ffn_AT_host"].rearrange("(c p) q -> p c q", p=128)), dma=True)
    p.op("vector", lambda e: e.memset(neg[:], -1e30))

    ps = p.pf[0][:, 0:4]
    pr = p.pf[1][0:1, 0:128]
    for i in range(NT):
        t0 = i * 128
        p.op("tensor", lambda e: e.matmul(ps, ones_row_bf[:], bgr[:], start=True, stop=False))
        for c in range(CH):
            p.op("tensor", lambda e, c=c, t0=t0: e.matmul(
                ps, lnxs(c, t0, 128), gwT[:, c * 4:(c + 1) * 4],
                start=False, stop=(c == CH - 1)))
        p.op("vector", lambda e: e.tensor_copy(out=g4[:], in_=ps))
        p.op("vector", lambda e: e.reduce_max(m1[:], g4[:], axis=AXX))
        p.op("vector", lambda e: e.tensor_scalar_mul(nm1[:], m1[:], -1.0))
        p.op("scalar", lambda e: e.activation(s4[:], g4[:], AF.Exp, bias=nm1[:, 0:1],
                                              accum_out=sum4[:]))
        p.op("vector", lambda e: e.reciprocal(sum4[:], sum4[:]))
        p.op("vector", lambda e: e.tensor_scalar(s4[:], s4[:], sum4[:, 0:1], None, ALU.mult))
        p.op("vector", lambda e: e.reduce_max(m1[:], s4[:], axis=AXX))
        p.op("vector", lambda e: e.tensor_scalar(selm[:], s4[:], m1[:, 0:1], None, ALU.is_lt))
        p.op("vector", lambda e: e.select(sel[:], selm[:], s4[:], neg[:]))
        p.op("vector", lambda e: e.reduce_max(m2[:], sel[:], axis=AXX))
        wg = WgT[:, i * 4:(i + 1) * 4]
        p.op("vector", lambda e, wg=wg: e.tensor_scalar(sel[:], s4[:], m2[:, 0:1], None, ALU.is_ge))
        p.op("vector", lambda e, wg=wg: e.tensor_mul(wg, s4[:], sel[:]))

    bankm = {}
    Wir = ins["L1_ffn_WiT_host"].rearrange("(c p) f -> p c f", p=128)
    Wor = ins["L1_ffn_WoT_host"].rearrange("(c p) d -> p c d", p=128)
    Br = ins["L1_ffn_BT_host"]
    HH = FF // 2
    for fh in range(2):
        p.op("gpsimd", lambda e, fh=fh: e.dma_start(
            out=Wi[:].rearrange("p (c f) -> p c f", f=HH),
            in_=Wir[:, :, fh * HH:(fh + 1) * HH]), dma=True)
        p.op("gpsimd", lambda e, fh=fh: e.dma_start(
            out=Wo[:].rearrange("p (c d) -> p c d", d=D),
            in_=Wor[:, fh * 16:(fh + 1) * 16, :]), dma=True)
        for ee in range(E):
            p.op("gpsimd", lambda e, ee=ee, fh=fh: e.dma_start(
                out=Blh[:], in_=Br[ee][:, fh * HH:(fh + 1) * HH]), dma=True)
            pl = p.pf[0][0:4, :]
            for tb in range(4):
                t0 = tb * 512
                for c in range(CH):
                    p.op("tensor", lambda e, ee=ee, c=c, t0=t0: e.matmul(
                        pl, lnxs(c, t0, 512),
                        Alh[:, c * 4 * E + ee * 4: c * 4 * E + ee * 4 + 4],
                        start=(c == 0), stop=(c == CH - 1))
                    if False else e.matmul(
                        pl, Alh[:, c * 4 * E + ee * 4: c * 4 * E + ee * 4 + 4],
                        lnxs(c, t0, 512), start=(c == 0), stop=(c == CH - 1)))
                p.op("vector", lambda e, t0=t0: e.tensor_copy(
                    out=lows[:, t0:t0 + 512], in_=pl))
            for tb in range(4):
                t0 = tb * 512
                for j in range(4):
                    i = tb * 4 + j
                    p.op("tensor", lambda e, i=i, ee=ee: e.matmul(
                        pr, WgT[:, i * 4 + ee: i * 4 + ee + 1], ident_f[:],
                        start=True, stop=True))
                    p.op("vector", lambda e, j=j: e.tensor_copy(
                        out=wrow[:, j * 128:(j + 1) * 128], in_=pr))
                pwb = p.pf[2]
                p.op("tensor", lambda e: e.matmul(
                    pwb[:], ones_row_f[:], wrow[:], start=True, stop=True))
                p.op("vector", lambda e: e.tensor_copy(out=wbb[:], in_=pwb[:]))
                for bb in range(4):
                    bankm[bb] = p.mark()
                for oc in range(16):
                    ph = p.pf[oc % 2]
                    for c in range(CH):
                        p.op("tensor", lambda e, ph=ph, oc=oc, c=c, t0=t0: e.matmul(
                            ph[:], Wi[:, c * HH + oc * 128: c * HH + oc * 128 + 128],
                            lnxs(c, t0, 512), start=(c == 0), stop=False),
                            wait=((None if oc < 2 else bankm.get(oc % 2)) if c == 0 else False),
                            inc=False)
                    p.op("tensor", lambda e, ph=ph, oc=oc, t0=t0: e.matmul(
                        ph[:], Blh[:, oc * 128:(oc + 1) * 128],
                        lows[:, t0:t0 + 512], start=False, stop=True),
                        wait=False, inc=True)
                    p.op("vector", lambda e, ph=ph: e.tensor_scalar_max(cm.SC1[:], ph[:], 0.0))
                    p.op("vector", lambda e, oc=oc: e.tensor_mul(
                        hb[:, oc * 512:(oc + 1) * 512], cm.SC1[:], wbb[:]))
                    bankm[oc % 2] = p.mark()
                for oc in range(CH):
                    py = p.pf[2 + (oc % 2)]
                    for fc in range(16):
                        p.op("tensor", lambda e, py=py, oc=oc, fc=fc: e.matmul(
                            py[:], Wo[:, fc * D + oc * 128: fc * D + (oc + 1) * 128],
                            hb[:, fc * 512:(fc + 1) * 512],
                            start=(fc == 0), stop=(fc == 15)),
                            wait=((None if oc < 2 else bankm.get(2 + oc % 2)) if fc == 0 else False),
                            inc=(fc == 15))
                    x = xss(oc, t0, 512)
                    p.op("vector", lambda e, py=py, x=x: e.tensor_add(x, x, py[:]))
                    bankm[2 + oc % 2] = p.mark()
    p.free_prefix(pref)


def emit_L1_attn(cm: Common, pref="l1a"):
    p, C, ins = cm.p, cm.consts, cm.ins
    _LNXB = cm.LNX
    _XSB = cm.XS
    def lnxs(c, t0=0, n=T, _b=_LNXB):
        return _b[:, c * T + t0: c * T + t0 + n]
    def xss(c, t0=0, n=T, _b=_XSB):
        return _b[:, c * T + t0: c * T + t0 + n]
    DI = 2048
    NC = 16
    ones_row_f = C["c_ones_row_f"]
    ones_col_f = C["c_ones_col_f"]

    xc_d = p.dram(pref + "xc_d", [DI, T], BF16)
    z_d = p.dram(pref + "z_d", [DI, T], BF16)
    y_d = p.dram(pref + "y_d", [DI, T], BF16)
    cc_in = p.dram(pref + "cc_in", [DI, NT * 3], BF16)
    cc_out = p.dram(pref + "cc_out", [2 * DI, NT * 3], BF16)

    cw = p.sb(pref + "cw", [128, NC * 4], F32)
    cb = p.sb(pref + "cb", [128, NC], F32)
    Dcol = p.sb(pref + "Dc", [128, NC], F32)
    mW0 = p.sb(pref + "mW0", [128, 1], F32)
    mW1 = p.sb(pref + "mW1", [128, 1], F32)
    Wq = p.sb(pref + "t1_Wq", [128, CH * 512], BF16)
    tails = p.sb(pref + "t1_tl", [128, CH * NT * 3], BF16)
    otails = p.sb(pref + "t1_otl", [128, NC * NT * 3], BF16)
    G = p.sb(pref + "t1_G", [128, 2 * NC * NT * 3], BF16)
    xbt = p.sb(pref + "t1_xbt", [128, NC * 512], BF16)
    xct = p.sb(pref + "t1_xct", [128, NC * 512], BF16)
    pad = p.sb(pref + "t1_pad", [128, 131], BF16)
    halo = p.sb(pref + "t1_halo", [128, 3], BF16)

    p.op("gpsimd", lambda e: e.dma_start(
        out=cw[:].rearrange("p (c k) -> p c k", k=4),
        in_=ins["L1_attn_conv_w"].rearrange("(c p) k -> p c k", p=128)), dma=True)
    p.op("sync", lambda e: e.dma_start(out=cb[:], in_=ins["conv_b_col"][:, :]), dma=True)
    p.op("sync", lambda e: e.dma_start(out=Dcol[:], in_=ins["D_col"][:, :]), dma=True)
    p.op("sync", lambda e: e.dma_start(out=mW0[:], in_=ins["c_mW0"][:, :]), dma=True)
    p.op("sync", lambda e: e.dma_start(out=mW1[:], in_=ins["c_mW1"][:, :]), dma=True)

    WinT = ins["L1_attn_W_inT_host"].rearrange("(c p) f -> p c f", p=128)

    for c in range(CH):
        for i in range(NT):
            p.op("vector", lambda e, c=c, i=i: e.tensor_copy(
                out=tails[:, c * NT * 3 + i * 3: c * NT * 3 + (i + 1) * 3],
                in_=lnxs(c, i * 128 + 125, 3)))
    for q in range(4):
        p.op("gpsimd", lambda e, q=q: e.dma_start(
            out=Wq[:].rearrange("p (c f) -> p c f", f=512),
            in_=WinT[:, :, q * 512:(q + 1) * 512]), dma=True)
        for oc in range(4):
            ps = p.pf[oc % 2][:, 0:NT * 3]
            for c in range(CH):
                p.op("tensor", lambda e, ps=ps, q=q, oc=oc, c=c: e.matmul(
                    ps, Wq[:, c * 512 + oc * 128: c * 512 + oc * 128 + 128],
                    tails[:, c * NT * 3:(c + 1) * NT * 3],
                    start=(c == 0), stop=(c == CH - 1)))
            p.op("vector", lambda e, ps=ps, q=q, oc=oc: e.tensor_copy(
                out=otails[:, (q * 4 + oc) * NT * 3:(q * 4 + oc + 1) * NT * 3], in_=ps))
    p.op("sync", lambda e: e.dma_start(
        out=cc_in.rearrange("(c p) t -> p c t", p=128),
        in_=otails[:].rearrange("p (c t) -> p c t", t=NT * 3)), dma=True)
    p.op_collective(lambda e: e.collective_compute(
        "AllGather", ALU.bypass, replica_groups=[[0, 1], [2, 3], [4, 5], [6, 7]],
        ins=[cc_in.opt()], outs=[cc_out.opt()]))
    p.op("sync", lambda e: e.dma_start(
        out=G[:].rearrange("p (q t) -> p q t", t=NT * 3),
        in_=cc_out.rearrange("(q p) t -> p q t", p=128)), dma=True)

    for tb in range(4):
        t0 = tb * 512
        for half in range(2):
            base_e = half * DI
            for q in range(4):
                p.op("gpsimd", lambda e, q=q, base_e=base_e: e.dma_start(
                    out=Wq[:].rearrange("p (c f) -> p c f", f=512),
                    in_=WinT[:, :, base_e + q * 512: base_e + (q + 1) * 512]), dma=True)
                for oc in range(4):
                    ps = p.pf[oc % 2]
                    for c in range(CH):
                        p.op("tensor", lambda e, ps=ps, q=q, oc=oc, c=c, t0=t0: e.matmul(
                            ps[:], Wq[:, c * 512 + oc * 128: c * 512 + oc * 128 + 128],
                            lnxs(c, t0, 512), start=(c == 0), stop=(c == CH - 1)))
                    dst = xbt[:, (q * 4 + oc) * 512:(q * 4 + oc + 1) * 512]
                    p.op("vector", lambda e, ps=ps, dst=dst: e.tensor_copy(out=dst, in_=ps[:]))
            if half == 1:
                p.op("sync", lambda e, t0=t0: e.dma_start(
                    out=z_d.rearrange("(c p) t -> p c t", p=128)[:, :, t0:t0 + 512],
                    in_=xbt[:].rearrange("p (c t) -> p c t", t=512)), dma=True)
                continue
            for j in range(4):
                i = tb * 4 + j
                for oc in range(NC):
                    xb_ij = xbt[:, oc * 512 + j * 128: oc * 512 + (j + 1) * 128]
                    g0 = G[:, 0 * NC * NT * 3 + oc * NT * 3 + i * 3:
                           0 * NC * NT * 3 + oc * NT * 3 + (i + 1) * 3]
                    p.op("vector", lambda e, g0=g0: e.tensor_scalar(
                        halo[:], g0, mW0[:, 0:1], None, ALU.mult))
                    if i > 0:
                        g1 = G[:, 1 * NC * NT * 3 + oc * NT * 3 + (i - 1) * 3:
                               1 * NC * NT * 3 + oc * NT * 3 + i * 3]
                        p.op("vector", lambda e, g1=g1: e.scalar_tensor_tensor(
                            halo[:], g1, mW1[:, 0:1], halo[:], ALU.mult, ALU.add))
                    p.op("vector", lambda e: e.tensor_copy(out=pad[:, 0:3], in_=halo[:]))
                    p.op("vector", lambda e, xb_ij=xb_ij: e.tensor_copy(
                        out=pad[:, 3:131], in_=xb_ij))
                    xc_ij = xct[:, oc * 512 + j * 128: oc * 512 + (j + 1) * 128]
                    p.op("vector", lambda e, xc_ij=xc_ij, oc=oc: e.tensor_scalar(
                        xc_ij, pad[:, 3:131], cw[:, oc * 4 + 3: oc * 4 + 4],
                        cb[:, oc:oc + 1], ALU.mult, ALU.add))
                    for k in range(3):
                        p.op("vector", lambda e, xc_ij=xc_ij, oc=oc, k=k: e.scalar_tensor_tensor(
                            xc_ij, pad[:, k:k + 128], cw[:, oc * 4 + k: oc * 4 + k + 1],
                            xc_ij, ALU.mult, ALU.add))
            for oc in range(NC):
                sl = xct[:, oc * 512:(oc + 1) * 512]
                p.op("scalar", lambda e, sl=sl: e.activation(sl, sl, AF.Silu))
            p.op("sync", lambda e, t0=t0: e.dma_start(
                out=xc_d.rearrange("(c p) t -> p c t", p=128)[:, :, t0:t0 + 512],
                in_=xct[:].rearrange("p (c t) -> p c t", t=512)), dma=True)
    p.free_prefix(pref + "t1")

    WxT = p.sb(pref + "t2_WxT", [128, NC * 2080], BF16)
    xcs = p.sb(pref + "t2_xcs", [128, NC * 512], BF16)
    zs = p.sb(pref + "t2_zs", [128, 512], BF16)
    bcp = p.sb(pref + "t2_bcp", [16, 512], F32)
    bcb = p.sb(pref + "t2_bcb", [128, 512], F32)
    delta = p.sb(pref + "t2_dl", [128, 512], F32)
    yt = p.sb(pref + "t2_yt", [128, 512], BF16)
    p.op("gpsimd", lambda e: e.dma_start(
        out=WxT[:].rearrange("p (c f) -> p c f", f=2080),
        in_=ins["L1_attn_W_xprojT_host"].rearrange("(c p) f -> p c f", p=128)), dma=True)
    for tb in range(4):
        t0 = tb * 512
        p.op("gpsimd", lambda e, t0=t0: e.dma_start(
            out=xcs[:].rearrange("p (c t) -> p c t", t=512),
            in_=xc_d.rearrange("(c p) t -> p c t", p=128)[:, :, t0:t0 + 512]), dma=True)
        pB = p.pf[2][0:16, :]
        pC = p.pf[3][0:16, :]
        for c in range(NC):
            p.op("tensor", lambda e, c=c: e.matmul(
                pB, WxT[:, c * 2080: c * 2080 + 16], xcs[:, c * 512:(c + 1) * 512],
                start=(c == 0), stop=(c == NC - 1)))
        for c in range(NC):
            p.op("tensor", lambda e, c=c: e.matmul(
                pC, WxT[:, c * 2080 + 16: c * 2080 + 32], xcs[:, c * 512:(c + 1) * 512],
                start=(c == 0), stop=(c == NC - 1)))
        p.op("vector", lambda e: e.tensor_copy(out=bcp[:], in_=pB))
        p.op("vector", lambda e: e.tensor_mul(bcp[:], bcp[:], pC))
        pS = p.pf[4][0:1, :]
        p.op("tensor", lambda e: e.matmul(pS, ones_col_f[0:16, :], bcp[:],
                                          start=True, stop=True))
        p.op("vector", lambda e: e.tensor_copy(out=cm.SROW[:, 0:512], in_=pS))
        pBC = p.pf[5]
        p.op("tensor", lambda e: e.matmul(pBC[:], ones_row_f[:], cm.SROW[:, 0:512],
                                          start=True, stop=True))
        p.op("vector", lambda e: e.tensor_copy(out=bcb[:], in_=pBC[:]))
        for oc in range(NC):
            pd = p.pf[oc % 2]
            for c in range(NC):
                p.op("tensor", lambda e, pd=pd, oc=oc, c=c: e.matmul(
                    pd[:], WxT[:, c * 2080 + 32 + oc * 128: c * 2080 + 32 + (oc + 1) * 128],
                    xcs[:, c * 512:(c + 1) * 512], start=(c == 0), stop=(c == NC - 1)))
            p.op("scalar", lambda e, pd=pd: e.activation(cm.SC1[:], pd[:], AF.Abs))
            p.op("scalar", lambda e: e.activation(cm.SC1[:], cm.SC1[:], AF.Exp, scale=-1.0))
            p.op("vector", lambda e: e.tensor_scalar_add(cm.SC1[:], cm.SC1[:], 1.0))
            p.op("scalar", lambda e: e.activation(cm.SC1[:], cm.SC1[:], AF.Ln))
            p.op("vector", lambda e, pd=pd: e.tensor_scalar_max(cm.SC2[:], pd[:], 0.0))
            p.op("vector", lambda e: e.tensor_add(delta[:], cm.SC1[:], cm.SC2[:]))
            xc_oc = xcs[:, oc * 512:(oc + 1) * 512]
            z_oc = zs[:]
            p.op("gpsimd", lambda e, oc=oc, t0=t0: e.dma_start(
                out=zs[:], in_=z_d[oc * 128:(oc + 1) * 128, t0:t0 + 512]), dma=True)
            p.op("vector", lambda e, xc_oc=xc_oc: e.tensor_mul(delta[:], delta[:], xc_oc))
            p.op("vector", lambda e: e.tensor_mul(delta[:], delta[:], bcb[:]))
            p.op("vector", lambda e, xc_oc=xc_oc, oc=oc: e.scalar_tensor_tensor(
                delta[:], xc_oc, Dcol[:, oc:oc + 1], delta[:], ALU.mult, ALU.add))
            p.op("scalar", lambda e, z_oc=z_oc: e.activation(cm.SC1[:], z_oc, AF.Silu))
            p.op("vector", lambda e: e.tensor_mul(yt[:], delta[:], cm.SC1[:]))
            p.op("sync", lambda e, oc=oc, t0=t0: e.dma_start(
                out=y_d[oc * 128:(oc + 1) * 128, t0:t0 + 512], in_=yt[:]), dma=True)
    p.free_prefix(pref + "t2")

    WoT = p.sb(pref + "t3_WoT", [128, NC * D], BF16)
    ys = p.sb(pref + "t3_ys", [128, NC * 512], BF16)
    p.op("gpsimd", lambda e: e.dma_start(
        out=WoT[:].rearrange("p (c f) -> p c f", f=D),
        in_=ins["L1_attn_W_outT_host"].rearrange("(c p) f -> p c f", p=128)), dma=True)
    for tb in range(4):
        t0 = tb * 512
        p.op("gpsimd", lambda e, t0=t0: e.dma_start(
            out=ys[:].rearrange("p (c t) -> p c t", t=512),
            in_=y_d.rearrange("(c p) t -> p c t", p=128)[:, :, t0:t0 + 512]), dma=True)
        for oc in range(CH):
            py = p.pf[2 + (oc % 2)]
            for c in range(NC):
                p.op("tensor", lambda e, py=py, oc=oc, c=c: e.matmul(
                    py[:], WoT[:, c * D + oc * 128: c * D + (oc + 1) * 128],
                    ys[:, c * 512:(c + 1) * 512], start=(c == 0), stop=(c == NC - 1)))
            x = xss(oc, t0, 512)
            p.op("vector", lambda e, py=py, x=x: e.tensor_add(x, x, py[:]))
    p.free_prefix(pref)


def emit_L3_attn(cm: Common, pref="l3a"):
    p, C, ins = cm.p, cm.consts, cm.ins
    _LNXB = cm.LNX
    _XSB = cm.XS
    def lnxs(c, t0=0, n=T, _b=_LNXB):
        return _b[:, c * T + t0: c * T + t0 + n]
    def xss(c, t0=0, n=T, _b=_XSB):
        return _b[:, c * T + t0: c * T + t0 + n]
    xin_d = cm.x_in
    xout_d = cm.x_out
    NH = 16
    ones_row_f = C["c_ones_row_f"]
    ones_row_bf = C["c_ones_row_bf"]
    ones_col_f = C["c_ones_col_f"]
    ones_col_bf = C["c_ones_col_bf"]
    ident_bf = C["c_ident_bf"]
    RG = [[0, 1], [2, 3], [4, 5], [6, 7]]

    ccK_in = p.dram(pref + "ccK_in", [D + 128, T], BF16)
    ccK_out = p.dram(pref + "ccK_out", [2 * (D + 128), T], BF16)
    ccV_in = p.dram(pref + "ccV_in", [T, D], BF16)
    ccV_out = p.dram(pref + "ccV_out", [2 * T, D], BF16)

    QB = p.sb(pref + "QB", [128, CH * T], BF16)
    ATT = p.sb(pref + "ATT", [128, CH * T], BF16)
    cosb = p.sb(pref + "cos", [128, T], BF16)
    sinb = p.sb(pref + "sin", [128, T], BF16)
    NM16 = p.sb(pref + "NM16", [128, 16], F32)
    mska = p.sb(pref + "mska", [128, 512], BF16)
    mskb = p.sb(pref + "mskb", [128, 512], BF16)

    p.op("gpsimd", lambda e: e.dma_start(out=cosb[:], in_=ins["c_l3cos"][:, :]), dma=True)
    p.op("gpsimd", lambda e: e.dma_start(out=sinb[:], in_=ins["c_l3sin"][:, :]), dma=True)
    p.op("gpsimd", lambda e: e.dma_start(out=mska[:], in_=ins["c_l3m_a4"][:, :]), dma=True)
    p.op("gpsimd", lambda e: e.dma_start(out=mskb[:], in_=ins["c_l3m_b4"][:, :]), dma=True)
    if DBG.get("l3_stop") == "Bz":
        p.free_prefix(pref)
        return

    WT = p.sb(pref + "t1_WT", [128, CH * D], BF16)
    brow = p.sb(pref + "t1_brow", [1, D], F32)
    KVB = p.sb(pref + "t1_KVB", [128, CH * T], BF16)

    def proj_fm(wname, bname, dstbuf):
        p.op("gpsimd", lambda e, wname=wname: e.dma_start(
            out=WT[:].rearrange("p (c f) -> p c f", f=D),
            in_=ins[wname].rearrange("(c p) f -> p c f", p=128)), dma=True)
        p.op("sync", lambda e, bname=bname: e.dma_start(
            out=brow[:], in_=ins[bname].rearrange("(u f) -> u f", u=1)), dma=True)
        bm = {}
        for oc in range(CH):
            for tb in range(4):
                t0 = tb * 512
                ph = p.pf[tb % 2]
                first = (oc == 0 and tb < 2)
                p.op("tensor", lambda e, ph=ph, oc=oc: e.matmul(
                    ph[:], brow[:, oc * 128:(oc + 1) * 128], C["c_ones512_f"][:],
                    start=True, stop=False),
                    wait=(None if first else bm.get(tb % 2)), inc=False)
                for c in range(CH):
                    p.op("tensor", lambda e, ph=ph, oc=oc, c=c, t0=t0: e.matmul(
                        ph[:], WT[:, c * D + oc * 128: c * D + (oc + 1) * 128],
                        lnxs(c, t0, 512), start=False, stop=(c == CH - 1)),
                        wait=False, inc=(c == CH - 1))
                p.op("vector", lambda e, ph=ph, oc=oc, t0=t0, dstbuf=dstbuf: e.tensor_copy(
                    out=dstbuf[:, oc * T + t0: oc * T + t0 + 512], in_=ph[:]))
                bm[tb % 2] = p.mark()

    def rope(buf):
        for h in range(NH):
            c, off = h // 2, (h % 2) * 64
            for tb in range(4):
                t0 = tb * 512
                a = buf[off:off + 32, c * T + t0: c * T + t0 + 512]
                b = buf[off + 32:off + 64, c * T + t0: c * T + t0 + 512]
                ca = cosb[off:off + 32, t0:t0 + 512]
                sa = sinb[off:off + 32, t0:t0 + 512]
                cb2 = cosb[off + 32:off + 64, t0:t0 + 512]
                sb2 = sinb[off + 32:off + 64, t0:t0 + 512]
                u = cm.SC1[off:off + 32, :]
                v = cm.SC2[off:off + 32, :]
                u2 = cm.SC1[off + 32:off + 64, :]
                v2 = cm.SC2[off + 32:off + 64, :]
                p.op("vector", lambda e, u2=u2, a=a: e.tensor_copy(out=u2, in_=a))
                p.op("vector", lambda e, v=v, b=b: e.tensor_copy(out=v, in_=b))
                p.op("vector", lambda e, u=u, a=a, ca=ca: e.tensor_mul(u, a, ca))
                p.op("vector", lambda e, v=v, sa=sa: e.tensor_mul(v, v, sa))
                p.op("vector", lambda e, u=u, v=v: e.tensor_sub(u, u, v))
                p.op("vector", lambda e, v2=v2, b=b, cb2=cb2: e.tensor_mul(v2, b, cb2))
                p.op("vector", lambda e, u2=u2, sb2=sb2: e.tensor_mul(u2, u2, sb2))
                p.op("vector", lambda e, v2=v2, u2=u2: e.tensor_add(v2, v2, u2))
                p.op("vector", lambda e, a=a, u=u: e.tensor_copy(out=a, in_=u))
                p.op("vector", lambda e, b=b, v2=v2: e.tensor_copy(out=b, in_=v2))
    proj_fm("L3_attn_WqT_host", "L3_attn_bq", QB)
    if DBG.get("l3_dump_q") == "prerope":
        qd = p.dram(pref + "q_d", [D, T], F32, kind="ExternalOutput")
        for c in range(CH):
            for tb in range(4):
                p.op("vector", lambda e, c=c, tb=tb: e.tensor_copy(
                    out=cm.SC1[:], in_=QB[:, c * T + tb * 512: c * T + tb * 512 + 512]))
                p.op("sync", lambda e, c=c, tb=tb: e.dma_start(
                    out=qd[c * 128:(c + 1) * 128, tb * 512:(tb + 1) * 512], in_=cm.SC1[:]), dma=True)
    if DBG.get("l3_stop") == "B0":
        p.free_prefix(pref)
        return
    rope(QB)
    if DBG.get("l3_dump_q") == "postrope":
        qd = p.dram(pref + "q_d", [D, T], F32, kind="ExternalOutput")
        for c in range(CH):
            for tb in range(4):
                p.op("vector", lambda e, c=c, tb=tb: e.tensor_copy(
                    out=cm.SC1[:], in_=QB[:, c * T + tb * 512: c * T + tb * 512 + 512]))
                p.op("sync", lambda e, c=c, tb=tb: e.dma_start(
                    out=qd[c * 128:(c + 1) * 128, tb * 512:(tb + 1) * 512], in_=cm.SC1[:]), dma=True)
    if DBG.get("l3_stop") == "B0r":
        p.free_prefix(pref)
        return
    proj_fm("L3_attn_WkT_host", "L3_attn_bk", KVB)
    rope(KVB)
    if DBG.get("l3_stop") == "B1":
        p.free_prefix(pref)
        return
    pn2 = p.pf[4][0:2, :]
    half_ones = C["c_half_ones"]
    kqmax = p.sb(pref + "t1_kqm", [2, 16], F32)
    p.op("vector", lambda e: e.memset(kqmax[:], 0.0))
    for which, buf in ((0, KVB), (1, QB)):
        for tb in range(4):
            t0 = tb * 512
            for c in range(CH):
                sl = buf[:, c * T + t0: c * T + t0 + 512]
                p.op("scalar", lambda e, sl=sl: e.activation(cm.SC1[:], sl, AF.Square))
                p.op("tensor", lambda e: e.matmul(pn2, half_ones[:], cm.SC1[:],
                                                  start=True, stop=True))
                p.op("vector", lambda e: e.tensor_copy(out=cm.SC2[0:2, :], in_=pn2))
                red = cm.SC2[0:2, 512 - 1:512] if False else None
                p.op("vector", lambda e: e.reduce_max(cm.SROW[0:2, 0:1] if False else cm.SC1[0:2, 0:1], cm.SC2[0:2, :], axis=AXX))
                dcol = kqmax[:, which * 8 + c: which * 8 + c + 1]
                p.op("vector", lambda e, dcol=dcol: e.tensor_max(dcol, dcol, cm.SC1[0:2, 0:1]))
    for c in range(CH):
        p.op("sync", lambda e, c=c: e.dma_start(
            out=ccK_in[c * 128:(c + 1) * 128, :], in_=KVB[:, c * T:(c + 1) * T]), dma=True)
    p.op("gpsimd", lambda e: e.dma_start(out=ccK_in[D:D + 2, 0:16], in_=kqmax[:]), dma=True)
    p.op("gpsimd", lambda e: e.dma_start(
        out=WT[:].rearrange("p (c f) -> p c f", f=D),
        in_=ins["L3_attn_WvT_host"].rearrange("(c p) f -> p c f", p=128)), dma=True)
    p.op("sync", lambda e: e.dma_start(
        out=brow[:], in_=ins["L3_attn_bv"].rearrange("(u f) -> u f", u=1)), dma=True)
    bmv = {}
    for i in range(NT):
        t0 = i * 128
        for ob in range(2):
            ph = p.pf[ob % 2]
            p.op("tensor", lambda e, ph=ph, ob=ob: e.matmul(
                ph[:], ones_row_f[:],
                brow[:, ob * 512:(ob + 1) * 512], start=True, stop=False),
                wait=(None if i == 0 else bmv.get(ob % 2)), inc=False)
            for c in range(CH):
                p.op("tensor", lambda e, ph=ph, ob=ob, c=c, t0=t0: e.matmul(
                    ph[:], lnxs(c, t0, 128),
                    WT[:, c * D + ob * 512: c * D + (ob + 1) * 512],
                    start=False, stop=(c == CH - 1)),
                    wait=False, inc=(c == CH - 1))
            p.op("vector", lambda e, ph=ph, i=i, ob=ob: e.tensor_copy(
                out=KVB[:, i * D + ob * 512: i * D + (ob + 1) * 512], in_=ph[:]))
            bmv[ob % 2] = p.mark()
        p.op("sync", lambda e, i=i: e.dma_start(
            out=ccV_in[i * 128:(i + 1) * 128, :], in_=KVB[:, i * D:(i + 1) * D]), dma=True)
    if DBG.get("l3_stop") == "B2":
        p.free_prefix(pref)
        return
    p.op_collective(lambda e: e.collective_compute(
        "AllGather", ALU.bypass, replica_groups=RG,
        ins=[ccK_in.opt()], outs=[ccK_out.opt()]))
    p.op_collective(lambda e: e.collective_compute(
        "AllGather", ALU.bypass, replica_groups=RG,
        ins=[ccV_in.opt()], outs=[ccV_out.opt()]))
    if DBG.get("l3_stop") == "B3":
        p.free_prefix(pref)
        return
    KD = D + 128
    gn = p.sb(pref + "t1_gn", [2, 32], F32)
    nmr = p.sb(pref + "t1_nmr", [1, 16], F32)
    p.op("gpsimd", lambda e: e.dma_start(out=gn[:, 0:16], in_=ccK_out[D:D + 2, 0:16]), dma=True)
    p.op("gpsimd", lambda e: e.dma_start(out=gn[:, 16:32], in_=ccK_out[KD + D:KD + D + 2, 0:16]), dma=True)
    p.op("vector", lambda e: e.tensor_max(gn[:, 0:16], gn[:, 0:16], gn[:, 16:32]))
    p.op("vector", lambda e: e.tensor_max(gn[:, 0:8], gn[:, 0:8], kqmax[:, 0:8]))
    p.op("scalar", lambda e: e.activation(gn[:, 0:8], gn[:, 0:8], AF.Sqrt))
    p.op("scalar", lambda e: e.activation(gn[:, 8:16], kqmax[:, 8:16], AF.Sqrt))
    p.op("vector", lambda e: e.tensor_mul(gn[:, 0:8], gn[:, 0:8], gn[:, 8:16]))
    p.op("vector", lambda e: e.tensor_scalar_mul(gn[:, 0:8], gn[:, 0:8], -1.0 / 8.0))
    pr2 = p.pf[5][0:1, 0:2]
    for c in range(CH):
        p.op("tensor", lambda e, c=c: e.matmul(
            pr2, gn[:, c:c + 1], C["c_ident_f"][0:2, 0:2], start=True, stop=True))
        p.op("vector", lambda e, c=c: e.tensor_copy(out=nmr[:, 2 * c:2 * c + 2], in_=pr2))
    pm = p.pf[4][:, 0:16]
    p.op("tensor", lambda e: e.matmul(pm, ones_row_f[:], nmr[:], start=True, stop=True))
    p.op("vector", lambda e: e.tensor_copy(out=NM16[:], in_=pm))
    if DBG.get("l3_dump_kv"):
        kd = p.dram(pref + "k_d", [2 * (D + 128), 256], F32, kind="ExternalOutput")
        vd = p.dram(pref + "v_d", [512, D], F32, kind="ExternalOutput")
        kbuf = p.sb(pref + "t9_kb", [128, 256], BF16)
        for r in range(2):
            for c in range(9):
                p.op("gpsimd", lambda e, r=r, c=c: e.dma_start(
                    out=kbuf[:], in_=ccK_out[r * 1152 + c * 128: r * 1152 + (c + 1) * 128, 0:256]), dma=True)
                p.op("vector", lambda e: e.tensor_copy(out=cm.SC1[:, 0:256], in_=kbuf[:]))
                p.op("sync", lambda e, r=r, c=c: e.dma_start(
                    out=kd[r * 1152 + c * 128: r * 1152 + (c + 1) * 128, :], in_=cm.SC1[:, 0:256]), dma=True)
        vbuf = p.sb(pref + "t9_vb", [128, D], BF16)
        for blk in range(4):
            src_r = [0, 0, 1, 1][blk]; src_lt = [0, 1, 0, 1][blk]
            p.op("gpsimd", lambda e, src_r=src_r, src_lt=src_lt: e.dma_start(
                out=vbuf[:], in_=ccV_out[(src_r * NT + src_lt) * 128:(src_r * NT + src_lt + 1) * 128, :]), dma=True)
            for half in range(2):
                p.op("vector", lambda e, half=half: e.tensor_copy(out=cm.SC1[:], in_=vbuf[:, half * 512:(half + 1) * 512]))
                p.op("sync", lambda e, blk=blk, half=half: e.dma_start(
                    out=vd[blk * 128:(blk + 1) * 128, half * 512:(half + 1) * 512], in_=cm.SC1[:]), dma=True)
        p.free_prefix(pref + "t9")
    p.free_prefix(pref + "t1")
    if DBG.get("l3_stop") == "B":
        p.free_prefix(pref)
        return

    KCH2 = [p.sb(pref + "t2_KCHa", [128, CH * 128], BF16),
            p.sb(pref + "t2_KCHb", [128, CH * 128], BF16)]
    VCH2 = [p.sb(pref + "t2_VCHa", [128, D], BF16),
            p.sb(pref + "t2_VCHb", [128, D], BF16)]
    PT = p.sb(pref + "t2_PT", [128, 4 * 512], BF16)
    SUMS = p.sb(pref + "t2_SUMS", [128, NT], F32)
    RES = p.sb(pref + "t2_RES", [128, 1024], BF16)
    rcp = p.sb(pref + "t2_rcp", [128, 16], F32)
    ccK4 = ccK_out.rearrange("(r c p) t -> r p c t", p=128, c=9)
    ccV3 = ccV_out.rearrange("(q p) o -> q p o", p=128)
    usemark = [None, None]
    for i in range(NT):
        q0 = i * 128
        pAV = [p.pf[2], p.pf[3]]
        psums = p.pf[4][:, 0:16]
        ngt = 2 * i + 2
        for gt in range(ngt):
            r, lt = gt % 2, gt // 2
            par = gt % 2
            KCH = KCH2[par]
            VCH = VCH2[par]
            p.op("gpsimd", lambda e, r=r, lt=lt, KCH=KCH: e.dma_start(
                out=KCH[:].rearrange("p (c t) -> p c t", t=128),
                in_=ccK4[r][:, 0:CH, lt * 128:(lt + 1) * 128]), dma=True)
            p.op("gpsimd", lambda e, r=r, lt=lt, VCH=VCH: e.dma_start(
                out=VCH[:], in_=ccV_out[(r * NT + lt) * 128:(r * NT + lt + 1) * 128, :]), dma=True)
            for bk in range(4):
                pst = p.pf[bk % 2]
                for hh in range(4):
                    h = bk * 4 + hh
                    c, off = h // 2, (h % 2) * 64
                    p.op("tensor", lambda e, pst=pst, hh=hh, c=c, off=off, q0=q0, KCH=KCH: e.matmul(
                        pst[:, hh * 128:(hh + 1) * 128],
                        KCH[off + 0:off + 64, c * 128:(c + 1) * 128],
                        QB[off + 0:off + 64, c * T + q0: c * T + q0 + 128],
                        start=(hh == 0), stop=(hh == 3)))
                dst = PT[:, bk * 512:(bk + 1) * 512]
                for hh in range(4):
                    h = bk * 4 + hh
                    p.op("scalar", lambda e, pst=pst, dst=dst, hh=hh, h=h: e.activation(
                        dst[:, hh * 128:(hh + 1) * 128], pst[:, hh * 128:(hh + 1) * 128],
                        AF.Exp, bias=NM16[:, h:h + 1], scale=0.125))
                if gt == 2 * i:
                    p.op("vector", lambda e, dst=dst: e.tensor_mul(dst, dst, mska[:]))
                elif gt == 2 * i + 1:
                    p.op("vector", lambda e, dst=dst: e.tensor_mul(dst, dst, mskb[:]))
            for h in range(NH):
                pav = pAV[h // 8]
                slot = h % 8
                pth = PT[:, h * 128:(h + 1) * 128]
                p.op("tensor", lambda e, pav=pav, slot=slot, pth=pth, h=h, gt=gt, VCH=VCH: e.matmul(
                    pav[:, slot * 64:(slot + 1) * 64], pth, VCH[:, h * 64:(h + 1) * 64],
                    start=(gt == 0 and slot == 0), stop=(gt == ngt - 1 and slot == 7)))
                p.op("tensor", lambda e, pth=pth, h=h, gt=gt: e.matmul(
                    psums[:, h:h + 1], pth, ones_col_bf[:],
                    start=(gt == 0 and h == 0), stop=(gt == ngt - 1 and h == NH - 1)))
            usemark[par] = p.mark()
        p.op("vector", lambda e: e.tensor_scalar_add(rcp[:], psums, 1e-30))
        p.op("vector", lambda e: e.reciprocal(rcp[:], rcp[:]))
        for h in range(NH):
            pav = pAV[h // 8]
            slot = h % 8
            p.op("vector", lambda e, pav=pav, slot=slot, h=h: e.tensor_scalar(
                RES[:, h * 64:(h + 1) * 64], pav[:, slot * 64:(slot + 1) * 64],
                rcp[:, h:h + 1], None, ALU.mult))
        for c in range(CH):
            ptr = p.pb[0][:, 0:128]
            p.op("tensor", lambda e, c=c: e.tensor.transpose(ptr, RES[:, c * 128:(c + 1) * 128], ident_bf[:])
                 if False else e.transpose(ptr, RES[:, c * 128:(c + 1) * 128], ident_bf[:]))
            p.op("vector", lambda e, c=c, q0=q0: e.tensor_copy(
                out=ATT[:, c * T + q0: c * T + q0 + 128], in_=ptr))
    if DBG.get("l3_dump"):
        att_d = p.dram(pref + "att_d", [D, T], F32, kind="ExternalOutput")
        for c in range(CH):
            p.op("vector", lambda e, c=c: e.tensor_copy(out=cm.SC1[:], in_=ATT[:, c * T: c * T + 512]))
            p.op("sync", lambda e, c=c: e.dma_start(out=att_d[c * 128:(c + 1) * 128, 0:512], in_=cm.SC1[:]), dma=True)
        for c in range(CH):
            for tb in range(1, 4):
                p.op("vector", lambda e, c=c, tb=tb: e.tensor_copy(
                    out=cm.SC1[:], in_=ATT[:, c * T + tb * 512: c * T + tb * 512 + 512]))
                p.op("sync", lambda e, c=c, tb=tb: e.dma_start(
                    out=att_d[c * 128:(c + 1) * 128, tb * 512:(tb + 1) * 512], in_=cm.SC1[:]), dma=True)
    p.free_prefix(pref + "t2")
    if DBG.get("l3_stop") == "C":
        p.free_prefix(pref)
        return

    WT2 = p.sb(pref + "t3_WT", [128, CH * D], BF16)
    brow2 = p.sb(pref + "t3_brow", [1, D], F32)
    xsl = p.sb(pref + "t3_xsl", [128, 512], F32)
    p.op("gpsimd", lambda e: e.dma_start(
        out=WT2[:].rearrange("p (c f) -> p c f", f=D),
        in_=ins["L3_attn_WoT_host"].rearrange("(c p) f -> p c f", p=128)), dma=True)
    p.op("sync", lambda e: e.dma_start(
        out=brow2[:], in_=ins["L3_attn_bo"].rearrange("(u f) -> u f", u=1)), dma=True)
    for oc in range(CH):
        for tb in range(4):
            t0 = tb * 512
            ph = p.pf[tb % 2]
            p.op("tensor", lambda e, ph=ph, oc=oc: e.matmul(
                ph[:], brow2[:, oc * 128:(oc + 1) * 128], C["c_ones512_f"][:],
                start=True, stop=False))
            for c in range(CH):
                p.op("tensor", lambda e, ph=ph, oc=oc, c=c, t0=t0: e.matmul(
                    ph[:], WT2[:, c * D + oc * 128: c * D + (oc + 1) * 128],
                    ATT[:, c * T + t0: c * T + t0 + 512], start=False, stop=(c == CH - 1)))
            p.op("sync", lambda e, oc=oc, t0=t0: e.dma_start(
                out=xsl[:], in_=xin_d[oc * 128:(oc + 1) * 128, t0:t0 + 512]), dma=True)
            p.op("vector", lambda e, ph=ph: e.tensor_add(xsl[:], xsl[:], ph[:]))
            p.op("sync", lambda e, oc=oc, t0=t0: e.dma_start(
                out=xout_d[oc * 128:(oc + 1) * 128, t0:t0 + 512], in_=xsl[:]), dma=True)
    p.free_prefix(pref)


def emit_L2_attn(cm: Common, pref="l2a"):
    p, C, ins = cm.p, cm.consts, cm.ins
    xin_d = cm.x_in
    xout_d = cm.x_out
    ones_row_f = C["c_ones_row_f"]
    ones_col_f = C["c_ones_col_f"]
    ones_col_bf = C["c_ones_col_bf"]
    ones512_f = C["c_ones512_f"]
    ident_f = C["c_ident_f"]
    RG = [[0, 1], [2, 3], [4, 5], [6, 7]]
    KD = D + 128

    ccK_in = p.dram(pref + "ccK_in", [KD, T], BF16)
    ccK_out = p.dram(pref + "ccK_out", [2 * KD, T], BF16)
    ccV_in = p.dram(pref + "ccV_in", [T, D], BF16)
    ccV_out = p.dram(pref + "ccV_out", [2 * T, D], BF16)
    isc_d = p.dram(pref + "isc_d", [T, S], FP16)

    QB = p.sb(pref + "QB", [128, CH * T], BF16)
    NM = p.sb(pref + "NM", [128, 1], F32)
    THR = p.sb(pref + "THR", [128, NT], F32)

    iqB = p.sb(pref + "t2_iq", [128, 2 * T], BF16)
    iq2 = p.sb(pref + "t2_iq2", [64, 4 * T], BF16)
    IWT = p.sb(pref + "t2_iwt", [128, NT * 4], F32)
    ikg = p.sb(pref + "t2_ikg", [64, S], BF16)
    kqn = p.sb(pref + "t2_kqn", [1, T], F32)

    cm.LNX = p.sb("z2_LNX", [128, CH * T], BF16)
    _LNXB = cm.LNX
    _XSB = cm.XS
    def lnxs(c, t0=0, n=T, _b=_LNXB):
        return _b[:, c * T + t0: c * T + t0 + n]
    def xss(c, t0=0, n=T, _b=_XSB):
        return _b[:, c * T + t0: c * T + t0 + n]
    cm.XS = p.sb("z3_XS", [128, CH * T], F32)
    cm.load_X(xin_d)
    cm.layernorm()
    p.free_prefix("z3_XS")

    WT = p.sb(pref + "t1_WT", [128, CH * D], BF16)
    brow = p.sb(pref + "t1_brow", [1, D], F32)
    KVB = p.sb(pref + "t1_KVB", [128, CH * T], BF16)
    cos2 = p.sb(pref + "t1_cos", [128, 4 * T], BF16)
    sin2 = p.sb(pref + "t1_sin", [128, 4 * T], BF16)

    p.op("gpsimd", lambda e: e.dma_start(out=cos2[:], in_=ins["c_l2cos"][:, :]), dma=True)
    p.op("gpsimd", lambda e: e.dma_start(out=sin2[:], in_=ins["c_l2sin"][:, :]), dma=True)

    def proj_fm(wname, brow_src, dstbuf, fw):
        p.op("gpsimd", lambda e, wname=wname, fw=fw: e.dma_start(
            out=WT[:, 0:fw * CH].rearrange("p (c f) -> p c f", f=fw),
            in_=ins[wname].rearrange("(c p) f -> p c f", p=128)), dma=True)
        p.op("sync", lambda e, fw=fw: e.dma_start(
            out=brow[:, 0:fw], in_=brow_src), dma=True)
        ocs = (fw + 127) // 128
        bm = {}
        for oc in range(ocs):
            npart = min(128, fw - oc * 128)
            for tb in range(4):
                t0 = tb * 512
                ph = p.pf[tb % 2]
                dst_ps = ph[0:npart, :]
                first = (oc == 0 and tb < 2)
                p.op("tensor", lambda e, dst_ps=dst_ps, oc=oc, npart=npart: e.matmul(
                    dst_ps, brow[:, oc * 128: oc * 128 + npart], ones512_f[:],
                    start=True, stop=False),
                    wait=(None if first else bm.get(tb % 2)), inc=False)
                for c in range(CH):
                    p.op("tensor", lambda e, dst_ps=dst_ps, oc=oc, c=c, t0=t0, fw=fw, npart=npart: e.matmul(
                        dst_ps, WT[:, c * fw + oc * 128: c * fw + oc * 128 + npart],
                        lnxs(c, t0, 512), start=False, stop=(c == CH - 1)),
                        wait=False, inc=(c == CH - 1))
                p.op("vector", lambda e, dst_ps=dst_ps, oc=oc, t0=t0, npart=npart, dstbuf=dstbuf: e.tensor_copy(
                    out=dstbuf[0:npart, oc * T + t0: oc * T + t0 + 512], in_=dst_ps))
                bm[tb % 2] = p.mark()

    def rope2(buf):
        for c in range(4):
            for tb in range(4):
                t0 = tb * 512
                a = buf[:, c * T + t0: c * T + t0 + 512]
                b = buf[:, (c + 4) * T + t0: (c + 4) * T + t0 + 512]
                cs = cos2[:, c * T + t0: c * T + t0 + 512]
                sn = sin2[:, c * T + t0: c * T + t0 + 512]
                p.op("vector", lambda e, a=a, cs=cs: e.tensor_mul(cm.SC1[:], a, cs))
                p.op("vector", lambda e, b=b, sn=sn: e.tensor_mul(cm.SC2[:], b, sn))
                p.op("vector", lambda e: e.tensor_sub(cm.SC1[:], cm.SC1[:], cm.SC2[:]))
                p.op("vector", lambda e, a=a, sn=sn: e.tensor_mul(cm.SC2[:], a, sn))
                p.op("vector", lambda e, a=a: e.tensor_copy(out=a, in_=cm.SC1[:]))
                p.op("vector", lambda e, b=b, cs=cs: e.tensor_mul(cm.SC1[:], b, cs))
                p.op("vector", lambda e, b=b: e.tensor_add(b, cm.SC1[:], cm.SC2[:]))

    proj_fm("L2_wqT_host", ins["b_q_row"][:, :], QB, D)
    rope2(QB)
    proj_fm("L2_wkT_host", ins["b_k_row"][:, :], KVB, D)
    rope2(KVB)
    pn = p.pf[4][0:1, :]
    for tb in range(4):
        t0 = tb * 512
        for c in range(CH):
            sl = KVB[:, c * T + t0: c * T + t0 + 512]
            p.op("scalar", lambda e, sl=sl: e.activation(cm.SC1[:], sl, AF.Square))
            p.op("tensor", lambda e, c=c: e.matmul(pn, ones_col_f[:], cm.SC1[:],
                                                   start=(c == 0), stop=(c == CH - 1)))
        p.op("vector", lambda e, t0=t0: e.tensor_copy(out=kqn[:, t0:t0 + 512], in_=pn))
    p.op("vector", lambda e: e.reduce_max(cm.SROW[:, 0:1], kqn[:], axis=AXX))
    for c in range(CH):
        p.op("sync", lambda e, c=c: e.dma_start(
            out=ccK_in[c * 128:(c + 1) * 128, :], in_=KVB[:, c * T:(c + 1) * T]), dma=True)
    proj_fm("L2_wiqT_host", ins["b_iq_row"][:, :], iqB, 256)
    for h in range(4):
        coff, off = (h // 2) * T, (h % 2) * 64
        for tb in range(4):
            t0 = tb * 512
            p.op("vector", lambda e, h=h, coff=coff, off=off, t0=t0: e.tensor_copy(
                out=iq2[0:64, h * T + t0: h * T + t0 + 512],
                in_=iqB[off:off + 64, coff + t0: coff + t0 + 512]))
    ikB = KVB
    proj_fm("L2_wikT_host", ins["b_ik_row"][:, :], ikB, 64)
    p.op("sync", lambda e: e.dma_start(
        out=ccK_in[D:D + 64, :], in_=ikB[0:64, 0:T]), dma=True)
    p.op("gpsimd", lambda e: e.dma_start(out=ccK_in[D + 64:D + 65, 0:1],
                                         in_=cm.SROW[:, 0:1]), dma=True)
    p.op("gpsimd", lambda e: e.dma_start(
        out=WT[:, 0:CH * 4].rearrange("p (c f) -> p c f", f=4),
        in_=ins["L2_wiwT_host"].rearrange("(c p) f -> p c f", p=128)), dma=True)
    p.op("sync", lambda e: e.dma_start(out=brow[:, 0:4], in_=ins["b_iw_row"][:, :]), dma=True)
    for i in range(NT):
        t0 = i * 128
        psw = p.pf[2][:, 0:4]
        p.op("tensor", lambda e: e.matmul(psw, C["c_ones_row_bf"][:] if False else ones_row_f[:],
                                          brow[:, 0:4], start=True, stop=False))
        for c in range(CH):
            p.op("tensor", lambda e, c=c, t0=t0: e.matmul(
                psw, lnxs(c, t0, 128), WT[:, c * 4:(c + 1) * 4],
                start=False, stop=(c == CH - 1)))
        p.op("vector", lambda e, i=i: e.tensor_copy(out=IWT[:, i * 4:(i + 1) * 4], in_=psw))
    p.op("gpsimd", lambda e: e.dma_start(
        out=WT[:].rearrange("p (c f) -> p c f", f=D),
        in_=ins["L2_wvT_host"].rearrange("(c p) f -> p c f", p=128)), dma=True)
    p.op("sync", lambda e: e.dma_start(out=brow[:, 0:D], in_=ins["b_v_row"][:, :]), dma=True)
    bmv2 = {}
    vship = None
    for i in range(NT):
        t0 = i * 128
        for ob in range(2):
            ph = p.pf[ob % 2]
            p.op("tensor", lambda e, ph=ph, ob=ob: e.matmul(
                ph[:], ones_row_f[:], brow[:, ob * 512:(ob + 1) * 512], start=True, stop=False),
                wait=(None if i == 0 else bmv2.get(ob % 2)), inc=False)
            for c in range(CH):
                p.op("tensor", lambda e, ph=ph, ob=ob, c=c, t0=t0: e.matmul(
                    ph[:], lnxs(c, t0, 128),
                    WT[:, c * D + ob * 512: c * D + (ob + 1) * 512],
                    start=False, stop=(c == CH - 1)),
                    wait=False, inc=(c == CH - 1))
            p.op("vector", lambda e, ph=ph, ob=ob: e.tensor_copy(
                out=KVB[:, ob * 512:(ob + 1) * 512], in_=ph[:]),
                wait=(None if vship is None else [vship, p.mark()]))
            bmv2[ob % 2] = p.mark()
        p.op("sync", lambda e, i=i: e.dma_start(
            out=ccV_in[i * 128:(i + 1) * 128, :], in_=KVB[:, 0:D]), dma=True)
        vship = p.mark()
    for tb in range(4):
        t0 = tb * 512
        for c in range(CH):
            sl = QB[:, c * T + t0: c * T + t0 + 512]
            p.op("scalar", lambda e, sl=sl: e.activation(cm.SC1[:], sl, AF.Square))
            p.op("tensor", lambda e, c=c: e.matmul(pn, ones_col_f[:], cm.SC1[:],
                                                   start=(c == 0), stop=(c == CH - 1)))
        p.op("vector", lambda e, t0=t0: e.tensor_copy(out=kqn[:, t0:t0 + 512], in_=pn))
    p.op("vector", lambda e: e.reduce_max(cm.SROW[:, 1:2], kqn[:], axis=AXX))
    p.op_collective(lambda e: e.collective_compute(
        "AllGather", ALU.bypass, replica_groups=RG,
        ins=[ccK_in.opt()], outs=[ccK_out.opt()]))
    p.op_collective(lambda e: e.collective_compute(
        "AllGather", ALU.bypass, replica_groups=RG,
        ins=[ccV_in.opt()], outs=[ccV_out.opt()]))
    p.op("gpsimd", lambda e: e.dma_start(out=cm.SROW[:, 2:3], in_=ccK_out[D + 64:D + 65, 0:1]), dma=True)
    p.op("gpsimd", lambda e: e.dma_start(out=cm.SROW[:, 3:4], in_=ccK_out[KD + D + 64:KD + D + 65, 0:1]), dma=True)
    p.op("vector", lambda e: e.tensor_max(cm.SROW[:, 2:3], cm.SROW[:, 2:3], cm.SROW[:, 3:4]))
    p.op("scalar", lambda e: e.activation(cm.SROW[:, 1:3], cm.SROW[:, 1:3], AF.Sqrt))
    p.op("vector", lambda e: e.tensor_mul(cm.SROW[:, 1:2], cm.SROW[:, 1:2], cm.SROW[:, 2:3]))
    p.op("vector", lambda e: e.tensor_scalar_mul(cm.SROW[:, 1:2], cm.SROW[:, 1:2], -1.0 / 32.0))
    pm = p.pf[4][:, 0:1]
    p.op("tensor", lambda e: e.matmul(pm, ones_row_f[:], cm.SROW[:, 1:2], start=True, stop=True))
    p.op("vector", lambda e: e.tensor_copy(out=NM[:], in_=pm))
    for gt in range(32):
        r, lt = gt % 2, gt // 2
        p.op("gpsimd", lambda e, r=r, lt=lt, gt=gt: e.dma_start(
            out=ikg[:, gt * 128:(gt + 1) * 128],
            in_=ccK_out[r * KD + D: r * KD + D + 64, lt * 128:(lt + 1) * 128]), dma=True)
    p.free_prefix(pref + "t1")
    cm.p.free_prefix("z2_LNX")

    ISC = p.sb(pref + "t3_ISC", [128, 8 * S], FP16)
    DUMP = p.sb(pref + "t3_DUMP", [128, S], U8)
    bis = p.sb(pref + "t3_bis", [128, 8], F32)
    bism = p.sb(pref + "t3_bism", [128, 1], U8)
    for half in range(2):
        for iq_i in range(8):
            i = half * 8 + iq_i
            q0 = i * 128
            pst = p.pf[0]
            isc_i = ISC[:, iq_i * S:(iq_i + 1) * S]
            iscm = None
            for gt in range(32):
                for h in range(4):
                    p.op("tensor", lambda e, pst=pst, h=h, q0=q0, gt=gt: e.matmul(
                        pst[:, h * 128:(h + 1) * 128],
                        iq2[0:64, h * T + q0: h * T + q0 + 128],
                        ikg[0:64, gt * 128:(gt + 1) * 128],
                        start=(h == 0), stop=(h == 3)),
                        wait=((None if iscm is None else iscm) if h == 0 else False),
                        inc=(h == 3))
                for h in range(4):
                    iwcol = IWT[:, i * 4 + h: i * 4 + h + 1]
                    if h == 0:
                        p.op("vector", lambda e, pst=pst, h=h, iwcol=iwcol: e.tensor_scalar(
                            cm.SC1[:, 0:128], pst[:, h * 128:(h + 1) * 128],
                            0.0, iwcol, ALU.max, ALU.mult))
                    else:
                        p.op("vector", lambda e, pst=pst, h=h, iwcol=iwcol: e.tensor_scalar(
                            cm.SC2[:, 0:128], pst[:, h * 128:(h + 1) * 128],
                            0.0, iwcol, ALU.max, ALU.mult))
                        p.op("vector", lambda e: e.tensor_add(
                            cm.SC1[:, 0:128], cm.SC1[:, 0:128], cm.SC2[:, 0:128]))
                p.op("vector", lambda e, isc_i=isc_i, gt=gt: e.tensor_copy(
                    out=isc_i[:, gt * 128:(gt + 1) * 128], in_=cm.SC1[:, 0:128]))
                iscm = p.mark()
            p.op("sync", lambda e, isc_i=isc_i, q0=q0: e.dma_start(
                out=isc_d[q0:q0 + 128, :], in_=isc_i[:]), dma=True)
            p.op("vector", lambda e, isc_i=isc_i: e.reduce_max(bis[:, 1:2], isc_i[:], axis=AXX))
            p.op("vector", lambda e, isc_i=isc_i: e.tensor_reduce(
                bis[:, 0:1], isc_i[:], AXX, ALU.min))
            for it in range(30):
                p.op("vector", lambda e: e.tensor_add(bis[:, 2:3], bis[:, 0:1], bis[:, 1:2]),
                     wait=(None if it == 0 else False), inc=False)
                p.op("vector", lambda e: e.tensor_scalar_mul(bis[:, 2:3], bis[:, 2:3], 0.5),
                     wait=False, inc=False)
                p.op("vector", lambda e, isc_i=isc_i: e.tensor_scalar(
                    DUMP[:], isc_i[:], bis[:, 2:3], 0.0, ALU.is_ge, ALU.add,
                    accum_out=bis[:, 3:4]), wait=False, inc=False)
                p.op("vector", lambda e: e.tensor_scalar(
                    bism[:], bis[:, 3:4], 256.0, None, ALU.is_ge), wait=False, inc=False)
                p.op("vector", lambda e: e.select(bis[:, 0:1], bism[:], bis[:, 2:3], bis[:, 0:1]),
                     wait=False, inc=False)
                p.op("vector", lambda e: e.select(bis[:, 1:2], bism[:], bis[:, 1:2], bis[:, 2:3]),
                     wait=False, inc=False)
            p.op("vector", lambda e, i=i: e.tensor_copy(out=THR[:, i:i + 1], in_=bis[:, 0:1]),
                 wait=False, inc=True)
    p.free_prefix(pref + "t3")
    cm.p.free_prefix("z")

    THRB = p.sb(pref + "t4_THRB", [128, T], F32)
    ATT = p.sb(pref + "t4_ATT", [128, CH * T], BF16)
    PTA = p.sb(pref + "t4_PTA", [128, 32 * 512], BF16)
    KCH2 = [p.sb(pref + "t4_KCHa", [128, CH * 128], BF16),
            p.sb(pref + "t4_KCHb", [128, CH * 128], BF16)]
    VCH2 = [p.sb(pref + "t4_VCHa", [128, D], BF16),
            p.sb(pref + "t4_VCHb", [128, D], BF16)]
    ISCT2 = [p.sb(pref + "t4_ISCTa", [128, 512], FP16),
             p.sb(pref + "t4_ISCTb", [128, 512], FP16)]
    MT = p.sb(pref + "t4_MT", [128, 512], U8)
    NEG = p.sb(pref + "t4_NEG", [128, 512], F32)
    SUMR = p.sb(pref + "t4_SUMR", [1, 512], F32)
    thrrow = p.sb(pref + "t4_thrr", [1, T], F32)
    p.op("vector", lambda e: e.memset(NEG[:], -1e30))
    pr = p.pf[1][0:1, 0:128]
    for i in range(NT):
        p.op("tensor", lambda e, i=i: e.matmul(
            pr, THR[:, i:i + 1], ident_f[:], start=True, stop=True))
        p.op("vector", lambda e, i=i: e.tensor_copy(
            out=thrrow[:, i * 128:(i + 1) * 128], in_=pr))
    for tb in range(4):
        pb = p.pf[2]
        p.op("tensor", lambda e, tb=tb: e.matmul(
            pb[:], ones_row_f[:], thrrow[:, tb * 512:(tb + 1) * 512], start=True, stop=True))
        p.op("vector", lambda e, tb=tb: e.tensor_copy(out=THRB[:, tb * 512:(tb + 1) * 512], in_=pb[:]))
    ccK4 = ccK_out.rearrange("(r c p) t -> r p c t", p=128, c=9)
    kuse = [None, None]
    for qg in range(4):
        g0 = qg * 512
        for gt in range(32):
            r, lt = gt % 2, gt // 2
            par = gt % 2
            KCH = KCH2[par]
            ISCT = ISCT2[par]
            pS = p.pf[4 + par]
            p.op("gpsimd", lambda e, r=r, lt=lt, KCH=KCH: e.dma_start(
                out=KCH[:].rearrange("p (c t) -> p c t", t=128),
                in_=ccK4[r][:, 0:CH, lt * 128:(lt + 1) * 128]),
                dma=True, wait=(kuse[par] if kuse[par] else None))
            p.op("sync", lambda e, gt=gt, g0=g0, ISCT=ISCT: e.dma_start_transpose(
                out=ISCT[:], in_=isc_d[g0:g0 + 512, gt * 128:(gt + 1) * 128]),
                dma=True, wait=False)
            for c in range(CH):
                p.op("tensor", lambda e, pS=pS, c=c, g0=g0, KCH=KCH: e.matmul(
                    pS[:], KCH[:, c * 128:(c + 1) * 128],
                    QB[:, c * T + g0: c * T + g0 + 512],
                    start=(c == 0), stop=(c == CH - 1)),
                    wait=(None if c == 0 else False), inc=(c == CH - 1))
            p.op("vector", lambda e, g0=g0, ISCT=ISCT: e.tensor_tensor(
                MT[:], ISCT[:], THRB[:, g0:g0 + 512], ALU.is_ge))
            p.op("vector", lambda e, pS=pS: e.select(cm.SC1[:], MT[:], pS[:], NEG[:]))
            p.op("scalar", lambda e, gt=gt: e.activation(
                PTA[:, gt * 512:(gt + 1) * 512], cm.SC1[:], AF.Exp,
                bias=NM[:, 0:1], scale=1.0 / 32.0))
            kuse[par] = p.mark()
        psr = p.pf[4][0:1, :]
        for gt in range(32):
            p.op("tensor", lambda e, gt=gt: e.matmul(
                psr, ones_col_bf[:], PTA[:, gt * 512:(gt + 1) * 512],
                start=(gt == 0), stop=(gt == 31)),
                wait=(None if gt == 0 else False), inc=(gt == 31))
        p.op("vector", lambda e: e.tensor_copy(out=SUMR[:], in_=psr))
        p.op("vector", lambda e: e.tensor_scalar_add(SUMR[:], SUMR[:], 1e-30))
        p.op("vector", lambda e: e.reciprocal(SUMR[:], SUMR[:]))
        prb = p.pf[5]
        p.op("tensor", lambda e: e.matmul(prb[:], ones_row_f[:], SUMR[:], start=True, stop=True))
        p.op("vector", lambda e: e.tensor_copy(out=cm.SC2[:], in_=prb[:]))
        vuse = [None, None]
        for rnd in range(2):
            for gt in range(32):
                r, lt = gt % 2, gt // 2
                par = gt % 2
                VCH = VCH2[par]
                p.op("gpsimd", lambda e, r=r, lt=lt, VCH=VCH: e.dma_start(
                    out=VCH[:], in_=ccV_out[(r * NT + lt) * 128:(r * NT + lt + 1) * 128, :]),
                    dma=True, wait=(vuse[par] if vuse[par] else None))
                for db in range(4):
                    dblk = rnd * 4 + db
                    pav = p.pf[db]
                    p.op("tensor", lambda e, pav=pav, dblk=dblk, gt=gt, VCH=VCH: e.matmul(
                        pav[:], VCH[:, dblk * 128:(dblk + 1) * 128],
                        PTA[:, gt * 512:(gt + 1) * 512],
                        start=(gt == 0), stop=(gt == 31)),
                        wait=(None if db == 0 else False), inc=(db == 3))
                vuse[par] = p.mark()
            for db in range(4):
                dblk = rnd * 4 + db
                pav = p.pf[db]
                p.op("vector", lambda e, pav=pav: e.tensor_copy(out=cm.SC1[:], in_=pav[:]))
                p.op("vector", lambda e, pav=pav, dblk=dblk, g0=g0: e.tensor_mul(
                    ATT[:, dblk * T + g0: dblk * T + g0 + 512], cm.SC1[:], cm.SC2[:]))
    WT2 = p.sb(pref + "t5_WT", [128, CH * D], BF16)
    brow2 = p.sb(pref + "t5_brow", [1, D], F32)
    xsl = p.sb(pref + "t5_xsl", [128, 512], F32)
    p.op("gpsimd", lambda e: e.dma_start(
        out=WT2[:].rearrange("p (c f) -> p c f", f=D),
        in_=ins["L2_woT_host"].rearrange("(c p) f -> p c f", p=128)), dma=True)
    p.op("sync", lambda e: e.dma_start(
        out=brow2[:], in_=ins["b_o_row"][:, :]), dma=True)
    for oc in range(CH):
        for tb in range(4):
            t0 = tb * 512
            ph = p.pf[tb % 2]
            p.op("tensor", lambda e, ph=ph, oc=oc: e.matmul(
                ph[:], brow2[:, oc * 128:(oc + 1) * 128], ones512_f[:],
                start=True, stop=False))
            for c in range(CH):
                p.op("tensor", lambda e, ph=ph, oc=oc, c=c, t0=t0: e.matmul(
                    ph[:], WT2[:, c * D + oc * 128: c * D + (oc + 1) * 128],
                    ATT[:, c * T + t0: c * T + t0 + 512], start=False, stop=(c == CH - 1)))
            p.op("sync", lambda e, oc=oc, t0=t0: e.dma_start(
                out=xsl[:], in_=xin_d[oc * 128:(oc + 1) * 128, t0:t0 + 512]), dma=True)
            p.op("vector", lambda e, ph=ph: e.tensor_add(xsl[:], xsl[:], ph[:]))
            p.op("sync", lambda e, oc=oc, t0=t0: e.dma_start(
                out=xout_d[oc * 128:(oc + 1) * 128, t0:t0 + 512], in_=xsl[:]), dma=True)
    p.free_prefix(pref)


def emit_L0_attn(cm: Common, pref="l0a"):
    p, C, ins = cm.p, cm.consts, cm.ins
    xin_d = cm.x_in
    xout_d = cm.x_out
    NH = 16
    ones_row_f = C["c_ones_row_f"]
    ones_col_f = C["c_ones_col_f"]
    ones_col_bf = C["c_ones_col_bf"]
    ones512_f = C["c_ones512_f"]
    ident_bf = C["c_ident_bf"]
    RG = [[0, 1], [2, 3], [4, 5], [6, 7]]
    EPS = 1e-6

    qt_d = p.dram(pref + "qt_d", [2048, T], BF16)
    cc_in = p.dram(pref + "cc_in", [T, 1040], BF16)
    cc_out = p.dram(pref + "cc_out", [2 * T, 1040], BF16)

    H = p.sb(pref + "H", [128, CH * T], F32)
    n2w = p.sb(pref + "n2w", [128, CH], F32)
    CTX = p.sb(pref + "g1_CTX", [128, NT * 1040], BF16)
    STATE = p.sb(pref + "g1_STATE", [128, NH * 64], F32)
    STATEN = p.sb(pref + "g1_STATEN", [128, NH], F32)
    cosb = p.sb(pref + "g1_csb", [128, T], BF16)
    sinb = cosb
    tri = p.sb(pref + "g1_tri", [128, 128], BF16)
    n1w = p.sb(pref + "g1_n1w", [128, CH], F32)
    mW0 = p.sb(pref + "g1_mW0", [128, 1], F32)
    mW1 = p.sb(pref + "g1_mW1", [128, 1], F32)
    p.op("gpsimd", lambda e: e.dma_start(out=cosb[:], in_=ins["c_l0cs"][:, :]), dma=True)
    p.op("gpsimd", lambda e: e.dma_start(out=tri[:], in_=ins["c_l0tri"][:, :]), dma=True)
    p.op("sync", lambda e: e.dma_start(out=n1w[:], in_=ins["n1w_col"][:, :]), dma=True)
    p.op("sync", lambda e: e.dma_start(out=n2w[:], in_=ins["n2w_col"][:, :]), dma=True)
    p.op("sync", lambda e: e.dma_start(out=mW0[:], in_=ins["c_mW0"][:, :]), dma=True)
    p.op("sync", lambda e: e.dma_start(out=mW1[:], in_=ins["c_mW1"][:, :]), dma=True)
    p.op("vector", lambda e: e.memset(STATE[:], 0.0))
    p.op("vector", lambda e: e.memset(STATEN[:], 0.0))

    cm.LNX = p.sb("z2_LNX", [128, CH * T], BF16)
    _LNXB = cm.LNX
    _XSB = cm.XS
    def lnxs(c, t0=0, n=T, _b=_LNXB):
        return _b[:, c * T + t0: c * T + t0 + n]
    def xss(c, t0=0, n=T, _b=_XSB):
        return _b[:, c * T + t0: c * T + t0 + n]
    XB = p.sb("z3_XB", [128, 512], F32)
    psA = p.pf[4][0:1, :]
    psB = p.pf[5][0:1, :]
    pbA = p.pf[2]
    pbB = p.pf[3]
    for tb in range(4):
        t0 = tb * 512
        for c in range(CH):
            p.op("sync", lambda e, c=c, t0=t0: e.dma_start(
                out=XB[:], in_=xin_d[c * 128:(c + 1) * 128, t0:t0 + 512]), dma=True)
            p.op("scalar", lambda e: e.activation(cm.SC1[:], XB[:], AF.Square))
            p.op("tensor", lambda e, c=c: e.matmul(psA, C["c_ones_col_f"][:], XB[:],
                                                   start=(c == 0), stop=(c == CH - 1)))
            p.op("tensor", lambda e, c=c: e.matmul(psB, C["c_ones_col_f"][:], cm.SC1[:],
                                                   start=(c == 0), stop=(c == CH - 1)))
        rr = cm.SROW[:, 0:512]
        rn = cm.SROW[:, 512:1024]
        p.op("vector", lambda e, rn=rn: e.tensor_scalar_mul(rn, psA, 1.0 / D))
        p.op("vector", lambda e, rr=rr: e.tensor_scalar_mul(rr, psB, 1.0 / D))
        p.op("vector", lambda e, rn=rn: e.scalar_tensor_tensor(
            cm.SC1[0:1, :], rn, 1.0, rn, ALU.mult, ALU.mult))
        p.op("vector", lambda e, rr=rr: e.tensor_sub(rr, rr, cm.SC1[0:1, :]))
        p.op("vector", lambda e, rr=rr: e.tensor_scalar_add(rr, rr, 1e-5))
        p.op("scalar", lambda e, rr=rr: e.activation(rr, rr, AF.Sqrt))
        p.op("vector", lambda e, rr=rr: e.reciprocal(rr, rr))
        p.op("vector", lambda e, rr=rr, rn=rn: e.scalar_tensor_tensor(
            rn, rn, -1.0, rr, ALU.mult, ALU.mult))
        p.op("tensor", lambda e, rr=rr: e.matmul(pbA[:], C["c_ones_row_f"][:], rr, start=True, stop=True))
        p.op("tensor", lambda e, rn=rn: e.matmul(pbB[:], C["c_ones_row_f"][:], rn, start=True, stop=True))
        for c in range(CH):
            p.op("sync", lambda e, c=c, t0=t0: e.dma_start(
                out=XB[:], in_=xin_d[c * 128:(c + 1) * 128, t0:t0 + 512]), dma=True)
            p.op("vector", lambda e: e.tensor_mul(cm.SC2[:], XB[:], pbA[:]))
            dst = lnxs(c, t0, 512)
            hd = H[:, c * T + t0: c * T + t0 + 512]
            p.op("vector", lambda e, dst=dst: e.tensor_add(dst, cm.SC2[:], pbB[:]))
            p.op("vector", lambda e, hd=hd: e.tensor_add(hd, cm.SC2[:], pbB[:]))
    p.free_prefix("z3_XB")

    psum_r = p.pf[4][0:1, :]
    pb_r = p.pf[2]
    for tb in range(4):
        t0 = tb * 512
        for c in range(CH):
            sl = lnxs(c, t0, 512)
            p.op("scalar", lambda e, sl=sl: e.activation(cm.SC1[:], sl, AF.Square))
            p.op("tensor", lambda e, c=c: e.matmul(psum_r, ones_col_f[:], cm.SC1[:],
                                                   start=(c == 0), stop=(c == CH - 1)))
        rr = cm.SROW[:, 0:512]
        p.op("vector", lambda e, rr=rr: e.tensor_scalar_mul(rr, psum_r, 1.0 / D))
        p.op("vector", lambda e, rr=rr: e.tensor_scalar_add(rr, rr, EPS))
        p.op("scalar", lambda e, rr=rr: e.activation(rr, rr, AF.Sqrt))
        p.op("vector", lambda e, rr=rr: e.reciprocal(rr, rr))
        p.op("tensor", lambda e, rr=rr: e.matmul(pb_r[:], ones_row_f[:], rr, start=True, stop=True))
        for c in range(CH):
            sl = lnxs(c, t0, 512)
            p.op("vector", lambda e, sl=sl: e.tensor_mul(sl, sl, pb_r[:]))

    WQKV = p.sb(pref + "t1_W", [128, CH * 512], BF16)
    bqp = p.sb(pref + "t1_bqp", [1, D], F32)
    qg = p.sb(pref + "t1_qg", [128, NH * 512], BF16)
    kg = p.sb(pref + "t1_kg", [128, CH * 512], BF16)
    vg = p.sb(pref + "t1_vg", [128, CH * 512], BF16)
    kt = p.sb(pref + "t1_kt", [128, NH * 128], BF16)
    ktm = p.sb(pref + "t1_ktm", [128, NH * 128], BF16)
    vtm = p.sb(pref + "t1_vtm", [128, D], BF16)
    dse = p.sb(pref + "t1_dse", [128, 1040], BF16)
    STB = p.sb(pref + "t1_STB", [128, NH * 64], BF16)
    STBN = p.sb(pref + "t1_STBN", [128, NH], BF16)
    PTb = p.sb(pref + "t1_PT", [128, 128], BF16)

    qkv_w3 = ins["L0_attn_qkv_w"].rearrange("(c p) (w f) -> w p c f", p=128, w=3)
    qkvm = {}
    for g in range(4):
        t0 = g * 512
        for part, dst in ((0, None), (1, kg), (2, vg)):
            p.op("sync", lambda e, part=part: e.dma_start(
                out=bqp[:], in_=ins["qkv_b_row"][0:1, part * D:(part + 1) * D]), dma=True)
            for wh in range(2):
                p.op("gpsimd", lambda e, part=part, wh=wh: e.dma_start(
                    out=WQKV[:].rearrange("p (c f) -> p c f", f=512),
                    in_=qkv_w3[part][:, :, wh * 512:(wh + 1) * 512]), dma=True)
                for c in range(CH):
                    p.op("vector", lambda e, c=c: e.tensor_scalar(
                        WQKV[:, c * 512:(c + 1) * 512], WQKV[:, c * 512:(c + 1) * 512],
                        n1w[:, c:c + 1], None, ALU.mult))
                for loc in range(4):
                    oc = wh * 4 + loc
                    ph = p.pf[oc % 2]
                    first = (loc < 2)
                    p.op("tensor", lambda e, ph=ph, oc=oc: e.matmul(
                        ph[:], bqp[:, oc * 128:(oc + 1) * 128], ones512_f[:],
                        start=True, stop=False),
                        wait=(None if first else qkvm.get(oc % 2)), inc=False)
                    for c in range(CH):
                        p.op("tensor", lambda e, ph=ph, loc=loc, c=c, t0=t0: e.matmul(
                            ph[:], WQKV[:, c * 512 + loc * 128: c * 512 + loc * 128 + 128],
                            lnxs(c, t0, 512), start=False, stop=(c == CH - 1)),
                            wait=False, inc=(c == CH - 1))
                    if part == 0:
                        for sub in range(2):
                            h = oc * 2 + sub
                            off = sub * 64
                            p.op("vector", lambda e, ph=ph, off=off: e.tensor_copy(
                                out=cm.SC1[0:64, :], in_=ph[off:off + 64, :]))
                            p.op("vector", lambda e, h=h, t0=t0: e.scalar_tensor_tensor(
                                qg[0:64, h * 512:(h + 1) * 512], cm.SC1[0:64, :], 0.0,
                                cosb[0:64, t0:t0 + 512], ALU.max, ALU.mult))
                            p.op("vector", lambda e: e.tensor_copy(
                                out=cm.SC2[64:128, :], in_=cm.SC1[0:64, :]))
                            p.op("vector", lambda e, h=h, t0=t0: e.scalar_tensor_tensor(
                                qg[64:128, h * 512:(h + 1) * 512], cm.SC2[64:128, :], 0.0,
                                sinb[64:128, t0:t0 + 512], ALU.max, ALU.mult))
                        qkvm[oc % 2] = p.mark()
                    else:
                        p.op("vector", lambda e, ph=ph, dst=dst, oc=oc: e.tensor_copy(
                            out=dst[:, oc * 512:(oc + 1) * 512], in_=ph[:]))
                    qkvm[oc % 2] = p.mark()
        for hc in range(NH):
            p.op("sync", lambda e, hc=hc, t0=t0: e.dma_start(
                out=qt_d[hc * 128:(hc + 1) * 128, t0:t0 + 512],
                in_=qg[:, hc * 512:(hc + 1) * 512]), dma=True)
        for j in range(4):
            i = g * 4 + j
            j0 = j * 128
            for h in range(NH):
                c, off = h // 2, (h % 2) * 64
                p.op("vector", lambda e, c=c, off=off, j0=j0: e.tensor_copy(
                    out=cm.SC1[0:64, 0:128],
                    in_=kg[off:off + 64, c * 512 + j0: c * 512 + j0 + 128]))
                p.op("vector", lambda e, h=h, j0=j0, t0=t0: e.scalar_tensor_tensor(
                    kt[0:64, h * 128:(h + 1) * 128], cm.SC1[0:64, 0:128], 0.0,
                    cosb[0:64, t0 + j0: t0 + j0 + 128], ALU.max, ALU.mult))
                p.op("vector", lambda e: e.tensor_copy(out=cm.SC2[64:128, 0:128], in_=cm.SC1[0:64, 0:128]))
                p.op("vector", lambda e, h=h, j0=j0, t0=t0: e.scalar_tensor_tensor(
                    kt[64:128, h * 128:(h + 1) * 128], cm.SC2[64:128, 0:128], 0.0,
                    sinb[64:128, t0 + j0: t0 + j0 + 128], ALU.max, ALU.mult))
            for h in range(NH):
                ptr = p.pb[0][:, 0:128]
                p.op("tensor", lambda e, h=h: e.transpose(
                    ptr, kt[:, h * 128:(h + 1) * 128], ident_bf[:]))
                p.op("vector", lambda e, h=h: e.tensor_copy(
                    out=ktm[:, h * 128:(h + 1) * 128], in_=ptr))
            for c in range(CH):
                ptr = p.pb[0][:, 0:128]
                p.op("tensor", lambda e, c=c, j0=j0: e.transpose(
                    ptr, vg[:, c * 512 + j0: c * 512 + j0 + 128], ident_bf[:]))
                p.op("vector", lambda e, c=c: e.tensor_copy(
                    out=vtm[:, c * 128:(c + 1) * 128], in_=ptr))
            pctx = [p.pf[1], p.pf[2]]
            pctxn = p.pf[3][:, 0:16]
            p.op("vector", lambda e: e.tensor_copy(out=STB[:], in_=STATE[:]))
            p.op("vector", lambda e: e.tensor_copy(out=STBN[:], in_=STATEN[:]))
            for h in range(NH):
                pst = p.pf[0][:, 0:128]
                p.op("tensor", lambda e, pst=pst, h=h, j0=j0: e.matmul(
                    pst, kt[:, h * 128:(h + 1) * 128],
                    qg[:, h * 512 + j0: h * 512 + j0 + 128], start=True, stop=True))
                p.op("vector", lambda e, pst=pst: e.tensor_mul(PTb[:], pst, tri[:]))
                pc = pctx[h // 8]
                slot = h % 8
                p.op("tensor", lambda e, pc=pc, slot=slot, h=h: e.matmul(
                    pc[:, slot * 64:(slot + 1) * 64], PTb[:], vtm[:, h * 64:(h + 1) * 64],
                    start=(slot == 0), stop=False))
                p.op("tensor", lambda e, h=h: e.matmul(
                    pctxn[:, h:h + 1], PTb[:], ones_col_bf[:],
                    start=(h == 0), stop=False))
                p.op("tensor", lambda e, pc=pc, slot=slot, h=h, j0=j0: e.matmul(
                    pc[:, slot * 64:(slot + 1) * 64],
                    qg[:, h * 512 + j0: h * 512 + j0 + 128],
                    STB[:, h * 64:(h + 1) * 64], start=False, stop=(slot == 7)))
                p.op("tensor", lambda e, h=h, j0=j0: e.matmul(
                    pctxn[:, h:h + 1], qg[:, h * 512 + j0: h * 512 + j0 + 128],
                    STBN[:, h:h + 1], start=False, stop=(h == NH - 1)))
                pup = p.pf[4][:, 0:65]
                p.op("tensor", lambda e, pup=pup, h=h: e.matmul(
                    pup[:, 0:64], ktm[:, h * 128:(h + 1) * 128], vtm[:, h * 64:(h + 1) * 64],
                    start=True, stop=False))
                p.op("tensor", lambda e, pup=pup, h=h: e.matmul(
                    pup[:, 64:65], ktm[:, h * 128:(h + 1) * 128], ones_col_bf[:],
                    start=False, stop=True))
                p.op("vector", lambda e, pup=pup, h=h: e.tensor_copy(
                    out=dse[:, h * 64:(h + 1) * 64], in_=pup[:, 0:64]))
                p.op("vector", lambda e, pup=pup, h=h: e.tensor_copy(
                    out=dse[:, NH * 64 + h: NH * 64 + h + 1], in_=pup[:, 64:65]))
                p.op("vector", lambda e, pup=pup, h=h: e.tensor_add(
                    STATE[:, h * 64:(h + 1) * 64], STATE[:, h * 64:(h + 1) * 64], pup[:, 0:64]))
                p.op("vector", lambda e, pup=pup, h=h: e.tensor_add(
                    STATEN[:, h:h + 1], STATEN[:, h:h + 1], pup[:, 64:65]))
            for half in range(2):
                p.op("vector", lambda e, i=i, half=half: e.tensor_copy(
                    out=CTX[:, i * 1040 + half * 512: i * 1040 + (half + 1) * 512],
                    in_=pctx[half][:]))
            p.op("vector", lambda e, i=i: e.tensor_copy(
                out=CTX[:, i * 1040 + 1024: i * 1040 + 1040], in_=pctxn))
            p.op("sync", lambda e, i=i: e.dma_start(
                out=cc_in[i * 128:(i + 1) * 128, :], in_=dse[:]), dma=True)
    p.free_prefix(pref + "t1")
    cm.p.free_prefix("z2_LNX")
    p.op_collective(lambda e: e.collective_compute(
        "AllGather", ALU.bypass, replica_groups=RG,
        ins=[cc_in.opt()], outs=[cc_out.opt()]))

    SCOR = p.sb(pref + "t2_SCOR", [128, 1040], F32)
    SUSE = p.sb(pref + "t2_SUSE", [128, 1040], BF16)
    PD = p.sb(pref + "t2_PD", [128, 1040], F32)
    GB0 = p.sb(pref + "t2_GB0", [128, 1040], BF16)
    GB1 = p.sb(pref + "t2_GB1", [128, 1040], BF16)
    qtt = p.sb(pref + "t2_qtt", [128, NH * 128], BF16)
    OTM2 = p.sb(pref + "t2_OTM", [128, D], BF16)
    rcpn = p.sb(pref + "t2_rcp", [128, 16], F32)
    OFM = p.sb(pref + "t2_OFM", [128, CH * T], BF16)
    p.op("vector", lambda e: e.memset(SCOR[:], 0.0))
    for i in range(NT):
        p.op("gpsimd", lambda e, i=i: e.dma_start(
            out=GB0[:], in_=cc_out[i * 128:(i + 1) * 128, :]), dma=True)
        p.op("gpsimd", lambda e, i=i: e.dma_start(
            out=GB1[:], in_=cc_out[T + i * 128: T + (i + 1) * 128, :]), dma=True)
        p.op("gpsimd", lambda e, i=i: e.dma_start(
            out=qtt[:].rearrange("p (h t) -> p h t", t=128),
            in_=qt_d.rearrange("(h p) t -> p h t", p=128)[:, :, i * 128:(i + 1) * 128]), dma=True)
        p.op("vector", lambda e: e.tensor_scalar(PD[:], GB0[:], mW0[:, 0:1], None, ALU.mult))
        p.op("vector", lambda e: e.scalar_tensor_tensor(PD[:], GB1[:], mW1[:, 0:1], PD[:], ALU.mult, ALU.add))
        p.op("vector", lambda e: e.scalar_tensor_tensor(SUSE[:], PD[:], mW0[:, 0:1], SCOR[:], ALU.mult, ALU.add))
        p.op("vector", lambda e: e.tensor_add(SCOR[:], SCOR[:], PD[:]))
        pcor = [p.pf[1], p.pf[2]]
        pcorn = p.pf[3][:, 0:16]
        for h in range(NH):
            pc = pcor[h // 8]
            slot = h % 8
            p.op("tensor", lambda e, pc=pc, slot=slot, h=h: e.matmul(
                pc[:, slot * 64:(slot + 1) * 64], qtt[:, h * 128:(h + 1) * 128],
                SUSE[:, h * 64:(h + 1) * 64], start=(slot == 0), stop=(slot == 7)))
            p.op("tensor", lambda e, h=h: e.matmul(
                pcorn[:, h:h + 1], qtt[:, h * 128:(h + 1) * 128],
                SUSE[:, NH * 64 + h: NH * 64 + h + 1],
                start=(h == 0), stop=(h == NH - 1)))
        p.op("vector", lambda e, i=i: e.tensor_copy(out=cm.SC1[:, 0:16], in_=pcorn))
        p.op("vector", lambda e, i=i: e.tensor_add(
            rcpn[:], cm.SC1[:, 0:16], CTX[:, i * 1040 + 1024: i * 1040 + 1040]))
        p.op("vector", lambda e: e.tensor_scalar_add(rcpn[:], rcpn[:], EPS))
        p.op("vector", lambda e: e.reciprocal(rcpn[:], rcpn[:]))
        for half in range(2):
            p.op("vector", lambda e, i=i, half=half: e.tensor_copy(
                out=cm.SC1[:], in_=CTX[:, i * 1040 + half * 512: i * 1040 + (half + 1) * 512]))
            p.op("vector", lambda e, half=half: e.tensor_add(cm.SC1[:], cm.SC1[:], pcor[half][:]))
            for sh in range(8):
                h = half * 8 + sh
                p.op("vector", lambda e, h=h, sh=sh: e.tensor_scalar(
                    OTM2[:, h * 64:(h + 1) * 64], cm.SC1[:, sh * 64:(sh + 1) * 64],
                    rcpn[:, h:h + 1], None, ALU.mult))
        for c in range(CH):
            ptr = p.pb[0][:, 0:128]
            p.op("tensor", lambda e, c=c: e.transpose(
                ptr, OTM2[:, c * 128:(c + 1) * 128], ident_bf[:]))
            p.op("vector", lambda e, c=c, i=i: e.tensor_copy(
                out=OFM[:, c * T + i * 128: c * T + (i + 1) * 128], in_=ptr))
    WO = p.sb(pref + "t2_WO", [128, CH * D], BF16)
    brow3 = p.sb(pref + "t2_brow", [1, D], F32)
    p.op("gpsimd", lambda e: e.dma_start(
        out=WO[:].rearrange("p (c f) -> p c f", f=D),
        in_=ins["L0_attn_out_w"].rearrange("(c p) f -> p c f", p=128)), dma=True)
    p.op("sync", lambda e: e.dma_start(
        out=brow3[:], in_=ins["L0_attn_out_b"].rearrange("(u f) -> u f", u=1)), dma=True)
    for oc in range(CH):
        for tb in range(4):
            t0 = tb * 512
            ph = p.pf[tb % 2]
            p.op("tensor", lambda e, ph=ph, oc=oc: e.matmul(
                ph[:], brow3[:, oc * 128:(oc + 1) * 128], ones512_f[:],
                start=True, stop=False))
            for c in range(CH):
                p.op("tensor", lambda e, ph=ph, oc=oc, c=c, t0=t0: e.matmul(
                    ph[:], WO[:, c * D + oc * 128: c * D + (oc + 1) * 128],
                    OFM[:, c * T + t0: c * T + t0 + 512], start=False, stop=(c == CH - 1)))
            hsl = H[:, oc * T + t0: oc * T + t0 + 512]
            p.op("vector", lambda e, ph=ph, hsl=hsl: e.tensor_add(hsl, hsl, ph[:]))
    p.free_prefix(pref + "t2")
    p.free_prefix(pref + "g1")

    R = p.sb(pref + "t3_R", [128, CH * T], BF16)
    for tb in range(4):
        t0 = tb * 512
        for c in range(CH):
            sl = H[:, c * T + t0: c * T + t0 + 512]
            p.op("scalar", lambda e, sl=sl: e.activation(cm.SC1[:], sl, AF.Square))
            p.op("tensor", lambda e, c=c: e.matmul(psum_r, ones_col_f[:], cm.SC1[:],
                                                   start=(c == 0), stop=(c == CH - 1)))
        rr = cm.SROW[:, 0:512]
        p.op("vector", lambda e, rr=rr: e.tensor_scalar_mul(rr, psum_r, 1.0 / D))
        p.op("vector", lambda e, rr=rr: e.tensor_scalar_add(rr, rr, EPS))
        p.op("scalar", lambda e, rr=rr: e.activation(rr, rr, AF.Sqrt))
        p.op("vector", lambda e, rr=rr: e.reciprocal(rr, rr))
        p.op("tensor", lambda e, rr=rr: e.matmul(pb_r[:], ones_row_f[:], rr, start=True, stop=True))
        for c in range(CH):
            hsl = H[:, c * T + t0: c * T + t0 + 512]
            p.op("vector", lambda e, hsl=hsl, c=c, t0=t0: e.tensor_mul(
                R[:, c * T + t0: c * T + t0 + 512], hsl, pb_r[:]))
    W1 = p.sb(pref + "t3_W1", [128, CH * T], BF16)
    W2 = p.sb(pref + "t3_W2", [128, NT * D], BF16)
    b1r = p.sb(pref + "t3_b1", [1, 4096], F32)
    b2r = p.sb(pref + "t3_b2", [1, D], F32)
    h1 = p.sb(pref + "t3_h1", [128, NT * 512], BF16)
    w1r = ins["L0_attn_mlp_w1"].rearrange("(c p) f -> p c f", p=128)
    w2r = ins["L0_attn_mlp_w2"].rearrange("(c p) f -> p c f", p=128)
    p.op("sync", lambda e: e.dma_start(
        out=b1r[:], in_=ins["L0_attn_mlp_b1"].rearrange("(u f) -> u f", u=1)), dma=True)
    p.op("sync", lambda e: e.dma_start(
        out=b2r[:], in_=ins["L0_attn_mlp_b2"].rearrange("(u f) -> u f", u=1)), dma=True)
    mlpm = {}
    for fh in range(2):
        p.op("gpsimd", lambda e, fh=fh: e.dma_start(
            out=W1[:].rearrange("p (c f) -> p c f", f=T),
            in_=w1r[:, :, fh * T:(fh + 1) * T]), dma=True)
        for c in range(CH):
            p.op("vector", lambda e, c=c: e.tensor_scalar(
                W1[:, c * T:(c + 1) * T], W1[:, c * T:(c + 1) * T],
                n2w[:, c:c + 1], None, ALU.mult))
        p.op("gpsimd", lambda e, fh=fh: e.dma_start(
            out=W2[:].rearrange("p (c f) -> p c f", f=D),
            in_=w2r[:, fh * NT:(fh + 1) * NT, :]), dma=True)
        for tb in range(4):
            t0 = tb * 512
            for oc in range(NT):
                ph = p.pf[oc % 2]
                p.op("tensor", lambda e, ph=ph, oc=oc, fh=fh: e.matmul(
                    ph[:], b1r[:, fh * T + oc * 128: fh * T + oc * 128 + 128],
                    ones512_f[:], start=True, stop=False),
                    wait=(None if oc < 2 else mlpm.get(oc % 2)), inc=False)
                for c in range(CH):
                    p.op("tensor", lambda e, ph=ph, oc=oc, c=c, t0=t0: e.matmul(
                        ph[:], W1[:, c * T + oc * 128: c * T + oc * 128 + 128],
                        R[:, c * T + t0: c * T + t0 + 512], start=False, stop=(c == CH - 1)),
                        wait=False, inc=(c == CH - 1))
                p.op("scalar", lambda e, ph=ph, oc=oc: e.activation(
                    h1[:, oc * 512:(oc + 1) * 512], ph[:], AF.Gelu_apprx_tanh))
                mlpm[oc % 2] = p.mark()
            for oc in range(CH):
                py = p.pf[2 + (oc % 2)]
                first = True
                if fh == 0:
                    p.op("tensor", lambda e, py=py, oc=oc: e.matmul(
                        py[:], b2r[:, oc * 128:(oc + 1) * 128], ones512_f[:],
                        start=True, stop=False),
                        wait=(None if oc < 2 else mlpm.get(2 + oc % 2)), inc=False)
                    first = False
                for fc in range(NT):
                    p.op("tensor", lambda e, py=py, oc=oc, fc=fc, fh=fh: e.matmul(
                        py[:], W2[:, fc * D + oc * 128: fc * D + (oc + 1) * 128],
                        h1[:, fc * 512:(fc + 1) * 512],
                        start=(fh == 1 and fc == 0), stop=(fc == NT - 1)),
                        wait=((None if oc < 2 else mlpm.get(2 + oc % 2)) if (first and fc == 0) else False),
                        inc=(fc == NT - 1))
                hsl = H[:, oc * T + t0: oc * T + t0 + 512]
                p.op("vector", lambda e, py=py, hsl=hsl: e.tensor_add(hsl, hsl, py[:]))
                mlpm[2 + oc % 2] = p.mark()
    p.free_prefix(pref + "t3")

    xsl = p.sb(pref + "t4_xsl", [128, 512], F32)
    for oc in range(CH):
        for tb in range(4):
            t0 = tb * 512
            p.op("sync", lambda e, oc=oc, t0=t0: e.dma_start(
                out=xsl[:], in_=xin_d[oc * 128:(oc + 1) * 128, t0:t0 + 512]), dma=True)
            p.op("vector", lambda e, oc=oc, t0=t0: e.tensor_add(
                xsl[:], xsl[:], H[:, oc * T + t0: oc * T + t0 + 512]))
            p.op("sync", lambda e, oc=oc, t0=t0: e.dma_start(
                out=xout_d[oc * 128:(oc + 1) * 128, t0:t0 + 512], in_=xsl[:]), dma=True)
    p.free_prefix(pref)


def sl_L0_attn(cm):
    emit_L0_attn(cm)


def sl_L2_attn(cm):
    emit_L2_attn(cm)


def sl_L3_attn(cm):
    std_prologue(cm, lnx_first=True)
    cm.p.free_prefix("z3_XS")
    emit_L3_attn(cm)
    if DBG.get("l3_stop"):
        cm.p.op("sync", lambda e: e.dma_start(out=cm.x_out[:, :], in_=cm.x_in[:, :]), dma=True)
    cm.p.free_prefix("z")


def sl_L1_attn(cm):
    std_prologue(cm)
    emit_L1_attn(cm)
    std_epilogue(cm)


def sl_L1_ffn(cm):
    std_prologue(cm)
    emit_L1_ffn(cm)
    std_epilogue(cm)


def sl_L2_ffn(cm):
    std_prologue(cm)
    emit_L2_ffn(cm)
    std_epilogue(cm)


def sl_L3_ffn(cm):
    std_prologue(cm)
    emit_L3_ffn(cm)
    std_epilogue(cm)


def sl_L0_ffn(cm):
    std_prologue(cm)
    emit_L0_ffn(cm)
    std_epilogue(cm)


def shard_x(x):
    out = []
    for core in range(8):
        b, h = core // 2, core % 2
        xc = x[b].reshape(32, 128, D)[h::2]
        out.append(np.ascontiguousarray(xc.transpose(2, 0, 1).reshape(D, T)))
    return out


def unshard_x(parts):
    x = np.zeros((B, S, D), dtype=np.float32)
    for core in range(8):
        b, h = core // 2, core % 2
        xc = parts[core].reshape(D, NT, 128).transpose(1, 2, 0)
        x[b].reshape(32, 128, D)[h::2] = xc
    return x


def derive_inputs(inp):
    d = {}
    d["route_scale_col"] = np.ascontiguousarray(
        np.broadcast_to(np.asarray(inp["L3_ffn_route_scale"], dtype=np.float32), (128, 1)))
    d["L1_ffn_WgT_host"] = np.ascontiguousarray(np.asarray(inp["L1_ffn_Wg"], dtype=np.float32).T)
    d["L1_ffn_BT_host"] = np.ascontiguousarray(
        np.asarray(inp["L1_ffn_B"], dtype=np.float32).transpose(0, 2, 1))
    d["L1_ffn_AT_host"] = np.ascontiguousarray(
        np.asarray(inp["L1_ffn_A"], dtype=np.float32).transpose(2, 0, 1).reshape(D, 16))
    d["L1_attn_W_inT_host"] = np.ascontiguousarray(np.asarray(inp["L1_attn_W_in"], dtype=np.float32).T)
    d["conv_b_col"] = np.ascontiguousarray(np.asarray(inp["L1_attn_conv_b"], dtype=np.float32).reshape(16, 128).T)
    d["D_col"] = np.ascontiguousarray(np.asarray(inp["L1_attn_D"], dtype=np.float32).reshape(16, 128).T)
    d["L1_attn_W_xprojT_host"] = np.ascontiguousarray(np.asarray(inp["L1_attn_W_xproj"], dtype=np.float32).T)
    d["L1_attn_W_outT_host"] = np.ascontiguousarray(np.asarray(inp["L1_attn_W_out"], dtype=np.float32).T)
    d["qkv_b_row"] = np.ascontiguousarray(np.asarray(inp["L0_attn_qkv_b"], dtype=np.float32).reshape(1, 3 * D))
    d["n1w_col"] = np.ascontiguousarray(np.asarray(inp["L0_attn_norm1_w"], dtype=np.float32).reshape(CH, 128).T)
    d["n2w_col"] = np.ascontiguousarray(np.asarray(inp["L0_attn_norm2_w"], dtype=np.float32).reshape(CH, 128).T)
    wqkv = np.asarray(inp["L2_attn_w_qkv"], dtype=np.float32)
    d["L2_wqT_host"] = np.ascontiguousarray(wqkv[0:D].T)
    d["L2_wkT_host"] = np.ascontiguousarray(wqkv[D:2 * D].T)
    d["L2_wvT_host"] = np.ascontiguousarray(wqkv[2 * D:3 * D].T)
    d["L2_woT_host"] = np.ascontiguousarray(np.asarray(inp["L2_attn_w_o"], dtype=np.float32).T)
    d["L2_wiqT_host"] = np.ascontiguousarray(np.asarray(inp["L2_attn_w_iq"], dtype=np.float32).T)
    d["L2_wikT_host"] = np.ascontiguousarray(np.asarray(inp["L2_attn_w_ik"], dtype=np.float32).T)
    d["L2_wiwT_host"] = np.ascontiguousarray(np.asarray(inp["L2_attn_w_iw"], dtype=np.float32).T)
    bqkv = np.asarray(inp["L2_attn_b_qkv"], dtype=np.float32)
    d["b_q_row"] = np.ascontiguousarray(bqkv[0:D].reshape(1, D))
    d["b_k_row"] = np.ascontiguousarray(bqkv[D:2 * D].reshape(1, D))
    d["b_v_row"] = np.ascontiguousarray(bqkv[2 * D:3 * D].reshape(1, D))
    d["b_o_row"] = np.ascontiguousarray(np.asarray(inp["L2_attn_b_o"], dtype=np.float32).reshape(1, D))
    d["b_iq_row"] = np.ascontiguousarray(np.asarray(inp["L2_attn_b_iq"], dtype=np.float32).reshape(1, 256))
    d["b_ik_row"] = np.ascontiguousarray(np.asarray(inp["L2_attn_b_ik"], dtype=np.float32).reshape(1, 64))
    d["b_iw_row"] = np.ascontiguousarray(np.asarray(inp["L2_attn_b_iw"], dtype=np.float32).reshape(1, 4))
    for w in ("Wq", "Wk", "Wv", "Wo"):
        d[f"L3_attn_{w}T_host"] = np.ascontiguousarray(
            np.asarray(inp[f"L3_attn_{w}"], dtype=np.float32).T)
    d["L1_ffn_WiT_host"] = np.ascontiguousarray(np.asarray(inp["L1_ffn_Wi"], dtype=np.float32).T)
    d["L1_ffn_WoT_host"] = np.ascontiguousarray(np.asarray(inp["L1_ffn_Wo"], dtype=np.float32).T)
    return d


BF16_INPUTS = {"c_l3cos", "c_l3sin", "c_l3m_a4", "c_l3m_b4",
               "c_l2cos", "c_l2sin", "c_l0cs", "c_l0tri"}

SUBLAYER_EMITTERS = {
    "L0_ffn": sl_L0_ffn,
    "L3_ffn": sl_L3_ffn,
    "L2_ffn": sl_L2_ffn,
    "L1_ffn": sl_L1_ffn,
    "L1_attn": sl_L1_attn,
    "L3_attn": sl_L3_attn,
    "L2_attn": sl_L2_attn,
    "L0_attn": sl_L0_attn,
}

SUBLAYER_WEIGHTS = {
    "L0_attn": {
        "L0_attn_qkv_w": (D, 3 * D), "qkv_b_row": (1, 3 * D),
        "L0_attn_out_w": (D, D), "L0_attn_out_b": (D,),
        "n1w_col": (128, CH), "n2w_col": (128, CH),
        "L0_attn_mlp_w1": (D, 4096), "L0_attn_mlp_b1": (4096,),
        "L0_attn_mlp_w2": (4096, D), "L0_attn_mlp_b2": (D,),
        "c_l0cs": (128, T), "c_l0tri": (128, 128),
        "c_mW0": (128, 1), "c_mW1": (128, 1),
    },
    "L2_attn": {
        "L2_wqT_host": (D, D), "L2_wkT_host": (D, D), "L2_wvT_host": (D, D),
        "L2_woT_host": (D, D), "L2_wiqT_host": (D, 256), "L2_wikT_host": (D, 64),
        "L2_wiwT_host": (D, 4),
        "b_q_row": (1, D), "b_k_row": (1, D), "b_v_row": (1, D), "b_o_row": (1, D),
        "b_iq_row": (1, 256), "b_ik_row": (1, 64), "b_iw_row": (1, 4),
        "c_l2cos": (128, 4 * T), "c_l2sin": (128, 4 * T),
    },
    "L3_attn": {
        "L3_attn_WqT_host": (D, D), "L3_attn_bq": (D,),
        "L3_attn_WkT_host": (D, D), "L3_attn_bk": (D,),
        "L3_attn_WvT_host": (D, D), "L3_attn_bv": (D,),
        "L3_attn_WoT_host": (D, D), "L3_attn_bo": (D,),
        "c_l3cos": (128, T), "c_l3sin": (128, T),
        "c_l3m_a4": (128, 512), "c_l3m_b4": (128, 512),
    },
    "L1_attn": {
        "L1_attn_W_inT_host": (D, 4096), "L1_attn_conv_w": (2048, 4),
        "conv_b_col": (128, 16), "D_col": (128, 16),
        "L1_attn_W_xprojT_host": (2048, 2080), "L1_attn_W_outT_host": (2048, D),
        "c_mW0": (128, 1), "c_mW1": (128, 1),
    },
    "L1_ffn": {
        "L1_ffn_WgT_host": (D, 4), "L1_ffn_bg": (4,),
        "L1_ffn_WiT_host": (D, 4096), "L1_ffn_WoT_host": (4096, D),
        "L1_ffn_AT_host": (D, 16), "L1_ffn_BT_host": (4, 4, 4096),
    },
    "L2_ffn": {
        "L2_ffn_gate_w": (D, 8),
        "L2_ffn_ew1": (8, D, 2048), "L2_ffn_ew2": (8, D, 2048), "L2_ffn_ew3": (8, 2048, D),
        "L2_ffn_sw1": (D, 2048), "L2_ffn_sw2": (D, 2048), "L2_ffn_sw3": (2048, D),
    },
    "L3_ffn": {
        "L3_ffn_gate_W": (D, 8),
        "L3_ffn_W1": (8, D, 512), "L3_ffn_b1": (8, 512),
        "L3_ffn_W2": (8, 512, D), "L3_ffn_b2": (8, D),
        "L3_ffn_Ws1": (D, 512), "L3_ffn_bs1": (512,),
        "L3_ffn_Ws2": (512, D), "L3_ffn_bs2": (D,),
        "route_scale_col": (128, 1),
    },
    "L0_ffn": {
        "L0_ffn_gate_w": (D, 8), "L0_ffn_gate_b": (8,),
        "L0_ffn_w1": (8, D, 2048), "L0_ffn_b1": (8, 2048),
        "L0_ffn_w2": (8, 2048, D), "L0_ffn_b2": (8, D),
    },
}


def std_prologue(cm, lnx_first=False):
    p = cm.p
    if lnx_first:
        cm.LNX = p.sb("z2_LNX", [128, CH * T], BF16)
        cm.XS = p.sb("z3_XS", [128, CH * T], F32)
    else:
        cm.XS = p.sb("z0_XS", [128, CH * T], F32)
        cm.LNX = p.sb("z1_LNX", [128, CH * T], BF16)
    cm.load_X(cm.x_in)
    cm.layernorm()


def std_epilogue(cm):
    cm.store_X(cm.x_out)
    cm.p.free_prefix("z")


def build_program(sublayers, weight_names):
    p = Prog()
    ins = {}
    ins["x_in"] = p.dram("x_in", [D, T], F32, kind="ExternalInput")
    x_out = p.dram("x_out", [D, T], F32, kind="ExternalOutput")
    for n, (shape, dt) in const_decls().items():
        ins[n] = p.dram(n, list(shape), dt, kind="ExternalInput")
    for n, shape in weight_names.items():
        dt = BF16 if n in BF16_INPUTS else F32
        ins[n] = p.dram(n, list(shape), dt, kind="ExternalInput")
    cm = Common(p, ins)
    chain = [ins["x_in"]]
    for j in range(len(sublayers) - 1):
        chain.append(p.dram(f"x_mid{j}", [D, T], F32))
    chain.append(x_out)
    for j, sl in enumerate(sublayers):
        cm.x_in = chain[j]
        cm.x_out = chain[j + 1]
        p.relax = sl in RELAXED
        SUBLAYER_EMITTERS[sl](cm)
        p.relax = True
    p.final_wait()
    nc = p.finalize()
    return nc


RELAXED = {"L0_ffn", "L1_ffn", "L2_ffn", "L3_ffn", "L3_attn"}

SUBLAYER_ORDER = ["L0_attn", "L0_ffn", "L1_attn", "L1_ffn",
                  "L2_attn", "L2_ffn", "L3_attn", "L3_ffn"]

LAUNCH_PLAN = [SUBLAYER_ORDER]


def _core_in_maps(groups_weights, inp, der, xs_parts):
    allowed = set(const_decls()) | set(groups_weights) | {"x_in"}
    in_maps = []
    for core in range(8):
        m = dict(host_constants(core % 2))
        m["x_in"] = np.ascontiguousarray(xs_parts[core], dtype=np.float32)
        for n in groups_weights:
            if n in der:
                m[n] = der[n]
            elif n in m:
                pass
            else:
                m[n] = np.ascontiguousarray(np.asarray(inp[n], dtype=np.float32))
        m = {k: v for k, v in m.items() if k in allowed}
        in_maps.append(m)
    return in_maps


def kernel(**inputs):
    inp = inputs
    der = derive_inputs(inp)
    x = np.ascontiguousarray(np.asarray(inp["L0_attn_x"], dtype=np.float32))
    xs_parts = shard_x(x)
    for group in LAUNCH_PLAN:
        wn = {}
        for s in group:
            wn.update(SUBLAYER_WEIGHTS[s])
        nc = build_program(group, wn)
        in_maps = _core_in_maps(wn, inp, der, xs_parts)
        res = run_bass_kernel_spmd(nc, in_maps, core_ids=list(range(8)))
        xs_parts = [np.asarray(res.results[c]["x_out"]) for c in range(8)]
    return unshard_x(xs_parts)


# version 18
# speedup vs baseline: 1.1581x; 1.1581x over previous
import numpy as np
import ml_dtypes
import concourse.bass as bass
import concourse.mybir as mybir
from concourse.bass_utils import run_bass_kernel_spmd

F32 = mybir.dt.float32
BF16 = mybir.dt.bfloat16
FP16 = mybir.dt.float16
U8 = mybir.dt.uint8
AF = mybir.ActivationFunctionType
ALU = mybir.AluOpType
AXX = mybir.AxisListType.X

D = 1024
T = 2048
NT = 16
S = 4096
B = 4
CH = 8
ENGINES = ("tensor", "vector", "scalar", "gpsimd", "sync")

BF = ml_dtypes.bfloat16
DBG = {"l3_stop": None}


class Prog:
    def __init__(self):
        self.nc = bass.Bass("TRN2", target_bir_lowering=False)
        self.ops = {e: [] for e in ENGINES}
        self._ctx = []
        self.sems = {}
        self.cnt = {}
        self._phidx = 0
        self._cur = "ph0"
        self._pending = {e: [] for e in ENGINES}
        self.sem("ph0")
        self.pf = [self._ps(f"pbF{i}", [128, 512], F32) for i in range(6)]
        self.pb = [self._ps(f"pbB{i}", [128, 1024], BF16) for i in range(2)]

    def sem(self, name):
        if name not in self.sems:
            g = self.nc.semaphore(name)
            self.sems[name] = g.__enter__()
            self._ctx.append(g)
            self.cnt[name] = 0
        return name

    def sb(self, name, shape, dt):
        self._uid = getattr(self, "_uid", 0) + 1
        uname = f"s_{name}__{self._uid}"
        g = self.nc.sbuf_tensor(uname, shape, dt)
        h = g.__enter__()
        self._guards = getattr(self, "_guards", {})
        self._guards[uname] = g
        return h

    def free_prefix(self, pref):
        gd = getattr(self, "_guards", {})
        for n in reversed(list(gd)):
            if n.startswith("s_" + pref):
                gd.pop(n).__exit__(None, None, None)

    def _ps(self, name, shape, dt):
        g = self.nc.psum_tensor(name, shape, dt)
        h = g.__enter__()
        self._ctx.append(g)
        return h

    def dram(self, name, shape, dt, kind="Internal"):
        return self.nc.dram_tensor(name, shape, dt, kind=kind).ap()

    SEM_LIMIT = 48000

    def _rotate(self):
        old, final = self._cur, self.cnt[self._cur]
        self._phidx += 1
        self._cur = f"ph{self._phidx}"
        self.sem(self._cur)
        for e in ENGINES:
            self._pending[e].append((old, final))

    def mark(self):
        return (self._cur, self.cnt[self._cur])

    relax = True

    def op(self, eng, fn, dma=False, wait=None, inc=True):
        if not self.relax:
            wait = None
            inc = True
        amt = 16 if dma else 1
        if inc and self.cnt[self._cur] + amt > self.SEM_LIMIT:
            self._rotate()
        cur = self._cur
        if wait is None:
            wait_t = [(cur, self.cnt[cur])]
        elif wait is False:
            wait_t = []
        elif isinstance(wait, list):
            wait_t = [w for w in wait if w is not None]
            if len(wait_t) != len(wait):
                wait_t = [(cur, self.cnt[cur])]
        else:
            wait_t = [wait]
        if inc:
            inc_t = (cur, amt)
            self.cnt[cur] += amt
        else:
            inc_t = None
        pend = tuple(self._pending[eng])
        self._pending[eng] = []

        wm = {}
        for s, v in wait_t:
            wm[s] = max(wm.get(s, 0), v)
        wait_t = tuple(wm.items())

        def run(e, fn=fn, wait_t=wait_t, inc_t=inc_t, pend=pend):
            for s, v in pend:
                e.wait_ge(self.sems[s], v)
            for s, v in wait_t:
                e.wait_ge(self.sems[s], v)
            ins = fn(e)
            if inc_t is not None:
                ins.then_inc(self.sems[inc_t[0]], inc_t[1])
        self.ops[eng].append(run)

    def op_collective(self, fn):
        self.sem("cc")
        if self.cnt[self._cur] + 1 > self.SEM_LIMIT:
            self._rotate()
        cur = self._cur
        wait_val = self.cnt[cur]
        self.cnt["cc"] += 1
        cc_target = self.cnt["cc"]
        self.cnt[cur] += 1
        pend = tuple(self._pending["gpsimd"])
        self._pending["gpsimd"] = []

        def run(e, fn=fn, cur=cur, wait_val=wait_val, cc_target=cc_target, pend=pend):
            for s, v in pend:
                e.wait_ge(self.sems[s], v)
            e.wait_ge(self.sems[cur], wait_val)
            fn(e).then_inc(self.sems["cc"])
            e.wait_ge(self.sems["cc"], cc_target)
            e.engine_nop().then_inc(self.sems[cur], 1)
        self.ops["gpsimd"].append(run)

    def raw(self, eng, fn):
        self.ops[eng].append(fn)

    def final_wait(self):
        cur = self._cur
        v = self.cnt[cur]
        pend = tuple(self._pending["sync"])
        self._pending["sync"] = []

        def run(e, cur=cur, v=v, pend=pend):
            for s, vv in pend:
                e.wait_ge(self.sems[s], vv)
            e.wait_ge(self.sems[cur], v)
        self.ops["sync"].append(run)

    def finalize(self):
        blk = self.nc.Block()
        block = blk.__enter__()
        for eng in ENGINES:
            fns = self.ops[eng]
            if not fns:
                continue
            getattr(block, eng)(lambda e, fns=fns: [f(e) for f in fns])
        blk.__exit__(None, None, None)
        return self.nc


def host_constants(h):
    c = {}
    c["c_mW0"] = np.full((128, 1), float(h), dtype=np.float32)
    gp = ((np.arange(T) // 128) * 2 + h) * 128 + (np.arange(T) % 128)
    theta = 10000.0 ** (-2.0 * np.arange(32, dtype=np.float64) / 64.0)
    ang = (gp[None, :] + 1.0) * theta[(np.arange(128) % 32)][:, None]
    c["c_l3cos"] = np.cos(ang).astype(BF)
    c["c_l3sin"] = np.sin(ang).astype(BF)
    half = 512
    inv_freq = 1.0 / 10000.0 ** (np.arange(half, dtype=np.float64) / half)
    ang2 = gp[None, :] * inv_freq[:, None]
    cos2 = np.cos(ang2); sin2 = np.sin(ang2)
    tri0 = (np.arange(128)[:, None] <= np.arange(128)[None, :]).astype(np.float32)
    c["c_l0tri"] = tri0.astype(BF)
    angc = (np.pi / 2.0) * gp.astype(np.float64) / 4096.0
    cs = np.zeros((128, T), np.float32)
    cs[0:64, :] = np.cos(angc)[None, :]
    cs[64:128, :] = np.sin(angc)[None, :]
    c["c_l0cs"] = cs.astype(BF)
    c["c_l2cos"] = np.ascontiguousarray(
        cos2.reshape(4, 128, T).transpose(1, 0, 2).reshape(128, 4 * T)).astype(BF)
    c["c_l2sin"] = np.ascontiguousarray(
        sin2.reshape(4, 128, T).transpose(1, 0, 2).reshape(128, 4 * T)).astype(BF)
    tri = (np.arange(128)[:, None] <= np.arange(128)[None, :]).astype(np.float32)
    if h == 0:
        ma, mb = tri, np.zeros((128, 128), np.float32)
    else:
        ma, mb = np.ones((128, 128), np.float32), tri
    c["c_l3m_a4"] = np.tile(ma, (1, 4)).astype(BF)
    c["c_l3m_b4"] = np.tile(mb, (1, 4)).astype(BF)
    c["c_mW1"] = np.full((128, 1), float(1 - h), dtype=np.float32)
    gpos = (np.arange(T) // 128 * 2 + h) * 128 + (np.arange(T) % 128)
    c["c_ident_bf"] = np.eye(128, dtype=np.float32).astype(BF)
    c["c_ident_f"] = np.eye(128, dtype=np.float32)
    c["c_ones_col_f"] = np.ones((128, 1), dtype=np.float32)
    c["c_ones_col_bf"] = np.ones((128, 1), dtype=np.float32).astype(BF)
    c["c_ones_row_f"] = np.ones((1, 128), dtype=np.float32)
    c["c_ones_row_bf"] = np.ones((1, 128), dtype=np.float32).astype(BF)
    c["c_gpos"] = gpos.astype(np.float32).reshape(1, T)
    c["c_ones512_f"] = np.ones((1, 512), dtype=np.float32)
    ho = np.zeros((128, 2), dtype=np.float32)
    ho[0:64, 0] = 1.0
    ho[64:128, 1] = 1.0
    c["c_half_ones"] = ho
    return c


def const_decls():
    return {
        "c_ident_bf": ([128, 128], BF16),
        "c_ident_f": ([128, 128], F32),
        "c_ones_col_f": ([128, 1], F32),
        "c_ones_col_bf": ([128, 1], BF16),
        "c_ones_row_f": ([1, 128], F32),
        "c_ones_row_bf": ([1, 128], BF16),
        "c_gpos": ([1, T], F32),
        "c_ones512_f": ([1, 512], F32),
        "c_half_ones": ([128, 2], F32),
    }


class _LazyConsts:
    def __init__(self, cm):
        self.cm = cm
        self.d = {}

    def __getitem__(self, n):
        if n not in self.d:
            self.d[n] = self.cm._load_const(n)
        return self.d[n]


class Common:

    def __init__(self, p: Prog, ins):
        self.p = p
        self.ins = ins
        self.XS = None
        self.LNX = None
        self.x_in = None
        self.x_out = None
        self.SC1 = p.sb("SC1", [128, 512], F32)
        self.SC2 = p.sb("SC2", [128, 512], F32)
        self.SROW = p.sb("SROW", [1, 1024], F32)
        self._const_decls = const_decls()
        self.consts = _LazyConsts(self)
        for n in self._const_decls:
            if n != "c_gpos":
                self.consts[n]

    def _load_const(self, n):
        shape, dt = self._const_decls[n]
        t = self.p.sb(n, shape, dt)
        self.p.op("sync", lambda e, t=t, n=n: e.dma_start(out=t[:], in_=self.ins[n][:, :]), dma=True)
        return t

    def xs(self, c, t0=0, n=T):
        return self.XS[:, c * T + t0: c * T + t0 + n]

    def lnx(self, c, t0=0, n=T):
        return self.LNX[:, c * T + t0: c * T + t0 + n]

    def load_X(self, xdram):
        p = self.p
        XSB = self.XS
        for c in range(CH):
            p.op("sync", lambda e, c=c, XSB=XSB: e.dma_start(
                out=XSB[:, c * T:(c + 1) * T],
                in_=xdram[c * 128:(c + 1) * 128, :]), dma=True)

    def store_X(self, xdram):
        p = self.p
        XSB = self.XS
        for c in range(CH):
            p.op("sync", lambda e, c=c, XSB=XSB: e.dma_start(
                out=xdram[c * 128:(c + 1) * 128, :],
                in_=XSB[:, c * T:(c + 1) * T]), dma=True)

    def layernorm(self, rows=None, eps=1e-5, rms=False, src=None, fp32_dst=None):
        p = self.p
        C = self.consts
        ones_col_f = C["c_ones_col_f"]
        ones_row_f = C["c_ones_row_f"]
        XSB = self.XS
        LNXB = self.LNX
        if src is None:
            src = lambda c, t0, n, XSB=XSB: XSB[:, c * T + t0: c * T + t0 + n]
        psum = p.pf[4][0:1, :]
        psumsq = p.pf[5][0:1, :]
        pb_a = p.pf[2]
        pb_b = p.pf[3]
        for tb in range(4):
            t0 = tb * 512
            for c in range(CH):
                x = src(c, t0, 512)
                p.op("scalar", lambda e, x=x: e.activation(self.SC1[:], x, AF.Square))
                if not rms:
                    p.op("tensor", lambda e, x=x, st=(c == 0): e.matmul(
                        psum, ones_col_f[:], x, start=st, stop=(c == CH - 1)))
                p.op("tensor", lambda e, st=(c == 0): e.matmul(
                    psumsq, ones_col_f[:], self.SC1[:], start=st, stop=(c == CH - 1)))
            if rows is not None:
                rr = rows[0][:, t0:t0 + 512]
                rn = rows[1][:, t0:t0 + 512]
            else:
                rr = self.SROW[:, 0:512]
                rn = self.SROW[:, 512:1024]
            if not rms:
                p.op("vector", lambda e, rn=rn: e.tensor_scalar_mul(rn, psum, 1.0 / D))
                p.op("vector", lambda e, rr=rr: e.tensor_scalar_mul(rr, psumsq, 1.0 / D))
                p.op("vector", lambda e, rn=rn: e.scalar_tensor_tensor(
                    self.SC1[0:1, :], rn, 1.0, rn, ALU.mult, ALU.mult))
                p.op("vector", lambda e, rr=rr: e.tensor_sub(rr, rr, self.SC1[0:1, :]))
            else:
                p.op("vector", lambda e, rr=rr: e.tensor_scalar_mul(rr, psumsq, 1.0 / D))
            p.op("vector", lambda e, rr=rr, eps=eps: e.tensor_scalar_add(rr, rr, float(eps)))
            p.op("scalar", lambda e, rr=rr: e.activation(rr, rr, AF.Sqrt))
            p.op("vector", lambda e, rr=rr: e.reciprocal(rr, rr))
            if not rms:
                p.op("vector", lambda e, rr=rr, rn=rn: e.scalar_tensor_tensor(
                    rn, rn, -1.0, rr, ALU.mult, ALU.mult))
            p.op("tensor", lambda e, rr=rr: e.matmul(pb_a[:], ones_row_f[:], rr, start=True, stop=True))
            if not rms:
                p.op("tensor", lambda e, rn=rn: e.matmul(pb_b[:], ones_row_f[:], rn, start=True, stop=True))
            for c in range(CH):
                x = src(c, t0, 512)
                dst = LNXB[:, c * T + t0: c * T + t0 + 512]
                p.op("vector", lambda e, x=x: e.tensor_mul(self.SC2[:], x, pb_a[:]))
                if not rms:
                    p.op("vector", lambda e, dst=dst: e.tensor_add(dst, self.SC2[:], pb_b[:]))
                    if fp32_dst is not None:
                        hd = fp32_dst[:, c * T + t0: c * T + t0 + 512]
                        p.op("vector", lambda e, hd=hd: e.tensor_add(hd, self.SC2[:], pb_b[:]))
                else:
                    p.op("vector", lambda e, dst=dst: e.tensor_copy(out=dst, in_=self.SC2[:]))


def emit_L0_ffn(cm: Common, pref="l0f"):
    p, C, ins = cm.p, cm.consts, cm.ins
    _LNXB = cm.LNX
    _XSB = cm.XS
    def lnxs(c, t0=0, n=T, _b=_LNXB):
        return _b[:, c * T + t0: c * T + t0 + n]
    def xss(c, t0=0, n=T, _b=_XSB):
        return _b[:, c * T + t0: c * T + t0 + n]
    E, F = 8, 2048
    gw = p.sb(pref + "gw", [128, CH * 8], BF16)
    gb = p.sb(pref + "gb", [1, 8], BF16)
    WgT = p.sb(pref + "WgT", [128, NT * 8], F32)
    wrow = p.sb(pref + "wrow", [1, 512], F32)
    w1 = p.sb(pref + "w1", [128, CH * F], BF16)
    w2 = p.sb(pref + "w2", [128, 8 * D], BF16)
    b1 = p.sb(pref + "b1", [1, F], F32)
    b2 = p.sb(pref + "b2", [1, D], F32)
    hb = p.sb(pref + "hb", [128, (F // 128) * 512], BF16)
    wbb = p.sb(pref + "wb", [128, 512], BF16)
    g8 = p.sb(pref + "g8", [128, 8], F32)
    m1 = p.sb(pref + "m1", [128, 1], F32)
    nm1 = p.sb(pref + "nm1", [128, 1], F32)
    m2 = p.sb(pref + "m2", [128, 1], F32)
    sel = p.sb(pref + "sel", [128, 8], F32)
    selm = p.sb(pref + "selm", [128, 8], U8)
    neg = p.sb(pref + "neg", [128, 8], F32)
    den = p.sb(pref + "den", [128, 1], F32)

    p.op("gpsimd", lambda e: e.dma_start(
        out=gw[:].rearrange("p (c e) -> p c e", e=8),
        in_=ins["L0_ffn_gate_w"].rearrange("(c p) e -> p c e", p=128)), dma=True)
    p.op("gpsimd", lambda e: e.dma_start(out=gb[:], in_=ins["L0_ffn_gate_b"].rearrange(
        "(u e) -> u e", u=1)), dma=True)
    ones_row_bf = C["c_ones_row_bf"]
    ones_row_f = C["c_ones_row_f"]
    ones512_f = C["c_ones512_f"]
    ident_f = C["c_ident_f"]
    p.op("vector", lambda e: e.memset(neg[:], -1e30))

    ps = p.pf[0][:, 0:8]
    pr = p.pf[1][0:1, 0:128]
    for i in range(NT):
        t0 = i * 128
        p.op("tensor", lambda e: e.matmul(ps, ones_row_bf[:], gb[:],
                                          start=True, stop=False))
        for c in range(CH):
            p.op("tensor", lambda e, c=c, t0=t0: e.matmul(
                ps, lnxs(c, t0, 128), gw[:, c * 8:(c + 1) * 8],
                start=False, stop=(c == CH - 1)))
        p.op("vector", lambda e: e.tensor_copy(out=g8[:], in_=ps))
        p.op("vector", lambda e: e.reduce_max(m1[:], g8[:], axis=AXX))
        p.op("vector", lambda e: e.tensor_scalar_mul(nm1[:], m1[:], -1.0))
        p.op("vector", lambda e: e.tensor_scalar(selm[:], g8[:], m1[:, 0:1], None, ALU.is_lt))
        p.op("vector", lambda e: e.select(sel[:], selm[:], g8[:], neg[:]))
        p.op("vector", lambda e: e.reduce_max(m2[:], sel[:], axis=AXX))
        wg = WgT[:, i * 8:(i + 1) * 8]
        p.op("scalar", lambda e, wg=wg: e.activation(wg, g8[:], AF.Exp, bias=nm1[:, 0:1]))
        p.op("vector", lambda e: e.tensor_scalar(sel[:], g8[:], m2[:, 0:1], None, ALU.is_ge))
        p.op("vector", lambda e, wg=wg: e.tensor_mul(wg, wg, sel[:]))
        p.op("vector", lambda e: e.tensor_sub(den[:], m2[:], m1[:]))
        p.op("scalar", lambda e: e.activation(den[:], den[:], AF.Exp))
        p.op("vector", lambda e: e.tensor_scalar_add(den[:], den[:], 1.0))
        p.op("vector", lambda e: e.reciprocal(den[:], den[:]))
        p.op("vector", lambda e, wg=wg: e.tensor_scalar(wg, wg, den[:, 0:1], None, ALU.mult))

    w1r = ins["L0_ffn_w1"].rearrange("e (c p) f -> e p c f", p=128)
    w2r = ins["L0_ffn_w2"].rearrange("e (c p) f -> e p c f", p=128)
    b1r = ins["L0_ffn_b1"].rearrange("e (u f) -> e u f", u=1)
    NF = F // 128
    b2r = ins["L0_ffn_b2"].rearrange("e (u d) -> e u d", u=1)
    bankm = {}
    for ee in range(E):
        p.op("gpsimd", lambda e, ee=ee: e.dma_start(
            out=w1[:].rearrange("p (c f) -> p c f", f=F), in_=w1r[ee]), dma=True)
        p.op("sync", lambda e, ee=ee: e.dma_start(out=b1[:], in_=b1r[ee]), dma=True)
        p.op("sync", lambda e, ee=ee: e.dma_start(out=b2[:], in_=b2r[ee]), dma=True)
        for tb in range(4):
            t0 = tb * 512
            for j in range(4):
                i = tb * 4 + j
                p.op("tensor", lambda e, i=i, ee=ee: e.matmul(
                    pr, WgT[:, i * 8 + ee: i * 8 + ee + 1], ident_f[:],
                    start=True, stop=True))
                p.op("vector", lambda e, j=j: e.tensor_copy(
                    out=wrow[:, j * 128:(j + 1) * 128], in_=pr))
            pwb = p.pf[2]
            p.op("tensor", lambda e: e.matmul(
                pwb[:], ones_row_f[:], wrow[:], start=True, stop=True))
            p.op("vector", lambda e: e.tensor_copy(out=wbb[:], in_=pwb[:]))
            for bb in range(4):
                bankm[bb] = p.mark()
            for oc in range(NF):
                ph = p.pf[oc % 2]
                p.op("tensor", lambda e, ph=ph, oc=oc: e.matmul(
                    ph[:], b1[:, oc * 128:(oc + 1) * 128], ones512_f[:],
                    start=True, stop=False),
                    wait=(None if oc < 2 else bankm.get(oc % 2)), inc=False)
                for c in range(CH):
                    p.op("tensor", lambda e, ph=ph, oc=oc, c=c, t0=t0: e.matmul(
                        ph[:], w1[:, c * F + oc * 128: c * F + oc * 128 + 128],
                        lnxs(c, t0, 512), start=False, stop=(c == CH - 1)),
                        wait=False, inc=(c == CH - 1))
                p.op("scalar", lambda e, ph=ph, oc=oc: e.activation(
                    cm.SC1[:], ph[:], AF.Gelu))
                p.op("vector", lambda e, oc=oc: e.tensor_mul(
                    hb[:, oc * 512:(oc + 1) * 512], cm.SC1[:], wbb[:]))
                bankm[oc % 2] = p.mark()
            for half in range(2):
                p.op("gpsimd", lambda e, ee=ee, half=half: e.dma_start(
                    out=w2[:].rearrange("p (c f) -> p c f", f=D),
                    in_=w2r[ee][:, half * 8:(half + 1) * 8, :]), dma=True)
                for oc in range(CH):
                    py = p.pf[2 + (oc % 2)]
                    first = True
                    if half == 0:
                        p.op("tensor", lambda e, py=py, oc=oc: e.matmul(
                            py[:], b2[:, oc * 128:(oc + 1) * 128],
                            wrow[:], start=True, stop=False),
                            wait=(None if oc < 2 else bankm.get(2 + oc % 2)), inc=False)
                        first = False
                    for fc in range(8):
                        p.op("tensor", lambda e, py=py, oc=oc, fc=fc, half=half: e.matmul(
                            py[:], w2[:, fc * D + oc * 128: fc * D + (oc + 1) * 128],
                            hb[:, (half * 8 + fc) * 512:(half * 8 + fc + 1) * 512],
                            start=(half == 1 and fc == 0), stop=(fc == 7)),
                            wait=((None if oc < 2 else bankm.get(2 + oc % 2)) if (first and fc == 0) else False),
                            inc=(fc == 7))
                    x = xss(oc, t0, 512)
                    p.op("vector", lambda e, py=py, x=x: e.tensor_add(x, x, py[:]))
                    bankm[2 + oc % 2] = p.mark()
    p.free_prefix(pref)


def emit_L3_ffn(cm: Common, pref="l3f"):
    p, C, ins = cm.p, cm.consts, cm.ins
    _LNXB = cm.LNX
    _XSB = cm.XS
    def lnxs(c, t0=0, n=T, _b=_LNXB):
        return _b[:, c * T + t0: c * T + t0 + n]
    def xss(c, t0=0, n=T, _b=_XSB):
        return _b[:, c * T + t0: c * T + t0 + n]
    E, H = 8, 512
    NF = H // 128
    gw = p.sb(pref + "gw", [128, CH * 8], BF16)
    rs = p.sb(pref + "rs", [128, 1], F32)
    WgT = p.sb(pref + "WgT", [128, NT * 8], F32)
    wrow = p.sb(pref + "wrow", [1, 512], F32)
    w1 = p.sb(pref + "w1", [128, CH * H], BF16)
    w2 = p.sb(pref + "w2", [128, NF * D], BF16)
    b1 = p.sb(pref + "b1", [1, H], F32)
    b2 = p.sb(pref + "b2", [1, D], F32)
    hb = p.sb(pref + "hb", [128, NF * 512], BF16)
    wbb = p.sb(pref + "wb", [128, 512], BF16)
    g8 = p.sb(pref + "g8", [128, 8], F32)
    m1 = p.sb(pref + "m1", [128, 1], F32)
    m2 = p.sb(pref + "m2", [128, 1], F32)
    sel = p.sb(pref + "sel", [128, 8], F32)
    selm = p.sb(pref + "selm", [128, 8], U8)
    neg = p.sb(pref + "neg", [128, 8], F32)
    den = p.sb(pref + "den", [128, 1], F32)
    ones_row_f = C["c_ones_row_f"]
    ones512_f = C["c_ones512_f"]
    ident_f = C["c_ident_f"]

    p.op("gpsimd", lambda e: e.dma_start(
        out=gw[:].rearrange("p (c e) -> p c e", e=8),
        in_=ins["L3_ffn_gate_W"].rearrange("(c p) e -> p c e", p=128)), dma=True)
    p.op("sync", lambda e: e.dma_start(out=rs[:], in_=ins["route_scale_col"][:, :]), dma=True)
    p.op("vector", lambda e: e.memset(neg[:], -1e30))

    ps = p.pf[0][:, 0:8]
    pr = p.pf[1][0:1, 0:128]
    for i in range(NT):
        t0 = i * 128
        for c in range(CH):
            p.op("tensor", lambda e, c=c, t0=t0: e.matmul(
                ps, lnxs(c, t0, 128), gw[:, c * 8:(c + 1) * 8],
                start=(c == 0), stop=(c == CH - 1)))
        p.op("scalar", lambda e: e.activation(g8[:], ps, AF.Sigmoid))
        p.op("vector", lambda e: e.tensor_scalar(g8[:], g8[:], rs[:, 0:1], None, ALU.mult))
        p.op("vector", lambda e: e.reduce_max(m1[:], g8[:], axis=AXX))
        p.op("vector", lambda e: e.tensor_scalar(selm[:], g8[:], m1[:, 0:1], None, ALU.is_lt))
        p.op("vector", lambda e: e.select(sel[:], selm[:], g8[:], neg[:]))
        p.op("vector", lambda e: e.reduce_max(m2[:], sel[:], axis=AXX))
        wg = WgT[:, i * 8:(i + 1) * 8]
        p.op("vector", lambda e: e.tensor_add(den[:], m1[:], m2[:]))
        p.op("vector", lambda e: e.reciprocal(den[:], den[:]))
        p.op("vector", lambda e, wg=wg: e.tensor_scalar(sel[:], g8[:], m2[:, 0:1], None, ALU.is_ge))
        p.op("vector", lambda e, wg=wg: e.tensor_mul(wg, g8[:], sel[:]))
        p.op("vector", lambda e, wg=wg: e.tensor_scalar(wg, wg, den[:, 0:1], None, ALU.mult))

    w1r = ins["L3_ffn_W1"].rearrange("e (c p) f -> e p c f", p=128)
    w2r = ins["L3_ffn_W2"].rearrange("e (c p) f -> e p c f", p=128)
    b1r = ins["L3_ffn_b1"].rearrange("e (u f) -> e u f", u=1)
    b2r = ins["L3_ffn_b2"].rearrange("e (u f) -> e u f", u=1)
    ws1r = ins["L3_ffn_Ws1"].rearrange("(c p) f -> p c f", p=128)
    ws2r = ins["L3_ffn_Ws2"].rearrange("(c p) f -> p c f", p=128)
    bankm = {}
    for ee in range(E + 1):
        shared = ee == E
        if shared:
            p.op("gpsimd", lambda e: e.dma_start(
                out=w1[:].rearrange("p (c f) -> p c f", f=H), in_=ws1r), dma=True)
            p.op("gpsimd", lambda e: e.dma_start(
                out=w2[:].rearrange("p (c f) -> p c f", f=D), in_=ws2r), dma=True)
            p.op("sync", lambda e: e.dma_start(
                out=b1[:], in_=ins["L3_ffn_bs1"].rearrange("(u f) -> u f", u=1)), dma=True)
            p.op("sync", lambda e: e.dma_start(
                out=b2[:], in_=ins["L3_ffn_bs2"].rearrange("(u f) -> u f", u=1)), dma=True)
        else:
            p.op("gpsimd", lambda e, ee=ee: e.dma_start(
                out=w1[:].rearrange("p (c f) -> p c f", f=H), in_=w1r[ee]), dma=True)
            p.op("gpsimd", lambda e, ee=ee: e.dma_start(
                out=w2[:].rearrange("p (c f) -> p c f", f=D), in_=w2r[ee]), dma=True)
            p.op("sync", lambda e, ee=ee: e.dma_start(out=b1[:], in_=b1r[ee]), dma=True)
            p.op("sync", lambda e, ee=ee: e.dma_start(out=b2[:], in_=b2r[ee]), dma=True)
        for tb in range(4):
            t0 = tb * 512
            if not shared:
                for j in range(4):
                    i = tb * 4 + j
                    p.op("tensor", lambda e, i=i, ee=ee: e.matmul(
                        pr, WgT[:, i * 8 + ee: i * 8 + ee + 1], ident_f[:],
                        start=True, stop=True))
                    p.op("vector", lambda e, j=j: e.tensor_copy(
                        out=wrow[:, j * 128:(j + 1) * 128], in_=pr))
                pwb = p.pf[2]
                p.op("tensor", lambda e: e.matmul(
                    pwb[:], ones_row_f[:], wrow[:], start=True, stop=True))
                p.op("vector", lambda e: e.tensor_copy(out=wbb[:], in_=pwb[:]))
                for bb in range(4):
                    bankm[bb] = p.mark()
            for oc in range(NF):
                ph = p.pf[oc % 2]
                p.op("tensor", lambda e, ph=ph, oc=oc: e.matmul(
                    ph[:], b1[:, oc * 128:(oc + 1) * 128], ones512_f[:],
                    start=True, stop=False),
                    wait=(None if oc < 2 else bankm.get(oc % 2)), inc=False)
                for c in range(CH):
                    p.op("tensor", lambda e, ph=ph, oc=oc, c=c, t0=t0: e.matmul(
                        ph[:], w1[:, c * H + oc * 128: c * H + oc * 128 + 128],
                        lnxs(c, t0, 512), start=False, stop=(c == CH - 1)),
                        wait=False, inc=(c == CH - 1))
                p.op("scalar", lambda e, ph=ph: e.activation(cm.SC1[:], ph[:], AF.Gelu))
                if shared:
                    p.op("vector", lambda e, oc=oc: e.tensor_copy(
                        out=hb[:, oc * 512:(oc + 1) * 512], in_=cm.SC1[:]))
                else:
                    p.op("vector", lambda e, oc=oc: e.tensor_mul(
                        hb[:, oc * 512:(oc + 1) * 512], cm.SC1[:], wbb[:]))
                bankm[oc % 2] = p.mark()
            for oc in range(CH):
                py = p.pf[2 + (oc % 2)]
                if shared:
                    p.op("tensor", lambda e, py=py, oc=oc: e.matmul(
                        py[:], b2[:, oc * 128:(oc + 1) * 128], ones512_f[:],
                        start=True, stop=False),
                        wait=(None if oc < 2 else bankm.get(2 + oc % 2)), inc=False)
                else:
                    p.op("tensor", lambda e, py=py, oc=oc: e.matmul(
                        py[:], b2[:, oc * 128:(oc + 1) * 128], wrow[:],
                        start=True, stop=False),
                        wait=(None if oc < 2 else bankm.get(2 + oc % 2)), inc=False)
                for fc in range(NF):
                    p.op("tensor", lambda e, py=py, oc=oc, fc=fc: e.matmul(
                        py[:], w2[:, fc * D + oc * 128: fc * D + (oc + 1) * 128],
                        hb[:, fc * 512:(fc + 1) * 512], start=False, stop=(fc == NF - 1)),
                        wait=False, inc=(fc == NF - 1))
                x = xss(oc, t0, 512)
                p.op("vector", lambda e, py=py, x=x: e.tensor_add(x, x, py[:]))
                bankm[2 + oc % 2] = p.mark()
    p.free_prefix(pref)


def emit_L2_ffn(cm: Common, pref="l2f"):
    p, C, ins = cm.p, cm.consts, cm.ins
    _LNXB = cm.LNX
    _XSB = cm.XS
    def lnxs(c, t0=0, n=T, _b=_LNXB):
        return _b[:, c * T + t0: c * T + t0 + n]
    def xss(c, t0=0, n=T, _b=_XSB):
        return _b[:, c * T + t0: c * T + t0 + n]
    E, H = 8, 2048
    gw = p.sb(pref + "gw", [128, CH * 8], BF16)
    WgT = p.sb(pref + "WgT", [128, NT * 8], F32)
    wrow = p.sb(pref + "wrow", [1, T], F32)
    w1 = p.sb(pref + "w1", [128, CH * (H // 2)], BF16)
    w2 = p.sb(pref + "w2", [128, CH * (H // 2)], BF16)
    w3 = p.sb(pref + "w3", [128, (H // 128) * D], BF16)
    hb = p.sb(pref + "hb", [128, (H // 128) * 512], BF16)
    wbb = p.sb(pref + "wb", [128, T], BF16)
    g8 = p.sb(pref + "g8", [128, 8], F32)
    m1 = p.sb(pref + "m1", [128, 1], F32)
    nm1 = p.sb(pref + "nm1", [128, 1], F32)
    m2 = p.sb(pref + "m2", [128, 1], F32)
    sel = p.sb(pref + "sel", [128, 8], F32)
    selm = p.sb(pref + "selm", [128, 8], U8)
    neg = p.sb(pref + "neg", [128, 8], F32)
    den = p.sb(pref + "den", [128, 1], F32)
    ones_row_f = C["c_ones_row_f"]
    ident_f = C["c_ident_f"]

    p.op("gpsimd", lambda e: e.dma_start(
        out=gw[:].rearrange("p (c e) -> p c e", e=8),
        in_=ins["L2_ffn_gate_w"].rearrange("(c p) e -> p c e", p=128)), dma=True)
    p.op("vector", lambda e: e.memset(neg[:], -1e30))

    ps = p.pf[0][:, 0:8]
    pr = p.pf[1][0:1, 0:128]
    for i in range(NT):
        t0 = i * 128
        for c in range(CH):
            p.op("tensor", lambda e, c=c, t0=t0: e.matmul(
                ps, lnxs(c, t0, 128), gw[:, c * 8:(c + 1) * 8],
                start=(c == 0), stop=(c == CH - 1)))
        p.op("vector", lambda e: e.tensor_copy(out=g8[:], in_=ps))
        p.op("vector", lambda e: e.reduce_max(m1[:], g8[:], axis=AXX))
        p.op("vector", lambda e: e.tensor_scalar_mul(nm1[:], m1[:], -1.0))
        p.op("vector", lambda e: e.tensor_scalar(selm[:], g8[:], m1[:, 0:1], None, ALU.is_lt))
        p.op("vector", lambda e: e.select(sel[:], selm[:], g8[:], neg[:]))
        p.op("vector", lambda e: e.reduce_max(m2[:], sel[:], axis=AXX))
        wg = WgT[:, i * 8:(i + 1) * 8]
        p.op("scalar", lambda e, wg=wg: e.activation(wg, g8[:], AF.Exp, bias=nm1[:, 0:1]))
        p.op("vector", lambda e: e.tensor_scalar(sel[:], g8[:], m2[:, 0:1], None, ALU.is_ge))
        p.op("vector", lambda e, wg=wg: e.tensor_mul(wg, wg, sel[:]))
        p.op("vector", lambda e: e.tensor_sub(den[:], m2[:], m1[:]))
        p.op("scalar", lambda e: e.activation(den[:], den[:], AF.Exp))
        p.op("vector", lambda e: e.tensor_scalar_add(den[:], den[:], 1.0))
        p.op("vector", lambda e: e.reciprocal(den[:], den[:]))
        p.op("vector", lambda e, wg=wg: e.tensor_scalar(wg, wg, den[:, 0:1], None, ALU.mult))

    w1r = ins["L2_ffn_ew1"].rearrange("e (c p) f -> e p c f", p=128)
    w2r = ins["L2_ffn_ew2"].rearrange("e (c p) f -> e p c f", p=128)
    w3r = ins["L2_ffn_ew3"].rearrange("e (c p) f -> e p c f", p=128)
    sw1r = ins["L2_ffn_sw1"].rearrange("(c p) f -> p c f", p=128)
    sw2r = ins["L2_ffn_sw2"].rearrange("(c p) f -> p c f", p=128)
    sw3r = ins["L2_ffn_sw3"].rearrange("(c p) f -> p c f", p=128)
    NF = H // 128
    HH = H // 2
    bankm = {0: None, 1: None, 2: None, 3: None}
    for ee in range(E + 1):
        shared = ee == E
        if shared:
            p.op("gpsimd", lambda e: e.dma_start(
                out=w3[:].rearrange("p (c f) -> p c f", f=D), in_=sw3r), dma=True)
        else:
            p.op("gpsimd", lambda e, ee=ee: e.dma_start(
                out=w3[:].rearrange("p (c f) -> p c f", f=D), in_=w3r[ee]), dma=True)
            for tb in range(4):
                for j in range(4):
                    i = tb * 4 + j
                    p.op("tensor", lambda e, i=i, ee=ee: e.matmul(
                        pr, WgT[:, i * 8 + ee: i * 8 + ee + 1], ident_f[:],
                        start=True, stop=True))
                    p.op("vector", lambda e, i=i: e.tensor_copy(
                        out=wrow[:, i * 128:(i + 1) * 128], in_=pr))
            pwb = p.pf[4]
            for tb in range(4):
                p.op("tensor", lambda e, tb=tb: e.matmul(
                    pwb[:], ones_row_f[:], wrow[:, tb * 512:(tb + 1) * 512],
                    start=True, stop=True))
                p.op("vector", lambda e, tb=tb: e.tensor_copy(
                    out=wbb[:, tb * 512:(tb + 1) * 512], in_=pwb[:]))
        for fh in range(2):
            if shared:
                p.op("gpsimd", lambda e, fh=fh: e.dma_start(
                    out=w1[:].rearrange("p (c f) -> p c f", f=HH),
                    in_=sw1r[:, :, fh * HH:(fh + 1) * HH]), dma=True)
                p.op("gpsimd", lambda e, fh=fh: e.dma_start(
                    out=w2[:].rearrange("p (c f) -> p c f", f=HH),
                    in_=sw2r[:, :, fh * HH:(fh + 1) * HH]), dma=True)
            else:
                p.op("gpsimd", lambda e, ee=ee, fh=fh: e.dma_start(
                    out=w1[:].rearrange("p (c f) -> p c f", f=HH),
                    in_=w1r[ee][:, :, fh * HH:(fh + 1) * HH]), dma=True)
                p.op("gpsimd", lambda e, ee=ee, fh=fh: e.dma_start(
                    out=w2[:].rearrange("p (c f) -> p c f", f=HH),
                    in_=w2r[ee][:, :, fh * HH:(fh + 1) * HH]), dma=True)
            for tb in range(4):
                t0 = tb * 512
                for oc in range(8):
                    b1i = (oc % 2) * 2
                    ph1 = p.pf[b1i]
                    ph2 = p.pf[b1i + 1]
                    for c in range(CH):
                        p.op("tensor", lambda e, ph1=ph1, oc=oc, c=c, t0=t0: e.matmul(
                            ph1[:], w1[:, c * HH + oc * 128: c * HH + oc * 128 + 128],
                            lnxs(c, t0, 512), start=(c == 0), stop=(c == CH - 1)),
                            wait=((None if (oc == 0 and tb == 0) else bankm[b1i]) if c == 0 else False), inc=(c == CH - 1))
                    for c in range(CH):
                        p.op("tensor", lambda e, ph2=ph2, oc=oc, c=c, t0=t0: e.matmul(
                            ph2[:], w2[:, c * HH + oc * 128: c * HH + oc * 128 + 128],
                            lnxs(c, t0, 512), start=(c == 0), stop=(c == CH - 1)),
                            wait=((None if (oc == 0 and tb == 0) else bankm[b1i + 1]) if c == 0 else False), inc=(c == CH - 1))
                    p.op("scalar", lambda e, ph1=ph1: e.activation(cm.SC1[:], ph1[:], AF.Silu))
                    hdst = hb[:, oc * 512:(oc + 1) * 512]
                    if shared:
                        p.op("vector", lambda e, ph2=ph2, hdst=hdst: e.tensor_mul(hdst, cm.SC1[:], ph2[:]))
                    else:
                        p.op("vector", lambda e, ph2=ph2: e.tensor_mul(cm.SC2[:], cm.SC1[:], ph2[:]))
                        p.op("vector", lambda e, hdst=hdst, t0=t0: e.tensor_mul(
                            hdst, cm.SC2[:], wbb[:, t0:t0 + 512]))
                    bankm[b1i] = p.mark()
                    bankm[b1i + 1] = p.mark()
                for oc in range(CH):
                    b2i = (oc % 2) * 2
                    py = p.pf[b2i]
                    for fc in range(8):
                        p.op("tensor", lambda e, py=py, oc=oc, fc=fc, fh=fh: e.matmul(
                            py[:], w3[:, (fh * 8 + fc) * D + oc * 128: (fh * 8 + fc) * D + (oc + 1) * 128],
                            hb[:, fc * 512:(fc + 1) * 512],
                            start=(fc == 0), stop=(fc == 7)),
                            wait=(bankm[b2i] if fc == 0 else False), inc=(fc == 7))
                    x = xss(oc, t0, 512)
                    p.op("vector", lambda e, py=py, x=x: e.tensor_add(x, x, py[:]))
                    bankm[b2i] = p.mark()
    p.free_prefix(pref)


def emit_L1_ffn(cm: Common, pref="l1f"):
    p, C, ins = cm.p, cm.consts, cm.ins
    _LNXB = cm.LNX
    _XSB = cm.XS
    def lnxs(c, t0=0, n=T, _b=_LNXB):
        return _b[:, c * T + t0: c * T + t0 + n]
    def xss(c, t0=0, n=T, _b=_XSB):
        return _b[:, c * T + t0: c * T + t0 + n]
    E, FF, R = 4, 4096, 4
    gwT = p.sb(pref + "gwT", [128, CH * 4], BF16)
    bgr = p.sb(pref + "bgr", [1, 4], BF16)
    WgT = p.sb(pref + "WgT", [128, NT * 4], F32)
    wrow = p.sb(pref + "wrow", [1, 512], F32)
    Alh = p.sb(pref + "Alh", [128, CH * 4 * E], BF16)
    Blh = p.sb(pref + "Blh", [4, FF // 2], BF16)
    lows = p.sb(pref + "lows", [4, T], BF16)
    Wi = p.sb(pref + "Wi", [128, CH * (FF // 2)], BF16)
    Wo = p.sb(pref + "Wo", [128, (FF // 2 // 128) * D], BF16)
    hb = p.sb(pref + "hb", [128, 16 * 512], BF16)
    wbb = p.sb(pref + "wb", [128, 512], BF16)
    g4 = p.sb(pref + "g4", [128, 4], F32)
    s4 = p.sb(pref + "s4", [128, 4], F32)
    m1 = p.sb(pref + "m1", [128, 1], F32)
    nm1 = p.sb(pref + "nm1", [128, 1], F32)
    m2 = p.sb(pref + "m2", [128, 1], F32)
    sum4 = p.sb(pref + "sum4", [128, 1], F32)
    sel = p.sb(pref + "sel", [128, 4], F32)
    selm = p.sb(pref + "selm", [128, 4], U8)
    neg = p.sb(pref + "neg", [128, 4], F32)
    ones_row_bf = C["c_ones_row_bf"]
    ones_row_f = C["c_ones_row_f"]
    ident_f = C["c_ident_f"]

    p.op("gpsimd", lambda e: e.dma_start(
        out=gwT[:].rearrange("p (c e) -> p c e", e=4),
        in_=ins["L1_ffn_WgT_host"].rearrange("(c p) e -> p c e", p=128)), dma=True)
    p.op("gpsimd", lambda e: e.dma_start(
        out=bgr[:], in_=ins["L1_ffn_bg"].rearrange("(u e) -> u e", u=1)), dma=True)
    p.op("gpsimd", lambda e: e.dma_start(
        out=Alh[:].rearrange("p (c r) -> p c r", r=4 * E),
        in_=ins["L1_ffn_AT_host"].rearrange("(c p) q -> p c q", p=128)), dma=True)
    p.op("vector", lambda e: e.memset(neg[:], -1e30))

    ps = p.pf[0][:, 0:4]
    pr = p.pf[1][0:1, 0:128]
    for i in range(NT):
        t0 = i * 128
        p.op("tensor", lambda e: e.matmul(ps, ones_row_bf[:], bgr[:], start=True, stop=False))
        for c in range(CH):
            p.op("tensor", lambda e, c=c, t0=t0: e.matmul(
                ps, lnxs(c, t0, 128), gwT[:, c * 4:(c + 1) * 4],
                start=False, stop=(c == CH - 1)))
        p.op("vector", lambda e: e.tensor_copy(out=g4[:], in_=ps))
        p.op("vector", lambda e: e.reduce_max(m1[:], g4[:], axis=AXX))
        p.op("vector", lambda e: e.tensor_scalar_mul(nm1[:], m1[:], -1.0))
        p.op("scalar", lambda e: e.activation(s4[:], g4[:], AF.Exp, bias=nm1[:, 0:1],
                                              accum_out=sum4[:]))
        p.op("vector", lambda e: e.reciprocal(sum4[:], sum4[:]))
        p.op("vector", lambda e: e.tensor_scalar(s4[:], s4[:], sum4[:, 0:1], None, ALU.mult))
        p.op("vector", lambda e: e.reduce_max(m1[:], s4[:], axis=AXX))
        p.op("vector", lambda e: e.tensor_scalar(selm[:], s4[:], m1[:, 0:1], None, ALU.is_lt))
        p.op("vector", lambda e: e.select(sel[:], selm[:], s4[:], neg[:]))
        p.op("vector", lambda e: e.reduce_max(m2[:], sel[:], axis=AXX))
        wg = WgT[:, i * 4:(i + 1) * 4]
        p.op("vector", lambda e, wg=wg: e.tensor_scalar(sel[:], s4[:], m2[:, 0:1], None, ALU.is_ge))
        p.op("vector", lambda e, wg=wg: e.tensor_mul(wg, s4[:], sel[:]))

    bankm = {}
    Wir = ins["L1_ffn_WiT_host"].rearrange("(c p) f -> p c f", p=128)
    Wor = ins["L1_ffn_WoT_host"].rearrange("(c p) d -> p c d", p=128)
    Br = ins["L1_ffn_BT_host"]
    HH = FF // 2
    for fh in range(2):
        p.op("gpsimd", lambda e, fh=fh: e.dma_start(
            out=Wi[:].rearrange("p (c f) -> p c f", f=HH),
            in_=Wir[:, :, fh * HH:(fh + 1) * HH]), dma=True)
        p.op("gpsimd", lambda e, fh=fh: e.dma_start(
            out=Wo[:].rearrange("p (c d) -> p c d", d=D),
            in_=Wor[:, fh * 16:(fh + 1) * 16, :]), dma=True)
        for ee in range(E):
            p.op("gpsimd", lambda e, ee=ee, fh=fh: e.dma_start(
                out=Blh[:], in_=Br[ee][:, fh * HH:(fh + 1) * HH]), dma=True)
            pl = p.pf[0][0:4, :]
            for tb in range(4):
                t0 = tb * 512
                for c in range(CH):
                    p.op("tensor", lambda e, ee=ee, c=c, t0=t0: e.matmul(
                        pl, lnxs(c, t0, 512),
                        Alh[:, c * 4 * E + ee * 4: c * 4 * E + ee * 4 + 4],
                        start=(c == 0), stop=(c == CH - 1))
                    if False else e.matmul(
                        pl, Alh[:, c * 4 * E + ee * 4: c * 4 * E + ee * 4 + 4],
                        lnxs(c, t0, 512), start=(c == 0), stop=(c == CH - 1)))
                p.op("vector", lambda e, t0=t0: e.tensor_copy(
                    out=lows[:, t0:t0 + 512], in_=pl))
            for tb in range(4):
                t0 = tb * 512
                for j in range(4):
                    i = tb * 4 + j
                    p.op("tensor", lambda e, i=i, ee=ee: e.matmul(
                        pr, WgT[:, i * 4 + ee: i * 4 + ee + 1], ident_f[:],
                        start=True, stop=True))
                    p.op("vector", lambda e, j=j: e.tensor_copy(
                        out=wrow[:, j * 128:(j + 1) * 128], in_=pr))
                pwb = p.pf[2]
                p.op("tensor", lambda e: e.matmul(
                    pwb[:], ones_row_f[:], wrow[:], start=True, stop=True))
                p.op("vector", lambda e: e.tensor_copy(out=wbb[:], in_=pwb[:]))
                for bb in range(4):
                    bankm[bb] = p.mark()
                for oc in range(16):
                    ph = p.pf[oc % 2]
                    for c in range(CH):
                        p.op("tensor", lambda e, ph=ph, oc=oc, c=c, t0=t0: e.matmul(
                            ph[:], Wi[:, c * HH + oc * 128: c * HH + oc * 128 + 128],
                            lnxs(c, t0, 512), start=(c == 0), stop=False),
                            wait=((None if oc < 2 else bankm.get(oc % 2)) if c == 0 else False),
                            inc=False)
                    p.op("tensor", lambda e, ph=ph, oc=oc, t0=t0: e.matmul(
                        ph[:], Blh[:, oc * 128:(oc + 1) * 128],
                        lows[:, t0:t0 + 512], start=False, stop=True),
                        wait=False, inc=True)
                    p.op("vector", lambda e, ph=ph: e.tensor_scalar_max(cm.SC1[:], ph[:], 0.0))
                    p.op("vector", lambda e, oc=oc: e.tensor_mul(
                        hb[:, oc * 512:(oc + 1) * 512], cm.SC1[:], wbb[:]))
                    bankm[oc % 2] = p.mark()
                for oc in range(CH):
                    py = p.pf[2 + (oc % 2)]
                    for fc in range(16):
                        p.op("tensor", lambda e, py=py, oc=oc, fc=fc: e.matmul(
                            py[:], Wo[:, fc * D + oc * 128: fc * D + (oc + 1) * 128],
                            hb[:, fc * 512:(fc + 1) * 512],
                            start=(fc == 0), stop=(fc == 15)),
                            wait=((None if oc < 2 else bankm.get(2 + oc % 2)) if fc == 0 else False),
                            inc=(fc == 15))
                    x = xss(oc, t0, 512)
                    p.op("vector", lambda e, py=py, x=x: e.tensor_add(x, x, py[:]))
                    bankm[2 + oc % 2] = p.mark()
    p.free_prefix(pref)


def emit_L1_attn(cm: Common, pref="l1a"):
    p, C, ins = cm.p, cm.consts, cm.ins
    _LNXB = cm.LNX
    _XSB = cm.XS
    def lnxs(c, t0=0, n=T, _b=_LNXB):
        return _b[:, c * T + t0: c * T + t0 + n]
    def xss(c, t0=0, n=T, _b=_XSB):
        return _b[:, c * T + t0: c * T + t0 + n]
    DI = 2048
    NC = 16
    ones_row_f = C["c_ones_row_f"]
    ones_col_f = C["c_ones_col_f"]

    xc_d = p.dram(pref + "xc_d", [DI, T], BF16)
    z_d = p.dram(pref + "z_d", [DI, T], BF16)
    y_d = p.dram(pref + "y_d", [DI, T], BF16)
    cc_in = p.dram(pref + "cc_in", [DI, NT * 3], BF16)
    cc_out = p.dram(pref + "cc_out", [2 * DI, NT * 3], BF16)

    cw = p.sb(pref + "cw", [128, NC * 4], F32)
    cb = p.sb(pref + "cb", [128, NC], F32)
    Dcol = p.sb(pref + "Dc", [128, NC], F32)
    mW0 = p.sb(pref + "mW0", [128, 1], F32)
    mW1 = p.sb(pref + "mW1", [128, 1], F32)
    Wq = p.sb(pref + "t1_Wq", [128, CH * 512], BF16)
    tails = p.sb(pref + "t1_tl", [128, CH * NT * 3], BF16)
    otails = p.sb(pref + "t1_otl", [128, NC * NT * 3], BF16)
    G = p.sb(pref + "t1_G", [128, 2 * NC * NT * 3], BF16)
    xbt = p.sb(pref + "t1_xbt", [128, NC * 512], BF16)
    xct = p.sb(pref + "t1_xct", [128, NC * 512], BF16)
    pad = p.sb(pref + "t1_pad", [128, 131], BF16)
    halo = p.sb(pref + "t1_halo", [128, 3], BF16)

    p.op("gpsimd", lambda e: e.dma_start(
        out=cw[:].rearrange("p (c k) -> p c k", k=4),
        in_=ins["L1_attn_conv_w"].rearrange("(c p) k -> p c k", p=128)), dma=True)
    p.op("sync", lambda e: e.dma_start(out=cb[:], in_=ins["conv_b_col"][:, :]), dma=True)
    p.op("sync", lambda e: e.dma_start(out=Dcol[:], in_=ins["D_col"][:, :]), dma=True)
    p.op("sync", lambda e: e.dma_start(out=mW0[:], in_=ins["c_mW0"][:, :]), dma=True)
    p.op("sync", lambda e: e.dma_start(out=mW1[:], in_=ins["c_mW1"][:, :]), dma=True)

    WinT = ins["L1_attn_W_inT_host"].rearrange("(c p) f -> p c f", p=128)

    for c in range(CH):
        for i in range(NT):
            p.op("vector", lambda e, c=c, i=i: e.tensor_copy(
                out=tails[:, c * NT * 3 + i * 3: c * NT * 3 + (i + 1) * 3],
                in_=lnxs(c, i * 128 + 125, 3)))
    for q in range(4):
        p.op("gpsimd", lambda e, q=q: e.dma_start(
            out=Wq[:].rearrange("p (c f) -> p c f", f=512),
            in_=WinT[:, :, q * 512:(q + 1) * 512]), dma=True)
        for oc in range(4):
            ps = p.pf[oc % 2][:, 0:NT * 3]
            for c in range(CH):
                p.op("tensor", lambda e, ps=ps, q=q, oc=oc, c=c: e.matmul(
                    ps, Wq[:, c * 512 + oc * 128: c * 512 + oc * 128 + 128],
                    tails[:, c * NT * 3:(c + 1) * NT * 3],
                    start=(c == 0), stop=(c == CH - 1)))
            p.op("vector", lambda e, ps=ps, q=q, oc=oc: e.tensor_copy(
                out=otails[:, (q * 4 + oc) * NT * 3:(q * 4 + oc + 1) * NT * 3], in_=ps))
    p.op("sync", lambda e: e.dma_start(
        out=cc_in.rearrange("(c p) t -> p c t", p=128),
        in_=otails[:].rearrange("p (c t) -> p c t", t=NT * 3)), dma=True)
    p.op_collective(lambda e: e.collective_compute(
        "AllGather", ALU.bypass, replica_groups=[[0, 1], [2, 3], [4, 5], [6, 7]],
        ins=[cc_in.opt()], outs=[cc_out.opt()]))
    p.op("sync", lambda e: e.dma_start(
        out=G[:].rearrange("p (q t) -> p q t", t=NT * 3),
        in_=cc_out.rearrange("(q p) t -> p q t", p=128)), dma=True)

    for tb in range(4):
        t0 = tb * 512
        for half in range(2):
            base_e = half * DI
            for q in range(4):
                p.op("gpsimd", lambda e, q=q, base_e=base_e: e.dma_start(
                    out=Wq[:].rearrange("p (c f) -> p c f", f=512),
                    in_=WinT[:, :, base_e + q * 512: base_e + (q + 1) * 512]), dma=True)
                for oc in range(4):
                    ps = p.pf[oc % 2]
                    for c in range(CH):
                        p.op("tensor", lambda e, ps=ps, q=q, oc=oc, c=c, t0=t0: e.matmul(
                            ps[:], Wq[:, c * 512 + oc * 128: c * 512 + oc * 128 + 128],
                            lnxs(c, t0, 512), start=(c == 0), stop=(c == CH - 1)))
                    dst = xbt[:, (q * 4 + oc) * 512:(q * 4 + oc + 1) * 512]
                    p.op("vector", lambda e, ps=ps, dst=dst: e.tensor_copy(out=dst, in_=ps[:]))
            if half == 1:
                p.op("sync", lambda e, t0=t0: e.dma_start(
                    out=z_d.rearrange("(c p) t -> p c t", p=128)[:, :, t0:t0 + 512],
                    in_=xbt[:].rearrange("p (c t) -> p c t", t=512)), dma=True)
                continue
            for j in range(4):
                i = tb * 4 + j
                for oc in range(NC):
                    xb_ij = xbt[:, oc * 512 + j * 128: oc * 512 + (j + 1) * 128]
                    g0 = G[:, 0 * NC * NT * 3 + oc * NT * 3 + i * 3:
                           0 * NC * NT * 3 + oc * NT * 3 + (i + 1) * 3]
                    p.op("vector", lambda e, g0=g0: e.tensor_scalar(
                        halo[:], g0, mW0[:, 0:1], None, ALU.mult))
                    if i > 0:
                        g1 = G[:, 1 * NC * NT * 3 + oc * NT * 3 + (i - 1) * 3:
                               1 * NC * NT * 3 + oc * NT * 3 + i * 3]
                        p.op("vector", lambda e, g1=g1: e.scalar_tensor_tensor(
                            halo[:], g1, mW1[:, 0:1], halo[:], ALU.mult, ALU.add))
                    p.op("vector", lambda e: e.tensor_copy(out=pad[:, 0:3], in_=halo[:]))
                    p.op("vector", lambda e, xb_ij=xb_ij: e.tensor_copy(
                        out=pad[:, 3:131], in_=xb_ij))
                    xc_ij = xct[:, oc * 512 + j * 128: oc * 512 + (j + 1) * 128]
                    p.op("vector", lambda e, xc_ij=xc_ij, oc=oc: e.tensor_scalar(
                        xc_ij, pad[:, 3:131], cw[:, oc * 4 + 3: oc * 4 + 4],
                        cb[:, oc:oc + 1], ALU.mult, ALU.add))
                    for k in range(3):
                        p.op("vector", lambda e, xc_ij=xc_ij, oc=oc, k=k: e.scalar_tensor_tensor(
                            xc_ij, pad[:, k:k + 128], cw[:, oc * 4 + k: oc * 4 + k + 1],
                            xc_ij, ALU.mult, ALU.add))
            for oc in range(NC):
                sl = xct[:, oc * 512:(oc + 1) * 512]
                p.op("scalar", lambda e, sl=sl: e.activation(sl, sl, AF.Silu))
            p.op("sync", lambda e, t0=t0: e.dma_start(
                out=xc_d.rearrange("(c p) t -> p c t", p=128)[:, :, t0:t0 + 512],
                in_=xct[:].rearrange("p (c t) -> p c t", t=512)), dma=True)
    p.free_prefix(pref + "t1")

    WxT = p.sb(pref + "t2_WxT", [128, NC * 2080], BF16)
    xcs = p.sb(pref + "t2_xcs", [128, NC * 512], BF16)
    zs = p.sb(pref + "t2_zs", [128, 512], BF16)
    bcp = p.sb(pref + "t2_bcp", [16, 512], F32)
    bcb = p.sb(pref + "t2_bcb", [128, 512], F32)
    delta = p.sb(pref + "t2_dl", [128, 512], F32)
    yt = p.sb(pref + "t2_yt", [128, 512], BF16)
    p.op("gpsimd", lambda e: e.dma_start(
        out=WxT[:].rearrange("p (c f) -> p c f", f=2080),
        in_=ins["L1_attn_W_xprojT_host"].rearrange("(c p) f -> p c f", p=128)), dma=True)
    for tb in range(4):
        t0 = tb * 512
        p.op("gpsimd", lambda e, t0=t0: e.dma_start(
            out=xcs[:].rearrange("p (c t) -> p c t", t=512),
            in_=xc_d.rearrange("(c p) t -> p c t", p=128)[:, :, t0:t0 + 512]), dma=True)
        pB = p.pf[2][0:16, :]
        pC = p.pf[3][0:16, :]
        for c in range(NC):
            p.op("tensor", lambda e, c=c: e.matmul(
                pB, WxT[:, c * 2080: c * 2080 + 16], xcs[:, c * 512:(c + 1) * 512],
                start=(c == 0), stop=(c == NC - 1)))
        for c in range(NC):
            p.op("tensor", lambda e, c=c: e.matmul(
                pC, WxT[:, c * 2080 + 16: c * 2080 + 32], xcs[:, c * 512:(c + 1) * 512],
                start=(c == 0), stop=(c == NC - 1)))
        p.op("vector", lambda e: e.tensor_copy(out=bcp[:], in_=pB))
        p.op("vector", lambda e: e.tensor_mul(bcp[:], bcp[:], pC))
        pS = p.pf[4][0:1, :]
        p.op("tensor", lambda e: e.matmul(pS, ones_col_f[0:16, :], bcp[:],
                                          start=True, stop=True))
        p.op("vector", lambda e: e.tensor_copy(out=cm.SROW[:, 0:512], in_=pS))
        pBC = p.pf[5]
        p.op("tensor", lambda e: e.matmul(pBC[:], ones_row_f[:], cm.SROW[:, 0:512],
                                          start=True, stop=True))
        p.op("vector", lambda e: e.tensor_copy(out=bcb[:], in_=pBC[:]))
        for oc in range(NC):
            pd = p.pf[oc % 2]
            for c in range(NC):
                p.op("tensor", lambda e, pd=pd, oc=oc, c=c: e.matmul(
                    pd[:], WxT[:, c * 2080 + 32 + oc * 128: c * 2080 + 32 + (oc + 1) * 128],
                    xcs[:, c * 512:(c + 1) * 512], start=(c == 0), stop=(c == NC - 1)))
            p.op("scalar", lambda e, pd=pd: e.activation(cm.SC1[:], pd[:], AF.Abs))
            p.op("scalar", lambda e: e.activation(cm.SC1[:], cm.SC1[:], AF.Exp, scale=-1.0))
            p.op("vector", lambda e: e.tensor_scalar_add(cm.SC1[:], cm.SC1[:], 1.0))
            p.op("scalar", lambda e: e.activation(cm.SC1[:], cm.SC1[:], AF.Ln))
            p.op("vector", lambda e, pd=pd: e.tensor_scalar_max(cm.SC2[:], pd[:], 0.0))
            p.op("vector", lambda e: e.tensor_add(delta[:], cm.SC1[:], cm.SC2[:]))
            xc_oc = xcs[:, oc * 512:(oc + 1) * 512]
            z_oc = zs[:]
            p.op("gpsimd", lambda e, oc=oc, t0=t0: e.dma_start(
                out=zs[:], in_=z_d[oc * 128:(oc + 1) * 128, t0:t0 + 512]), dma=True)
            p.op("vector", lambda e, xc_oc=xc_oc: e.tensor_mul(delta[:], delta[:], xc_oc))
            p.op("vector", lambda e: e.tensor_mul(delta[:], delta[:], bcb[:]))
            p.op("vector", lambda e, xc_oc=xc_oc, oc=oc: e.scalar_tensor_tensor(
                delta[:], xc_oc, Dcol[:, oc:oc + 1], delta[:], ALU.mult, ALU.add))
            p.op("scalar", lambda e, z_oc=z_oc: e.activation(cm.SC1[:], z_oc, AF.Silu))
            p.op("vector", lambda e: e.tensor_mul(yt[:], delta[:], cm.SC1[:]))
            p.op("sync", lambda e, oc=oc, t0=t0: e.dma_start(
                out=y_d[oc * 128:(oc + 1) * 128, t0:t0 + 512], in_=yt[:]), dma=True)
    p.free_prefix(pref + "t2")

    WoT = p.sb(pref + "t3_WoT", [128, NC * D], BF16)
    ys = p.sb(pref + "t3_ys", [128, NC * 512], BF16)
    p.op("gpsimd", lambda e: e.dma_start(
        out=WoT[:].rearrange("p (c f) -> p c f", f=D),
        in_=ins["L1_attn_W_outT_host"].rearrange("(c p) f -> p c f", p=128)), dma=True)
    for tb in range(4):
        t0 = tb * 512
        p.op("gpsimd", lambda e, t0=t0: e.dma_start(
            out=ys[:].rearrange("p (c t) -> p c t", t=512),
            in_=y_d.rearrange("(c p) t -> p c t", p=128)[:, :, t0:t0 + 512]), dma=True)
        for oc in range(CH):
            py = p.pf[2 + (oc % 2)]
            for c in range(NC):
                p.op("tensor", lambda e, py=py, oc=oc, c=c: e.matmul(
                    py[:], WoT[:, c * D + oc * 128: c * D + (oc + 1) * 128],
                    ys[:, c * 512:(c + 1) * 512], start=(c == 0), stop=(c == NC - 1)))
            x = xss(oc, t0, 512)
            p.op("vector", lambda e, py=py, x=x: e.tensor_add(x, x, py[:]))
    p.free_prefix(pref)


def emit_L3_attn(cm: Common, pref="l3a"):
    p, C, ins = cm.p, cm.consts, cm.ins
    _LNXB = cm.LNX
    _XSB = cm.XS
    def lnxs(c, t0=0, n=T, _b=_LNXB):
        return _b[:, c * T + t0: c * T + t0 + n]
    def xss(c, t0=0, n=T, _b=_XSB):
        return _b[:, c * T + t0: c * T + t0 + n]
    xin_d = cm.x_in
    xout_d = cm.x_out
    NH = 16
    ones_row_f = C["c_ones_row_f"]
    ones_row_bf = C["c_ones_row_bf"]
    ones_col_f = C["c_ones_col_f"]
    ones_col_bf = C["c_ones_col_bf"]
    ident_bf = C["c_ident_bf"]
    RG = [[0, 1], [2, 3], [4, 5], [6, 7]]

    ccK_in = p.dram(pref + "ccK_in", [D + 128, T], BF16)
    ccK_out = p.dram(pref + "ccK_out", [2 * (D + 128), T], BF16)
    ccV_in = p.dram(pref + "ccV_in", [T, D], BF16)
    ccV_out = p.dram(pref + "ccV_out", [2 * T, D], BF16)

    QB = p.sb(pref + "QB", [128, CH * T], BF16)
    ATT = p.sb(pref + "ATT", [128, CH * T], BF16)
    cosb = p.sb(pref + "cos", [128, T], BF16)
    sinb = p.sb(pref + "sin", [128, T], BF16)
    NM16 = p.sb(pref + "NM16", [128, 16], F32)
    mska = p.sb(pref + "mska", [128, 512], BF16)
    mskb = p.sb(pref + "mskb", [128, 512], BF16)

    p.op("gpsimd", lambda e: e.dma_start(out=cosb[:], in_=ins["c_l3cos"][:, :]), dma=True)
    p.op("gpsimd", lambda e: e.dma_start(out=sinb[:], in_=ins["c_l3sin"][:, :]), dma=True)
    p.op("gpsimd", lambda e: e.dma_start(out=mska[:], in_=ins["c_l3m_a4"][:, :]), dma=True)
    p.op("gpsimd", lambda e: e.dma_start(out=mskb[:], in_=ins["c_l3m_b4"][:, :]), dma=True)
    if DBG.get("l3_stop") == "Bz":
        p.free_prefix(pref)
        return

    WT = p.sb(pref + "t1_WT", [128, CH * D], BF16)
    brow = p.sb(pref + "t1_brow", [1, D], F32)
    KVB = p.sb(pref + "t1_KVB", [128, CH * T], BF16)

    def proj_fm(wname, bname, dstbuf):
        p.op("gpsimd", lambda e, wname=wname: e.dma_start(
            out=WT[:].rearrange("p (c f) -> p c f", f=D),
            in_=ins[wname].rearrange("(c p) f -> p c f", p=128)), dma=True)
        p.op("sync", lambda e, bname=bname: e.dma_start(
            out=brow[:], in_=ins[bname].rearrange("(u f) -> u f", u=1)), dma=True)
        bm = {}
        for oc in range(CH):
            for tb in range(4):
                t0 = tb * 512
                ph = p.pf[tb % 2]
                first = (oc == 0 and tb < 2)
                p.op("tensor", lambda e, ph=ph, oc=oc: e.matmul(
                    ph[:], brow[:, oc * 128:(oc + 1) * 128], C["c_ones512_f"][:],
                    start=True, stop=False),
                    wait=(None if first else bm.get(tb % 2)), inc=False)
                for c in range(CH):
                    p.op("tensor", lambda e, ph=ph, oc=oc, c=c, t0=t0: e.matmul(
                        ph[:], WT[:, c * D + oc * 128: c * D + (oc + 1) * 128],
                        lnxs(c, t0, 512), start=False, stop=(c == CH - 1)),
                        wait=False, inc=(c == CH - 1))
                p.op("vector", lambda e, ph=ph, oc=oc, t0=t0, dstbuf=dstbuf: e.tensor_copy(
                    out=dstbuf[:, oc * T + t0: oc * T + t0 + 512], in_=ph[:]))
                bm[tb % 2] = p.mark()

    def rope(buf):
        for h in range(NH):
            c, off = h // 2, (h % 2) * 64
            for tb in range(4):
                t0 = tb * 512
                a = buf[off:off + 32, c * T + t0: c * T + t0 + 512]
                b = buf[off + 32:off + 64, c * T + t0: c * T + t0 + 512]
                ca = cosb[off:off + 32, t0:t0 + 512]
                sa = sinb[off:off + 32, t0:t0 + 512]
                cb2 = cosb[off + 32:off + 64, t0:t0 + 512]
                sb2 = sinb[off + 32:off + 64, t0:t0 + 512]
                u = cm.SC1[off:off + 32, :]
                v = cm.SC2[off:off + 32, :]
                u2 = cm.SC1[off + 32:off + 64, :]
                v2 = cm.SC2[off + 32:off + 64, :]
                p.op("vector", lambda e, u2=u2, a=a: e.tensor_copy(out=u2, in_=a))
                p.op("vector", lambda e, v=v, b=b: e.tensor_copy(out=v, in_=b))
                p.op("vector", lambda e, u=u, a=a, ca=ca: e.tensor_mul(u, a, ca))
                p.op("vector", lambda e, v=v, sa=sa: e.tensor_mul(v, v, sa))
                p.op("vector", lambda e, u=u, v=v: e.tensor_sub(u, u, v))
                p.op("vector", lambda e, v2=v2, b=b, cb2=cb2: e.tensor_mul(v2, b, cb2))
                p.op("vector", lambda e, u2=u2, sb2=sb2: e.tensor_mul(u2, u2, sb2))
                p.op("vector", lambda e, v2=v2, u2=u2: e.tensor_add(v2, v2, u2))
                p.op("vector", lambda e, a=a, u=u: e.tensor_copy(out=a, in_=u))
                p.op("vector", lambda e, b=b, v2=v2: e.tensor_copy(out=b, in_=v2))
    proj_fm("L3_attn_WqT_host", "L3_attn_bq", QB)
    if DBG.get("l3_dump_q") == "prerope":
        qd = p.dram(pref + "q_d", [D, T], F32, kind="ExternalOutput")
        for c in range(CH):
            for tb in range(4):
                p.op("vector", lambda e, c=c, tb=tb: e.tensor_copy(
                    out=cm.SC1[:], in_=QB[:, c * T + tb * 512: c * T + tb * 512 + 512]))
                p.op("sync", lambda e, c=c, tb=tb: e.dma_start(
                    out=qd[c * 128:(c + 1) * 128, tb * 512:(tb + 1) * 512], in_=cm.SC1[:]), dma=True)
    if DBG.get("l3_stop") == "B0":
        p.free_prefix(pref)
        return
    rope(QB)
    if DBG.get("l3_dump_q") == "postrope":
        qd = p.dram(pref + "q_d", [D, T], F32, kind="ExternalOutput")
        for c in range(CH):
            for tb in range(4):
                p.op("vector", lambda e, c=c, tb=tb: e.tensor_copy(
                    out=cm.SC1[:], in_=QB[:, c * T + tb * 512: c * T + tb * 512 + 512]))
                p.op("sync", lambda e, c=c, tb=tb: e.dma_start(
                    out=qd[c * 128:(c + 1) * 128, tb * 512:(tb + 1) * 512], in_=cm.SC1[:]), dma=True)
    if DBG.get("l3_stop") == "B0r":
        p.free_prefix(pref)
        return
    proj_fm("L3_attn_WkT_host", "L3_attn_bk", KVB)
    rope(KVB)
    if DBG.get("l3_stop") == "B1":
        p.free_prefix(pref)
        return
    pn2 = p.pf[4][0:2, :]
    half_ones = C["c_half_ones"]
    kqmax = p.sb(pref + "t1_kqm", [2, 16], F32)
    p.op("vector", lambda e: e.memset(kqmax[:], 0.0))
    for which, buf in ((0, KVB), (1, QB)):
        for tb in range(4):
            t0 = tb * 512
            for c in range(CH):
                sl = buf[:, c * T + t0: c * T + t0 + 512]
                p.op("scalar", lambda e, sl=sl: e.activation(cm.SC1[:], sl, AF.Square))
                p.op("tensor", lambda e: e.matmul(pn2, half_ones[:], cm.SC1[:],
                                                  start=True, stop=True))
                p.op("vector", lambda e: e.tensor_copy(out=cm.SC2[0:2, :], in_=pn2))
                red = cm.SC2[0:2, 512 - 1:512] if False else None
                p.op("vector", lambda e: e.reduce_max(cm.SROW[0:2, 0:1] if False else cm.SC1[0:2, 0:1], cm.SC2[0:2, :], axis=AXX))
                dcol = kqmax[:, which * 8 + c: which * 8 + c + 1]
                p.op("vector", lambda e, dcol=dcol: e.tensor_max(dcol, dcol, cm.SC1[0:2, 0:1]))
    for c in range(CH):
        p.op("sync", lambda e, c=c: e.dma_start(
            out=ccK_in[c * 128:(c + 1) * 128, :], in_=KVB[:, c * T:(c + 1) * T]), dma=True)
    p.op("gpsimd", lambda e: e.dma_start(out=ccK_in[D:D + 2, 0:16], in_=kqmax[:]), dma=True)
    p.op("gpsimd", lambda e: e.dma_start(
        out=WT[:].rearrange("p (c f) -> p c f", f=D),
        in_=ins["L3_attn_WvT_host"].rearrange("(c p) f -> p c f", p=128)), dma=True)
    p.op("sync", lambda e: e.dma_start(
        out=brow[:], in_=ins["L3_attn_bv"].rearrange("(u f) -> u f", u=1)), dma=True)
    bmv = {}
    for i in range(NT):
        t0 = i * 128
        for ob in range(2):
            ph = p.pf[ob % 2]
            p.op("tensor", lambda e, ph=ph, ob=ob: e.matmul(
                ph[:], ones_row_f[:],
                brow[:, ob * 512:(ob + 1) * 512], start=True, stop=False),
                wait=(None if i == 0 else bmv.get(ob % 2)), inc=False)
            for c in range(CH):
                p.op("tensor", lambda e, ph=ph, ob=ob, c=c, t0=t0: e.matmul(
                    ph[:], lnxs(c, t0, 128),
                    WT[:, c * D + ob * 512: c * D + (ob + 1) * 512],
                    start=False, stop=(c == CH - 1)),
                    wait=False, inc=(c == CH - 1))
            p.op("vector", lambda e, ph=ph, i=i, ob=ob: e.tensor_copy(
                out=KVB[:, i * D + ob * 512: i * D + (ob + 1) * 512], in_=ph[:]))
            bmv[ob % 2] = p.mark()
        p.op("sync", lambda e, i=i: e.dma_start(
            out=ccV_in[i * 128:(i + 1) * 128, :], in_=KVB[:, i * D:(i + 1) * D]), dma=True)
    if DBG.get("l3_stop") == "B2":
        p.free_prefix(pref)
        return
    p.op_collective(lambda e: e.collective_compute(
        "AllGather", ALU.bypass, replica_groups=RG,
        ins=[ccK_in.opt()], outs=[ccK_out.opt()]))
    p.op_collective(lambda e: e.collective_compute(
        "AllGather", ALU.bypass, replica_groups=RG,
        ins=[ccV_in.opt()], outs=[ccV_out.opt()]))
    if DBG.get("l3_stop") == "B3":
        p.free_prefix(pref)
        return
    KD = D + 128
    gn = p.sb(pref + "t1_gn", [2, 32], F32)
    nmr = p.sb(pref + "t1_nmr", [1, 16], F32)
    p.op("gpsimd", lambda e: e.dma_start(out=gn[:, 0:16], in_=ccK_out[D:D + 2, 0:16]), dma=True)
    p.op("gpsimd", lambda e: e.dma_start(out=gn[:, 16:32], in_=ccK_out[KD + D:KD + D + 2, 0:16]), dma=True)
    p.op("vector", lambda e: e.tensor_max(gn[:, 0:16], gn[:, 0:16], gn[:, 16:32]))
    p.op("vector", lambda e: e.tensor_max(gn[:, 0:8], gn[:, 0:8], kqmax[:, 0:8]))
    p.op("scalar", lambda e: e.activation(gn[:, 0:8], gn[:, 0:8], AF.Sqrt))
    p.op("scalar", lambda e: e.activation(gn[:, 8:16], kqmax[:, 8:16], AF.Sqrt))
    p.op("vector", lambda e: e.tensor_mul(gn[:, 0:8], gn[:, 0:8], gn[:, 8:16]))
    p.op("vector", lambda e: e.tensor_scalar_mul(gn[:, 0:8], gn[:, 0:8], -1.0 / 8.0))
    pr2 = p.pf[5][0:1, 0:2]
    for c in range(CH):
        p.op("tensor", lambda e, c=c: e.matmul(
            pr2, gn[:, c:c + 1], C["c_ident_f"][0:2, 0:2], start=True, stop=True))
        p.op("vector", lambda e, c=c: e.tensor_copy(out=nmr[:, 2 * c:2 * c + 2], in_=pr2))
    pm = p.pf[4][:, 0:16]
    p.op("tensor", lambda e: e.matmul(pm, ones_row_f[:], nmr[:], start=True, stop=True))
    p.op("vector", lambda e: e.tensor_copy(out=NM16[:], in_=pm))
    if DBG.get("l3_dump_kv"):
        kd = p.dram(pref + "k_d", [2 * (D + 128), 256], F32, kind="ExternalOutput")
        vd = p.dram(pref + "v_d", [512, D], F32, kind="ExternalOutput")
        kbuf = p.sb(pref + "t9_kb", [128, 256], BF16)
        for r in range(2):
            for c in range(9):
                p.op("gpsimd", lambda e, r=r, c=c: e.dma_start(
                    out=kbuf[:], in_=ccK_out[r * 1152 + c * 128: r * 1152 + (c + 1) * 128, 0:256]), dma=True)
                p.op("vector", lambda e: e.tensor_copy(out=cm.SC1[:, 0:256], in_=kbuf[:]))
                p.op("sync", lambda e, r=r, c=c: e.dma_start(
                    out=kd[r * 1152 + c * 128: r * 1152 + (c + 1) * 128, :], in_=cm.SC1[:, 0:256]), dma=True)
        vbuf = p.sb(pref + "t9_vb", [128, D], BF16)
        for blk in range(4):
            src_r = [0, 0, 1, 1][blk]; src_lt = [0, 1, 0, 1][blk]
            p.op("gpsimd", lambda e, src_r=src_r, src_lt=src_lt: e.dma_start(
                out=vbuf[:], in_=ccV_out[(src_r * NT + src_lt) * 128:(src_r * NT + src_lt + 1) * 128, :]), dma=True)
            for half in range(2):
                p.op("vector", lambda e, half=half: e.tensor_copy(out=cm.SC1[:], in_=vbuf[:, half * 512:(half + 1) * 512]))
                p.op("sync", lambda e, blk=blk, half=half: e.dma_start(
                    out=vd[blk * 128:(blk + 1) * 128, half * 512:(half + 1) * 512], in_=cm.SC1[:]), dma=True)
        p.free_prefix(pref + "t9")
    p.free_prefix(pref + "t1")
    if DBG.get("l3_stop") == "B":
        p.free_prefix(pref)
        return

    KCH2 = [p.sb(pref + "t2_KCHa", [128, CH * 128], BF16),
            p.sb(pref + "t2_KCHb", [128, CH * 128], BF16)]
    VCH2 = [p.sb(pref + "t2_VCHa", [128, D], BF16),
            p.sb(pref + "t2_VCHb", [128, D], BF16)]
    PT = p.sb(pref + "t2_PT", [128, 4 * 512], BF16)
    SUMS = p.sb(pref + "t2_SUMS", [128, NT], F32)
    RES = p.sb(pref + "t2_RES", [128, 1024], BF16)
    rcp = p.sb(pref + "t2_rcp", [128, 16], F32)
    ccK4 = ccK_out.rearrange("(r c p) t -> r p c t", p=128, c=9)
    ccV3 = ccV_out.rearrange("(q p) o -> q p o", p=128)
    usemark = [None, None]
    for i in range(NT):
        q0 = i * 128
        pAV = [p.pf[2], p.pf[3]]
        psums = p.pf[4][:, 0:16]
        ngt = 2 * i + 2
        for gt in range(ngt):
            r, lt = gt % 2, gt // 2
            par = gt % 2
            KCH = KCH2[par]
            VCH = VCH2[par]
            p.op("gpsimd", lambda e, r=r, lt=lt, KCH=KCH: e.dma_start(
                out=KCH[:].rearrange("p (c t) -> p c t", t=128),
                in_=ccK4[r][:, 0:CH, lt * 128:(lt + 1) * 128]), dma=True)
            p.op("gpsimd", lambda e, r=r, lt=lt, VCH=VCH: e.dma_start(
                out=VCH[:], in_=ccV_out[(r * NT + lt) * 128:(r * NT + lt + 1) * 128, :]), dma=True)
            for bk in range(4):
                pst = p.pf[bk % 2]
                for hh in range(4):
                    h = bk * 4 + hh
                    c, off = h // 2, (h % 2) * 64
                    p.op("tensor", lambda e, pst=pst, hh=hh, c=c, off=off, q0=q0, KCH=KCH: e.matmul(
                        pst[:, hh * 128:(hh + 1) * 128],
                        KCH[off + 0:off + 64, c * 128:(c + 1) * 128],
                        QB[off + 0:off + 64, c * T + q0: c * T + q0 + 128],
                        start=(hh == 0), stop=(hh == 3)))
                dst = PT[:, bk * 512:(bk + 1) * 512]
                for hh in range(4):
                    h = bk * 4 + hh
                    p.op("scalar", lambda e, pst=pst, dst=dst, hh=hh, h=h: e.activation(
                        dst[:, hh * 128:(hh + 1) * 128], pst[:, hh * 128:(hh + 1) * 128],
                        AF.Exp, bias=NM16[:, h:h + 1], scale=0.125))
                if gt == 2 * i:
                    p.op("vector", lambda e, dst=dst: e.tensor_mul(dst, dst, mska[:]))
                elif gt == 2 * i + 1:
                    p.op("vector", lambda e, dst=dst: e.tensor_mul(dst, dst, mskb[:]))
            for h in range(NH):
                pav = pAV[h // 8]
                slot = h % 8
                pth = PT[:, h * 128:(h + 1) * 128]
                p.op("tensor", lambda e, pav=pav, slot=slot, pth=pth, h=h, gt=gt, VCH=VCH: e.matmul(
                    pav[:, slot * 64:(slot + 1) * 64], pth, VCH[:, h * 64:(h + 1) * 64],
                    start=(gt == 0 and slot == 0), stop=(gt == ngt - 1 and slot == 7)))
                p.op("tensor", lambda e, pth=pth, h=h, gt=gt: e.matmul(
                    psums[:, h:h + 1], pth, ones_col_bf[:],
                    start=(gt == 0 and h == 0), stop=(gt == ngt - 1 and h == NH - 1)))
            usemark[par] = p.mark()
        p.op("vector", lambda e: e.tensor_scalar_add(rcp[:], psums, 1e-30))
        p.op("vector", lambda e: e.reciprocal(rcp[:], rcp[:]))
        for h in range(NH):
            pav = pAV[h // 8]
            slot = h % 8
            p.op("vector", lambda e, pav=pav, slot=slot, h=h: e.tensor_scalar(
                RES[:, h * 64:(h + 1) * 64], pav[:, slot * 64:(slot + 1) * 64],
                rcp[:, h:h + 1], None, ALU.mult))
        for c in range(CH):
            ptr = p.pb[0][:, 0:128]
            p.op("tensor", lambda e, c=c: e.tensor.transpose(ptr, RES[:, c * 128:(c + 1) * 128], ident_bf[:])
                 if False else e.transpose(ptr, RES[:, c * 128:(c + 1) * 128], ident_bf[:]))
            p.op("vector", lambda e, c=c, q0=q0: e.tensor_copy(
                out=ATT[:, c * T + q0: c * T + q0 + 128], in_=ptr))
    if DBG.get("l3_dump"):
        att_d = p.dram(pref + "att_d", [D, T], F32, kind="ExternalOutput")
        for c in range(CH):
            p.op("vector", lambda e, c=c: e.tensor_copy(out=cm.SC1[:], in_=ATT[:, c * T: c * T + 512]))
            p.op("sync", lambda e, c=c: e.dma_start(out=att_d[c * 128:(c + 1) * 128, 0:512], in_=cm.SC1[:]), dma=True)
        for c in range(CH):
            for tb in range(1, 4):
                p.op("vector", lambda e, c=c, tb=tb: e.tensor_copy(
                    out=cm.SC1[:], in_=ATT[:, c * T + tb * 512: c * T + tb * 512 + 512]))
                p.op("sync", lambda e, c=c, tb=tb: e.dma_start(
                    out=att_d[c * 128:(c + 1) * 128, tb * 512:(tb + 1) * 512], in_=cm.SC1[:]), dma=True)
    p.free_prefix(pref + "t2")
    if DBG.get("l3_stop") == "C":
        p.free_prefix(pref)
        return

    WT2 = p.sb(pref + "t3_WT", [128, CH * D], BF16)
    brow2 = p.sb(pref + "t3_brow", [1, D], F32)
    xsl = p.sb(pref + "t3_xsl", [128, 512], F32)
    p.op("gpsimd", lambda e: e.dma_start(
        out=WT2[:].rearrange("p (c f) -> p c f", f=D),
        in_=ins["L3_attn_WoT_host"].rearrange("(c p) f -> p c f", p=128)), dma=True)
    p.op("sync", lambda e: e.dma_start(
        out=brow2[:], in_=ins["L3_attn_bo"].rearrange("(u f) -> u f", u=1)), dma=True)
    for oc in range(CH):
        for tb in range(4):
            t0 = tb * 512
            ph = p.pf[tb % 2]
            p.op("tensor", lambda e, ph=ph, oc=oc: e.matmul(
                ph[:], brow2[:, oc * 128:(oc + 1) * 128], C["c_ones512_f"][:],
                start=True, stop=False))
            for c in range(CH):
                p.op("tensor", lambda e, ph=ph, oc=oc, c=c, t0=t0: e.matmul(
                    ph[:], WT2[:, c * D + oc * 128: c * D + (oc + 1) * 128],
                    ATT[:, c * T + t0: c * T + t0 + 512], start=False, stop=(c == CH - 1)))
            p.op("sync", lambda e, oc=oc, t0=t0: e.dma_start(
                out=xsl[:], in_=xin_d[oc * 128:(oc + 1) * 128, t0:t0 + 512]), dma=True)
            p.op("vector", lambda e, ph=ph: e.tensor_add(xsl[:], xsl[:], ph[:]))
            p.op("sync", lambda e, oc=oc, t0=t0: e.dma_start(
                out=xout_d[oc * 128:(oc + 1) * 128, t0:t0 + 512], in_=xsl[:]), dma=True)
    p.free_prefix(pref)


def emit_L2_attn(cm: Common, pref="l2a"):
    p, C, ins = cm.p, cm.consts, cm.ins
    xin_d = cm.x_in
    xout_d = cm.x_out
    ones_row_f = C["c_ones_row_f"]
    ones_col_f = C["c_ones_col_f"]
    ones_col_bf = C["c_ones_col_bf"]
    ones512_f = C["c_ones512_f"]
    ident_f = C["c_ident_f"]
    RG = [[0, 1], [2, 3], [4, 5], [6, 7]]
    KD = D + 128

    ccK_in = p.dram(pref + "ccK_in", [KD, T], BF16)
    ccK_out = p.dram(pref + "ccK_out", [2 * KD, T], BF16)
    ccV_in = p.dram(pref + "ccV_in", [T, D], BF16)
    ccV_out = p.dram(pref + "ccV_out", [2 * T, D], BF16)
    isc_d = p.dram(pref + "isc_d", [T, S], FP16)

    QB = p.sb(pref + "QB", [128, CH * T], BF16)
    NM = p.sb(pref + "NM", [128, 1], F32)
    THR = p.sb(pref + "THR", [128, NT], F32)

    iqB = p.sb(pref + "t2_iq", [128, 2 * T], BF16)
    iq2 = p.sb(pref + "t2_iq2", [64, 4 * T], BF16)
    IWT = p.sb(pref + "t2_iwt", [128, NT * 4], F32)
    ikg = p.sb(pref + "t2_ikg", [64, S], BF16)
    kqn = p.sb(pref + "t2_kqn", [1, T], F32)

    cm.LNX = p.sb("z2_LNX", [128, CH * T], BF16)
    _LNXB = cm.LNX
    _XSB = cm.XS
    def lnxs(c, t0=0, n=T, _b=_LNXB):
        return _b[:, c * T + t0: c * T + t0 + n]
    def xss(c, t0=0, n=T, _b=_XSB):
        return _b[:, c * T + t0: c * T + t0 + n]
    cm.XS = p.sb("z3_XS", [128, CH * T], F32)
    cm.load_X(xin_d)
    cm.layernorm()
    p.free_prefix("z3_XS")

    WT = p.sb(pref + "t1_WT", [128, CH * D], BF16)
    brow = p.sb(pref + "t1_brow", [1, D], F32)
    KVB = p.sb(pref + "t1_KVB", [128, CH * T], BF16)
    cos2 = p.sb(pref + "t1_cos", [128, 4 * T], BF16)
    sin2 = p.sb(pref + "t1_sin", [128, 4 * T], BF16)

    p.op("gpsimd", lambda e: e.dma_start(out=cos2[:], in_=ins["c_l2cos"][:, :]), dma=True)
    p.op("gpsimd", lambda e: e.dma_start(out=sin2[:], in_=ins["c_l2sin"][:, :]), dma=True)

    def proj_fm(wname, brow_src, dstbuf, fw):
        p.op("gpsimd", lambda e, wname=wname, fw=fw: e.dma_start(
            out=WT[:, 0:fw * CH].rearrange("p (c f) -> p c f", f=fw),
            in_=ins[wname].rearrange("(c p) f -> p c f", p=128)), dma=True)
        p.op("sync", lambda e, fw=fw: e.dma_start(
            out=brow[:, 0:fw], in_=brow_src), dma=True)
        ocs = (fw + 127) // 128
        bm = {}
        for oc in range(ocs):
            npart = min(128, fw - oc * 128)
            for tb in range(4):
                t0 = tb * 512
                ph = p.pf[tb % 2]
                dst_ps = ph[0:npart, :]
                first = (oc == 0 and tb < 2)
                p.op("tensor", lambda e, dst_ps=dst_ps, oc=oc, npart=npart: e.matmul(
                    dst_ps, brow[:, oc * 128: oc * 128 + npart], ones512_f[:],
                    start=True, stop=False),
                    wait=(None if first else bm.get(tb % 2)), inc=False)
                for c in range(CH):
                    p.op("tensor", lambda e, dst_ps=dst_ps, oc=oc, c=c, t0=t0, fw=fw, npart=npart: e.matmul(
                        dst_ps, WT[:, c * fw + oc * 128: c * fw + oc * 128 + npart],
                        lnxs(c, t0, 512), start=False, stop=(c == CH - 1)),
                        wait=False, inc=(c == CH - 1))
                p.op("vector", lambda e, dst_ps=dst_ps, oc=oc, t0=t0, npart=npart, dstbuf=dstbuf: e.tensor_copy(
                    out=dstbuf[0:npart, oc * T + t0: oc * T + t0 + 512], in_=dst_ps))
                bm[tb % 2] = p.mark()

    def rope2(buf):
        for c in range(4):
            for tb in range(4):
                t0 = tb * 512
                a = buf[:, c * T + t0: c * T + t0 + 512]
                b = buf[:, (c + 4) * T + t0: (c + 4) * T + t0 + 512]
                cs = cos2[:, c * T + t0: c * T + t0 + 512]
                sn = sin2[:, c * T + t0: c * T + t0 + 512]
                p.op("vector", lambda e, a=a, cs=cs: e.tensor_mul(cm.SC1[:], a, cs))
                p.op("vector", lambda e, b=b, sn=sn: e.tensor_mul(cm.SC2[:], b, sn))
                p.op("vector", lambda e: e.tensor_sub(cm.SC1[:], cm.SC1[:], cm.SC2[:]))
                p.op("vector", lambda e, a=a, sn=sn: e.tensor_mul(cm.SC2[:], a, sn))
                p.op("vector", lambda e, a=a: e.tensor_copy(out=a, in_=cm.SC1[:]))
                p.op("vector", lambda e, b=b, cs=cs: e.tensor_mul(cm.SC1[:], b, cs))
                p.op("vector", lambda e, b=b: e.tensor_add(b, cm.SC1[:], cm.SC2[:]))

    proj_fm("L2_wqT_host", ins["b_q_row"][:, :], QB, D)
    rope2(QB)
    proj_fm("L2_wkT_host", ins["b_k_row"][:, :], KVB, D)
    rope2(KVB)
    pn = p.pf[4][0:1, :]
    for tb in range(4):
        t0 = tb * 512
        for c in range(CH):
            sl = KVB[:, c * T + t0: c * T + t0 + 512]
            p.op("scalar", lambda e, sl=sl: e.activation(cm.SC1[:], sl, AF.Square))
            p.op("tensor", lambda e, c=c: e.matmul(pn, ones_col_f[:], cm.SC1[:],
                                                   start=(c == 0), stop=(c == CH - 1)))
        p.op("vector", lambda e, t0=t0: e.tensor_copy(out=kqn[:, t0:t0 + 512], in_=pn))
    p.op("vector", lambda e: e.reduce_max(cm.SROW[:, 0:1], kqn[:], axis=AXX))
    for c in range(CH):
        p.op("sync", lambda e, c=c: e.dma_start(
            out=ccK_in[c * 128:(c + 1) * 128, :], in_=KVB[:, c * T:(c + 1) * T]), dma=True)
    proj_fm("L2_wiqT_host", ins["b_iq_row"][:, :], iqB, 256)
    for h in range(4):
        coff, off = (h // 2) * T, (h % 2) * 64
        for tb in range(4):
            t0 = tb * 512
            p.op("vector", lambda e, h=h, coff=coff, off=off, t0=t0: e.tensor_copy(
                out=iq2[0:64, h * T + t0: h * T + t0 + 512],
                in_=iqB[off:off + 64, coff + t0: coff + t0 + 512]))
    ikB = KVB
    proj_fm("L2_wikT_host", ins["b_ik_row"][:, :], ikB, 64)
    p.op("sync", lambda e: e.dma_start(
        out=ccK_in[D:D + 64, :], in_=ikB[0:64, 0:T]), dma=True)
    p.op("gpsimd", lambda e: e.dma_start(out=ccK_in[D + 64:D + 65, 0:1],
                                         in_=cm.SROW[:, 0:1]), dma=True)
    p.op("gpsimd", lambda e: e.dma_start(
        out=WT[:, 0:CH * 4].rearrange("p (c f) -> p c f", f=4),
        in_=ins["L2_wiwT_host"].rearrange("(c p) f -> p c f", p=128)), dma=True)
    p.op("sync", lambda e: e.dma_start(out=brow[:, 0:4], in_=ins["b_iw_row"][:, :]), dma=True)
    for i in range(NT):
        t0 = i * 128
        psw = p.pf[2][:, 0:4]
        p.op("tensor", lambda e: e.matmul(psw, C["c_ones_row_bf"][:] if False else ones_row_f[:],
                                          brow[:, 0:4], start=True, stop=False))
        for c in range(CH):
            p.op("tensor", lambda e, c=c, t0=t0: e.matmul(
                psw, lnxs(c, t0, 128), WT[:, c * 4:(c + 1) * 4],
                start=False, stop=(c == CH - 1)))
        p.op("vector", lambda e, i=i: e.tensor_copy(out=IWT[:, i * 4:(i + 1) * 4], in_=psw))
    p.op("gpsimd", lambda e: e.dma_start(
        out=WT[:].rearrange("p (c f) -> p c f", f=D),
        in_=ins["L2_wvT_host"].rearrange("(c p) f -> p c f", p=128)), dma=True)
    p.op("sync", lambda e: e.dma_start(out=brow[:, 0:D], in_=ins["b_v_row"][:, :]), dma=True)
    bmv2 = {}
    vship = None
    for i in range(NT):
        t0 = i * 128
        for ob in range(2):
            ph = p.pf[ob % 2]
            p.op("tensor", lambda e, ph=ph, ob=ob: e.matmul(
                ph[:], ones_row_f[:], brow[:, ob * 512:(ob + 1) * 512], start=True, stop=False),
                wait=(None if i == 0 else bmv2.get(ob % 2)), inc=False)
            for c in range(CH):
                p.op("tensor", lambda e, ph=ph, ob=ob, c=c, t0=t0: e.matmul(
                    ph[:], lnxs(c, t0, 128),
                    WT[:, c * D + ob * 512: c * D + (ob + 1) * 512],
                    start=False, stop=(c == CH - 1)),
                    wait=False, inc=(c == CH - 1))
            p.op("vector", lambda e, ph=ph, ob=ob: e.tensor_copy(
                out=KVB[:, ob * 512:(ob + 1) * 512], in_=ph[:]),
                wait=(None if vship is None else [vship, p.mark()]))
            bmv2[ob % 2] = p.mark()
        p.op("sync", lambda e, i=i: e.dma_start(
            out=ccV_in[i * 128:(i + 1) * 128, :], in_=KVB[:, 0:D]), dma=True)
        vship = p.mark()
    for tb in range(4):
        t0 = tb * 512
        for c in range(CH):
            sl = QB[:, c * T + t0: c * T + t0 + 512]
            p.op("scalar", lambda e, sl=sl: e.activation(cm.SC1[:], sl, AF.Square))
            p.op("tensor", lambda e, c=c: e.matmul(pn, ones_col_f[:], cm.SC1[:],
                                                   start=(c == 0), stop=(c == CH - 1)))
        p.op("vector", lambda e, t0=t0: e.tensor_copy(out=kqn[:, t0:t0 + 512], in_=pn))
    p.op("vector", lambda e: e.reduce_max(cm.SROW[:, 1:2], kqn[:], axis=AXX))
    p.op_collective(lambda e: e.collective_compute(
        "AllGather", ALU.bypass, replica_groups=RG,
        ins=[ccK_in.opt()], outs=[ccK_out.opt()]))
    p.op_collective(lambda e: e.collective_compute(
        "AllGather", ALU.bypass, replica_groups=RG,
        ins=[ccV_in.opt()], outs=[ccV_out.opt()]))
    p.op("gpsimd", lambda e: e.dma_start(out=cm.SROW[:, 2:3], in_=ccK_out[D + 64:D + 65, 0:1]), dma=True)
    p.op("gpsimd", lambda e: e.dma_start(out=cm.SROW[:, 3:4], in_=ccK_out[KD + D + 64:KD + D + 65, 0:1]), dma=True)
    p.op("vector", lambda e: e.tensor_max(cm.SROW[:, 2:3], cm.SROW[:, 2:3], cm.SROW[:, 3:4]))
    p.op("scalar", lambda e: e.activation(cm.SROW[:, 1:3], cm.SROW[:, 1:3], AF.Sqrt))
    p.op("vector", lambda e: e.tensor_mul(cm.SROW[:, 1:2], cm.SROW[:, 1:2], cm.SROW[:, 2:3]))
    p.op("vector", lambda e: e.tensor_scalar_mul(cm.SROW[:, 1:2], cm.SROW[:, 1:2], -1.0 / 32.0))
    pm = p.pf[4][:, 0:1]
    p.op("tensor", lambda e: e.matmul(pm, ones_row_f[:], cm.SROW[:, 1:2], start=True, stop=True))
    p.op("vector", lambda e: e.tensor_copy(out=NM[:], in_=pm))
    for gt in range(32):
        r, lt = gt % 2, gt // 2
        p.op("gpsimd", lambda e, r=r, lt=lt, gt=gt: e.dma_start(
            out=ikg[:, gt * 128:(gt + 1) * 128],
            in_=ccK_out[r * KD + D: r * KD + D + 64, lt * 128:(lt + 1) * 128]), dma=True)
    p.free_prefix(pref + "t1")
    cm.p.free_prefix("z2_LNX")

    ISC = p.sb(pref + "t3_ISC", [128, 8 * S], FP16)
    DUMP = p.sb(pref + "t3_DUMP", [128, S], U8)
    bis = p.sb(pref + "t3_bis", [128, 8], F32)
    bism = p.sb(pref + "t3_bism", [128, 1], U8)
    for half in range(2):
        for iq_i in range(8):
            i = half * 8 + iq_i
            q0 = i * 128
            pst = p.pf[0]
            isc_i = ISC[:, iq_i * S:(iq_i + 1) * S]
            iscm = None
            for gt in range(32):
                for h in range(4):
                    p.op("tensor", lambda e, pst=pst, h=h, q0=q0, gt=gt: e.matmul(
                        pst[:, h * 128:(h + 1) * 128],
                        iq2[0:64, h * T + q0: h * T + q0 + 128],
                        ikg[0:64, gt * 128:(gt + 1) * 128],
                        start=(h == 0), stop=(h == 3)),
                        wait=((None if iscm is None else iscm) if h == 0 else False),
                        inc=(h == 3))
                for h in range(4):
                    iwcol = IWT[:, i * 4 + h: i * 4 + h + 1]
                    if h == 0:
                        p.op("vector", lambda e, pst=pst, h=h, iwcol=iwcol: e.tensor_scalar(
                            cm.SC1[:, 0:128], pst[:, h * 128:(h + 1) * 128],
                            0.0, iwcol, ALU.max, ALU.mult))
                    else:
                        p.op("vector", lambda e, pst=pst, h=h, iwcol=iwcol: e.tensor_scalar(
                            cm.SC2[:, 0:128], pst[:, h * 128:(h + 1) * 128],
                            0.0, iwcol, ALU.max, ALU.mult))
                        p.op("vector", lambda e: e.tensor_add(
                            cm.SC1[:, 0:128], cm.SC1[:, 0:128], cm.SC2[:, 0:128]))
                p.op("vector", lambda e, isc_i=isc_i, gt=gt: e.tensor_copy(
                    out=isc_i[:, gt * 128:(gt + 1) * 128], in_=cm.SC1[:, 0:128]))
                iscm = p.mark()
            p.op("sync", lambda e, isc_i=isc_i, q0=q0: e.dma_start(
                out=isc_d[q0:q0 + 128, :], in_=isc_i[:]), dma=True)
            p.op("vector", lambda e, isc_i=isc_i: e.reduce_max(bis[:, 1:2], isc_i[:], axis=AXX))
            p.op("vector", lambda e, isc_i=isc_i: e.tensor_reduce(
                bis[:, 0:1], isc_i[:], AXX, ALU.min))
            for it in range(30):
                p.op("vector", lambda e: e.tensor_add(bis[:, 2:3], bis[:, 0:1], bis[:, 1:2]),
                     wait=(None if it == 0 else False), inc=False)
                p.op("vector", lambda e: e.tensor_scalar_mul(bis[:, 2:3], bis[:, 2:3], 0.5),
                     wait=False, inc=False)
                p.op("vector", lambda e, isc_i=isc_i: e.tensor_scalar(
                    DUMP[:], isc_i[:], bis[:, 2:3], 0.0, ALU.is_ge, ALU.add,
                    accum_out=bis[:, 3:4]), wait=False, inc=False)
                p.op("vector", lambda e: e.tensor_scalar(
                    bism[:], bis[:, 3:4], 256.0, None, ALU.is_ge), wait=False, inc=False)
                p.op("vector", lambda e: e.select(bis[:, 0:1], bism[:], bis[:, 2:3], bis[:, 0:1]),
                     wait=False, inc=False)
                p.op("vector", lambda e: e.select(bis[:, 1:2], bism[:], bis[:, 1:2], bis[:, 2:3]),
                     wait=False, inc=False)
            p.op("vector", lambda e, i=i: e.tensor_copy(out=THR[:, i:i + 1], in_=bis[:, 0:1]),
                 wait=False, inc=True)
    p.free_prefix(pref + "t3")
    cm.p.free_prefix("z")

    THRB = p.sb(pref + "t4_THRB", [128, T], F32)
    ATT = p.sb(pref + "t4_ATT", [128, CH * T], BF16)
    PTA = p.sb(pref + "t4_PTA", [128, 32 * 512], BF16)
    KCH2 = [p.sb(pref + "t4_KCHa", [128, CH * 128], BF16),
            p.sb(pref + "t4_KCHb", [128, CH * 128], BF16)]
    VCH2 = [p.sb(pref + "t4_VCHa", [128, D], BF16),
            p.sb(pref + "t4_VCHb", [128, D], BF16)]
    ISCT2 = [p.sb(pref + "t4_ISCTa", [128, 512], FP16),
             p.sb(pref + "t4_ISCTb", [128, 512], FP16)]
    MT = p.sb(pref + "t4_MT", [128, 512], U8)
    NEG = p.sb(pref + "t4_NEG", [128, 512], F32)
    SUMR = p.sb(pref + "t4_SUMR", [1, 512], F32)
    thrrow = p.sb(pref + "t4_thrr", [1, T], F32)
    p.op("vector", lambda e: e.memset(NEG[:], -1e30))
    pr = p.pf[1][0:1, 0:128]
    for i in range(NT):
        p.op("tensor", lambda e, i=i: e.matmul(
            pr, THR[:, i:i + 1], ident_f[:], start=True, stop=True))
        p.op("vector", lambda e, i=i: e.tensor_copy(
            out=thrrow[:, i * 128:(i + 1) * 128], in_=pr))
    for tb in range(4):
        pb = p.pf[2]
        p.op("tensor", lambda e, tb=tb: e.matmul(
            pb[:], ones_row_f[:], thrrow[:, tb * 512:(tb + 1) * 512], start=True, stop=True))
        p.op("vector", lambda e, tb=tb: e.tensor_copy(out=THRB[:, tb * 512:(tb + 1) * 512], in_=pb[:]))
    ccK4 = ccK_out.rearrange("(r c p) t -> r p c t", p=128, c=9)
    kuse = [None, None]
    for qg in range(4):
        g0 = qg * 512
        for gt in range(32):
            r, lt = gt % 2, gt // 2
            par = gt % 2
            KCH = KCH2[par]
            ISCT = ISCT2[par]
            pS = p.pf[4 + par]
            p.op("gpsimd", lambda e, r=r, lt=lt, KCH=KCH: e.dma_start(
                out=KCH[:].rearrange("p (c t) -> p c t", t=128),
                in_=ccK4[r][:, 0:CH, lt * 128:(lt + 1) * 128]), dma=True)
            p.op("sync", lambda e, gt=gt, g0=g0, ISCT=ISCT: e.dma_start_transpose(
                out=ISCT[:], in_=isc_d[g0:g0 + 512, gt * 128:(gt + 1) * 128]), dma=True)
            for c in range(CH):
                p.op("tensor", lambda e, pS=pS, c=c, g0=g0, KCH=KCH: e.matmul(
                    pS[:], KCH[:, c * 128:(c + 1) * 128],
                    QB[:, c * T + g0: c * T + g0 + 512],
                    start=(c == 0), stop=(c == CH - 1)),
                    wait=(None if c == 0 else False), inc=(c == CH - 1))
            p.op("vector", lambda e, g0=g0, ISCT=ISCT: e.tensor_tensor(
                MT[:], ISCT[:], THRB[:, g0:g0 + 512], ALU.is_ge))
            p.op("vector", lambda e, pS=pS: e.select(cm.SC1[:], MT[:], pS[:], NEG[:]))
            p.op("scalar", lambda e, gt=gt: e.activation(
                PTA[:, gt * 512:(gt + 1) * 512], cm.SC1[:], AF.Exp,
                bias=NM[:, 0:1], scale=1.0 / 32.0))
            kuse[par] = p.mark()
        psr = p.pf[4][0:1, :]
        for gt in range(32):
            p.op("tensor", lambda e, gt=gt: e.matmul(
                psr, ones_col_bf[:], PTA[:, gt * 512:(gt + 1) * 512],
                start=(gt == 0), stop=(gt == 31)),
                wait=(None if gt == 0 else False), inc=(gt == 31))
        p.op("vector", lambda e: e.tensor_copy(out=SUMR[:], in_=psr))
        p.op("vector", lambda e: e.tensor_scalar_add(SUMR[:], SUMR[:], 1e-30))
        p.op("vector", lambda e: e.reciprocal(SUMR[:], SUMR[:]))
        prb = p.pf[5]
        p.op("tensor", lambda e: e.matmul(prb[:], ones_row_f[:], SUMR[:], start=True, stop=True))
        p.op("vector", lambda e: e.tensor_copy(out=cm.SC2[:], in_=prb[:]))
        vuse = [None, None]
        for rnd in range(2):
            for gt in range(32):
                r, lt = gt % 2, gt // 2
                par = gt % 2
                VCH = VCH2[par]
                p.op("gpsimd", lambda e, r=r, lt=lt, VCH=VCH: e.dma_start(
                    out=VCH[:], in_=ccV_out[(r * NT + lt) * 128:(r * NT + lt + 1) * 128, :]), dma=True)
                for db in range(4):
                    dblk = rnd * 4 + db
                    pav = p.pf[db]
                    p.op("tensor", lambda e, pav=pav, dblk=dblk, gt=gt, VCH=VCH: e.matmul(
                        pav[:], VCH[:, dblk * 128:(dblk + 1) * 128],
                        PTA[:, gt * 512:(gt + 1) * 512],
                        start=(gt == 0), stop=(gt == 31)),
                        wait=(None if db == 0 else False), inc=(db == 3))
            for db in range(4):
                dblk = rnd * 4 + db
                pav = p.pf[db]
                p.op("vector", lambda e, pav=pav: e.tensor_copy(out=cm.SC1[:], in_=pav[:]))
                p.op("vector", lambda e, pav=pav, dblk=dblk, g0=g0: e.tensor_mul(
                    ATT[:, dblk * T + g0: dblk * T + g0 + 512], cm.SC1[:], cm.SC2[:]))
    WT2 = p.sb(pref + "t5_WT", [128, CH * D], BF16)
    brow2 = p.sb(pref + "t5_brow", [1, D], F32)
    xsl = p.sb(pref + "t5_xsl", [128, 512], F32)
    p.op("gpsimd", lambda e: e.dma_start(
        out=WT2[:].rearrange("p (c f) -> p c f", f=D),
        in_=ins["L2_woT_host"].rearrange("(c p) f -> p c f", p=128)), dma=True)
    p.op("sync", lambda e: e.dma_start(
        out=brow2[:], in_=ins["b_o_row"][:, :]), dma=True)
    for oc in range(CH):
        for tb in range(4):
            t0 = tb * 512
            ph = p.pf[tb % 2]
            p.op("tensor", lambda e, ph=ph, oc=oc: e.matmul(
                ph[:], brow2[:, oc * 128:(oc + 1) * 128], ones512_f[:],
                start=True, stop=False))
            for c in range(CH):
                p.op("tensor", lambda e, ph=ph, oc=oc, c=c, t0=t0: e.matmul(
                    ph[:], WT2[:, c * D + oc * 128: c * D + (oc + 1) * 128],
                    ATT[:, c * T + t0: c * T + t0 + 512], start=False, stop=(c == CH - 1)))
            p.op("sync", lambda e, oc=oc, t0=t0: e.dma_start(
                out=xsl[:], in_=xin_d[oc * 128:(oc + 1) * 128, t0:t0 + 512]), dma=True)
            p.op("vector", lambda e, ph=ph: e.tensor_add(xsl[:], xsl[:], ph[:]))
            p.op("sync", lambda e, oc=oc, t0=t0: e.dma_start(
                out=xout_d[oc * 128:(oc + 1) * 128, t0:t0 + 512], in_=xsl[:]), dma=True)
    p.free_prefix(pref)


def emit_L0_attn(cm: Common, pref="l0a"):
    p, C, ins = cm.p, cm.consts, cm.ins
    xin_d = cm.x_in
    xout_d = cm.x_out
    NH = 16
    ones_row_f = C["c_ones_row_f"]
    ones_col_f = C["c_ones_col_f"]
    ones_col_bf = C["c_ones_col_bf"]
    ones512_f = C["c_ones512_f"]
    ident_bf = C["c_ident_bf"]
    RG = [[0, 1], [2, 3], [4, 5], [6, 7]]
    EPS = 1e-6

    qt_d = p.dram(pref + "qt_d", [2048, T], BF16)
    cc_in = p.dram(pref + "cc_in", [T, 1040], BF16)
    cc_out = p.dram(pref + "cc_out", [2 * T, 1040], BF16)

    H = p.sb(pref + "H", [128, CH * T], F32)
    n2w = p.sb(pref + "n2w", [128, CH], F32)
    CTX = p.sb(pref + "g1_CTX", [128, NT * 1040], BF16)
    STATE = p.sb(pref + "g1_STATE", [128, NH * 64], F32)
    STATEN = p.sb(pref + "g1_STATEN", [128, NH], F32)
    cosb = p.sb(pref + "g1_csb", [128, T], BF16)
    sinb = cosb
    tri = p.sb(pref + "g1_tri", [128, 128], BF16)
    n1w = p.sb(pref + "g1_n1w", [128, CH], F32)
    mW0 = p.sb(pref + "g1_mW0", [128, 1], F32)
    mW1 = p.sb(pref + "g1_mW1", [128, 1], F32)
    p.op("gpsimd", lambda e: e.dma_start(out=cosb[:], in_=ins["c_l0cs"][:, :]), dma=True)
    p.op("gpsimd", lambda e: e.dma_start(out=tri[:], in_=ins["c_l0tri"][:, :]), dma=True)
    p.op("sync", lambda e: e.dma_start(out=n1w[:], in_=ins["n1w_col"][:, :]), dma=True)
    p.op("sync", lambda e: e.dma_start(out=n2w[:], in_=ins["n2w_col"][:, :]), dma=True)
    p.op("sync", lambda e: e.dma_start(out=mW0[:], in_=ins["c_mW0"][:, :]), dma=True)
    p.op("sync", lambda e: e.dma_start(out=mW1[:], in_=ins["c_mW1"][:, :]), dma=True)
    p.op("vector", lambda e: e.memset(STATE[:], 0.0))
    p.op("vector", lambda e: e.memset(STATEN[:], 0.0))

    cm.LNX = p.sb("z2_LNX", [128, CH * T], BF16)
    _LNXB = cm.LNX
    _XSB = cm.XS
    def lnxs(c, t0=0, n=T, _b=_LNXB):
        return _b[:, c * T + t0: c * T + t0 + n]
    def xss(c, t0=0, n=T, _b=_XSB):
        return _b[:, c * T + t0: c * T + t0 + n]
    XB = p.sb("z3_XB", [128, 512], F32)
    psA = p.pf[4][0:1, :]
    psB = p.pf[5][0:1, :]
    pbA = p.pf[2]
    pbB = p.pf[3]
    for tb in range(4):
        t0 = tb * 512
        for c in range(CH):
            p.op("sync", lambda e, c=c, t0=t0: e.dma_start(
                out=XB[:], in_=xin_d[c * 128:(c + 1) * 128, t0:t0 + 512]), dma=True)
            p.op("scalar", lambda e: e.activation(cm.SC1[:], XB[:], AF.Square))
            p.op("tensor", lambda e, c=c: e.matmul(psA, C["c_ones_col_f"][:], XB[:],
                                                   start=(c == 0), stop=(c == CH - 1)))
            p.op("tensor", lambda e, c=c: e.matmul(psB, C["c_ones_col_f"][:], cm.SC1[:],
                                                   start=(c == 0), stop=(c == CH - 1)))
        rr = cm.SROW[:, 0:512]
        rn = cm.SROW[:, 512:1024]
        p.op("vector", lambda e, rn=rn: e.tensor_scalar_mul(rn, psA, 1.0 / D))
        p.op("vector", lambda e, rr=rr: e.tensor_scalar_mul(rr, psB, 1.0 / D))
        p.op("vector", lambda e, rn=rn: e.scalar_tensor_tensor(
            cm.SC1[0:1, :], rn, 1.0, rn, ALU.mult, ALU.mult))
        p.op("vector", lambda e, rr=rr: e.tensor_sub(rr, rr, cm.SC1[0:1, :]))
        p.op("vector", lambda e, rr=rr: e.tensor_scalar_add(rr, rr, 1e-5))
        p.op("scalar", lambda e, rr=rr: e.activation(rr, rr, AF.Sqrt))
        p.op("vector", lambda e, rr=rr: e.reciprocal(rr, rr))
        p.op("vector", lambda e, rr=rr, rn=rn: e.scalar_tensor_tensor(
            rn, rn, -1.0, rr, ALU.mult, ALU.mult))
        p.op("tensor", lambda e, rr=rr: e.matmul(pbA[:], C["c_ones_row_f"][:], rr, start=True, stop=True))
        p.op("tensor", lambda e, rn=rn: e.matmul(pbB[:], C["c_ones_row_f"][:], rn, start=True, stop=True))
        for c in range(CH):
            p.op("sync", lambda e, c=c, t0=t0: e.dma_start(
                out=XB[:], in_=xin_d[c * 128:(c + 1) * 128, t0:t0 + 512]), dma=True)
            p.op("vector", lambda e: e.tensor_mul(cm.SC2[:], XB[:], pbA[:]))
            dst = lnxs(c, t0, 512)
            hd = H[:, c * T + t0: c * T + t0 + 512]
            p.op("vector", lambda e, dst=dst: e.tensor_add(dst, cm.SC2[:], pbB[:]))
            p.op("vector", lambda e, hd=hd: e.tensor_add(hd, cm.SC2[:], pbB[:]))
    p.free_prefix("z3_XB")

    psum_r = p.pf[4][0:1, :]
    pb_r = p.pf[2]
    for tb in range(4):
        t0 = tb * 512
        for c in range(CH):
            sl = lnxs(c, t0, 512)
            p.op("scalar", lambda e, sl=sl: e.activation(cm.SC1[:], sl, AF.Square))
            p.op("tensor", lambda e, c=c: e.matmul(psum_r, ones_col_f[:], cm.SC1[:],
                                                   start=(c == 0), stop=(c == CH - 1)))
        rr = cm.SROW[:, 0:512]
        p.op("vector", lambda e, rr=rr: e.tensor_scalar_mul(rr, psum_r, 1.0 / D))
        p.op("vector", lambda e, rr=rr: e.tensor_scalar_add(rr, rr, EPS))
        p.op("scalar", lambda e, rr=rr: e.activation(rr, rr, AF.Sqrt))
        p.op("vector", lambda e, rr=rr: e.reciprocal(rr, rr))
        p.op("tensor", lambda e, rr=rr: e.matmul(pb_r[:], ones_row_f[:], rr, start=True, stop=True))
        for c in range(CH):
            sl = lnxs(c, t0, 512)
            p.op("vector", lambda e, sl=sl: e.tensor_mul(sl, sl, pb_r[:]))

    WQKV = p.sb(pref + "t1_W", [128, CH * 512], BF16)
    bqp = p.sb(pref + "t1_bqp", [1, D], F32)
    qg = p.sb(pref + "t1_qg", [128, NH * 512], BF16)
    kg = p.sb(pref + "t1_kg", [128, CH * 512], BF16)
    vg = p.sb(pref + "t1_vg", [128, CH * 512], BF16)
    kt = p.sb(pref + "t1_kt", [128, NH * 128], BF16)
    ktm = p.sb(pref + "t1_ktm", [128, NH * 128], BF16)
    vtm = p.sb(pref + "t1_vtm", [128, D], BF16)
    dse = p.sb(pref + "t1_dse", [128, 1040], BF16)
    STB = p.sb(pref + "t1_STB", [128, NH * 64], BF16)
    STBN = p.sb(pref + "t1_STBN", [128, NH], BF16)
    PTb = p.sb(pref + "t1_PT", [128, 128], BF16)

    qkv_w3 = ins["L0_attn_qkv_w"].rearrange("(c p) (w f) -> w p c f", p=128, w=3)
    qkvm = {}
    for g in range(4):
        t0 = g * 512
        for part, dst in ((0, None), (1, kg), (2, vg)):
            p.op("sync", lambda e, part=part: e.dma_start(
                out=bqp[:], in_=ins["qkv_b_row"][0:1, part * D:(part + 1) * D]), dma=True)
            for wh in range(2):
                p.op("gpsimd", lambda e, part=part, wh=wh: e.dma_start(
                    out=WQKV[:].rearrange("p (c f) -> p c f", f=512),
                    in_=qkv_w3[part][:, :, wh * 512:(wh + 1) * 512]), dma=True)
                for c in range(CH):
                    p.op("vector", lambda e, c=c: e.tensor_scalar(
                        WQKV[:, c * 512:(c + 1) * 512], WQKV[:, c * 512:(c + 1) * 512],
                        n1w[:, c:c + 1], None, ALU.mult))
                for loc in range(4):
                    oc = wh * 4 + loc
                    ph = p.pf[oc % 2]
                    first = (loc < 2)
                    p.op("tensor", lambda e, ph=ph, oc=oc: e.matmul(
                        ph[:], bqp[:, oc * 128:(oc + 1) * 128], ones512_f[:],
                        start=True, stop=False),
                        wait=(None if first else qkvm.get(oc % 2)), inc=False)
                    for c in range(CH):
                        p.op("tensor", lambda e, ph=ph, loc=loc, c=c, t0=t0: e.matmul(
                            ph[:], WQKV[:, c * 512 + loc * 128: c * 512 + loc * 128 + 128],
                            lnxs(c, t0, 512), start=False, stop=(c == CH - 1)),
                            wait=False, inc=(c == CH - 1))
                    if part == 0:
                        for sub in range(2):
                            h = oc * 2 + sub
                            off = sub * 64
                            p.op("vector", lambda e, ph=ph, off=off: e.tensor_copy(
                                out=cm.SC1[0:64, :], in_=ph[off:off + 64, :]))
                            p.op("vector", lambda e, h=h, t0=t0: e.scalar_tensor_tensor(
                                qg[0:64, h * 512:(h + 1) * 512], cm.SC1[0:64, :], 0.0,
                                cosb[0:64, t0:t0 + 512], ALU.max, ALU.mult))
                            p.op("vector", lambda e: e.tensor_copy(
                                out=cm.SC2[64:128, :], in_=cm.SC1[0:64, :]))
                            p.op("vector", lambda e, h=h, t0=t0: e.scalar_tensor_tensor(
                                qg[64:128, h * 512:(h + 1) * 512], cm.SC2[64:128, :], 0.0,
                                sinb[64:128, t0:t0 + 512], ALU.max, ALU.mult))
                        qkvm[oc % 2] = p.mark()
                    else:
                        p.op("vector", lambda e, ph=ph, dst=dst, oc=oc: e.tensor_copy(
                            out=dst[:, oc * 512:(oc + 1) * 512], in_=ph[:]))
                    qkvm[oc % 2] = p.mark()
        for hc in range(NH):
            p.op("sync", lambda e, hc=hc, t0=t0: e.dma_start(
                out=qt_d[hc * 128:(hc + 1) * 128, t0:t0 + 512],
                in_=qg[:, hc * 512:(hc + 1) * 512]), dma=True)
        for j in range(4):
            i = g * 4 + j
            j0 = j * 128
            for h in range(NH):
                c, off = h // 2, (h % 2) * 64
                p.op("vector", lambda e, c=c, off=off, j0=j0: e.tensor_copy(
                    out=cm.SC1[0:64, 0:128],
                    in_=kg[off:off + 64, c * 512 + j0: c * 512 + j0 + 128]))
                p.op("vector", lambda e, h=h, j0=j0, t0=t0: e.scalar_tensor_tensor(
                    kt[0:64, h * 128:(h + 1) * 128], cm.SC1[0:64, 0:128], 0.0,
                    cosb[0:64, t0 + j0: t0 + j0 + 128], ALU.max, ALU.mult))
                p.op("vector", lambda e: e.tensor_copy(out=cm.SC2[64:128, 0:128], in_=cm.SC1[0:64, 0:128]))
                p.op("vector", lambda e, h=h, j0=j0, t0=t0: e.scalar_tensor_tensor(
                    kt[64:128, h * 128:(h + 1) * 128], cm.SC2[64:128, 0:128], 0.0,
                    sinb[64:128, t0 + j0: t0 + j0 + 128], ALU.max, ALU.mult))
            for h in range(NH):
                ptr = p.pb[0][:, 0:128]
                p.op("tensor", lambda e, h=h: e.transpose(
                    ptr, kt[:, h * 128:(h + 1) * 128], ident_bf[:]))
                p.op("vector", lambda e, h=h: e.tensor_copy(
                    out=ktm[:, h * 128:(h + 1) * 128], in_=ptr))
            for c in range(CH):
                ptr = p.pb[0][:, 0:128]
                p.op("tensor", lambda e, c=c, j0=j0: e.transpose(
                    ptr, vg[:, c * 512 + j0: c * 512 + j0 + 128], ident_bf[:]))
                p.op("vector", lambda e, c=c: e.tensor_copy(
                    out=vtm[:, c * 128:(c + 1) * 128], in_=ptr))
            pctx = [p.pf[1], p.pf[2]]
            pctxn = p.pf[3][:, 0:16]
            p.op("vector", lambda e: e.tensor_copy(out=STB[:], in_=STATE[:]))
            p.op("vector", lambda e: e.tensor_copy(out=STBN[:], in_=STATEN[:]))
            for h in range(NH):
                pst = p.pf[0][:, 0:128]
                p.op("tensor", lambda e, pst=pst, h=h, j0=j0: e.matmul(
                    pst, kt[:, h * 128:(h + 1) * 128],
                    qg[:, h * 512 + j0: h * 512 + j0 + 128], start=True, stop=True))
                p.op("vector", lambda e, pst=pst: e.tensor_mul(PTb[:], pst, tri[:]))
                pc = pctx[h // 8]
                slot = h % 8
                p.op("tensor", lambda e, pc=pc, slot=slot, h=h: e.matmul(
                    pc[:, slot * 64:(slot + 1) * 64], PTb[:], vtm[:, h * 64:(h + 1) * 64],
                    start=(slot == 0), stop=False))
                p.op("tensor", lambda e, h=h: e.matmul(
                    pctxn[:, h:h + 1], PTb[:], ones_col_bf[:],
                    start=(h == 0), stop=False))
                p.op("tensor", lambda e, pc=pc, slot=slot, h=h, j0=j0: e.matmul(
                    pc[:, slot * 64:(slot + 1) * 64],
                    qg[:, h * 512 + j0: h * 512 + j0 + 128],
                    STB[:, h * 64:(h + 1) * 64], start=False, stop=(slot == 7)))
                p.op("tensor", lambda e, h=h, j0=j0: e.matmul(
                    pctxn[:, h:h + 1], qg[:, h * 512 + j0: h * 512 + j0 + 128],
                    STBN[:, h:h + 1], start=False, stop=(h == NH - 1)))
                pup = p.pf[4][:, 0:65]
                p.op("tensor", lambda e, pup=pup, h=h: e.matmul(
                    pup[:, 0:64], ktm[:, h * 128:(h + 1) * 128], vtm[:, h * 64:(h + 1) * 64],
                    start=True, stop=False))
                p.op("tensor", lambda e, pup=pup, h=h: e.matmul(
                    pup[:, 64:65], ktm[:, h * 128:(h + 1) * 128], ones_col_bf[:],
                    start=False, stop=True))
                p.op("vector", lambda e, pup=pup, h=h: e.tensor_copy(
                    out=dse[:, h * 64:(h + 1) * 64], in_=pup[:, 0:64]))
                p.op("vector", lambda e, pup=pup, h=h: e.tensor_copy(
                    out=dse[:, NH * 64 + h: NH * 64 + h + 1], in_=pup[:, 64:65]))
                p.op("vector", lambda e, pup=pup, h=h: e.tensor_add(
                    STATE[:, h * 64:(h + 1) * 64], STATE[:, h * 64:(h + 1) * 64], pup[:, 0:64]))
                p.op("vector", lambda e, pup=pup, h=h: e.tensor_add(
                    STATEN[:, h:h + 1], STATEN[:, h:h + 1], pup[:, 64:65]))
            for half in range(2):
                p.op("vector", lambda e, i=i, half=half: e.tensor_copy(
                    out=CTX[:, i * 1040 + half * 512: i * 1040 + (half + 1) * 512],
                    in_=pctx[half][:]))
            p.op("vector", lambda e, i=i: e.tensor_copy(
                out=CTX[:, i * 1040 + 1024: i * 1040 + 1040], in_=pctxn))
            p.op("sync", lambda e, i=i: e.dma_start(
                out=cc_in[i * 128:(i + 1) * 128, :], in_=dse[:]), dma=True)
    p.free_prefix(pref + "t1")
    cm.p.free_prefix("z2_LNX")
    p.op_collective(lambda e: e.collective_compute(
        "AllGather", ALU.bypass, replica_groups=RG,
        ins=[cc_in.opt()], outs=[cc_out.opt()]))

    SCOR = p.sb(pref + "t2_SCOR", [128, 1040], F32)
    SUSE = p.sb(pref + "t2_SUSE", [128, 1040], BF16)
    PD = p.sb(pref + "t2_PD", [128, 1040], F32)
    GB0 = p.sb(pref + "t2_GB0", [128, 1040], BF16)
    GB1 = p.sb(pref + "t2_GB1", [128, 1040], BF16)
    qtt = p.sb(pref + "t2_qtt", [128, NH * 128], BF16)
    OTM2 = p.sb(pref + "t2_OTM", [128, D], BF16)
    rcpn = p.sb(pref + "t2_rcp", [128, 16], F32)
    OFM = p.sb(pref + "t2_OFM", [128, CH * T], BF16)
    p.op("vector", lambda e: e.memset(SCOR[:], 0.0))
    for i in range(NT):
        p.op("gpsimd", lambda e, i=i: e.dma_start(
            out=GB0[:], in_=cc_out[i * 128:(i + 1) * 128, :]), dma=True)
        p.op("gpsimd", lambda e, i=i: e.dma_start(
            out=GB1[:], in_=cc_out[T + i * 128: T + (i + 1) * 128, :]), dma=True)
        p.op("gpsimd", lambda e, i=i: e.dma_start(
            out=qtt[:].rearrange("p (h t) -> p h t", t=128),
            in_=qt_d.rearrange("(h p) t -> p h t", p=128)[:, :, i * 128:(i + 1) * 128]), dma=True)
        p.op("vector", lambda e: e.tensor_scalar(PD[:], GB0[:], mW0[:, 0:1], None, ALU.mult))
        p.op("vector", lambda e: e.scalar_tensor_tensor(PD[:], GB1[:], mW1[:, 0:1], PD[:], ALU.mult, ALU.add))
        p.op("vector", lambda e: e.scalar_tensor_tensor(SUSE[:], PD[:], mW0[:, 0:1], SCOR[:], ALU.mult, ALU.add))
        p.op("vector", lambda e: e.tensor_add(SCOR[:], SCOR[:], PD[:]))
        pcor = [p.pf[1], p.pf[2]]
        pcorn = p.pf[3][:, 0:16]
        for h in range(NH):
            pc = pcor[h // 8]
            slot = h % 8
            p.op("tensor", lambda e, pc=pc, slot=slot, h=h: e.matmul(
                pc[:, slot * 64:(slot + 1) * 64], qtt[:, h * 128:(h + 1) * 128],
                SUSE[:, h * 64:(h + 1) * 64], start=(slot == 0), stop=(slot == 7)))
            p.op("tensor", lambda e, h=h: e.matmul(
                pcorn[:, h:h + 1], qtt[:, h * 128:(h + 1) * 128],
                SUSE[:, NH * 64 + h: NH * 64 + h + 1],
                start=(h == 0), stop=(h == NH - 1)))
        p.op("vector", lambda e, i=i: e.tensor_copy(out=cm.SC1[:, 0:16], in_=pcorn))
        p.op("vector", lambda e, i=i: e.tensor_add(
            rcpn[:], cm.SC1[:, 0:16], CTX[:, i * 1040 + 1024: i * 1040 + 1040]))
        p.op("vector", lambda e: e.tensor_scalar_add(rcpn[:], rcpn[:], EPS))
        p.op("vector", lambda e: e.reciprocal(rcpn[:], rcpn[:]))
        for half in range(2):
            p.op("vector", lambda e, i=i, half=half: e.tensor_copy(
                out=cm.SC1[:], in_=CTX[:, i * 1040 + half * 512: i * 1040 + (half + 1) * 512]))
            p.op("vector", lambda e, half=half: e.tensor_add(cm.SC1[:], cm.SC1[:], pcor[half][:]))
            for sh in range(8):
                h = half * 8 + sh
                p.op("vector", lambda e, h=h, sh=sh: e.tensor_scalar(
                    OTM2[:, h * 64:(h + 1) * 64], cm.SC1[:, sh * 64:(sh + 1) * 64],
                    rcpn[:, h:h + 1], None, ALU.mult))
        for c in range(CH):
            ptr = p.pb[0][:, 0:128]
            p.op("tensor", lambda e, c=c: e.transpose(
                ptr, OTM2[:, c * 128:(c + 1) * 128], ident_bf[:]))
            p.op("vector", lambda e, c=c, i=i: e.tensor_copy(
                out=OFM[:, c * T + i * 128: c * T + (i + 1) * 128], in_=ptr))
    WO = p.sb(pref + "t2_WO", [128, CH * D], BF16)
    brow3 = p.sb(pref + "t2_brow", [1, D], F32)
    p.op("gpsimd", lambda e: e.dma_start(
        out=WO[:].rearrange("p (c f) -> p c f", f=D),
        in_=ins["L0_attn_out_w"].rearrange("(c p) f -> p c f", p=128)), dma=True)
    p.op("sync", lambda e: e.dma_start(
        out=brow3[:], in_=ins["L0_attn_out_b"].rearrange("(u f) -> u f", u=1)), dma=True)
    for oc in range(CH):
        for tb in range(4):
            t0 = tb * 512
            ph = p.pf[tb % 2]
            p.op("tensor", lambda e, ph=ph, oc=oc: e.matmul(
                ph[:], brow3[:, oc * 128:(oc + 1) * 128], ones512_f[:],
                start=True, stop=False))
            for c in range(CH):
                p.op("tensor", lambda e, ph=ph, oc=oc, c=c, t0=t0: e.matmul(
                    ph[:], WO[:, c * D + oc * 128: c * D + (oc + 1) * 128],
                    OFM[:, c * T + t0: c * T + t0 + 512], start=False, stop=(c == CH - 1)))
            hsl = H[:, oc * T + t0: oc * T + t0 + 512]
            p.op("vector", lambda e, ph=ph, hsl=hsl: e.tensor_add(hsl, hsl, ph[:]))
    p.free_prefix(pref + "t2")
    p.free_prefix(pref + "g1")

    R = p.sb(pref + "t3_R", [128, CH * T], BF16)
    for tb in range(4):
        t0 = tb * 512
        for c in range(CH):
            sl = H[:, c * T + t0: c * T + t0 + 512]
            p.op("scalar", lambda e, sl=sl: e.activation(cm.SC1[:], sl, AF.Square))
            p.op("tensor", lambda e, c=c: e.matmul(psum_r, ones_col_f[:], cm.SC1[:],
                                                   start=(c == 0), stop=(c == CH - 1)))
        rr = cm.SROW[:, 0:512]
        p.op("vector", lambda e, rr=rr: e.tensor_scalar_mul(rr, psum_r, 1.0 / D))
        p.op("vector", lambda e, rr=rr: e.tensor_scalar_add(rr, rr, EPS))
        p.op("scalar", lambda e, rr=rr: e.activation(rr, rr, AF.Sqrt))
        p.op("vector", lambda e, rr=rr: e.reciprocal(rr, rr))
        p.op("tensor", lambda e, rr=rr: e.matmul(pb_r[:], ones_row_f[:], rr, start=True, stop=True))
        for c in range(CH):
            hsl = H[:, c * T + t0: c * T + t0 + 512]
            p.op("vector", lambda e, hsl=hsl, c=c, t0=t0: e.tensor_mul(
                R[:, c * T + t0: c * T + t0 + 512], hsl, pb_r[:]))
    W1 = p.sb(pref + "t3_W1", [128, CH * T], BF16)
    W2 = p.sb(pref + "t3_W2", [128, NT * D], BF16)
    b1r = p.sb(pref + "t3_b1", [1, 4096], F32)
    b2r = p.sb(pref + "t3_b2", [1, D], F32)
    h1 = p.sb(pref + "t3_h1", [128, NT * 512], BF16)
    w1r = ins["L0_attn_mlp_w1"].rearrange("(c p) f -> p c f", p=128)
    w2r = ins["L0_attn_mlp_w2"].rearrange("(c p) f -> p c f", p=128)
    p.op("sync", lambda e: e.dma_start(
        out=b1r[:], in_=ins["L0_attn_mlp_b1"].rearrange("(u f) -> u f", u=1)), dma=True)
    p.op("sync", lambda e: e.dma_start(
        out=b2r[:], in_=ins["L0_attn_mlp_b2"].rearrange("(u f) -> u f", u=1)), dma=True)
    mlpm = {}
    for fh in range(2):
        p.op("gpsimd", lambda e, fh=fh: e.dma_start(
            out=W1[:].rearrange("p (c f) -> p c f", f=T),
            in_=w1r[:, :, fh * T:(fh + 1) * T]), dma=True)
        for c in range(CH):
            p.op("vector", lambda e, c=c: e.tensor_scalar(
                W1[:, c * T:(c + 1) * T], W1[:, c * T:(c + 1) * T],
                n2w[:, c:c + 1], None, ALU.mult))
        p.op("gpsimd", lambda e, fh=fh: e.dma_start(
            out=W2[:].rearrange("p (c f) -> p c f", f=D),
            in_=w2r[:, fh * NT:(fh + 1) * NT, :]), dma=True)
        for tb in range(4):
            t0 = tb * 512
            for oc in range(NT):
                ph = p.pf[oc % 2]
                p.op("tensor", lambda e, ph=ph, oc=oc, fh=fh: e.matmul(
                    ph[:], b1r[:, fh * T + oc * 128: fh * T + oc * 128 + 128],
                    ones512_f[:], start=True, stop=False),
                    wait=(None if oc < 2 else mlpm.get(oc % 2)), inc=False)
                for c in range(CH):
                    p.op("tensor", lambda e, ph=ph, oc=oc, c=c, t0=t0: e.matmul(
                        ph[:], W1[:, c * T + oc * 128: c * T + oc * 128 + 128],
                        R[:, c * T + t0: c * T + t0 + 512], start=False, stop=(c == CH - 1)),
                        wait=False, inc=(c == CH - 1))
                p.op("scalar", lambda e, ph=ph, oc=oc: e.activation(
                    h1[:, oc * 512:(oc + 1) * 512], ph[:], AF.Gelu_apprx_tanh))
                mlpm[oc % 2] = p.mark()
            for oc in range(CH):
                py = p.pf[2 + (oc % 2)]
                first = True
                if fh == 0:
                    p.op("tensor", lambda e, py=py, oc=oc: e.matmul(
                        py[:], b2r[:, oc * 128:(oc + 1) * 128], ones512_f[:],
                        start=True, stop=False),
                        wait=(None if oc < 2 else mlpm.get(2 + oc % 2)), inc=False)
                    first = False
                for fc in range(NT):
                    p.op("tensor", lambda e, py=py, oc=oc, fc=fc, fh=fh: e.matmul(
                        py[:], W2[:, fc * D + oc * 128: fc * D + (oc + 1) * 128],
                        h1[:, fc * 512:(fc + 1) * 512],
                        start=(fh == 1 and fc == 0), stop=(fc == NT - 1)),
                        wait=((None if oc < 2 else mlpm.get(2 + oc % 2)) if (first and fc == 0) else False),
                        inc=(fc == NT - 1))
                hsl = H[:, oc * T + t0: oc * T + t0 + 512]
                p.op("vector", lambda e, py=py, hsl=hsl: e.tensor_add(hsl, hsl, py[:]))
                mlpm[2 + oc % 2] = p.mark()
    p.free_prefix(pref + "t3")

    xsl = p.sb(pref + "t4_xsl", [128, 512], F32)
    for oc in range(CH):
        for tb in range(4):
            t0 = tb * 512
            p.op("sync", lambda e, oc=oc, t0=t0: e.dma_start(
                out=xsl[:], in_=xin_d[oc * 128:(oc + 1) * 128, t0:t0 + 512]), dma=True)
            p.op("vector", lambda e, oc=oc, t0=t0: e.tensor_add(
                xsl[:], xsl[:], H[:, oc * T + t0: oc * T + t0 + 512]))
            p.op("sync", lambda e, oc=oc, t0=t0: e.dma_start(
                out=xout_d[oc * 128:(oc + 1) * 128, t0:t0 + 512], in_=xsl[:]), dma=True)
    p.free_prefix(pref)


def sl_L0_attn(cm):
    emit_L0_attn(cm)


def sl_L2_attn(cm):
    emit_L2_attn(cm)


def sl_L3_attn(cm):
    std_prologue(cm, lnx_first=True)
    cm.p.free_prefix("z3_XS")
    emit_L3_attn(cm)
    if DBG.get("l3_stop"):
        cm.p.op("sync", lambda e: e.dma_start(out=cm.x_out[:, :], in_=cm.x_in[:, :]), dma=True)
    cm.p.free_prefix("z")


def sl_L1_attn(cm):
    std_prologue(cm)
    emit_L1_attn(cm)
    std_epilogue(cm)


def sl_L1_ffn(cm):
    std_prologue(cm)
    emit_L1_ffn(cm)
    std_epilogue(cm)


def sl_L2_ffn(cm):
    std_prologue(cm)
    emit_L2_ffn(cm)
    std_epilogue(cm)


def sl_L3_ffn(cm):
    std_prologue(cm)
    emit_L3_ffn(cm)
    std_epilogue(cm)


def sl_L0_ffn(cm):
    std_prologue(cm)
    emit_L0_ffn(cm)
    std_epilogue(cm)


def shard_x(x):
    out = []
    for core in range(8):
        b, h = core // 2, core % 2
        xc = x[b].reshape(32, 128, D)[h::2]
        out.append(np.ascontiguousarray(xc.transpose(2, 0, 1).reshape(D, T)))
    return out


def unshard_x(parts):
    x = np.zeros((B, S, D), dtype=np.float32)
    for core in range(8):
        b, h = core // 2, core % 2
        xc = parts[core].reshape(D, NT, 128).transpose(1, 2, 0)
        x[b].reshape(32, 128, D)[h::2] = xc
    return x


def derive_inputs(inp):
    d = {}
    d["route_scale_col"] = np.ascontiguousarray(
        np.broadcast_to(np.asarray(inp["L3_ffn_route_scale"], dtype=np.float32), (128, 1)))
    d["L1_ffn_WgT_host"] = np.ascontiguousarray(np.asarray(inp["L1_ffn_Wg"], dtype=np.float32).T)
    d["L1_ffn_BT_host"] = np.ascontiguousarray(
        np.asarray(inp["L1_ffn_B"], dtype=np.float32).transpose(0, 2, 1))
    d["L1_ffn_AT_host"] = np.ascontiguousarray(
        np.asarray(inp["L1_ffn_A"], dtype=np.float32).transpose(2, 0, 1).reshape(D, 16))
    d["L1_attn_W_inT_host"] = np.ascontiguousarray(np.asarray(inp["L1_attn_W_in"], dtype=np.float32).T)
    d["conv_b_col"] = np.ascontiguousarray(np.asarray(inp["L1_attn_conv_b"], dtype=np.float32).reshape(16, 128).T)
    d["D_col"] = np.ascontiguousarray(np.asarray(inp["L1_attn_D"], dtype=np.float32).reshape(16, 128).T)
    d["L1_attn_W_xprojT_host"] = np.ascontiguousarray(np.asarray(inp["L1_attn_W_xproj"], dtype=np.float32).T)
    d["L1_attn_W_outT_host"] = np.ascontiguousarray(np.asarray(inp["L1_attn_W_out"], dtype=np.float32).T)
    d["qkv_b_row"] = np.ascontiguousarray(np.asarray(inp["L0_attn_qkv_b"], dtype=np.float32).reshape(1, 3 * D))
    d["n1w_col"] = np.ascontiguousarray(np.asarray(inp["L0_attn_norm1_w"], dtype=np.float32).reshape(CH, 128).T)
    d["n2w_col"] = np.ascontiguousarray(np.asarray(inp["L0_attn_norm2_w"], dtype=np.float32).reshape(CH, 128).T)
    wqkv = np.asarray(inp["L2_attn_w_qkv"], dtype=np.float32)
    d["L2_wqT_host"] = np.ascontiguousarray(wqkv[0:D].T)
    d["L2_wkT_host"] = np.ascontiguousarray(wqkv[D:2 * D].T)
    d["L2_wvT_host"] = np.ascontiguousarray(wqkv[2 * D:3 * D].T)
    d["L2_woT_host"] = np.ascontiguousarray(np.asarray(inp["L2_attn_w_o"], dtype=np.float32).T)
    d["L2_wiqT_host"] = np.ascontiguousarray(np.asarray(inp["L2_attn_w_iq"], dtype=np.float32).T)
    d["L2_wikT_host"] = np.ascontiguousarray(np.asarray(inp["L2_attn_w_ik"], dtype=np.float32).T)
    d["L2_wiwT_host"] = np.ascontiguousarray(np.asarray(inp["L2_attn_w_iw"], dtype=np.float32).T)
    bqkv = np.asarray(inp["L2_attn_b_qkv"], dtype=np.float32)
    d["b_q_row"] = np.ascontiguousarray(bqkv[0:D].reshape(1, D))
    d["b_k_row"] = np.ascontiguousarray(bqkv[D:2 * D].reshape(1, D))
    d["b_v_row"] = np.ascontiguousarray(bqkv[2 * D:3 * D].reshape(1, D))
    d["b_o_row"] = np.ascontiguousarray(np.asarray(inp["L2_attn_b_o"], dtype=np.float32).reshape(1, D))
    d["b_iq_row"] = np.ascontiguousarray(np.asarray(inp["L2_attn_b_iq"], dtype=np.float32).reshape(1, 256))
    d["b_ik_row"] = np.ascontiguousarray(np.asarray(inp["L2_attn_b_ik"], dtype=np.float32).reshape(1, 64))
    d["b_iw_row"] = np.ascontiguousarray(np.asarray(inp["L2_attn_b_iw"], dtype=np.float32).reshape(1, 4))
    for w in ("Wq", "Wk", "Wv", "Wo"):
        d[f"L3_attn_{w}T_host"] = np.ascontiguousarray(
            np.asarray(inp[f"L3_attn_{w}"], dtype=np.float32).T)
    d["L1_ffn_WiT_host"] = np.ascontiguousarray(np.asarray(inp["L1_ffn_Wi"], dtype=np.float32).T)
    d["L1_ffn_WoT_host"] = np.ascontiguousarray(np.asarray(inp["L1_ffn_Wo"], dtype=np.float32).T)
    return d


BF16_INPUTS = {"c_l3cos", "c_l3sin", "c_l3m_a4", "c_l3m_b4",
               "c_l2cos", "c_l2sin", "c_l0cs", "c_l0tri"}

SUBLAYER_EMITTERS = {
    "L0_ffn": sl_L0_ffn,
    "L3_ffn": sl_L3_ffn,
    "L2_ffn": sl_L2_ffn,
    "L1_ffn": sl_L1_ffn,
    "L1_attn": sl_L1_attn,
    "L3_attn": sl_L3_attn,
    "L2_attn": sl_L2_attn,
    "L0_attn": sl_L0_attn,
}

SUBLAYER_WEIGHTS = {
    "L0_attn": {
        "L0_attn_qkv_w": (D, 3 * D), "qkv_b_row": (1, 3 * D),
        "L0_attn_out_w": (D, D), "L0_attn_out_b": (D,),
        "n1w_col": (128, CH), "n2w_col": (128, CH),
        "L0_attn_mlp_w1": (D, 4096), "L0_attn_mlp_b1": (4096,),
        "L0_attn_mlp_w2": (4096, D), "L0_attn_mlp_b2": (D,),
        "c_l0cs": (128, T), "c_l0tri": (128, 128),
        "c_mW0": (128, 1), "c_mW1": (128, 1),
    },
    "L2_attn": {
        "L2_wqT_host": (D, D), "L2_wkT_host": (D, D), "L2_wvT_host": (D, D),
        "L2_woT_host": (D, D), "L2_wiqT_host": (D, 256), "L2_wikT_host": (D, 64),
        "L2_wiwT_host": (D, 4),
        "b_q_row": (1, D), "b_k_row": (1, D), "b_v_row": (1, D), "b_o_row": (1, D),
        "b_iq_row": (1, 256), "b_ik_row": (1, 64), "b_iw_row": (1, 4),
        "c_l2cos": (128, 4 * T), "c_l2sin": (128, 4 * T),
    },
    "L3_attn": {
        "L3_attn_WqT_host": (D, D), "L3_attn_bq": (D,),
        "L3_attn_WkT_host": (D, D), "L3_attn_bk": (D,),
        "L3_attn_WvT_host": (D, D), "L3_attn_bv": (D,),
        "L3_attn_WoT_host": (D, D), "L3_attn_bo": (D,),
        "c_l3cos": (128, T), "c_l3sin": (128, T),
        "c_l3m_a4": (128, 512), "c_l3m_b4": (128, 512),
    },
    "L1_attn": {
        "L1_attn_W_inT_host": (D, 4096), "L1_attn_conv_w": (2048, 4),
        "conv_b_col": (128, 16), "D_col": (128, 16),
        "L1_attn_W_xprojT_host": (2048, 2080), "L1_attn_W_outT_host": (2048, D),
        "c_mW0": (128, 1), "c_mW1": (128, 1),
    },
    "L1_ffn": {
        "L1_ffn_WgT_host": (D, 4), "L1_ffn_bg": (4,),
        "L1_ffn_WiT_host": (D, 4096), "L1_ffn_WoT_host": (4096, D),
        "L1_ffn_AT_host": (D, 16), "L1_ffn_BT_host": (4, 4, 4096),
    },
    "L2_ffn": {
        "L2_ffn_gate_w": (D, 8),
        "L2_ffn_ew1": (8, D, 2048), "L2_ffn_ew2": (8, D, 2048), "L2_ffn_ew3": (8, 2048, D),
        "L2_ffn_sw1": (D, 2048), "L2_ffn_sw2": (D, 2048), "L2_ffn_sw3": (2048, D),
    },
    "L3_ffn": {
        "L3_ffn_gate_W": (D, 8),
        "L3_ffn_W1": (8, D, 512), "L3_ffn_b1": (8, 512),
        "L3_ffn_W2": (8, 512, D), "L3_ffn_b2": (8, D),
        "L3_ffn_Ws1": (D, 512), "L3_ffn_bs1": (512,),
        "L3_ffn_Ws2": (512, D), "L3_ffn_bs2": (D,),
        "route_scale_col": (128, 1),
    },
    "L0_ffn": {
        "L0_ffn_gate_w": (D, 8), "L0_ffn_gate_b": (8,),
        "L0_ffn_w1": (8, D, 2048), "L0_ffn_b1": (8, 2048),
        "L0_ffn_w2": (8, 2048, D), "L0_ffn_b2": (8, D),
    },
}


def std_prologue(cm, lnx_first=False):
    p = cm.p
    if lnx_first:
        cm.LNX = p.sb("z2_LNX", [128, CH * T], BF16)
        cm.XS = p.sb("z3_XS", [128, CH * T], F32)
    else:
        cm.XS = p.sb("z0_XS", [128, CH * T], F32)
        cm.LNX = p.sb("z1_LNX", [128, CH * T], BF16)
    cm.load_X(cm.x_in)
    cm.layernorm()


def std_epilogue(cm):
    cm.store_X(cm.x_out)
    cm.p.free_prefix("z")


def build_program(sublayers, weight_names):
    p = Prog()
    ins = {}
    ins["x_in"] = p.dram("x_in", [D, T], F32, kind="ExternalInput")
    x_out = p.dram("x_out", [D, T], F32, kind="ExternalOutput")
    for n, (shape, dt) in const_decls().items():
        ins[n] = p.dram(n, list(shape), dt, kind="ExternalInput")
    for n, shape in weight_names.items():
        dt = BF16 if n in BF16_INPUTS else F32
        ins[n] = p.dram(n, list(shape), dt, kind="ExternalInput")
    cm = Common(p, ins)
    chain = [ins["x_in"]]
    for j in range(len(sublayers) - 1):
        chain.append(p.dram(f"x_mid{j}", [D, T], F32))
    chain.append(x_out)
    for j, sl in enumerate(sublayers):
        cm.x_in = chain[j]
        cm.x_out = chain[j + 1]
        p.relax = sl in RELAXED
        SUBLAYER_EMITTERS[sl](cm)
        p.relax = True
    p.final_wait()
    nc = p.finalize()
    return nc


RELAXED = {"L0_ffn", "L1_ffn", "L2_ffn", "L3_ffn", "L3_attn"}

SUBLAYER_ORDER = ["L0_attn", "L0_ffn", "L1_attn", "L1_ffn",
                  "L2_attn", "L2_ffn", "L3_attn", "L3_ffn"]

LAUNCH_PLAN = [SUBLAYER_ORDER]


def _core_in_maps(groups_weights, inp, der, xs_parts):
    allowed = set(const_decls()) | set(groups_weights) | {"x_in"}
    in_maps = []
    for core in range(8):
        m = dict(host_constants(core % 2))
        m["x_in"] = np.ascontiguousarray(xs_parts[core], dtype=np.float32)
        for n in groups_weights:
            if n in der:
                m[n] = der[n]
            elif n in m:
                pass
            else:
                m[n] = np.ascontiguousarray(np.asarray(inp[n], dtype=np.float32))
        m = {k: v for k, v in m.items() if k in allowed}
        in_maps.append(m)
    return in_maps


def kernel(**inputs):
    inp = inputs
    der = derive_inputs(inp)
    x = np.ascontiguousarray(np.asarray(inp["L0_attn_x"], dtype=np.float32))
    xs_parts = shard_x(x)
    for group in LAUNCH_PLAN:
        wn = {}
        for s in group:
            wn.update(SUBLAYER_WEIGHTS[s])
        nc = build_program(group, wn)
        in_maps = _core_in_maps(wn, inp, der, xs_parts)
        res = run_bass_kernel_spmd(nc, in_maps, core_ids=list(range(8)))
        xs_parts = [np.asarray(res.results[c]["x_out"]) for c in range(8)]
    return unshard_x(xs_parts)


# version 20 (final)
# speedup vs baseline: 1.2052x; 1.0407x over previous
import numpy as np
import ml_dtypes
import concourse.bass as bass
import concourse.mybir as mybir
from concourse.bass_utils import run_bass_kernel_spmd

F32 = mybir.dt.float32
BF16 = mybir.dt.bfloat16
FP16 = mybir.dt.float16
U8 = mybir.dt.uint8
AF = mybir.ActivationFunctionType
ALU = mybir.AluOpType
AXX = mybir.AxisListType.X

D = 1024
T = 2048
NT = 16
S = 4096
B = 4
CH = 8
ENGINES = ("tensor", "vector", "scalar", "gpsimd", "sync")

BF = ml_dtypes.bfloat16
DBG = {"l3_stop": None}


class Prog:
    def __init__(self):
        self.nc = bass.Bass("TRN2", target_bir_lowering=False)
        self.ops = {e: [] for e in ENGINES}
        self._ctx = []
        self.sems = {}
        self.cnt = {}
        self._phidx = 0
        self._cur = "ph0"
        self._pending = {e: [] for e in ENGINES}
        self.sem("ph0")
        self.pf = [self._ps(f"pbF{i}", [128, 512], F32) for i in range(6)]
        self.pb = [self._ps(f"pbB{i}", [128, 1024], BF16) for i in range(2)]

    def sem(self, name):
        if name not in self.sems:
            g = self.nc.semaphore(name)
            self.sems[name] = g.__enter__()
            self._ctx.append(g)
            self.cnt[name] = 0
        return name

    def sb(self, name, shape, dt):
        self._uid = getattr(self, "_uid", 0) + 1
        uname = f"s_{name}__{self._uid}"
        g = self.nc.sbuf_tensor(uname, shape, dt)
        h = g.__enter__()
        self._guards = getattr(self, "_guards", {})
        self._guards[uname] = g
        return h

    def free_prefix(self, pref):
        gd = getattr(self, "_guards", {})
        for n in reversed(list(gd)):
            if n.startswith("s_" + pref):
                gd.pop(n).__exit__(None, None, None)

    def _ps(self, name, shape, dt):
        g = self.nc.psum_tensor(name, shape, dt)
        h = g.__enter__()
        self._ctx.append(g)
        return h

    def dram(self, name, shape, dt, kind="Internal"):
        return self.nc.dram_tensor(name, shape, dt, kind=kind).ap()

    SEM_LIMIT = 48000

    def _rotate(self):
        old, final = self._cur, self.cnt[self._cur]
        self._phidx += 1
        self._cur = f"ph{self._phidx}"
        self.sem(self._cur)
        for e in ENGINES:
            self._pending[e].append((old, final))

    def mark(self):
        return (self._cur, self.cnt[self._cur])

    relax = True

    def op(self, eng, fn, dma=False, wait=None, inc=True):
        if not self.relax:
            wait = None
            inc = True
        amt = 16 if dma else 1
        if inc and self.cnt[self._cur] + amt > self.SEM_LIMIT:
            self._rotate()
        cur = self._cur
        if wait is None:
            wait_t = [(cur, self.cnt[cur])]
        elif wait is False:
            wait_t = []
        elif isinstance(wait, list):
            wait_t = [w for w in wait if w is not None]
            if len(wait_t) != len(wait):
                wait_t = [(cur, self.cnt[cur])]
        else:
            wait_t = [wait]
        if inc:
            inc_t = (cur, amt)
            self.cnt[cur] += amt
        else:
            inc_t = None
        pend = tuple(self._pending[eng])
        self._pending[eng] = []

        wm = {}
        for s, v in wait_t:
            wm[s] = max(wm.get(s, 0), v)
        wait_t = tuple(wm.items())

        def run(e, fn=fn, wait_t=wait_t, inc_t=inc_t, pend=pend):
            for s, v in pend:
                e.wait_ge(self.sems[s], v)
            for s, v in wait_t:
                e.wait_ge(self.sems[s], v)
            ins = fn(e)
            if inc_t is not None:
                ins.then_inc(self.sems[inc_t[0]], inc_t[1])
        self.ops[eng].append(run)

    def op_collective(self, fn):
        self.sem("cc")
        if self.cnt[self._cur] + 1 > self.SEM_LIMIT:
            self._rotate()
        cur = self._cur
        wait_val = self.cnt[cur]
        self.cnt["cc"] += 1
        cc_target = self.cnt["cc"]
        self.cnt[cur] += 1
        pend = tuple(self._pending["gpsimd"])
        self._pending["gpsimd"] = []

        def run(e, fn=fn, cur=cur, wait_val=wait_val, cc_target=cc_target, pend=pend):
            for s, v in pend:
                e.wait_ge(self.sems[s], v)
            e.wait_ge(self.sems[cur], wait_val)
            fn(e).then_inc(self.sems["cc"])
            e.wait_ge(self.sems["cc"], cc_target)
            e.engine_nop().then_inc(self.sems[cur], 1)
        self.ops["gpsimd"].append(run)

    def raw(self, eng, fn):
        self.ops[eng].append(fn)

    def final_wait(self):
        cur = self._cur
        v = self.cnt[cur]
        pend = tuple(self._pending["sync"])
        self._pending["sync"] = []

        def run(e, cur=cur, v=v, pend=pend):
            for s, vv in pend:
                e.wait_ge(self.sems[s], vv)
            e.wait_ge(self.sems[cur], v)
        self.ops["sync"].append(run)

    def finalize(self):
        blk = self.nc.Block()
        block = blk.__enter__()
        for eng in ENGINES:
            fns = self.ops[eng]
            if not fns:
                continue
            getattr(block, eng)(lambda e, fns=fns: [f(e) for f in fns])
        blk.__exit__(None, None, None)
        return self.nc


def host_constants(h):
    c = {}
    c["c_mW0"] = np.full((128, 1), float(h), dtype=np.float32)
    gp = ((np.arange(T) // 128) * 2 + h) * 128 + (np.arange(T) % 128)
    theta = 10000.0 ** (-2.0 * np.arange(32, dtype=np.float64) / 64.0)
    ang = (gp[None, :] + 1.0) * theta[(np.arange(128) % 32)][:, None]
    c["c_l3cos"] = np.cos(ang).astype(BF)
    c["c_l3sin"] = np.sin(ang).astype(BF)
    half = 512
    inv_freq = 1.0 / 10000.0 ** (np.arange(half, dtype=np.float64) / half)
    ang2 = gp[None, :] * inv_freq[:, None]
    cos2 = np.cos(ang2); sin2 = np.sin(ang2)
    tri0 = (np.arange(128)[:, None] <= np.arange(128)[None, :]).astype(np.float32)
    c["c_l0tri"] = tri0.astype(BF)
    angc = (np.pi / 2.0) * gp.astype(np.float64) / 4096.0
    cs = np.zeros((128, T), np.float32)
    cs[0:64, :] = np.cos(angc)[None, :]
    cs[64:128, :] = np.sin(angc)[None, :]
    c["c_l0cs"] = cs.astype(BF)
    c["c_l2cos"] = np.ascontiguousarray(
        cos2.reshape(4, 128, T).transpose(1, 0, 2).reshape(128, 4 * T)).astype(BF)
    c["c_l2sin"] = np.ascontiguousarray(
        sin2.reshape(4, 128, T).transpose(1, 0, 2).reshape(128, 4 * T)).astype(BF)
    tri = (np.arange(128)[:, None] <= np.arange(128)[None, :]).astype(np.float32)
    if h == 0:
        ma, mb = tri, np.zeros((128, 128), np.float32)
    else:
        ma, mb = np.ones((128, 128), np.float32), tri
    c["c_l3m_a4"] = np.tile(ma, (1, 4)).astype(BF)
    c["c_l3m_b4"] = np.tile(mb, (1, 4)).astype(BF)
    c["c_mW1"] = np.full((128, 1), float(1 - h), dtype=np.float32)
    gpos = (np.arange(T) // 128 * 2 + h) * 128 + (np.arange(T) % 128)
    c["c_ident_bf"] = np.eye(128, dtype=np.float32).astype(BF)
    c["c_ident_f"] = np.eye(128, dtype=np.float32)
    c["c_ones_col_f"] = np.ones((128, 1), dtype=np.float32)
    c["c_ones_col_bf"] = np.ones((128, 1), dtype=np.float32).astype(BF)
    c["c_ones_row_f"] = np.ones((1, 128), dtype=np.float32)
    c["c_ones_row_bf"] = np.ones((1, 128), dtype=np.float32).astype(BF)
    c["c_gpos"] = gpos.astype(np.float32).reshape(1, T)
    c["c_ones512_f"] = np.ones((1, 512), dtype=np.float32)
    ho = np.zeros((128, 2), dtype=np.float32)
    ho[0:64, 0] = 1.0
    ho[64:128, 1] = 1.0
    c["c_half_ones"] = ho
    return c


def const_decls():
    return {
        "c_ident_bf": ([128, 128], BF16),
        "c_ident_f": ([128, 128], F32),
        "c_ones_col_f": ([128, 1], F32),
        "c_ones_col_bf": ([128, 1], BF16),
        "c_ones_row_f": ([1, 128], F32),
        "c_ones_row_bf": ([1, 128], BF16),
        "c_gpos": ([1, T], F32),
        "c_ones512_f": ([1, 512], F32),
        "c_half_ones": ([128, 2], F32),
    }


class _LazyConsts:
    def __init__(self, cm):
        self.cm = cm
        self.d = {}

    def __getitem__(self, n):
        if n not in self.d:
            self.d[n] = self.cm._load_const(n)
        return self.d[n]


class Common:

    def __init__(self, p: Prog, ins):
        self.p = p
        self.ins = ins
        self.XS = None
        self.LNX = None
        self.x_in = None
        self.x_out = None
        self.SC1 = p.sb("SC1", [128, 512], F32)
        self.SC2 = p.sb("SC2", [128, 512], F32)
        self.SROW = p.sb("SROW", [1, 1024], F32)
        self._const_decls = const_decls()
        self.consts = _LazyConsts(self)
        for n in self._const_decls:
            if n != "c_gpos":
                self.consts[n]

    def _load_const(self, n):
        shape, dt = self._const_decls[n]
        t = self.p.sb(n, shape, dt)
        self.p.op("sync", lambda e, t=t, n=n: e.dma_start(out=t[:], in_=self.ins[n][:, :]), dma=True)
        return t

    def xs(self, c, t0=0, n=T):
        return self.XS[:, c * T + t0: c * T + t0 + n]

    def lnx(self, c, t0=0, n=T):
        return self.LNX[:, c * T + t0: c * T + t0 + n]

    def load_X(self, xdram):
        p = self.p
        XSB = self.XS
        for c in range(CH):
            p.op("sync", lambda e, c=c, XSB=XSB: e.dma_start(
                out=XSB[:, c * T:(c + 1) * T],
                in_=xdram[c * 128:(c + 1) * 128, :]), dma=True)

    def store_X(self, xdram):
        p = self.p
        XSB = self.XS
        for c in range(CH):
            p.op("sync", lambda e, c=c, XSB=XSB: e.dma_start(
                out=xdram[c * 128:(c + 1) * 128, :],
                in_=XSB[:, c * T:(c + 1) * T]), dma=True)

    def layernorm(self, rows=None, eps=1e-5, rms=False, src=None, fp32_dst=None):
        p = self.p
        C = self.consts
        ones_col_f = C["c_ones_col_f"]
        ones_row_f = C["c_ones_row_f"]
        XSB = self.XS
        LNXB = self.LNX
        if src is None:
            src = lambda c, t0, n, XSB=XSB: XSB[:, c * T + t0: c * T + t0 + n]
        psum = p.pf[4][0:1, :]
        psumsq = p.pf[5][0:1, :]
        pb_a = p.pf[2]
        pb_b = p.pf[3]
        for tb in range(4):
            t0 = tb * 512
            for c in range(CH):
                x = src(c, t0, 512)
                p.op("scalar", lambda e, x=x: e.activation(self.SC1[:], x, AF.Square))
                if not rms:
                    p.op("tensor", lambda e, x=x, st=(c == 0): e.matmul(
                        psum, ones_col_f[:], x, start=st, stop=(c == CH - 1)))
                p.op("tensor", lambda e, st=(c == 0): e.matmul(
                    psumsq, ones_col_f[:], self.SC1[:], start=st, stop=(c == CH - 1)))
            if rows is not None:
                rr = rows[0][:, t0:t0 + 512]
                rn = rows[1][:, t0:t0 + 512]
            else:
                rr = self.SROW[:, 0:512]
                rn = self.SROW[:, 512:1024]
            if not rms:
                p.op("vector", lambda e, rn=rn: e.tensor_scalar_mul(rn, psum, 1.0 / D))
                p.op("vector", lambda e, rr=rr: e.tensor_scalar_mul(rr, psumsq, 1.0 / D))
                p.op("vector", lambda e, rn=rn: e.scalar_tensor_tensor(
                    self.SC1[0:1, :], rn, 1.0, rn, ALU.mult, ALU.mult))
                p.op("vector", lambda e, rr=rr: e.tensor_sub(rr, rr, self.SC1[0:1, :]))
            else:
                p.op("vector", lambda e, rr=rr: e.tensor_scalar_mul(rr, psumsq, 1.0 / D))
            p.op("vector", lambda e, rr=rr, eps=eps: e.tensor_scalar_add(rr, rr, float(eps)))
            p.op("scalar", lambda e, rr=rr: e.activation(rr, rr, AF.Sqrt))
            p.op("vector", lambda e, rr=rr: e.reciprocal(rr, rr))
            if not rms:
                p.op("vector", lambda e, rr=rr, rn=rn: e.scalar_tensor_tensor(
                    rn, rn, -1.0, rr, ALU.mult, ALU.mult))
            p.op("tensor", lambda e, rr=rr: e.matmul(pb_a[:], ones_row_f[:], rr, start=True, stop=True))
            if not rms:
                p.op("tensor", lambda e, rn=rn: e.matmul(pb_b[:], ones_row_f[:], rn, start=True, stop=True))
            for c in range(CH):
                x = src(c, t0, 512)
                dst = LNXB[:, c * T + t0: c * T + t0 + 512]
                p.op("vector", lambda e, x=x: e.tensor_mul(self.SC2[:], x, pb_a[:]))
                if not rms:
                    p.op("vector", lambda e, dst=dst: e.tensor_add(dst, self.SC2[:], pb_b[:]))
                    if fp32_dst is not None:
                        hd = fp32_dst[:, c * T + t0: c * T + t0 + 512]
                        p.op("vector", lambda e, hd=hd: e.tensor_add(hd, self.SC2[:], pb_b[:]))
                else:
                    p.op("vector", lambda e, dst=dst: e.tensor_copy(out=dst, in_=self.SC2[:]))


def emit_L0_ffn(cm: Common, pref="l0f"):
    p, C, ins = cm.p, cm.consts, cm.ins
    _LNXB = cm.LNX
    _XSB = cm.XS
    def lnxs(c, t0=0, n=T, _b=_LNXB):
        return _b[:, c * T + t0: c * T + t0 + n]
    def xss(c, t0=0, n=T, _b=_XSB):
        return _b[:, c * T + t0: c * T + t0 + n]
    E, F = 8, 2048
    gw = p.sb(pref + "gw", [128, CH * 8], BF16)
    gb = p.sb(pref + "gb", [1, 8], BF16)
    WgT = p.sb(pref + "WgT", [128, NT * 8], F32)
    wrow = p.sb(pref + "wrow", [1, 512], F32)
    w1 = p.sb(pref + "w1", [128, CH * F], BF16)
    w2 = p.sb(pref + "w2", [128, 8 * D], BF16)
    b1 = p.sb(pref + "b1", [1, F], F32)
    b2 = p.sb(pref + "b2", [1, D], F32)
    hb = p.sb(pref + "hb", [128, (F // 128) * 512], BF16)
    wbb = p.sb(pref + "wb", [128, 512], BF16)
    g8 = p.sb(pref + "g8", [128, 8], F32)
    m1 = p.sb(pref + "m1", [128, 1], F32)
    nm1 = p.sb(pref + "nm1", [128, 1], F32)
    m2 = p.sb(pref + "m2", [128, 1], F32)
    sel = p.sb(pref + "sel", [128, 8], F32)
    selm = p.sb(pref + "selm", [128, 8], U8)
    neg = p.sb(pref + "neg", [128, 8], F32)
    den = p.sb(pref + "den", [128, 1], F32)

    p.op("gpsimd", lambda e: e.dma_start(
        out=gw[:].rearrange("p (c e) -> p c e", e=8),
        in_=ins["L0_ffn_gate_w"].rearrange("(c p) e -> p c e", p=128)), dma=True)
    p.op("gpsimd", lambda e: e.dma_start(out=gb[:], in_=ins["L0_ffn_gate_b"].rearrange(
        "(u e) -> u e", u=1)), dma=True)
    ones_row_bf = C["c_ones_row_bf"]
    ones_row_f = C["c_ones_row_f"]
    ones512_f = C["c_ones512_f"]
    ident_f = C["c_ident_f"]
    p.op("vector", lambda e: e.memset(neg[:], -1e30))

    ps = p.pf[0][:, 0:8]
    pr = p.pf[1][0:1, 0:128]
    for i in range(NT):
        t0 = i * 128
        p.op("tensor", lambda e: e.matmul(ps, ones_row_bf[:], gb[:],
                                          start=True, stop=False))
        for c in range(CH):
            p.op("tensor", lambda e, c=c, t0=t0: e.matmul(
                ps, lnxs(c, t0, 128), gw[:, c * 8:(c + 1) * 8],
                start=False, stop=(c == CH - 1)))
        p.op("vector", lambda e: e.tensor_copy(out=g8[:], in_=ps))
        p.op("vector", lambda e: e.reduce_max(m1[:], g8[:], axis=AXX))
        p.op("vector", lambda e: e.tensor_scalar_mul(nm1[:], m1[:], -1.0))
        p.op("vector", lambda e: e.tensor_scalar(selm[:], g8[:], m1[:, 0:1], None, ALU.is_lt))
        p.op("vector", lambda e: e.select(sel[:], selm[:], g8[:], neg[:]))
        p.op("vector", lambda e: e.reduce_max(m2[:], sel[:], axis=AXX))
        wg = WgT[:, i * 8:(i + 1) * 8]
        p.op("scalar", lambda e, wg=wg: e.activation(wg, g8[:], AF.Exp, bias=nm1[:, 0:1]))
        p.op("vector", lambda e: e.tensor_scalar(sel[:], g8[:], m2[:, 0:1], None, ALU.is_ge))
        p.op("vector", lambda e, wg=wg: e.tensor_mul(wg, wg, sel[:]))
        p.op("vector", lambda e: e.tensor_sub(den[:], m2[:], m1[:]))
        p.op("scalar", lambda e: e.activation(den[:], den[:], AF.Exp))
        p.op("vector", lambda e: e.tensor_scalar_add(den[:], den[:], 1.0))
        p.op("vector", lambda e: e.reciprocal(den[:], den[:]))
        p.op("vector", lambda e, wg=wg: e.tensor_scalar(wg, wg, den[:, 0:1], None, ALU.mult))

    w1r = ins["L0_ffn_w1"].rearrange("e (c p) f -> e p c f", p=128)
    w2r = ins["L0_ffn_w2"].rearrange("e (c p) f -> e p c f", p=128)
    b1r = ins["L0_ffn_b1"].rearrange("e (u f) -> e u f", u=1)
    NF = F // 128
    b2r = ins["L0_ffn_b2"].rearrange("e (u d) -> e u d", u=1)
    bankm = {}
    for ee in range(E):
        p.op("gpsimd", lambda e, ee=ee: e.dma_start(
            out=w1[:].rearrange("p (c f) -> p c f", f=F), in_=w1r[ee]), dma=True)
        p.op("sync", lambda e, ee=ee: e.dma_start(out=b1[:], in_=b1r[ee]), dma=True)
        p.op("sync", lambda e, ee=ee: e.dma_start(out=b2[:], in_=b2r[ee]), dma=True)
        for tb in range(4):
            t0 = tb * 512
            for j in range(4):
                i = tb * 4 + j
                p.op("tensor", lambda e, i=i, ee=ee: e.matmul(
                    pr, WgT[:, i * 8 + ee: i * 8 + ee + 1], ident_f[:],
                    start=True, stop=True))
                p.op("vector", lambda e, j=j: e.tensor_copy(
                    out=wrow[:, j * 128:(j + 1) * 128], in_=pr))
            pwb = p.pf[2]
            p.op("tensor", lambda e: e.matmul(
                pwb[:], ones_row_f[:], wrow[:], start=True, stop=True))
            p.op("vector", lambda e: e.tensor_copy(out=wbb[:], in_=pwb[:]))
            for bb in range(4):
                bankm[bb] = p.mark()
            for oc in range(NF):
                ph = p.pf[oc % 2]
                p.op("tensor", lambda e, ph=ph, oc=oc: e.matmul(
                    ph[:], b1[:, oc * 128:(oc + 1) * 128], ones512_f[:],
                    start=True, stop=False),
                    wait=(None if oc < 2 else bankm.get(oc % 2)), inc=False)
                for c in range(CH):
                    p.op("tensor", lambda e, ph=ph, oc=oc, c=c, t0=t0: e.matmul(
                        ph[:], w1[:, c * F + oc * 128: c * F + oc * 128 + 128],
                        lnxs(c, t0, 512), start=False, stop=(c == CH - 1)),
                        wait=False, inc=(c == CH - 1))
                p.op("scalar", lambda e, ph=ph, oc=oc: e.activation(
                    cm.SC1[:], ph[:], AF.Gelu))
                p.op("vector", lambda e, oc=oc: e.tensor_mul(
                    hb[:, oc * 512:(oc + 1) * 512], cm.SC1[:], wbb[:]))
                bankm[oc % 2] = p.mark()
            for half in range(2):
                p.op("gpsimd", lambda e, ee=ee, half=half: e.dma_start(
                    out=w2[:].rearrange("p (c f) -> p c f", f=D),
                    in_=w2r[ee][:, half * 8:(half + 1) * 8, :]), dma=True)
                for oc in range(CH):
                    py = p.pf[2 + (oc % 2)]
                    first = True
                    if half == 0:
                        p.op("tensor", lambda e, py=py, oc=oc: e.matmul(
                            py[:], b2[:, oc * 128:(oc + 1) * 128],
                            wrow[:], start=True, stop=False),
                            wait=(None if oc < 2 else bankm.get(2 + oc % 2)), inc=False)
                        first = False
                    for fc in range(8):
                        p.op("tensor", lambda e, py=py, oc=oc, fc=fc, half=half: e.matmul(
                            py[:], w2[:, fc * D + oc * 128: fc * D + (oc + 1) * 128],
                            hb[:, (half * 8 + fc) * 512:(half * 8 + fc + 1) * 512],
                            start=(half == 1 and fc == 0), stop=(fc == 7)),
                            wait=((None if oc < 2 else bankm.get(2 + oc % 2)) if (first and fc == 0) else False),
                            inc=(fc == 7))
                    x = xss(oc, t0, 512)
                    p.op("vector", lambda e, py=py, x=x: e.tensor_add(x, x, py[:]))
                    bankm[2 + oc % 2] = p.mark()
    p.free_prefix(pref)


def emit_L3_ffn(cm: Common, pref="l3f"):
    p, C, ins = cm.p, cm.consts, cm.ins
    _LNXB = cm.LNX
    _XSB = cm.XS
    def lnxs(c, t0=0, n=T, _b=_LNXB):
        return _b[:, c * T + t0: c * T + t0 + n]
    def xss(c, t0=0, n=T, _b=_XSB):
        return _b[:, c * T + t0: c * T + t0 + n]
    E, H = 8, 512
    NF = H // 128
    gw = p.sb(pref + "gw", [128, CH * 8], BF16)
    rs = p.sb(pref + "rs", [128, 1], F32)
    WgT = p.sb(pref + "WgT", [128, NT * 8], F32)
    wrow = p.sb(pref + "wrow", [1, 512], F32)
    w1 = p.sb(pref + "w1", [128, CH * H], BF16)
    w2 = p.sb(pref + "w2", [128, NF * D], BF16)
    b1 = p.sb(pref + "b1", [1, H], F32)
    b2 = p.sb(pref + "b2", [1, D], F32)
    hb = p.sb(pref + "hb", [128, NF * 512], BF16)
    wbb = p.sb(pref + "wb", [128, 512], BF16)
    g8 = p.sb(pref + "g8", [128, 8], F32)
    m1 = p.sb(pref + "m1", [128, 1], F32)
    m2 = p.sb(pref + "m2", [128, 1], F32)
    sel = p.sb(pref + "sel", [128, 8], F32)
    selm = p.sb(pref + "selm", [128, 8], U8)
    neg = p.sb(pref + "neg", [128, 8], F32)
    den = p.sb(pref + "den", [128, 1], F32)
    ones_row_f = C["c_ones_row_f"]
    ones512_f = C["c_ones512_f"]
    ident_f = C["c_ident_f"]

    p.op("gpsimd", lambda e: e.dma_start(
        out=gw[:].rearrange("p (c e) -> p c e", e=8),
        in_=ins["L3_ffn_gate_W"].rearrange("(c p) e -> p c e", p=128)), dma=True)
    p.op("sync", lambda e: e.dma_start(out=rs[:], in_=ins["route_scale_col"][:, :]), dma=True)
    p.op("vector", lambda e: e.memset(neg[:], -1e30))

    ps = p.pf[0][:, 0:8]
    pr = p.pf[1][0:1, 0:128]
    for i in range(NT):
        t0 = i * 128
        for c in range(CH):
            p.op("tensor", lambda e, c=c, t0=t0: e.matmul(
                ps, lnxs(c, t0, 128), gw[:, c * 8:(c + 1) * 8],
                start=(c == 0), stop=(c == CH - 1)))
        p.op("scalar", lambda e: e.activation(g8[:], ps, AF.Sigmoid))
        p.op("vector", lambda e: e.tensor_scalar(g8[:], g8[:], rs[:, 0:1], None, ALU.mult))
        p.op("vector", lambda e: e.reduce_max(m1[:], g8[:], axis=AXX))
        p.op("vector", lambda e: e.tensor_scalar(selm[:], g8[:], m1[:, 0:1], None, ALU.is_lt))
        p.op("vector", lambda e: e.select(sel[:], selm[:], g8[:], neg[:]))
        p.op("vector", lambda e: e.reduce_max(m2[:], sel[:], axis=AXX))
        wg = WgT[:, i * 8:(i + 1) * 8]
        p.op("vector", lambda e: e.tensor_add(den[:], m1[:], m2[:]))
        p.op("vector", lambda e: e.reciprocal(den[:], den[:]))
        p.op("vector", lambda e, wg=wg: e.tensor_scalar(sel[:], g8[:], m2[:, 0:1], None, ALU.is_ge))
        p.op("vector", lambda e, wg=wg: e.tensor_mul(wg, g8[:], sel[:]))
        p.op("vector", lambda e, wg=wg: e.tensor_scalar(wg, wg, den[:, 0:1], None, ALU.mult))

    w1r = ins["L3_ffn_W1"].rearrange("e (c p) f -> e p c f", p=128)
    w2r = ins["L3_ffn_W2"].rearrange("e (c p) f -> e p c f", p=128)
    b1r = ins["L3_ffn_b1"].rearrange("e (u f) -> e u f", u=1)
    b2r = ins["L3_ffn_b2"].rearrange("e (u f) -> e u f", u=1)
    ws1r = ins["L3_ffn_Ws1"].rearrange("(c p) f -> p c f", p=128)
    ws2r = ins["L3_ffn_Ws2"].rearrange("(c p) f -> p c f", p=128)
    bankm = {}
    for ee in range(E + 1):
        shared = ee == E
        if shared:
            p.op("gpsimd", lambda e: e.dma_start(
                out=w1[:].rearrange("p (c f) -> p c f", f=H), in_=ws1r), dma=True)
            p.op("gpsimd", lambda e: e.dma_start(
                out=w2[:].rearrange("p (c f) -> p c f", f=D), in_=ws2r), dma=True)
            p.op("sync", lambda e: e.dma_start(
                out=b1[:], in_=ins["L3_ffn_bs1"].rearrange("(u f) -> u f", u=1)), dma=True)
            p.op("sync", lambda e: e.dma_start(
                out=b2[:], in_=ins["L3_ffn_bs2"].rearrange("(u f) -> u f", u=1)), dma=True)
        else:
            p.op("gpsimd", lambda e, ee=ee: e.dma_start(
                out=w1[:].rearrange("p (c f) -> p c f", f=H), in_=w1r[ee]), dma=True)
            p.op("gpsimd", lambda e, ee=ee: e.dma_start(
                out=w2[:].rearrange("p (c f) -> p c f", f=D), in_=w2r[ee]), dma=True)
            p.op("sync", lambda e, ee=ee: e.dma_start(out=b1[:], in_=b1r[ee]), dma=True)
            p.op("sync", lambda e, ee=ee: e.dma_start(out=b2[:], in_=b2r[ee]), dma=True)
        for tb in range(4):
            t0 = tb * 512
            if not shared:
                for j in range(4):
                    i = tb * 4 + j
                    p.op("tensor", lambda e, i=i, ee=ee: e.matmul(
                        pr, WgT[:, i * 8 + ee: i * 8 + ee + 1], ident_f[:],
                        start=True, stop=True))
                    p.op("vector", lambda e, j=j: e.tensor_copy(
                        out=wrow[:, j * 128:(j + 1) * 128], in_=pr))
                pwb = p.pf[2]
                p.op("tensor", lambda e: e.matmul(
                    pwb[:], ones_row_f[:], wrow[:], start=True, stop=True))
                p.op("vector", lambda e: e.tensor_copy(out=wbb[:], in_=pwb[:]))
                for bb in range(4):
                    bankm[bb] = p.mark()
            for oc in range(NF):
                ph = p.pf[oc % 2]
                p.op("tensor", lambda e, ph=ph, oc=oc: e.matmul(
                    ph[:], b1[:, oc * 128:(oc + 1) * 128], ones512_f[:],
                    start=True, stop=False),
                    wait=(None if oc < 2 else bankm.get(oc % 2)), inc=False)
                for c in range(CH):
                    p.op("tensor", lambda e, ph=ph, oc=oc, c=c, t0=t0: e.matmul(
                        ph[:], w1[:, c * H + oc * 128: c * H + oc * 128 + 128],
                        lnxs(c, t0, 512), start=False, stop=(c == CH - 1)),
                        wait=False, inc=(c == CH - 1))
                p.op("scalar", lambda e, ph=ph: e.activation(cm.SC1[:], ph[:], AF.Gelu))
                if shared:
                    p.op("vector", lambda e, oc=oc: e.tensor_copy(
                        out=hb[:, oc * 512:(oc + 1) * 512], in_=cm.SC1[:]))
                else:
                    p.op("vector", lambda e, oc=oc: e.tensor_mul(
                        hb[:, oc * 512:(oc + 1) * 512], cm.SC1[:], wbb[:]))
                bankm[oc % 2] = p.mark()
            for oc in range(CH):
                py = p.pf[2 + (oc % 2)]
                if shared:
                    p.op("tensor", lambda e, py=py, oc=oc: e.matmul(
                        py[:], b2[:, oc * 128:(oc + 1) * 128], ones512_f[:],
                        start=True, stop=False),
                        wait=(None if oc < 2 else bankm.get(2 + oc % 2)), inc=False)
                else:
                    p.op("tensor", lambda e, py=py, oc=oc: e.matmul(
                        py[:], b2[:, oc * 128:(oc + 1) * 128], wrow[:],
                        start=True, stop=False),
                        wait=(None if oc < 2 else bankm.get(2 + oc % 2)), inc=False)
                for fc in range(NF):
                    p.op("tensor", lambda e, py=py, oc=oc, fc=fc: e.matmul(
                        py[:], w2[:, fc * D + oc * 128: fc * D + (oc + 1) * 128],
                        hb[:, fc * 512:(fc + 1) * 512], start=False, stop=(fc == NF - 1)),
                        wait=False, inc=(fc == NF - 1))
                x = xss(oc, t0, 512)
                p.op("vector", lambda e, py=py, x=x: e.tensor_add(x, x, py[:]))
                bankm[2 + oc % 2] = p.mark()
    p.free_prefix(pref)


def emit_L2_ffn(cm: Common, pref="l2f"):
    p, C, ins = cm.p, cm.consts, cm.ins
    _LNXB = cm.LNX
    _XSB = cm.XS
    def lnxs(c, t0=0, n=T, _b=_LNXB):
        return _b[:, c * T + t0: c * T + t0 + n]
    def xss(c, t0=0, n=T, _b=_XSB):
        return _b[:, c * T + t0: c * T + t0 + n]
    E, H = 8, 2048
    gw = p.sb(pref + "gw", [128, CH * 8], BF16)
    WgT = p.sb(pref + "WgT", [128, NT * 8], F32)
    wrow = p.sb(pref + "wrow", [1, T], F32)
    w1 = p.sb(pref + "w1", [128, CH * (H // 2)], BF16)
    w2 = p.sb(pref + "w2", [128, CH * (H // 2)], BF16)
    w3 = p.sb(pref + "w3", [128, (H // 128) * D], BF16)
    hb = p.sb(pref + "hb", [128, (H // 128) * 512], BF16)
    wbb = p.sb(pref + "wb", [128, T], BF16)
    g8 = p.sb(pref + "g8", [128, 8], F32)
    m1 = p.sb(pref + "m1", [128, 1], F32)
    nm1 = p.sb(pref + "nm1", [128, 1], F32)
    m2 = p.sb(pref + "m2", [128, 1], F32)
    sel = p.sb(pref + "sel", [128, 8], F32)
    selm = p.sb(pref + "selm", [128, 8], U8)
    neg = p.sb(pref + "neg", [128, 8], F32)
    den = p.sb(pref + "den", [128, 1], F32)
    ones_row_f = C["c_ones_row_f"]
    ident_f = C["c_ident_f"]

    p.op("gpsimd", lambda e: e.dma_start(
        out=gw[:].rearrange("p (c e) -> p c e", e=8),
        in_=ins["L2_ffn_gate_w"].rearrange("(c p) e -> p c e", p=128)), dma=True)
    p.op("vector", lambda e: e.memset(neg[:], -1e30))

    ps = p.pf[0][:, 0:8]
    pr = p.pf[1][0:1, 0:128]
    for i in range(NT):
        t0 = i * 128
        for c in range(CH):
            p.op("tensor", lambda e, c=c, t0=t0: e.matmul(
                ps, lnxs(c, t0, 128), gw[:, c * 8:(c + 1) * 8],
                start=(c == 0), stop=(c == CH - 1)))
        p.op("vector", lambda e: e.tensor_copy(out=g8[:], in_=ps))
        p.op("vector", lambda e: e.reduce_max(m1[:], g8[:], axis=AXX))
        p.op("vector", lambda e: e.tensor_scalar_mul(nm1[:], m1[:], -1.0))
        p.op("vector", lambda e: e.tensor_scalar(selm[:], g8[:], m1[:, 0:1], None, ALU.is_lt))
        p.op("vector", lambda e: e.select(sel[:], selm[:], g8[:], neg[:]))
        p.op("vector", lambda e: e.reduce_max(m2[:], sel[:], axis=AXX))
        wg = WgT[:, i * 8:(i + 1) * 8]
        p.op("scalar", lambda e, wg=wg: e.activation(wg, g8[:], AF.Exp, bias=nm1[:, 0:1]))
        p.op("vector", lambda e: e.tensor_scalar(sel[:], g8[:], m2[:, 0:1], None, ALU.is_ge))
        p.op("vector", lambda e, wg=wg: e.tensor_mul(wg, wg, sel[:]))
        p.op("vector", lambda e: e.tensor_sub(den[:], m2[:], m1[:]))
        p.op("scalar", lambda e: e.activation(den[:], den[:], AF.Exp))
        p.op("vector", lambda e: e.tensor_scalar_add(den[:], den[:], 1.0))
        p.op("vector", lambda e: e.reciprocal(den[:], den[:]))
        p.op("vector", lambda e, wg=wg: e.tensor_scalar(wg, wg, den[:, 0:1], None, ALU.mult))

    w1r = ins["L2_ffn_ew1"].rearrange("e (c p) f -> e p c f", p=128)
    w2r = ins["L2_ffn_ew2"].rearrange("e (c p) f -> e p c f", p=128)
    w3r = ins["L2_ffn_ew3"].rearrange("e (c p) f -> e p c f", p=128)
    sw1r = ins["L2_ffn_sw1"].rearrange("(c p) f -> p c f", p=128)
    sw2r = ins["L2_ffn_sw2"].rearrange("(c p) f -> p c f", p=128)
    sw3r = ins["L2_ffn_sw3"].rearrange("(c p) f -> p c f", p=128)
    NF = H // 128
    HH = H // 2
    bankm = {0: None, 1: None, 2: None, 3: None}
    for ee in range(E + 1):
        shared = ee == E
        if shared:
            p.op("gpsimd", lambda e: e.dma_start(
                out=w3[:].rearrange("p (c f) -> p c f", f=D), in_=sw3r), dma=True)
        else:
            p.op("gpsimd", lambda e, ee=ee: e.dma_start(
                out=w3[:].rearrange("p (c f) -> p c f", f=D), in_=w3r[ee]), dma=True)
            for tb in range(4):
                for j in range(4):
                    i = tb * 4 + j
                    p.op("tensor", lambda e, i=i, ee=ee: e.matmul(
                        pr, WgT[:, i * 8 + ee: i * 8 + ee + 1], ident_f[:],
                        start=True, stop=True))
                    p.op("vector", lambda e, i=i: e.tensor_copy(
                        out=wrow[:, i * 128:(i + 1) * 128], in_=pr))
            pwb = p.pf[4]
            for tb in range(4):
                p.op("tensor", lambda e, tb=tb: e.matmul(
                    pwb[:], ones_row_f[:], wrow[:, tb * 512:(tb + 1) * 512],
                    start=True, stop=True))
                p.op("vector", lambda e, tb=tb: e.tensor_copy(
                    out=wbb[:, tb * 512:(tb + 1) * 512], in_=pwb[:]))
        for fh in range(2):
            if shared:
                p.op("gpsimd", lambda e, fh=fh: e.dma_start(
                    out=w1[:].rearrange("p (c f) -> p c f", f=HH),
                    in_=sw1r[:, :, fh * HH:(fh + 1) * HH]), dma=True)
                p.op("gpsimd", lambda e, fh=fh: e.dma_start(
                    out=w2[:].rearrange("p (c f) -> p c f", f=HH),
                    in_=sw2r[:, :, fh * HH:(fh + 1) * HH]), dma=True)
            else:
                p.op("gpsimd", lambda e, ee=ee, fh=fh: e.dma_start(
                    out=w1[:].rearrange("p (c f) -> p c f", f=HH),
                    in_=w1r[ee][:, :, fh * HH:(fh + 1) * HH]), dma=True)
                p.op("gpsimd", lambda e, ee=ee, fh=fh: e.dma_start(
                    out=w2[:].rearrange("p (c f) -> p c f", f=HH),
                    in_=w2r[ee][:, :, fh * HH:(fh + 1) * HH]), dma=True)
            for tb in range(4):
                t0 = tb * 512
                for oc in range(8):
                    b1i = (oc % 2) * 2
                    ph1 = p.pf[b1i]
                    ph2 = p.pf[b1i + 1]
                    for c in range(CH):
                        p.op("tensor", lambda e, ph1=ph1, oc=oc, c=c, t0=t0: e.matmul(
                            ph1[:], w1[:, c * HH + oc * 128: c * HH + oc * 128 + 128],
                            lnxs(c, t0, 512), start=(c == 0), stop=(c == CH - 1)),
                            wait=((None if (oc == 0 and tb == 0) else bankm[b1i]) if c == 0 else False), inc=(c == CH - 1))
                    for c in range(CH):
                        p.op("tensor", lambda e, ph2=ph2, oc=oc, c=c, t0=t0: e.matmul(
                            ph2[:], w2[:, c * HH + oc * 128: c * HH + oc * 128 + 128],
                            lnxs(c, t0, 512), start=(c == 0), stop=(c == CH - 1)),
                            wait=((None if (oc == 0 and tb == 0) else bankm[b1i + 1]) if c == 0 else False), inc=(c == CH - 1))
                    p.op("scalar", lambda e, ph1=ph1: e.activation(cm.SC1[:], ph1[:], AF.Silu))
                    hdst = hb[:, oc * 512:(oc + 1) * 512]
                    if shared:
                        p.op("vector", lambda e, ph2=ph2, hdst=hdst: e.tensor_mul(hdst, cm.SC1[:], ph2[:]))
                    else:
                        p.op("vector", lambda e, ph2=ph2: e.tensor_mul(cm.SC2[:], cm.SC1[:], ph2[:]))
                        p.op("vector", lambda e, hdst=hdst, t0=t0: e.tensor_mul(
                            hdst, cm.SC2[:], wbb[:, t0:t0 + 512]))
                    bankm[b1i] = p.mark()
                    bankm[b1i + 1] = p.mark()
                for oc in range(CH):
                    b2i = (oc % 2) * 2
                    py = p.pf[b2i]
                    for fc in range(8):
                        p.op("tensor", lambda e, py=py, oc=oc, fc=fc, fh=fh: e.matmul(
                            py[:], w3[:, (fh * 8 + fc) * D + oc * 128: (fh * 8 + fc) * D + (oc + 1) * 128],
                            hb[:, fc * 512:(fc + 1) * 512],
                            start=(fc == 0), stop=(fc == 7)),
                            wait=(bankm[b2i] if fc == 0 else False), inc=(fc == 7))
                    x = xss(oc, t0, 512)
                    p.op("vector", lambda e, py=py, x=x: e.tensor_add(x, x, py[:]))
                    bankm[b2i] = p.mark()
    p.free_prefix(pref)


def emit_L1_ffn(cm: Common, pref="l1f"):
    p, C, ins = cm.p, cm.consts, cm.ins
    _LNXB = cm.LNX
    _XSB = cm.XS
    def lnxs(c, t0=0, n=T, _b=_LNXB):
        return _b[:, c * T + t0: c * T + t0 + n]
    def xss(c, t0=0, n=T, _b=_XSB):
        return _b[:, c * T + t0: c * T + t0 + n]
    E, FF, R = 4, 4096, 4
    gwT = p.sb(pref + "gwT", [128, CH * 4], BF16)
    bgr = p.sb(pref + "bgr", [1, 4], BF16)
    WgT = p.sb(pref + "WgT", [128, NT * 4], F32)
    wrow = p.sb(pref + "wrow", [1, 512], F32)
    Alh = p.sb(pref + "Alh", [128, CH * 4 * E], BF16)
    Blh = p.sb(pref + "Blh", [4, FF // 2], BF16)
    lows = p.sb(pref + "lows", [4, T], BF16)
    Wi = p.sb(pref + "Wi", [128, CH * (FF // 2)], BF16)
    Wo = p.sb(pref + "Wo", [128, (FF // 2 // 128) * D], BF16)
    hb = p.sb(pref + "hb", [128, 16 * 512], BF16)
    wbb = p.sb(pref + "wb", [128, 512], BF16)
    g4 = p.sb(pref + "g4", [128, 4], F32)
    s4 = p.sb(pref + "s4", [128, 4], F32)
    m1 = p.sb(pref + "m1", [128, 1], F32)
    nm1 = p.sb(pref + "nm1", [128, 1], F32)
    m2 = p.sb(pref + "m2", [128, 1], F32)
    sum4 = p.sb(pref + "sum4", [128, 1], F32)
    sel = p.sb(pref + "sel", [128, 4], F32)
    selm = p.sb(pref + "selm", [128, 4], U8)
    neg = p.sb(pref + "neg", [128, 4], F32)
    ones_row_bf = C["c_ones_row_bf"]
    ones_row_f = C["c_ones_row_f"]
    ident_f = C["c_ident_f"]

    p.op("gpsimd", lambda e: e.dma_start(
        out=gwT[:].rearrange("p (c e) -> p c e", e=4),
        in_=ins["L1_ffn_WgT_host"].rearrange("(c p) e -> p c e", p=128)), dma=True)
    p.op("gpsimd", lambda e: e.dma_start(
        out=bgr[:], in_=ins["L1_ffn_bg"].rearrange("(u e) -> u e", u=1)), dma=True)
    p.op("gpsimd", lambda e: e.dma_start(
        out=Alh[:].rearrange("p (c r) -> p c r", r=4 * E),
        in_=ins["L1_ffn_AT_host"].rearrange("(c p) q -> p c q", p=128)), dma=True)
    p.op("vector", lambda e: e.memset(neg[:], -1e30))

    ps = p.pf[0][:, 0:4]
    pr = p.pf[1][0:1, 0:128]
    for i in range(NT):
        t0 = i * 128
        p.op("tensor", lambda e: e.matmul(ps, ones_row_bf[:], bgr[:], start=True, stop=False))
        for c in range(CH):
            p.op("tensor", lambda e, c=c, t0=t0: e.matmul(
                ps, lnxs(c, t0, 128), gwT[:, c * 4:(c + 1) * 4],
                start=False, stop=(c == CH - 1)))
        p.op("vector", lambda e: e.tensor_copy(out=g4[:], in_=ps))
        p.op("vector", lambda e: e.reduce_max(m1[:], g4[:], axis=AXX))
        p.op("vector", lambda e: e.tensor_scalar_mul(nm1[:], m1[:], -1.0))
        p.op("scalar", lambda e: e.activation(s4[:], g4[:], AF.Exp, bias=nm1[:, 0:1],
                                              accum_out=sum4[:]))
        p.op("vector", lambda e: e.reciprocal(sum4[:], sum4[:]))
        p.op("vector", lambda e: e.tensor_scalar(s4[:], s4[:], sum4[:, 0:1], None, ALU.mult))
        p.op("vector", lambda e: e.reduce_max(m1[:], s4[:], axis=AXX))
        p.op("vector", lambda e: e.tensor_scalar(selm[:], s4[:], m1[:, 0:1], None, ALU.is_lt))
        p.op("vector", lambda e: e.select(sel[:], selm[:], s4[:], neg[:]))
        p.op("vector", lambda e: e.reduce_max(m2[:], sel[:], axis=AXX))
        wg = WgT[:, i * 4:(i + 1) * 4]
        p.op("vector", lambda e, wg=wg: e.tensor_scalar(sel[:], s4[:], m2[:, 0:1], None, ALU.is_ge))
        p.op("vector", lambda e, wg=wg: e.tensor_mul(wg, s4[:], sel[:]))

    bankm = {}
    Wir = ins["L1_ffn_WiT_host"].rearrange("(c p) f -> p c f", p=128)
    Wor = ins["L1_ffn_WoT_host"].rearrange("(c p) d -> p c d", p=128)
    Br = ins["L1_ffn_BT_host"]
    HH = FF // 2
    for fh in range(2):
        p.op("gpsimd", lambda e, fh=fh: e.dma_start(
            out=Wi[:].rearrange("p (c f) -> p c f", f=HH),
            in_=Wir[:, :, fh * HH:(fh + 1) * HH]), dma=True)
        p.op("gpsimd", lambda e, fh=fh: e.dma_start(
            out=Wo[:].rearrange("p (c d) -> p c d", d=D),
            in_=Wor[:, fh * 16:(fh + 1) * 16, :]), dma=True)
        for ee in range(E):
            p.op("gpsimd", lambda e, ee=ee, fh=fh: e.dma_start(
                out=Blh[:], in_=Br[ee][:, fh * HH:(fh + 1) * HH]), dma=True)
            pl = p.pf[0][0:4, :]
            for tb in range(4):
                t0 = tb * 512
                for c in range(CH):
                    p.op("tensor", lambda e, ee=ee, c=c, t0=t0: e.matmul(
                        pl, lnxs(c, t0, 512),
                        Alh[:, c * 4 * E + ee * 4: c * 4 * E + ee * 4 + 4],
                        start=(c == 0), stop=(c == CH - 1))
                    if False else e.matmul(
                        pl, Alh[:, c * 4 * E + ee * 4: c * 4 * E + ee * 4 + 4],
                        lnxs(c, t0, 512), start=(c == 0), stop=(c == CH - 1)))
                p.op("vector", lambda e, t0=t0: e.tensor_copy(
                    out=lows[:, t0:t0 + 512], in_=pl))
            for tb in range(4):
                t0 = tb * 512
                for j in range(4):
                    i = tb * 4 + j
                    p.op("tensor", lambda e, i=i, ee=ee: e.matmul(
                        pr, WgT[:, i * 4 + ee: i * 4 + ee + 1], ident_f[:],
                        start=True, stop=True))
                    p.op("vector", lambda e, j=j: e.tensor_copy(
                        out=wrow[:, j * 128:(j + 1) * 128], in_=pr))
                pwb = p.pf[2]
                p.op("tensor", lambda e: e.matmul(
                    pwb[:], ones_row_f[:], wrow[:], start=True, stop=True))
                p.op("vector", lambda e: e.tensor_copy(out=wbb[:], in_=pwb[:]))
                for bb in range(4):
                    bankm[bb] = p.mark()
                for oc in range(16):
                    ph = p.pf[oc % 2]
                    for c in range(CH):
                        p.op("tensor", lambda e, ph=ph, oc=oc, c=c, t0=t0: e.matmul(
                            ph[:], Wi[:, c * HH + oc * 128: c * HH + oc * 128 + 128],
                            lnxs(c, t0, 512), start=(c == 0), stop=False),
                            wait=((None if oc < 2 else bankm.get(oc % 2)) if c == 0 else False),
                            inc=False)
                    p.op("tensor", lambda e, ph=ph, oc=oc, t0=t0: e.matmul(
                        ph[:], Blh[:, oc * 128:(oc + 1) * 128],
                        lows[:, t0:t0 + 512], start=False, stop=True),
                        wait=False, inc=True)
                    p.op("vector", lambda e, ph=ph: e.tensor_scalar_max(cm.SC1[:], ph[:], 0.0))
                    p.op("vector", lambda e, oc=oc: e.tensor_mul(
                        hb[:, oc * 512:(oc + 1) * 512], cm.SC1[:], wbb[:]))
                    bankm[oc % 2] = p.mark()
                for oc in range(CH):
                    py = p.pf[2 + (oc % 2)]
                    for fc in range(16):
                        p.op("tensor", lambda e, py=py, oc=oc, fc=fc: e.matmul(
                            py[:], Wo[:, fc * D + oc * 128: fc * D + (oc + 1) * 128],
                            hb[:, fc * 512:(fc + 1) * 512],
                            start=(fc == 0), stop=(fc == 15)),
                            wait=((None if oc < 2 else bankm.get(2 + oc % 2)) if fc == 0 else False),
                            inc=(fc == 15))
                    x = xss(oc, t0, 512)
                    p.op("vector", lambda e, py=py, x=x: e.tensor_add(x, x, py[:]))
                    bankm[2 + oc % 2] = p.mark()
    p.free_prefix(pref)


def emit_L1_attn(cm: Common, pref="l1a"):
    p, C, ins = cm.p, cm.consts, cm.ins
    _LNXB = cm.LNX
    _XSB = cm.XS
    def lnxs(c, t0=0, n=T, _b=_LNXB):
        return _b[:, c * T + t0: c * T + t0 + n]
    def xss(c, t0=0, n=T, _b=_XSB):
        return _b[:, c * T + t0: c * T + t0 + n]
    DI = 2048
    NC = 16
    ones_row_f = C["c_ones_row_f"]
    ones_col_f = C["c_ones_col_f"]

    xc_d = p.dram(pref + "xc_d", [DI, T], BF16)
    z_d = p.dram(pref + "z_d", [DI, T], BF16)
    y_d = p.dram(pref + "y_d", [DI, T], BF16)
    cc_in = p.dram(pref + "cc_in", [DI, NT * 3], BF16)
    cc_out = p.dram(pref + "cc_out", [2 * DI, NT * 3], BF16)

    cw = p.sb(pref + "cw", [128, NC * 4], F32)
    cb = p.sb(pref + "cb", [128, NC], F32)
    Dcol = p.sb(pref + "Dc", [128, NC], F32)
    mW0 = p.sb(pref + "mW0", [128, 1], F32)
    mW1 = p.sb(pref + "mW1", [128, 1], F32)
    Wq = p.sb(pref + "t1_Wq", [128, CH * 512], BF16)
    tails = p.sb(pref + "t1_tl", [128, CH * NT * 3], BF16)
    otails = p.sb(pref + "t1_otl", [128, NC * NT * 3], BF16)
    G = p.sb(pref + "t1_G", [128, 2 * NC * NT * 3], BF16)
    xbt = p.sb(pref + "t1_xbt", [128, NC * 512], BF16)
    xct = p.sb(pref + "t1_xct", [128, NC * 512], BF16)
    pad = p.sb(pref + "t1_pad", [128, 131], BF16)
    halo = p.sb(pref + "t1_halo", [128, 3], BF16)

    p.op("gpsimd", lambda e: e.dma_start(
        out=cw[:].rearrange("p (c k) -> p c k", k=4),
        in_=ins["L1_attn_conv_w"].rearrange("(c p) k -> p c k", p=128)), dma=True)
    p.op("sync", lambda e: e.dma_start(out=cb[:], in_=ins["conv_b_col"][:, :]), dma=True)
    p.op("sync", lambda e: e.dma_start(out=Dcol[:], in_=ins["D_col"][:, :]), dma=True)
    p.op("sync", lambda e: e.dma_start(out=mW0[:], in_=ins["c_mW0"][:, :]), dma=True)
    p.op("sync", lambda e: e.dma_start(out=mW1[:], in_=ins["c_mW1"][:, :]), dma=True)

    WinT = ins["L1_attn_W_inT_host"].rearrange("(c p) f -> p c f", p=128)

    for c in range(CH):
        for i in range(NT):
            p.op("vector", lambda e, c=c, i=i: e.tensor_copy(
                out=tails[:, c * NT * 3 + i * 3: c * NT * 3 + (i + 1) * 3],
                in_=lnxs(c, i * 128 + 125, 3)))
    for q in range(4):
        p.op("gpsimd", lambda e, q=q: e.dma_start(
            out=Wq[:].rearrange("p (c f) -> p c f", f=512),
            in_=WinT[:, :, q * 512:(q + 1) * 512]), dma=True)
        for oc in range(4):
            ps = p.pf[oc % 2][:, 0:NT * 3]
            for c in range(CH):
                p.op("tensor", lambda e, ps=ps, q=q, oc=oc, c=c: e.matmul(
                    ps, Wq[:, c * 512 + oc * 128: c * 512 + oc * 128 + 128],
                    tails[:, c * NT * 3:(c + 1) * NT * 3],
                    start=(c == 0), stop=(c == CH - 1)))
            p.op("vector", lambda e, ps=ps, q=q, oc=oc: e.tensor_copy(
                out=otails[:, (q * 4 + oc) * NT * 3:(q * 4 + oc + 1) * NT * 3], in_=ps))
    p.op("sync", lambda e: e.dma_start(
        out=cc_in.rearrange("(c p) t -> p c t", p=128),
        in_=otails[:].rearrange("p (c t) -> p c t", t=NT * 3)), dma=True)
    p.op_collective(lambda e: e.collective_compute(
        "AllGather", ALU.bypass, replica_groups=[[0, 1], [2, 3], [4, 5], [6, 7]],
        ins=[cc_in.opt()], outs=[cc_out.opt()]))
    p.op("sync", lambda e: e.dma_start(
        out=G[:].rearrange("p (q t) -> p q t", t=NT * 3),
        in_=cc_out.rearrange("(q p) t -> p q t", p=128)), dma=True)

    for tb in range(4):
        t0 = tb * 512
        for half in range(2):
            base_e = half * DI
            for q in range(4):
                p.op("gpsimd", lambda e, q=q, base_e=base_e: e.dma_start(
                    out=Wq[:].rearrange("p (c f) -> p c f", f=512),
                    in_=WinT[:, :, base_e + q * 512: base_e + (q + 1) * 512]), dma=True)
                for oc in range(4):
                    ps = p.pf[oc % 2]
                    for c in range(CH):
                        p.op("tensor", lambda e, ps=ps, q=q, oc=oc, c=c, t0=t0: e.matmul(
                            ps[:], Wq[:, c * 512 + oc * 128: c * 512 + oc * 128 + 128],
                            lnxs(c, t0, 512), start=(c == 0), stop=(c == CH - 1)))
                    dst = xbt[:, (q * 4 + oc) * 512:(q * 4 + oc + 1) * 512]
                    p.op("vector", lambda e, ps=ps, dst=dst: e.tensor_copy(out=dst, in_=ps[:]))
            if half == 1:
                p.op("sync", lambda e, t0=t0: e.dma_start(
                    out=z_d.rearrange("(c p) t -> p c t", p=128)[:, :, t0:t0 + 512],
                    in_=xbt[:].rearrange("p (c t) -> p c t", t=512)), dma=True)
                continue
            for j in range(4):
                i = tb * 4 + j
                for oc in range(NC):
                    xb_ij = xbt[:, oc * 512 + j * 128: oc * 512 + (j + 1) * 128]
                    g0 = G[:, 0 * NC * NT * 3 + oc * NT * 3 + i * 3:
                           0 * NC * NT * 3 + oc * NT * 3 + (i + 1) * 3]
                    p.op("vector", lambda e, g0=g0: e.tensor_scalar(
                        halo[:], g0, mW0[:, 0:1], None, ALU.mult))
                    if i > 0:
                        g1 = G[:, 1 * NC * NT * 3 + oc * NT * 3 + (i - 1) * 3:
                               1 * NC * NT * 3 + oc * NT * 3 + i * 3]
                        p.op("vector", lambda e, g1=g1: e.scalar_tensor_tensor(
                            halo[:], g1, mW1[:, 0:1], halo[:], ALU.mult, ALU.add))
                    p.op("vector", lambda e: e.tensor_copy(out=pad[:, 0:3], in_=halo[:]))
                    p.op("vector", lambda e, xb_ij=xb_ij: e.tensor_copy(
                        out=pad[:, 3:131], in_=xb_ij))
                    xc_ij = xct[:, oc * 512 + j * 128: oc * 512 + (j + 1) * 128]
                    p.op("vector", lambda e, xc_ij=xc_ij, oc=oc: e.tensor_scalar(
                        xc_ij, pad[:, 3:131], cw[:, oc * 4 + 3: oc * 4 + 4],
                        cb[:, oc:oc + 1], ALU.mult, ALU.add))
                    for k in range(3):
                        p.op("vector", lambda e, xc_ij=xc_ij, oc=oc, k=k: e.scalar_tensor_tensor(
                            xc_ij, pad[:, k:k + 128], cw[:, oc * 4 + k: oc * 4 + k + 1],
                            xc_ij, ALU.mult, ALU.add))
            for oc in range(NC):
                sl = xct[:, oc * 512:(oc + 1) * 512]
                p.op("scalar", lambda e, sl=sl: e.activation(sl, sl, AF.Silu))
            p.op("sync", lambda e, t0=t0: e.dma_start(
                out=xc_d.rearrange("(c p) t -> p c t", p=128)[:, :, t0:t0 + 512],
                in_=xct[:].rearrange("p (c t) -> p c t", t=512)), dma=True)
    p.free_prefix(pref + "t1")

    WxT = p.sb(pref + "t2_WxT", [128, NC * 2080], BF16)
    xcs = p.sb(pref + "t2_xcs", [128, NC * 512], BF16)
    zs = p.sb(pref + "t2_zs", [128, 512], BF16)
    bcp = p.sb(pref + "t2_bcp", [16, 512], F32)
    bcb = p.sb(pref + "t2_bcb", [128, 512], F32)
    delta = p.sb(pref + "t2_dl", [128, 512], F32)
    yt = p.sb(pref + "t2_yt", [128, 512], BF16)
    p.op("gpsimd", lambda e: e.dma_start(
        out=WxT[:].rearrange("p (c f) -> p c f", f=2080),
        in_=ins["L1_attn_W_xprojT_host"].rearrange("(c p) f -> p c f", p=128)), dma=True)
    for tb in range(4):
        t0 = tb * 512
        p.op("gpsimd", lambda e, t0=t0: e.dma_start(
            out=xcs[:].rearrange("p (c t) -> p c t", t=512),
            in_=xc_d.rearrange("(c p) t -> p c t", p=128)[:, :, t0:t0 + 512]), dma=True)
        pB = p.pf[2][0:16, :]
        pC = p.pf[3][0:16, :]
        for c in range(NC):
            p.op("tensor", lambda e, c=c: e.matmul(
                pB, WxT[:, c * 2080: c * 2080 + 16], xcs[:, c * 512:(c + 1) * 512],
                start=(c == 0), stop=(c == NC - 1)))
        for c in range(NC):
            p.op("tensor", lambda e, c=c: e.matmul(
                pC, WxT[:, c * 2080 + 16: c * 2080 + 32], xcs[:, c * 512:(c + 1) * 512],
                start=(c == 0), stop=(c == NC - 1)))
        p.op("vector", lambda e: e.tensor_copy(out=bcp[:], in_=pB))
        p.op("vector", lambda e: e.tensor_mul(bcp[:], bcp[:], pC))
        pS = p.pf[4][0:1, :]
        p.op("tensor", lambda e: e.matmul(pS, ones_col_f[0:16, :], bcp[:],
                                          start=True, stop=True))
        p.op("vector", lambda e: e.tensor_copy(out=cm.SROW[:, 0:512], in_=pS))
        pBC = p.pf[5]
        p.op("tensor", lambda e: e.matmul(pBC[:], ones_row_f[:], cm.SROW[:, 0:512],
                                          start=True, stop=True))
        p.op("vector", lambda e: e.tensor_copy(out=bcb[:], in_=pBC[:]))
        for oc in range(NC):
            pd = p.pf[oc % 2]
            for c in range(NC):
                p.op("tensor", lambda e, pd=pd, oc=oc, c=c: e.matmul(
                    pd[:], WxT[:, c * 2080 + 32 + oc * 128: c * 2080 + 32 + (oc + 1) * 128],
                    xcs[:, c * 512:(c + 1) * 512], start=(c == 0), stop=(c == NC - 1)))
            p.op("scalar", lambda e, pd=pd: e.activation(cm.SC1[:], pd[:], AF.Abs))
            p.op("scalar", lambda e: e.activation(cm.SC1[:], cm.SC1[:], AF.Exp, scale=-1.0))
            p.op("vector", lambda e: e.tensor_scalar_add(cm.SC1[:], cm.SC1[:], 1.0))
            p.op("scalar", lambda e: e.activation(cm.SC1[:], cm.SC1[:], AF.Ln))
            p.op("vector", lambda e, pd=pd: e.tensor_scalar_max(cm.SC2[:], pd[:], 0.0))
            p.op("vector", lambda e: e.tensor_add(delta[:], cm.SC1[:], cm.SC2[:]))
            xc_oc = xcs[:, oc * 512:(oc + 1) * 512]
            z_oc = zs[:]
            p.op("gpsimd", lambda e, oc=oc, t0=t0: e.dma_start(
                out=zs[:], in_=z_d[oc * 128:(oc + 1) * 128, t0:t0 + 512]), dma=True)
            p.op("vector", lambda e, xc_oc=xc_oc: e.tensor_mul(delta[:], delta[:], xc_oc))
            p.op("vector", lambda e: e.tensor_mul(delta[:], delta[:], bcb[:]))
            p.op("vector", lambda e, xc_oc=xc_oc, oc=oc: e.scalar_tensor_tensor(
                delta[:], xc_oc, Dcol[:, oc:oc + 1], delta[:], ALU.mult, ALU.add))
            p.op("scalar", lambda e, z_oc=z_oc: e.activation(cm.SC1[:], z_oc, AF.Silu))
            p.op("vector", lambda e: e.tensor_mul(yt[:], delta[:], cm.SC1[:]))
            p.op("sync", lambda e, oc=oc, t0=t0: e.dma_start(
                out=y_d[oc * 128:(oc + 1) * 128, t0:t0 + 512], in_=yt[:]), dma=True)
    p.free_prefix(pref + "t2")

    WoT = p.sb(pref + "t3_WoT", [128, NC * D], BF16)
    ys = p.sb(pref + "t3_ys", [128, NC * 512], BF16)
    p.op("gpsimd", lambda e: e.dma_start(
        out=WoT[:].rearrange("p (c f) -> p c f", f=D),
        in_=ins["L1_attn_W_outT_host"].rearrange("(c p) f -> p c f", p=128)), dma=True)
    for tb in range(4):
        t0 = tb * 512
        p.op("gpsimd", lambda e, t0=t0: e.dma_start(
            out=ys[:].rearrange("p (c t) -> p c t", t=512),
            in_=y_d.rearrange("(c p) t -> p c t", p=128)[:, :, t0:t0 + 512]), dma=True)
        for oc in range(CH):
            py = p.pf[2 + (oc % 2)]
            for c in range(NC):
                p.op("tensor", lambda e, py=py, oc=oc, c=c: e.matmul(
                    py[:], WoT[:, c * D + oc * 128: c * D + (oc + 1) * 128],
                    ys[:, c * 512:(c + 1) * 512], start=(c == 0), stop=(c == NC - 1)))
            x = xss(oc, t0, 512)
            p.op("vector", lambda e, py=py, x=x: e.tensor_add(x, x, py[:]))
    p.free_prefix(pref)


def emit_L3_attn(cm: Common, pref="l3a"):
    p, C, ins = cm.p, cm.consts, cm.ins
    _LNXB = cm.LNX
    _XSB = cm.XS
    def lnxs(c, t0=0, n=T, _b=_LNXB):
        return _b[:, c * T + t0: c * T + t0 + n]
    def xss(c, t0=0, n=T, _b=_XSB):
        return _b[:, c * T + t0: c * T + t0 + n]
    xin_d = cm.x_in
    xout_d = cm.x_out
    NH = 16
    ones_row_f = C["c_ones_row_f"]
    ones_row_bf = C["c_ones_row_bf"]
    ones_col_f = C["c_ones_col_f"]
    ones_col_bf = C["c_ones_col_bf"]
    ident_bf = C["c_ident_bf"]
    RG = [[0, 1], [2, 3], [4, 5], [6, 7]]

    ccK_in = p.dram(pref + "ccK_in", [D + 128, T], BF16)
    ccK_out = p.dram(pref + "ccK_out", [2 * (D + 128), T], BF16)
    ccV_in = p.dram(pref + "ccV_in", [T, D], BF16)
    ccV_out = p.dram(pref + "ccV_out", [2 * T, D], BF16)

    QB = p.sb(pref + "QB", [128, CH * T], BF16)
    ATT = p.sb(pref + "ATT", [128, CH * T], BF16)
    cosb = p.sb(pref + "cos", [128, T], BF16)
    sinb = p.sb(pref + "sin", [128, T], BF16)
    NM16 = p.sb(pref + "NM16", [128, 16], F32)
    mska = p.sb(pref + "mska", [128, 512], BF16)
    mskb = p.sb(pref + "mskb", [128, 512], BF16)

    p.op("gpsimd", lambda e: e.dma_start(out=cosb[:], in_=ins["c_l3cos"][:, :]), dma=True)
    p.op("gpsimd", lambda e: e.dma_start(out=sinb[:], in_=ins["c_l3sin"][:, :]), dma=True)
    p.op("gpsimd", lambda e: e.dma_start(out=mska[:], in_=ins["c_l3m_a4"][:, :]), dma=True)
    p.op("gpsimd", lambda e: e.dma_start(out=mskb[:], in_=ins["c_l3m_b4"][:, :]), dma=True)
    if DBG.get("l3_stop") == "Bz":
        p.free_prefix(pref)
        return

    WT = p.sb(pref + "t1_WT", [128, CH * D], BF16)
    brow = p.sb(pref + "t1_brow", [1, D], F32)
    KVB = p.sb(pref + "t1_KVB", [128, CH * T], BF16)

    def proj_fm(wname, bname, dstbuf):
        p.op("gpsimd", lambda e, wname=wname: e.dma_start(
            out=WT[:].rearrange("p (c f) -> p c f", f=D),
            in_=ins[wname].rearrange("(c p) f -> p c f", p=128)), dma=True)
        p.op("sync", lambda e, bname=bname: e.dma_start(
            out=brow[:], in_=ins[bname].rearrange("(u f) -> u f", u=1)), dma=True)
        bm = {}
        for oc in range(CH):
            for tb in range(4):
                t0 = tb * 512
                ph = p.pf[tb % 2]
                first = (oc == 0 and tb < 2)
                p.op("tensor", lambda e, ph=ph, oc=oc: e.matmul(
                    ph[:], brow[:, oc * 128:(oc + 1) * 128], C["c_ones512_f"][:],
                    start=True, stop=False),
                    wait=(None if first else bm.get(tb % 2)), inc=False)
                for c in range(CH):
                    p.op("tensor", lambda e, ph=ph, oc=oc, c=c, t0=t0: e.matmul(
                        ph[:], WT[:, c * D + oc * 128: c * D + (oc + 1) * 128],
                        lnxs(c, t0, 512), start=False, stop=(c == CH - 1)),
                        wait=False, inc=(c == CH - 1))
                p.op("vector", lambda e, ph=ph, oc=oc, t0=t0, dstbuf=dstbuf: e.tensor_copy(
                    out=dstbuf[:, oc * T + t0: oc * T + t0 + 512], in_=ph[:]))
                bm[tb % 2] = p.mark()

    def rope(buf):
        for h in range(NH):
            c, off = h // 2, (h % 2) * 64
            for tb in range(4):
                t0 = tb * 512
                a = buf[off:off + 32, c * T + t0: c * T + t0 + 512]
                b = buf[off + 32:off + 64, c * T + t0: c * T + t0 + 512]
                ca = cosb[off:off + 32, t0:t0 + 512]
                sa = sinb[off:off + 32, t0:t0 + 512]
                cb2 = cosb[off + 32:off + 64, t0:t0 + 512]
                sb2 = sinb[off + 32:off + 64, t0:t0 + 512]
                u = cm.SC1[off:off + 32, :]
                v = cm.SC2[off:off + 32, :]
                u2 = cm.SC1[off + 32:off + 64, :]
                v2 = cm.SC2[off + 32:off + 64, :]
                p.op("vector", lambda e, u2=u2, a=a: e.tensor_copy(out=u2, in_=a))
                p.op("vector", lambda e, v=v, b=b: e.tensor_copy(out=v, in_=b))
                p.op("vector", lambda e, u=u, a=a, ca=ca: e.tensor_mul(u, a, ca))
                p.op("vector", lambda e, v=v, sa=sa: e.tensor_mul(v, v, sa))
                p.op("vector", lambda e, u=u, v=v: e.tensor_sub(u, u, v))
                p.op("vector", lambda e, v2=v2, b=b, cb2=cb2: e.tensor_mul(v2, b, cb2))
                p.op("vector", lambda e, u2=u2, sb2=sb2: e.tensor_mul(u2, u2, sb2))
                p.op("vector", lambda e, v2=v2, u2=u2: e.tensor_add(v2, v2, u2))
                p.op("vector", lambda e, a=a, u=u: e.tensor_copy(out=a, in_=u))
                p.op("vector", lambda e, b=b, v2=v2: e.tensor_copy(out=b, in_=v2))
    proj_fm("L3_attn_WqT_host", "L3_attn_bq", QB)
    if DBG.get("l3_dump_q") == "prerope":
        qd = p.dram(pref + "q_d", [D, T], F32, kind="ExternalOutput")
        for c in range(CH):
            for tb in range(4):
                p.op("vector", lambda e, c=c, tb=tb: e.tensor_copy(
                    out=cm.SC1[:], in_=QB[:, c * T + tb * 512: c * T + tb * 512 + 512]))
                p.op("sync", lambda e, c=c, tb=tb: e.dma_start(
                    out=qd[c * 128:(c + 1) * 128, tb * 512:(tb + 1) * 512], in_=cm.SC1[:]), dma=True)
    if DBG.get("l3_stop") == "B0":
        p.free_prefix(pref)
        return
    rope(QB)
    if DBG.get("l3_dump_q") == "postrope":
        qd = p.dram(pref + "q_d", [D, T], F32, kind="ExternalOutput")
        for c in range(CH):
            for tb in range(4):
                p.op("vector", lambda e, c=c, tb=tb: e.tensor_copy(
                    out=cm.SC1[:], in_=QB[:, c * T + tb * 512: c * T + tb * 512 + 512]))
                p.op("sync", lambda e, c=c, tb=tb: e.dma_start(
                    out=qd[c * 128:(c + 1) * 128, tb * 512:(tb + 1) * 512], in_=cm.SC1[:]), dma=True)
    if DBG.get("l3_stop") == "B0r":
        p.free_prefix(pref)
        return
    proj_fm("L3_attn_WkT_host", "L3_attn_bk", KVB)
    rope(KVB)
    if DBG.get("l3_stop") == "B1":
        p.free_prefix(pref)
        return
    pn2 = p.pf[4][0:2, :]
    half_ones = C["c_half_ones"]
    kqmax = p.sb(pref + "t1_kqm", [2, 16], F32)
    p.op("vector", lambda e: e.memset(kqmax[:], 0.0))
    for which, buf in ((0, KVB), (1, QB)):
        for tb in range(4):
            t0 = tb * 512
            for c in range(CH):
                sl = buf[:, c * T + t0: c * T + t0 + 512]
                p.op("scalar", lambda e, sl=sl: e.activation(cm.SC1[:], sl, AF.Square))
                p.op("tensor", lambda e: e.matmul(pn2, half_ones[:], cm.SC1[:],
                                                  start=True, stop=True))
                p.op("vector", lambda e: e.tensor_copy(out=cm.SC2[0:2, :], in_=pn2))
                red = cm.SC2[0:2, 512 - 1:512] if False else None
                p.op("vector", lambda e: e.reduce_max(cm.SROW[0:2, 0:1] if False else cm.SC1[0:2, 0:1], cm.SC2[0:2, :], axis=AXX))
                dcol = kqmax[:, which * 8 + c: which * 8 + c + 1]
                p.op("vector", lambda e, dcol=dcol: e.tensor_max(dcol, dcol, cm.SC1[0:2, 0:1]))
    for c in range(CH):
        p.op("sync", lambda e, c=c: e.dma_start(
            out=ccK_in[c * 128:(c + 1) * 128, :], in_=KVB[:, c * T:(c + 1) * T]), dma=True)
    p.op("gpsimd", lambda e: e.dma_start(out=ccK_in[D:D + 2, 0:16], in_=kqmax[:]), dma=True)
    p.op("gpsimd", lambda e: e.dma_start(
        out=WT[:].rearrange("p (c f) -> p c f", f=D),
        in_=ins["L3_attn_WvT_host"].rearrange("(c p) f -> p c f", p=128)), dma=True)
    p.op("sync", lambda e: e.dma_start(
        out=brow[:], in_=ins["L3_attn_bv"].rearrange("(u f) -> u f", u=1)), dma=True)
    bmv = {}
    for i in range(NT):
        t0 = i * 128
        for ob in range(2):
            ph = p.pf[ob % 2]
            p.op("tensor", lambda e, ph=ph, ob=ob: e.matmul(
                ph[:], ones_row_f[:],
                brow[:, ob * 512:(ob + 1) * 512], start=True, stop=False),
                wait=(None if i == 0 else bmv.get(ob % 2)), inc=False)
            for c in range(CH):
                p.op("tensor", lambda e, ph=ph, ob=ob, c=c, t0=t0: e.matmul(
                    ph[:], lnxs(c, t0, 128),
                    WT[:, c * D + ob * 512: c * D + (ob + 1) * 512],
                    start=False, stop=(c == CH - 1)),
                    wait=False, inc=(c == CH - 1))
            p.op("vector", lambda e, ph=ph, i=i, ob=ob: e.tensor_copy(
                out=KVB[:, i * D + ob * 512: i * D + (ob + 1) * 512], in_=ph[:]))
            bmv[ob % 2] = p.mark()
        p.op("sync", lambda e, i=i: e.dma_start(
            out=ccV_in[i * 128:(i + 1) * 128, :], in_=KVB[:, i * D:(i + 1) * D]), dma=True)
    if DBG.get("l3_stop") == "B2":
        p.free_prefix(pref)
        return
    p.op_collective(lambda e: e.collective_compute(
        "AllGather", ALU.bypass, replica_groups=RG,
        ins=[ccK_in.opt()], outs=[ccK_out.opt()]))
    p.op_collective(lambda e: e.collective_compute(
        "AllGather", ALU.bypass, replica_groups=RG,
        ins=[ccV_in.opt()], outs=[ccV_out.opt()]))
    if DBG.get("l3_stop") == "B3":
        p.free_prefix(pref)
        return
    KD = D + 128
    gn = p.sb(pref + "t1_gn", [2, 32], F32)
    nmr = p.sb(pref + "t1_nmr", [1, 16], F32)
    p.op("gpsimd", lambda e: e.dma_start(out=gn[:, 0:16], in_=ccK_out[D:D + 2, 0:16]), dma=True)
    p.op("gpsimd", lambda e: e.dma_start(out=gn[:, 16:32], in_=ccK_out[KD + D:KD + D + 2, 0:16]), dma=True)
    p.op("vector", lambda e: e.tensor_max(gn[:, 0:16], gn[:, 0:16], gn[:, 16:32]))
    p.op("vector", lambda e: e.tensor_max(gn[:, 0:8], gn[:, 0:8], kqmax[:, 0:8]))
    p.op("scalar", lambda e: e.activation(gn[:, 0:8], gn[:, 0:8], AF.Sqrt))
    p.op("scalar", lambda e: e.activation(gn[:, 8:16], kqmax[:, 8:16], AF.Sqrt))
    p.op("vector", lambda e: e.tensor_mul(gn[:, 0:8], gn[:, 0:8], gn[:, 8:16]))
    p.op("vector", lambda e: e.tensor_scalar_mul(gn[:, 0:8], gn[:, 0:8], -1.0 / 8.0))
    pr2 = p.pf[5][0:1, 0:2]
    for c in range(CH):
        p.op("tensor", lambda e, c=c: e.matmul(
            pr2, gn[:, c:c + 1], C["c_ident_f"][0:2, 0:2], start=True, stop=True))
        p.op("vector", lambda e, c=c: e.tensor_copy(out=nmr[:, 2 * c:2 * c + 2], in_=pr2))
    pm = p.pf[4][:, 0:16]
    p.op("tensor", lambda e: e.matmul(pm, ones_row_f[:], nmr[:], start=True, stop=True))
    p.op("vector", lambda e: e.tensor_copy(out=NM16[:], in_=pm))
    if DBG.get("l3_dump_kv"):
        kd = p.dram(pref + "k_d", [2 * (D + 128), 256], F32, kind="ExternalOutput")
        vd = p.dram(pref + "v_d", [512, D], F32, kind="ExternalOutput")
        kbuf = p.sb(pref + "t9_kb", [128, 256], BF16)
        for r in range(2):
            for c in range(9):
                p.op("gpsimd", lambda e, r=r, c=c: e.dma_start(
                    out=kbuf[:], in_=ccK_out[r * 1152 + c * 128: r * 1152 + (c + 1) * 128, 0:256]), dma=True)
                p.op("vector", lambda e: e.tensor_copy(out=cm.SC1[:, 0:256], in_=kbuf[:]))
                p.op("sync", lambda e, r=r, c=c: e.dma_start(
                    out=kd[r * 1152 + c * 128: r * 1152 + (c + 1) * 128, :], in_=cm.SC1[:, 0:256]), dma=True)
        vbuf = p.sb(pref + "t9_vb", [128, D], BF16)
        for blk in range(4):
            src_r = [0, 0, 1, 1][blk]; src_lt = [0, 1, 0, 1][blk]
            p.op("gpsimd", lambda e, src_r=src_r, src_lt=src_lt: e.dma_start(
                out=vbuf[:], in_=ccV_out[(src_r * NT + src_lt) * 128:(src_r * NT + src_lt + 1) * 128, :]), dma=True)
            for half in range(2):
                p.op("vector", lambda e, half=half: e.tensor_copy(out=cm.SC1[:], in_=vbuf[:, half * 512:(half + 1) * 512]))
                p.op("sync", lambda e, blk=blk, half=half: e.dma_start(
                    out=vd[blk * 128:(blk + 1) * 128, half * 512:(half + 1) * 512], in_=cm.SC1[:]), dma=True)
        p.free_prefix(pref + "t9")
    p.free_prefix(pref + "t1")
    if DBG.get("l3_stop") == "B":
        p.free_prefix(pref)
        return

    KCH2 = [p.sb(pref + "t2_KCHa", [128, CH * 128], BF16),
            p.sb(pref + "t2_KCHb", [128, CH * 128], BF16)]
    VCH2 = [p.sb(pref + "t2_VCHa", [128, D], BF16),
            p.sb(pref + "t2_VCHb", [128, D], BF16)]
    PT = p.sb(pref + "t2_PT", [128, 4 * 512], BF16)
    SUMS = p.sb(pref + "t2_SUMS", [128, NT], F32)
    RES = p.sb(pref + "t2_RES", [128, 1024], BF16)
    rcp = p.sb(pref + "t2_rcp", [128, 16], F32)
    ccK4 = ccK_out.rearrange("(r c p) t -> r p c t", p=128, c=9)
    ccV3 = ccV_out.rearrange("(q p) o -> q p o", p=128)
    usemark = [None, None]
    for i in range(NT):
        q0 = i * 128
        pAV = [p.pf[2], p.pf[3]]
        psums = p.pf[4][:, 0:16]
        ngt = 2 * i + 2
        for gt in range(ngt):
            r, lt = gt % 2, gt // 2
            par = gt % 2
            KCH = KCH2[par]
            VCH = VCH2[par]
            p.op("gpsimd", lambda e, r=r, lt=lt, KCH=KCH: e.dma_start(
                out=KCH[:].rearrange("p (c t) -> p c t", t=128),
                in_=ccK4[r][:, 0:CH, lt * 128:(lt + 1) * 128]), dma=True)
            p.op("gpsimd", lambda e, r=r, lt=lt, VCH=VCH: e.dma_start(
                out=VCH[:], in_=ccV_out[(r * NT + lt) * 128:(r * NT + lt + 1) * 128, :]), dma=True)
            for bk in range(4):
                pst = p.pf[bk % 2]
                for hh in range(4):
                    h = bk * 4 + hh
                    c, off = h // 2, (h % 2) * 64
                    p.op("tensor", lambda e, pst=pst, hh=hh, c=c, off=off, q0=q0, KCH=KCH: e.matmul(
                        pst[:, hh * 128:(hh + 1) * 128],
                        KCH[off + 0:off + 64, c * 128:(c + 1) * 128],
                        QB[off + 0:off + 64, c * T + q0: c * T + q0 + 128],
                        start=(hh == 0), stop=(hh == 3)))
                dst = PT[:, bk * 512:(bk + 1) * 512]
                for hh in range(4):
                    h = bk * 4 + hh
                    p.op("scalar", lambda e, pst=pst, dst=dst, hh=hh, h=h: e.activation(
                        dst[:, hh * 128:(hh + 1) * 128], pst[:, hh * 128:(hh + 1) * 128],
                        AF.Exp, bias=NM16[:, h:h + 1], scale=0.125))
                if gt == 2 * i:
                    p.op("vector", lambda e, dst=dst: e.tensor_mul(dst, dst, mska[:]))
                elif gt == 2 * i + 1:
                    p.op("vector", lambda e, dst=dst: e.tensor_mul(dst, dst, mskb[:]))
            for h in range(NH):
                pav = pAV[h // 8]
                slot = h % 8
                pth = PT[:, h * 128:(h + 1) * 128]
                p.op("tensor", lambda e, pav=pav, slot=slot, pth=pth, h=h, gt=gt, VCH=VCH: e.matmul(
                    pav[:, slot * 64:(slot + 1) * 64], pth, VCH[:, h * 64:(h + 1) * 64],
                    start=(gt == 0 and slot == 0), stop=(gt == ngt - 1 and slot == 7)))
                p.op("tensor", lambda e, pth=pth, h=h, gt=gt: e.matmul(
                    psums[:, h:h + 1], pth, ones_col_bf[:],
                    start=(gt == 0 and h == 0), stop=(gt == ngt - 1 and h == NH - 1)))
            usemark[par] = p.mark()
        p.op("vector", lambda e: e.tensor_scalar_add(rcp[:], psums, 1e-30))
        p.op("vector", lambda e: e.reciprocal(rcp[:], rcp[:]))
        for h in range(NH):
            pav = pAV[h // 8]
            slot = h % 8
            p.op("vector", lambda e, pav=pav, slot=slot, h=h: e.tensor_scalar(
                RES[:, h * 64:(h + 1) * 64], pav[:, slot * 64:(slot + 1) * 64],
                rcp[:, h:h + 1], None, ALU.mult))
        for c in range(CH):
            ptr = p.pb[0][:, 0:128]
            p.op("tensor", lambda e, c=c: e.tensor.transpose(ptr, RES[:, c * 128:(c + 1) * 128], ident_bf[:])
                 if False else e.transpose(ptr, RES[:, c * 128:(c + 1) * 128], ident_bf[:]))
            p.op("vector", lambda e, c=c, q0=q0: e.tensor_copy(
                out=ATT[:, c * T + q0: c * T + q0 + 128], in_=ptr))
    if DBG.get("l3_dump"):
        att_d = p.dram(pref + "att_d", [D, T], F32, kind="ExternalOutput")
        for c in range(CH):
            p.op("vector", lambda e, c=c: e.tensor_copy(out=cm.SC1[:], in_=ATT[:, c * T: c * T + 512]))
            p.op("sync", lambda e, c=c: e.dma_start(out=att_d[c * 128:(c + 1) * 128, 0:512], in_=cm.SC1[:]), dma=True)
        for c in range(CH):
            for tb in range(1, 4):
                p.op("vector", lambda e, c=c, tb=tb: e.tensor_copy(
                    out=cm.SC1[:], in_=ATT[:, c * T + tb * 512: c * T + tb * 512 + 512]))
                p.op("sync", lambda e, c=c, tb=tb: e.dma_start(
                    out=att_d[c * 128:(c + 1) * 128, tb * 512:(tb + 1) * 512], in_=cm.SC1[:]), dma=True)
    p.free_prefix(pref + "t2")
    if DBG.get("l3_stop") == "C":
        p.free_prefix(pref)
        return

    WT2 = p.sb(pref + "t3_WT", [128, CH * D], BF16)
    brow2 = p.sb(pref + "t3_brow", [1, D], F32)
    xsl = p.sb(pref + "t3_xsl", [128, 512], F32)
    p.op("gpsimd", lambda e: e.dma_start(
        out=WT2[:].rearrange("p (c f) -> p c f", f=D),
        in_=ins["L3_attn_WoT_host"].rearrange("(c p) f -> p c f", p=128)), dma=True)
    p.op("sync", lambda e: e.dma_start(
        out=brow2[:], in_=ins["L3_attn_bo"].rearrange("(u f) -> u f", u=1)), dma=True)
    for oc in range(CH):
        for tb in range(4):
            t0 = tb * 512
            ph = p.pf[tb % 2]
            p.op("tensor", lambda e, ph=ph, oc=oc: e.matmul(
                ph[:], brow2[:, oc * 128:(oc + 1) * 128], C["c_ones512_f"][:],
                start=True, stop=False))
            for c in range(CH):
                p.op("tensor", lambda e, ph=ph, oc=oc, c=c, t0=t0: e.matmul(
                    ph[:], WT2[:, c * D + oc * 128: c * D + (oc + 1) * 128],
                    ATT[:, c * T + t0: c * T + t0 + 512], start=False, stop=(c == CH - 1)))
            p.op("sync", lambda e, oc=oc, t0=t0: e.dma_start(
                out=xsl[:], in_=xin_d[oc * 128:(oc + 1) * 128, t0:t0 + 512]), dma=True)
            p.op("vector", lambda e, ph=ph: e.tensor_add(xsl[:], xsl[:], ph[:]))
            p.op("sync", lambda e, oc=oc, t0=t0: e.dma_start(
                out=xout_d[oc * 128:(oc + 1) * 128, t0:t0 + 512], in_=xsl[:]), dma=True)
    p.free_prefix(pref)


def emit_L2_attn(cm: Common, pref="l2a"):
    p, C, ins = cm.p, cm.consts, cm.ins
    xin_d = cm.x_in
    xout_d = cm.x_out
    ones_row_f = C["c_ones_row_f"]
    ones_col_f = C["c_ones_col_f"]
    ones_col_bf = C["c_ones_col_bf"]
    ones512_f = C["c_ones512_f"]
    ident_f = C["c_ident_f"]
    RG = [[0, 1], [2, 3], [4, 5], [6, 7]]
    KD = D + 128

    ccK_in = p.dram(pref + "ccK_in", [KD, T], BF16)
    ccK_out = p.dram(pref + "ccK_out", [2 * KD, T], BF16)
    ccV_in = p.dram(pref + "ccV_in", [T, D], BF16)
    ccV_out = p.dram(pref + "ccV_out", [2 * T, D], BF16)
    isc_d = p.dram(pref + "isc_d", [T, S], FP16)

    QB = p.sb(pref + "QB", [128, CH * T], BF16)
    NM = p.sb(pref + "NM", [128, 1], F32)
    THR = p.sb(pref + "THR", [128, NT], F32)

    iqB = p.sb(pref + "t2_iq", [128, 2 * T], BF16)
    iq2 = p.sb(pref + "t2_iq2", [64, 4 * T], BF16)
    IWT = p.sb(pref + "t2_iwt", [128, NT * 4], F32)
    ikg = p.sb(pref + "t2_ikg", [64, S], BF16)
    kqn = p.sb(pref + "t2_kqn", [1, T], F32)

    cm.LNX = p.sb("z2_LNX", [128, CH * T], BF16)
    _LNXB = cm.LNX
    _XSB = cm.XS
    def lnxs(c, t0=0, n=T, _b=_LNXB):
        return _b[:, c * T + t0: c * T + t0 + n]
    def xss(c, t0=0, n=T, _b=_XSB):
        return _b[:, c * T + t0: c * T + t0 + n]
    cm.XS = p.sb("z3_XS", [128, CH * T], F32)
    cm.load_X(xin_d)
    cm.layernorm()
    p.free_prefix("z3_XS")

    WT = p.sb(pref + "t1_WT", [128, CH * D], BF16)
    brow = p.sb(pref + "t1_brow", [1, D], F32)
    KVB = p.sb(pref + "t1_KVB", [128, CH * T], BF16)
    cos2 = p.sb(pref + "t1_cos", [128, 4 * T], BF16)
    sin2 = p.sb(pref + "t1_sin", [128, 4 * T], BF16)

    p.op("gpsimd", lambda e: e.dma_start(out=cos2[:], in_=ins["c_l2cos"][:, :]), dma=True)
    p.op("gpsimd", lambda e: e.dma_start(out=sin2[:], in_=ins["c_l2sin"][:, :]), dma=True)

    def proj_fm(wname, brow_src, dstbuf, fw):
        p.op("gpsimd", lambda e, wname=wname, fw=fw: e.dma_start(
            out=WT[:, 0:fw * CH].rearrange("p (c f) -> p c f", f=fw),
            in_=ins[wname].rearrange("(c p) f -> p c f", p=128)), dma=True)
        p.op("sync", lambda e, fw=fw: e.dma_start(
            out=brow[:, 0:fw], in_=brow_src), dma=True)
        ocs = (fw + 127) // 128
        bm = {}
        for oc in range(ocs):
            npart = min(128, fw - oc * 128)
            for tb in range(4):
                t0 = tb * 512
                ph = p.pf[tb % 2]
                dst_ps = ph[0:npart, :]
                first = (oc == 0 and tb < 2)
                p.op("tensor", lambda e, dst_ps=dst_ps, oc=oc, npart=npart: e.matmul(
                    dst_ps, brow[:, oc * 128: oc * 128 + npart], ones512_f[:],
                    start=True, stop=False),
                    wait=(None if first else bm.get(tb % 2)), inc=False)
                for c in range(CH):
                    p.op("tensor", lambda e, dst_ps=dst_ps, oc=oc, c=c, t0=t0, fw=fw, npart=npart: e.matmul(
                        dst_ps, WT[:, c * fw + oc * 128: c * fw + oc * 128 + npart],
                        lnxs(c, t0, 512), start=False, stop=(c == CH - 1)),
                        wait=False, inc=(c == CH - 1))
                p.op("vector", lambda e, dst_ps=dst_ps, oc=oc, t0=t0, npart=npart, dstbuf=dstbuf: e.tensor_copy(
                    out=dstbuf[0:npart, oc * T + t0: oc * T + t0 + 512], in_=dst_ps))
                bm[tb % 2] = p.mark()

    def rope2(buf):
        for c in range(4):
            for tb in range(4):
                t0 = tb * 512
                a = buf[:, c * T + t0: c * T + t0 + 512]
                b = buf[:, (c + 4) * T + t0: (c + 4) * T + t0 + 512]
                cs = cos2[:, c * T + t0: c * T + t0 + 512]
                sn = sin2[:, c * T + t0: c * T + t0 + 512]
                p.op("vector", lambda e, a=a, cs=cs: e.tensor_mul(cm.SC1[:], a, cs))
                p.op("vector", lambda e, b=b, sn=sn: e.tensor_mul(cm.SC2[:], b, sn))
                p.op("vector", lambda e: e.tensor_sub(cm.SC1[:], cm.SC1[:], cm.SC2[:]))
                p.op("vector", lambda e, a=a, sn=sn: e.tensor_mul(cm.SC2[:], a, sn))
                p.op("vector", lambda e, a=a: e.tensor_copy(out=a, in_=cm.SC1[:]))
                p.op("vector", lambda e, b=b, cs=cs: e.tensor_mul(cm.SC1[:], b, cs))
                p.op("vector", lambda e, b=b: e.tensor_add(b, cm.SC1[:], cm.SC2[:]))

    proj_fm("L2_wqT_host", ins["b_q_row"][:, :], QB, D)
    rope2(QB)
    proj_fm("L2_wkT_host", ins["b_k_row"][:, :], KVB, D)
    rope2(KVB)
    pn = p.pf[4][0:1, :]
    for tb in range(4):
        t0 = tb * 512
        for c in range(CH):
            sl = KVB[:, c * T + t0: c * T + t0 + 512]
            p.op("scalar", lambda e, sl=sl: e.activation(cm.SC1[:], sl, AF.Square))
            p.op("tensor", lambda e, c=c: e.matmul(pn, ones_col_f[:], cm.SC1[:],
                                                   start=(c == 0), stop=(c == CH - 1)))
        p.op("vector", lambda e, t0=t0: e.tensor_copy(out=kqn[:, t0:t0 + 512], in_=pn))
    p.op("vector", lambda e: e.reduce_max(cm.SROW[:, 0:1], kqn[:], axis=AXX))
    for c in range(CH):
        p.op("sync", lambda e, c=c: e.dma_start(
            out=ccK_in[c * 128:(c + 1) * 128, :], in_=KVB[:, c * T:(c + 1) * T]), dma=True)
    proj_fm("L2_wiqT_host", ins["b_iq_row"][:, :], iqB, 256)
    for h in range(4):
        coff, off = (h // 2) * T, (h % 2) * 64
        for tb in range(4):
            t0 = tb * 512
            p.op("vector", lambda e, h=h, coff=coff, off=off, t0=t0: e.tensor_copy(
                out=iq2[0:64, h * T + t0: h * T + t0 + 512],
                in_=iqB[off:off + 64, coff + t0: coff + t0 + 512]))
    ikB = KVB
    proj_fm("L2_wikT_host", ins["b_ik_row"][:, :], ikB, 64)
    p.op("sync", lambda e: e.dma_start(
        out=ccK_in[D:D + 64, :], in_=ikB[0:64, 0:T]), dma=True)
    p.op("gpsimd", lambda e: e.dma_start(out=ccK_in[D + 64:D + 65, 0:1],
                                         in_=cm.SROW[:, 0:1]), dma=True)
    p.op("gpsimd", lambda e: e.dma_start(
        out=WT[:, 0:CH * 4].rearrange("p (c f) -> p c f", f=4),
        in_=ins["L2_wiwT_host"].rearrange("(c p) f -> p c f", p=128)), dma=True)
    p.op("sync", lambda e: e.dma_start(out=brow[:, 0:4], in_=ins["b_iw_row"][:, :]), dma=True)
    for i in range(NT):
        t0 = i * 128
        psw = p.pf[2][:, 0:4]
        p.op("tensor", lambda e: e.matmul(psw, C["c_ones_row_bf"][:] if False else ones_row_f[:],
                                          brow[:, 0:4], start=True, stop=False))
        for c in range(CH):
            p.op("tensor", lambda e, c=c, t0=t0: e.matmul(
                psw, lnxs(c, t0, 128), WT[:, c * 4:(c + 1) * 4],
                start=False, stop=(c == CH - 1)))
        p.op("vector", lambda e, i=i: e.tensor_copy(out=IWT[:, i * 4:(i + 1) * 4], in_=psw))
    p.op("gpsimd", lambda e: e.dma_start(
        out=WT[:].rearrange("p (c f) -> p c f", f=D),
        in_=ins["L2_wvT_host"].rearrange("(c p) f -> p c f", p=128)), dma=True)
    p.op("sync", lambda e: e.dma_start(out=brow[:, 0:D], in_=ins["b_v_row"][:, :]), dma=True)
    bmv2 = {}
    vship = None
    for i in range(NT):
        t0 = i * 128
        for ob in range(2):
            ph = p.pf[ob % 2]
            p.op("tensor", lambda e, ph=ph, ob=ob: e.matmul(
                ph[:], ones_row_f[:], brow[:, ob * 512:(ob + 1) * 512], start=True, stop=False),
                wait=(None if i == 0 else bmv2.get(ob % 2)), inc=False)
            for c in range(CH):
                p.op("tensor", lambda e, ph=ph, ob=ob, c=c, t0=t0: e.matmul(
                    ph[:], lnxs(c, t0, 128),
                    WT[:, c * D + ob * 512: c * D + (ob + 1) * 512],
                    start=False, stop=(c == CH - 1)),
                    wait=False, inc=(c == CH - 1))
            p.op("vector", lambda e, ph=ph, ob=ob: e.tensor_copy(
                out=KVB[:, ob * 512:(ob + 1) * 512], in_=ph[:]),
                wait=(None if vship is None else [vship, p.mark()]))
            bmv2[ob % 2] = p.mark()
        p.op("sync", lambda e, i=i: e.dma_start(
            out=ccV_in[i * 128:(i + 1) * 128, :], in_=KVB[:, 0:D]), dma=True)
        vship = p.mark()
    for tb in range(4):
        t0 = tb * 512
        for c in range(CH):
            sl = QB[:, c * T + t0: c * T + t0 + 512]
            p.op("scalar", lambda e, sl=sl: e.activation(cm.SC1[:], sl, AF.Square))
            p.op("tensor", lambda e, c=c: e.matmul(pn, ones_col_f[:], cm.SC1[:],
                                                   start=(c == 0), stop=(c == CH - 1)))
        p.op("vector", lambda e, t0=t0: e.tensor_copy(out=kqn[:, t0:t0 + 512], in_=pn))
    p.op("vector", lambda e: e.reduce_max(cm.SROW[:, 1:2], kqn[:], axis=AXX))
    p.op_collective(lambda e: e.collective_compute(
        "AllGather", ALU.bypass, replica_groups=RG,
        ins=[ccK_in.opt()], outs=[ccK_out.opt()]))
    p.op_collective(lambda e: e.collective_compute(
        "AllGather", ALU.bypass, replica_groups=RG,
        ins=[ccV_in.opt()], outs=[ccV_out.opt()]))
    p.op("gpsimd", lambda e: e.dma_start(out=cm.SROW[:, 2:3], in_=ccK_out[D + 64:D + 65, 0:1]), dma=True)
    p.op("gpsimd", lambda e: e.dma_start(out=cm.SROW[:, 3:4], in_=ccK_out[KD + D + 64:KD + D + 65, 0:1]), dma=True)
    p.op("vector", lambda e: e.tensor_max(cm.SROW[:, 2:3], cm.SROW[:, 2:3], cm.SROW[:, 3:4]))
    p.op("scalar", lambda e: e.activation(cm.SROW[:, 1:3], cm.SROW[:, 1:3], AF.Sqrt))
    p.op("vector", lambda e: e.tensor_mul(cm.SROW[:, 1:2], cm.SROW[:, 1:2], cm.SROW[:, 2:3]))
    p.op("vector", lambda e: e.tensor_scalar_mul(cm.SROW[:, 1:2], cm.SROW[:, 1:2], -1.0 / 32.0))
    pm = p.pf[4][:, 0:1]
    p.op("tensor", lambda e: e.matmul(pm, ones_row_f[:], cm.SROW[:, 1:2], start=True, stop=True))
    p.op("vector", lambda e: e.tensor_copy(out=NM[:], in_=pm))
    for gt in range(32):
        r, lt = gt % 2, gt // 2
        p.op("gpsimd", lambda e, r=r, lt=lt, gt=gt: e.dma_start(
            out=ikg[:, gt * 128:(gt + 1) * 128],
            in_=ccK_out[r * KD + D: r * KD + D + 64, lt * 128:(lt + 1) * 128]), dma=True)
    p.free_prefix(pref + "t1")
    cm.p.free_prefix("z2_LNX")

    ISC = p.sb(pref + "t3_ISC", [128, 8 * S], FP16)
    DUMP = p.sb(pref + "t3_DUMP", [128, S], U8)
    bis = p.sb(pref + "t3_bis", [128, 8], F32)
    bism = p.sb(pref + "t3_bism", [128, 1], U8)
    for half in range(2):
        for iq_i in range(8):
            i = half * 8 + iq_i
            q0 = i * 128
            pst = p.pf[0]
            isc_i = ISC[:, iq_i * S:(iq_i + 1) * S]
            iscm = None
            for gt in range(32):
                for h in range(4):
                    p.op("tensor", lambda e, pst=pst, h=h, q0=q0, gt=gt: e.matmul(
                        pst[:, h * 128:(h + 1) * 128],
                        iq2[0:64, h * T + q0: h * T + q0 + 128],
                        ikg[0:64, gt * 128:(gt + 1) * 128],
                        start=(h == 0), stop=(h == 3)),
                        wait=((None if iscm is None else iscm) if h == 0 else False),
                        inc=(h == 3))
                for h in range(4):
                    iwcol = IWT[:, i * 4 + h: i * 4 + h + 1]
                    if h == 0:
                        p.op("vector", lambda e, pst=pst, h=h, iwcol=iwcol: e.tensor_scalar(
                            cm.SC1[:, 0:128], pst[:, h * 128:(h + 1) * 128],
                            0.0, iwcol, ALU.max, ALU.mult))
                    else:
                        p.op("vector", lambda e, pst=pst, h=h, iwcol=iwcol: e.tensor_scalar(
                            cm.SC2[:, 0:128], pst[:, h * 128:(h + 1) * 128],
                            0.0, iwcol, ALU.max, ALU.mult))
                        p.op("vector", lambda e: e.tensor_add(
                            cm.SC1[:, 0:128], cm.SC1[:, 0:128], cm.SC2[:, 0:128]))
                p.op("vector", lambda e, isc_i=isc_i, gt=gt: e.tensor_copy(
                    out=isc_i[:, gt * 128:(gt + 1) * 128], in_=cm.SC1[:, 0:128]))
                iscm = p.mark()
            p.op("sync", lambda e, isc_i=isc_i, q0=q0: e.dma_start(
                out=isc_d[q0:q0 + 128, :], in_=isc_i[:]), dma=True)
            p.op("vector", lambda e, isc_i=isc_i: e.reduce_max(bis[:, 1:2], isc_i[:], axis=AXX))
            p.op("vector", lambda e, isc_i=isc_i: e.tensor_reduce(
                bis[:, 0:1], isc_i[:], AXX, ALU.min))
            for it in range(30):
                p.op("vector", lambda e: e.tensor_add(bis[:, 2:3], bis[:, 0:1], bis[:, 1:2]),
                     wait=(None if it == 0 else False), inc=False)
                p.op("vector", lambda e: e.tensor_scalar_mul(bis[:, 2:3], bis[:, 2:3], 0.5),
                     wait=False, inc=False)
                p.op("vector", lambda e, isc_i=isc_i: e.tensor_scalar(
                    DUMP[:], isc_i[:], bis[:, 2:3], 0.0, ALU.is_ge, ALU.add,
                    accum_out=bis[:, 3:4]), wait=False, inc=False)
                p.op("vector", lambda e: e.tensor_scalar(
                    bism[:], bis[:, 3:4], 256.0, None, ALU.is_ge), wait=False, inc=False)
                p.op("vector", lambda e: e.select(bis[:, 0:1], bism[:], bis[:, 2:3], bis[:, 0:1]),
                     wait=False, inc=False)
                p.op("vector", lambda e: e.select(bis[:, 1:2], bism[:], bis[:, 1:2], bis[:, 2:3]),
                     wait=False, inc=False)
            p.op("vector", lambda e, i=i: e.tensor_copy(out=THR[:, i:i + 1], in_=bis[:, 0:1]),
                 wait=False, inc=True)
    p.free_prefix(pref + "t3")
    cm.p.free_prefix("z")

    THRB = p.sb(pref + "t4_THRB", [128, T], F32)
    ATT = p.sb(pref + "t4_ATT", [128, CH * T], BF16)
    PTA = p.sb(pref + "t4_PTA", [128, 32 * 512], BF16)
    KCH2 = [p.sb(pref + "t4_KCHa", [128, CH * 128], BF16),
            p.sb(pref + "t4_KCHb", [128, CH * 128], BF16)]
    VCH2 = [p.sb(pref + "t4_VCHa", [128, D], BF16),
            p.sb(pref + "t4_VCHb", [128, D], BF16)]
    ISCT2 = [p.sb(pref + "t4_ISCTa", [128, 512], FP16),
             p.sb(pref + "t4_ISCTb", [128, 512], FP16)]
    MT = p.sb(pref + "t4_MT", [128, 512], U8)
    NEG = p.sb(pref + "t4_NEG", [128, 512], F32)
    SUMR = p.sb(pref + "t4_SUMR", [1, 512], F32)
    thrrow = p.sb(pref + "t4_thrr", [1, T], F32)
    p.op("vector", lambda e: e.memset(NEG[:], -1e30))
    pr = p.pf[1][0:1, 0:128]
    for i in range(NT):
        p.op("tensor", lambda e, i=i: e.matmul(
            pr, THR[:, i:i + 1], ident_f[:], start=True, stop=True))
        p.op("vector", lambda e, i=i: e.tensor_copy(
            out=thrrow[:, i * 128:(i + 1) * 128], in_=pr))
    for tb in range(4):
        pb = p.pf[2]
        p.op("tensor", lambda e, tb=tb: e.matmul(
            pb[:], ones_row_f[:], thrrow[:, tb * 512:(tb + 1) * 512], start=True, stop=True))
        p.op("vector", lambda e, tb=tb: e.tensor_copy(out=THRB[:, tb * 512:(tb + 1) * 512], in_=pb[:]))
    ccK4 = ccK_out.rearrange("(r c p) t -> r p c t", p=128, c=9)
    kuse = [None, None]
    for qg in range(4):
        g0 = qg * 512
        for gt in range(32):
            r, lt = gt % 2, gt // 2
            par = gt % 2
            KCH = KCH2[par]
            ISCT = ISCT2[par]
            pS = p.pf[4 + par]
            p.op("gpsimd", lambda e, r=r, lt=lt, KCH=KCH: e.dma_start(
                out=KCH[:].rearrange("p (c t) -> p c t", t=128),
                in_=ccK4[r][:, 0:CH, lt * 128:(lt + 1) * 128]), dma=True)
            p.op("sync", lambda e, gt=gt, g0=g0, ISCT=ISCT: e.dma_start_transpose(
                out=ISCT[:], in_=isc_d[g0:g0 + 512, gt * 128:(gt + 1) * 128]), dma=True)
            for c in range(CH):
                p.op("tensor", lambda e, pS=pS, c=c, g0=g0, KCH=KCH: e.matmul(
                    pS[:], KCH[:, c * 128:(c + 1) * 128],
                    QB[:, c * T + g0: c * T + g0 + 512],
                    start=(c == 0), stop=(c == CH - 1)),
                    wait=(None if c == 0 else False), inc=(c == CH - 1))
            p.op("vector", lambda e, g0=g0, ISCT=ISCT: e.tensor_tensor(
                MT[:], ISCT[:], THRB[:, g0:g0 + 512], ALU.is_ge))
            p.op("vector", lambda e, pS=pS: e.select(cm.SC1[:], MT[:], pS[:], NEG[:]))
            p.op("scalar", lambda e, gt=gt: e.activation(
                PTA[:, gt * 512:(gt + 1) * 512], cm.SC1[:], AF.Exp,
                bias=NM[:, 0:1], scale=1.0 / 32.0))
            kuse[par] = p.mark()
        psr = p.pf[4][0:1, :]
        for gt in range(32):
            p.op("tensor", lambda e, gt=gt: e.matmul(
                psr, ones_col_bf[:], PTA[:, gt * 512:(gt + 1) * 512],
                start=(gt == 0), stop=(gt == 31)),
                wait=(None if gt == 0 else False), inc=(gt == 31))
        p.op("vector", lambda e: e.tensor_copy(out=SUMR[:], in_=psr))
        p.op("vector", lambda e: e.tensor_scalar_add(SUMR[:], SUMR[:], 1e-30))
        p.op("vector", lambda e: e.reciprocal(SUMR[:], SUMR[:]))
        prb = p.pf[5]
        p.op("tensor", lambda e: e.matmul(prb[:], ones_row_f[:], SUMR[:], start=True, stop=True))
        p.op("vector", lambda e: e.tensor_copy(out=cm.SC2[:], in_=prb[:]))
        vuse = [None, None]
        for rnd in range(2):
            for gt in range(32):
                r, lt = gt % 2, gt // 2
                par = gt % 2
                VCH = VCH2[par]
                p.op("gpsimd", lambda e, r=r, lt=lt, VCH=VCH: e.dma_start(
                    out=VCH[:], in_=ccV_out[(r * NT + lt) * 128:(r * NT + lt + 1) * 128, :]), dma=True)
                for db in range(4):
                    dblk = rnd * 4 + db
                    pav = p.pf[db]
                    p.op("tensor", lambda e, pav=pav, dblk=dblk, gt=gt, VCH=VCH: e.matmul(
                        pav[:], VCH[:, dblk * 128:(dblk + 1) * 128],
                        PTA[:, gt * 512:(gt + 1) * 512],
                        start=(gt == 0), stop=(gt == 31)),
                        wait=(None if db == 0 else False), inc=(db == 3))
            for db in range(4):
                dblk = rnd * 4 + db
                pav = p.pf[db]
                p.op("vector", lambda e, pav=pav: e.tensor_copy(out=cm.SC1[:], in_=pav[:]))
                p.op("vector", lambda e, pav=pav, dblk=dblk, g0=g0: e.tensor_mul(
                    ATT[:, dblk * T + g0: dblk * T + g0 + 512], cm.SC1[:], cm.SC2[:]))
    WT2 = p.sb(pref + "t5_WT", [128, CH * D], BF16)
    brow2 = p.sb(pref + "t5_brow", [1, D], F32)
    xsl = p.sb(pref + "t5_xsl", [128, 512], F32)
    p.op("gpsimd", lambda e: e.dma_start(
        out=WT2[:].rearrange("p (c f) -> p c f", f=D),
        in_=ins["L2_woT_host"].rearrange("(c p) f -> p c f", p=128)), dma=True)
    p.op("sync", lambda e: e.dma_start(
        out=brow2[:], in_=ins["b_o_row"][:, :]), dma=True)
    for oc in range(CH):
        for tb in range(4):
            t0 = tb * 512
            ph = p.pf[tb % 2]
            p.op("tensor", lambda e, ph=ph, oc=oc: e.matmul(
                ph[:], brow2[:, oc * 128:(oc + 1) * 128], ones512_f[:],
                start=True, stop=False))
            for c in range(CH):
                p.op("tensor", lambda e, ph=ph, oc=oc, c=c, t0=t0: e.matmul(
                    ph[:], WT2[:, c * D + oc * 128: c * D + (oc + 1) * 128],
                    ATT[:, c * T + t0: c * T + t0 + 512], start=False, stop=(c == CH - 1)))
            p.op("sync", lambda e, oc=oc, t0=t0: e.dma_start(
                out=xsl[:], in_=xin_d[oc * 128:(oc + 1) * 128, t0:t0 + 512]), dma=True)
            p.op("vector", lambda e, ph=ph: e.tensor_add(xsl[:], xsl[:], ph[:]))
            p.op("sync", lambda e, oc=oc, t0=t0: e.dma_start(
                out=xout_d[oc * 128:(oc + 1) * 128, t0:t0 + 512], in_=xsl[:]), dma=True)
    p.free_prefix(pref)


def emit_L0_attn(cm: Common, pref="l0a"):
    p, C, ins = cm.p, cm.consts, cm.ins
    xin_d = cm.x_in
    xout_d = cm.x_out
    NH = 16
    ones_row_f = C["c_ones_row_f"]
    ones_col_f = C["c_ones_col_f"]
    ones_col_bf = C["c_ones_col_bf"]
    ones512_f = C["c_ones512_f"]
    ident_bf = C["c_ident_bf"]
    RG = [[0, 1], [2, 3], [4, 5], [6, 7]]
    EPS = 1e-6

    qt_d = p.dram(pref + "qt_d", [2048, T], BF16)
    cc_in = p.dram(pref + "cc_in", [T, 1040], BF16)
    cc_out = p.dram(pref + "cc_out", [2 * T, 1040], BF16)

    H = p.sb(pref + "H", [128, CH * T], F32)
    n2w = p.sb(pref + "n2w", [128, CH], F32)
    CTX = p.sb(pref + "g1_CTX", [128, NT * 1040], BF16)
    STATE = p.sb(pref + "g1_STATE", [128, NH * 64], F32)
    STATEN = p.sb(pref + "g1_STATEN", [128, NH], F32)
    cosb = p.sb(pref + "g1_csb", [128, T], BF16)
    sinb = cosb
    tri = p.sb(pref + "g1_tri", [128, 128], BF16)
    n1w = p.sb(pref + "g1_n1w", [128, CH], F32)
    mW0 = p.sb(pref + "g1_mW0", [128, 1], F32)
    mW1 = p.sb(pref + "g1_mW1", [128, 1], F32)
    p.op("gpsimd", lambda e: e.dma_start(out=cosb[:], in_=ins["c_l0cs"][:, :]), dma=True)
    p.op("gpsimd", lambda e: e.dma_start(out=tri[:], in_=ins["c_l0tri"][:, :]), dma=True)
    p.op("sync", lambda e: e.dma_start(out=n1w[:], in_=ins["n1w_col"][:, :]), dma=True)
    p.op("sync", lambda e: e.dma_start(out=n2w[:], in_=ins["n2w_col"][:, :]), dma=True)
    p.op("sync", lambda e: e.dma_start(out=mW0[:], in_=ins["c_mW0"][:, :]), dma=True)
    p.op("sync", lambda e: e.dma_start(out=mW1[:], in_=ins["c_mW1"][:, :]), dma=True)
    p.op("vector", lambda e: e.memset(STATE[:], 0.0))
    p.op("vector", lambda e: e.memset(STATEN[:], 0.0))

    cm.LNX = p.sb("z2_LNX", [128, CH * T], BF16)
    _LNXB = cm.LNX
    _XSB = cm.XS
    def lnxs(c, t0=0, n=T, _b=_LNXB):
        return _b[:, c * T + t0: c * T + t0 + n]
    def xss(c, t0=0, n=T, _b=_XSB):
        return _b[:, c * T + t0: c * T + t0 + n]
    XB = p.sb("z3_XB", [128, 512], F32)
    psA = p.pf[4][0:1, :]
    psB = p.pf[5][0:1, :]
    pbA = p.pf[2]
    pbB = p.pf[3]
    for tb in range(4):
        t0 = tb * 512
        for c in range(CH):
            p.op("sync", lambda e, c=c, t0=t0: e.dma_start(
                out=XB[:], in_=xin_d[c * 128:(c + 1) * 128, t0:t0 + 512]), dma=True)
            p.op("scalar", lambda e: e.activation(cm.SC1[:], XB[:], AF.Square))
            p.op("tensor", lambda e, c=c: e.matmul(psA, C["c_ones_col_f"][:], XB[:],
                                                   start=(c == 0), stop=(c == CH - 1)))
            p.op("tensor", lambda e, c=c: e.matmul(psB, C["c_ones_col_f"][:], cm.SC1[:],
                                                   start=(c == 0), stop=(c == CH - 1)))
        rr = cm.SROW[:, 0:512]
        rn = cm.SROW[:, 512:1024]
        p.op("vector", lambda e, rn=rn: e.tensor_scalar_mul(rn, psA, 1.0 / D))
        p.op("vector", lambda e, rr=rr: e.tensor_scalar_mul(rr, psB, 1.0 / D))
        p.op("vector", lambda e, rn=rn: e.scalar_tensor_tensor(
            cm.SC1[0:1, :], rn, 1.0, rn, ALU.mult, ALU.mult))
        p.op("vector", lambda e, rr=rr: e.tensor_sub(rr, rr, cm.SC1[0:1, :]))
        p.op("vector", lambda e, rr=rr: e.tensor_scalar_add(rr, rr, 1e-5))
        p.op("scalar", lambda e, rr=rr: e.activation(rr, rr, AF.Sqrt))
        p.op("vector", lambda e, rr=rr: e.reciprocal(rr, rr))
        p.op("vector", lambda e, rr=rr, rn=rn: e.scalar_tensor_tensor(
            rn, rn, -1.0, rr, ALU.mult, ALU.mult))
        p.op("tensor", lambda e, rr=rr: e.matmul(pbA[:], C["c_ones_row_f"][:], rr, start=True, stop=True))
        p.op("tensor", lambda e, rn=rn: e.matmul(pbB[:], C["c_ones_row_f"][:], rn, start=True, stop=True))
        for c in range(CH):
            p.op("sync", lambda e, c=c, t0=t0: e.dma_start(
                out=XB[:], in_=xin_d[c * 128:(c + 1) * 128, t0:t0 + 512]), dma=True)
            p.op("vector", lambda e: e.tensor_mul(cm.SC2[:], XB[:], pbA[:]))
            dst = lnxs(c, t0, 512)
            hd = H[:, c * T + t0: c * T + t0 + 512]
            p.op("vector", lambda e, dst=dst: e.tensor_add(dst, cm.SC2[:], pbB[:]))
            p.op("vector", lambda e, hd=hd: e.tensor_add(hd, cm.SC2[:], pbB[:]))
    p.free_prefix("z3_XB")

    psum_r = p.pf[4][0:1, :]
    pb_r = p.pf[2]
    for tb in range(4):
        t0 = tb * 512
        for c in range(CH):
            sl = lnxs(c, t0, 512)
            p.op("scalar", lambda e, sl=sl: e.activation(cm.SC1[:], sl, AF.Square))
            p.op("tensor", lambda e, c=c: e.matmul(psum_r, ones_col_f[:], cm.SC1[:],
                                                   start=(c == 0), stop=(c == CH - 1)))
        rr = cm.SROW[:, 0:512]
        p.op("vector", lambda e, rr=rr: e.tensor_scalar_mul(rr, psum_r, 1.0 / D))
        p.op("vector", lambda e, rr=rr: e.tensor_scalar_add(rr, rr, EPS))
        p.op("scalar", lambda e, rr=rr: e.activation(rr, rr, AF.Sqrt))
        p.op("vector", lambda e, rr=rr: e.reciprocal(rr, rr))
        p.op("tensor", lambda e, rr=rr: e.matmul(pb_r[:], ones_row_f[:], rr, start=True, stop=True))
        for c in range(CH):
            sl = lnxs(c, t0, 512)
            p.op("vector", lambda e, sl=sl: e.tensor_mul(sl, sl, pb_r[:]))

    WQKV = p.sb(pref + "t1_W", [128, CH * 512], BF16)
    bqp = p.sb(pref + "t1_bqp", [1, D], F32)
    qg = p.sb(pref + "t1_qg", [128, NH * 512], BF16)
    kg = p.sb(pref + "t1_kg", [128, CH * 512], BF16)
    vg = p.sb(pref + "t1_vg", [128, CH * 512], BF16)
    kt = p.sb(pref + "t1_kt", [128, NH * 128], BF16)
    ktm = p.sb(pref + "t1_ktm", [128, NH * 128], BF16)
    vtm = p.sb(pref + "t1_vtm", [128, D], BF16)
    dse = p.sb(pref + "t1_dse", [128, 1040], BF16)
    STB = p.sb(pref + "t1_STB", [128, NH * 64], BF16)
    STBN = p.sb(pref + "t1_STBN", [128, NH], BF16)
    PTb = p.sb(pref + "t1_PT", [128, 128], BF16)

    qkv_w3 = ins["L0_attn_qkv_w"].rearrange("(c p) (w f) -> w p c f", p=128, w=3)
    qkvm = {}
    for g in range(4):
        t0 = g * 512
        for part, dst in ((0, None), (1, kg), (2, vg)):
            p.op("sync", lambda e, part=part: e.dma_start(
                out=bqp[:], in_=ins["qkv_b_row"][0:1, part * D:(part + 1) * D]), dma=True)
            for wh in range(2):
                p.op("gpsimd", lambda e, part=part, wh=wh: e.dma_start(
                    out=WQKV[:].rearrange("p (c f) -> p c f", f=512),
                    in_=qkv_w3[part][:, :, wh * 512:(wh + 1) * 512]), dma=True)
                for c in range(CH):
                    p.op("vector", lambda e, c=c: e.tensor_scalar(
                        WQKV[:, c * 512:(c + 1) * 512], WQKV[:, c * 512:(c + 1) * 512],
                        n1w[:, c:c + 1], None, ALU.mult))
                for loc in range(4):
                    oc = wh * 4 + loc
                    ph = p.pf[oc % 2]
                    first = (loc < 2)
                    p.op("tensor", lambda e, ph=ph, oc=oc: e.matmul(
                        ph[:], bqp[:, oc * 128:(oc + 1) * 128], ones512_f[:],
                        start=True, stop=False),
                        wait=(None if first else qkvm.get(oc % 2)), inc=False)
                    for c in range(CH):
                        p.op("tensor", lambda e, ph=ph, loc=loc, c=c, t0=t0: e.matmul(
                            ph[:], WQKV[:, c * 512 + loc * 128: c * 512 + loc * 128 + 128],
                            lnxs(c, t0, 512), start=False, stop=(c == CH - 1)),
                            wait=False, inc=(c == CH - 1))
                    if part == 0:
                        for sub in range(2):
                            h = oc * 2 + sub
                            off = sub * 64
                            p.op("vector", lambda e, ph=ph, off=off: e.tensor_copy(
                                out=cm.SC1[0:64, :], in_=ph[off:off + 64, :]))
                            p.op("vector", lambda e, h=h, t0=t0: e.scalar_tensor_tensor(
                                qg[0:64, h * 512:(h + 1) * 512], cm.SC1[0:64, :], 0.0,
                                cosb[0:64, t0:t0 + 512], ALU.max, ALU.mult))
                            p.op("vector", lambda e: e.tensor_copy(
                                out=cm.SC2[64:128, :], in_=cm.SC1[0:64, :]))
                            p.op("vector", lambda e, h=h, t0=t0: e.scalar_tensor_tensor(
                                qg[64:128, h * 512:(h + 1) * 512], cm.SC2[64:128, :], 0.0,
                                sinb[64:128, t0:t0 + 512], ALU.max, ALU.mult))
                        qkvm[oc % 2] = p.mark()
                    else:
                        p.op("vector", lambda e, ph=ph, dst=dst, oc=oc: e.tensor_copy(
                            out=dst[:, oc * 512:(oc + 1) * 512], in_=ph[:]))
                    qkvm[oc % 2] = p.mark()
        for hc in range(NH):
            p.op("sync", lambda e, hc=hc, t0=t0: e.dma_start(
                out=qt_d[hc * 128:(hc + 1) * 128, t0:t0 + 512],
                in_=qg[:, hc * 512:(hc + 1) * 512]), dma=True)
        for j in range(4):
            i = g * 4 + j
            j0 = j * 128
            for h in range(NH):
                c, off = h // 2, (h % 2) * 64
                p.op("vector", lambda e, c=c, off=off, j0=j0: e.tensor_copy(
                    out=cm.SC1[0:64, 0:128],
                    in_=kg[off:off + 64, c * 512 + j0: c * 512 + j0 + 128]))
                p.op("vector", lambda e, h=h, j0=j0, t0=t0: e.scalar_tensor_tensor(
                    kt[0:64, h * 128:(h + 1) * 128], cm.SC1[0:64, 0:128], 0.0,
                    cosb[0:64, t0 + j0: t0 + j0 + 128], ALU.max, ALU.mult))
                p.op("vector", lambda e: e.tensor_copy(out=cm.SC2[64:128, 0:128], in_=cm.SC1[0:64, 0:128]))
                p.op("vector", lambda e, h=h, j0=j0, t0=t0: e.scalar_tensor_tensor(
                    kt[64:128, h * 128:(h + 1) * 128], cm.SC2[64:128, 0:128], 0.0,
                    sinb[64:128, t0 + j0: t0 + j0 + 128], ALU.max, ALU.mult))
            for h in range(NH):
                ptr = p.pb[0][:, 0:128]
                p.op("tensor", lambda e, h=h: e.transpose(
                    ptr, kt[:, h * 128:(h + 1) * 128], ident_bf[:]))
                p.op("vector", lambda e, h=h: e.tensor_copy(
                    out=ktm[:, h * 128:(h + 1) * 128], in_=ptr))
            for c in range(CH):
                ptr = p.pb[0][:, 0:128]
                p.op("tensor", lambda e, c=c, j0=j0: e.transpose(
                    ptr, vg[:, c * 512 + j0: c * 512 + j0 + 128], ident_bf[:]))
                p.op("vector", lambda e, c=c: e.tensor_copy(
                    out=vtm[:, c * 128:(c + 1) * 128], in_=ptr))
            pctx = [p.pf[1], p.pf[2]]
            pctxn = p.pf[3][:, 0:16]
            p.op("vector", lambda e: e.tensor_copy(out=STB[:], in_=STATE[:]))
            p.op("vector", lambda e: e.tensor_copy(out=STBN[:], in_=STATEN[:]))
            for h in range(NH):
                pst = p.pf[0][:, 0:128]
                p.op("tensor", lambda e, pst=pst, h=h, j0=j0: e.matmul(
                    pst, kt[:, h * 128:(h + 1) * 128],
                    qg[:, h * 512 + j0: h * 512 + j0 + 128], start=True, stop=True))
                p.op("vector", lambda e, pst=pst: e.tensor_mul(PTb[:], pst, tri[:]))
                pc = pctx[h // 8]
                slot = h % 8
                p.op("tensor", lambda e, pc=pc, slot=slot, h=h: e.matmul(
                    pc[:, slot * 64:(slot + 1) * 64], PTb[:], vtm[:, h * 64:(h + 1) * 64],
                    start=(slot == 0), stop=False))
                p.op("tensor", lambda e, h=h: e.matmul(
                    pctxn[:, h:h + 1], PTb[:], ones_col_bf[:],
                    start=(h == 0), stop=False))
                p.op("tensor", lambda e, pc=pc, slot=slot, h=h, j0=j0: e.matmul(
                    pc[:, slot * 64:(slot + 1) * 64],
                    qg[:, h * 512 + j0: h * 512 + j0 + 128],
                    STB[:, h * 64:(h + 1) * 64], start=False, stop=(slot == 7)))
                p.op("tensor", lambda e, h=h, j0=j0: e.matmul(
                    pctxn[:, h:h + 1], qg[:, h * 512 + j0: h * 512 + j0 + 128],
                    STBN[:, h:h + 1], start=False, stop=(h == NH - 1)))
                pup = p.pf[4][:, 0:65]
                p.op("tensor", lambda e, pup=pup, h=h: e.matmul(
                    pup[:, 0:64], ktm[:, h * 128:(h + 1) * 128], vtm[:, h * 64:(h + 1) * 64],
                    start=True, stop=False))
                p.op("tensor", lambda e, pup=pup, h=h: e.matmul(
                    pup[:, 64:65], ktm[:, h * 128:(h + 1) * 128], ones_col_bf[:],
                    start=False, stop=True))
                p.op("vector", lambda e, pup=pup, h=h: e.tensor_copy(
                    out=dse[:, h * 64:(h + 1) * 64], in_=pup[:, 0:64]))
                p.op("vector", lambda e, pup=pup, h=h: e.tensor_copy(
                    out=dse[:, NH * 64 + h: NH * 64 + h + 1], in_=pup[:, 64:65]))
                p.op("vector", lambda e, pup=pup, h=h: e.tensor_add(
                    STATE[:, h * 64:(h + 1) * 64], STATE[:, h * 64:(h + 1) * 64], pup[:, 0:64]))
                p.op("vector", lambda e, pup=pup, h=h: e.tensor_add(
                    STATEN[:, h:h + 1], STATEN[:, h:h + 1], pup[:, 64:65]))
            for half in range(2):
                p.op("vector", lambda e, i=i, half=half: e.tensor_copy(
                    out=CTX[:, i * 1040 + half * 512: i * 1040 + (half + 1) * 512],
                    in_=pctx[half][:]))
            p.op("vector", lambda e, i=i: e.tensor_copy(
                out=CTX[:, i * 1040 + 1024: i * 1040 + 1040], in_=pctxn))
            p.op("sync", lambda e, i=i: e.dma_start(
                out=cc_in[i * 128:(i + 1) * 128, :], in_=dse[:]), dma=True)
    p.free_prefix(pref + "t1")
    cm.p.free_prefix("z2_LNX")
    p.op_collective(lambda e: e.collective_compute(
        "AllGather", ALU.bypass, replica_groups=RG,
        ins=[cc_in.opt()], outs=[cc_out.opt()]))

    SCOR = p.sb(pref + "t2_SCOR", [128, 1040], F32)
    SUSE = p.sb(pref + "t2_SUSE", [128, 1040], BF16)
    PD = p.sb(pref + "t2_PD", [128, 1040], F32)
    GB0 = p.sb(pref + "t2_GB0", [128, 1040], BF16)
    GB1 = p.sb(pref + "t2_GB1", [128, 1040], BF16)
    qtt = p.sb(pref + "t2_qtt", [128, NH * 128], BF16)
    OTM2 = p.sb(pref + "t2_OTM", [128, D], BF16)
    rcpn = p.sb(pref + "t2_rcp", [128, 16], F32)
    OFM = p.sb(pref + "t2_OFM", [128, CH * T], BF16)
    p.op("vector", lambda e: e.memset(SCOR[:], 0.0))
    for i in range(NT):
        p.op("gpsimd", lambda e, i=i: e.dma_start(
            out=GB0[:], in_=cc_out[i * 128:(i + 1) * 128, :]), dma=True)
        p.op("gpsimd", lambda e, i=i: e.dma_start(
            out=GB1[:], in_=cc_out[T + i * 128: T + (i + 1) * 128, :]), dma=True)
        p.op("gpsimd", lambda e, i=i: e.dma_start(
            out=qtt[:].rearrange("p (h t) -> p h t", t=128),
            in_=qt_d.rearrange("(h p) t -> p h t", p=128)[:, :, i * 128:(i + 1) * 128]), dma=True)
        p.op("vector", lambda e: e.tensor_scalar(PD[:], GB0[:], mW0[:, 0:1], None, ALU.mult))
        p.op("vector", lambda e: e.scalar_tensor_tensor(PD[:], GB1[:], mW1[:, 0:1], PD[:], ALU.mult, ALU.add))
        p.op("vector", lambda e: e.scalar_tensor_tensor(SUSE[:], PD[:], mW0[:, 0:1], SCOR[:], ALU.mult, ALU.add))
        p.op("vector", lambda e: e.tensor_add(SCOR[:], SCOR[:], PD[:]))
        pcor = [p.pf[1], p.pf[2]]
        pcorn = p.pf[3][:, 0:16]
        for h in range(NH):
            pc = pcor[h // 8]
            slot = h % 8
            p.op("tensor", lambda e, pc=pc, slot=slot, h=h: e.matmul(
                pc[:, slot * 64:(slot + 1) * 64], qtt[:, h * 128:(h + 1) * 128],
                SUSE[:, h * 64:(h + 1) * 64], start=(slot == 0), stop=(slot == 7)))
            p.op("tensor", lambda e, h=h: e.matmul(
                pcorn[:, h:h + 1], qtt[:, h * 128:(h + 1) * 128],
                SUSE[:, NH * 64 + h: NH * 64 + h + 1],
                start=(h == 0), stop=(h == NH - 1)))
        p.op("vector", lambda e, i=i: e.tensor_copy(out=cm.SC1[:, 0:16], in_=pcorn))
        p.op("vector", lambda e, i=i: e.tensor_add(
            rcpn[:], cm.SC1[:, 0:16], CTX[:, i * 1040 + 1024: i * 1040 + 1040]))
        p.op("vector", lambda e: e.tensor_scalar_add(rcpn[:], rcpn[:], EPS))
        p.op("vector", lambda e: e.reciprocal(rcpn[:], rcpn[:]))
        for half in range(2):
            p.op("vector", lambda e, i=i, half=half: e.tensor_copy(
                out=cm.SC1[:], in_=CTX[:, i * 1040 + half * 512: i * 1040 + (half + 1) * 512]))
            p.op("vector", lambda e, half=half: e.tensor_add(cm.SC1[:], cm.SC1[:], pcor[half][:]))
            for sh in range(8):
                h = half * 8 + sh
                p.op("vector", lambda e, h=h, sh=sh: e.tensor_scalar(
                    OTM2[:, h * 64:(h + 1) * 64], cm.SC1[:, sh * 64:(sh + 1) * 64],
                    rcpn[:, h:h + 1], None, ALU.mult))
        for c in range(CH):
            ptr = p.pb[0][:, 0:128]
            p.op("tensor", lambda e, c=c: e.transpose(
                ptr, OTM2[:, c * 128:(c + 1) * 128], ident_bf[:]))
            p.op("vector", lambda e, c=c, i=i: e.tensor_copy(
                out=OFM[:, c * T + i * 128: c * T + (i + 1) * 128], in_=ptr))
    WO = p.sb(pref + "t2_WO", [128, CH * D], BF16)
    brow3 = p.sb(pref + "t2_brow", [1, D], F32)
    p.op("gpsimd", lambda e: e.dma_start(
        out=WO[:].rearrange("p (c f) -> p c f", f=D),
        in_=ins["L0_attn_out_w"].rearrange("(c p) f -> p c f", p=128)), dma=True)
    p.op("sync", lambda e: e.dma_start(
        out=brow3[:], in_=ins["L0_attn_out_b"].rearrange("(u f) -> u f", u=1)), dma=True)
    for oc in range(CH):
        for tb in range(4):
            t0 = tb * 512
            ph = p.pf[tb % 2]
            p.op("tensor", lambda e, ph=ph, oc=oc: e.matmul(
                ph[:], brow3[:, oc * 128:(oc + 1) * 128], ones512_f[:],
                start=True, stop=False))
            for c in range(CH):
                p.op("tensor", lambda e, ph=ph, oc=oc, c=c, t0=t0: e.matmul(
                    ph[:], WO[:, c * D + oc * 128: c * D + (oc + 1) * 128],
                    OFM[:, c * T + t0: c * T + t0 + 512], start=False, stop=(c == CH - 1)))
            hsl = H[:, oc * T + t0: oc * T + t0 + 512]
            p.op("vector", lambda e, ph=ph, hsl=hsl: e.tensor_add(hsl, hsl, ph[:]))
    p.free_prefix(pref + "t2")
    p.free_prefix(pref + "g1")

    R = p.sb(pref + "t3_R", [128, CH * T], BF16)
    for tb in range(4):
        t0 = tb * 512
        for c in range(CH):
            sl = H[:, c * T + t0: c * T + t0 + 512]
            p.op("scalar", lambda e, sl=sl: e.activation(cm.SC1[:], sl, AF.Square))
            p.op("tensor", lambda e, c=c: e.matmul(psum_r, ones_col_f[:], cm.SC1[:],
                                                   start=(c == 0), stop=(c == CH - 1)))
        rr = cm.SROW[:, 0:512]
        p.op("vector", lambda e, rr=rr: e.tensor_scalar_mul(rr, psum_r, 1.0 / D))
        p.op("vector", lambda e, rr=rr: e.tensor_scalar_add(rr, rr, EPS))
        p.op("scalar", lambda e, rr=rr: e.activation(rr, rr, AF.Sqrt))
        p.op("vector", lambda e, rr=rr: e.reciprocal(rr, rr))
        p.op("tensor", lambda e, rr=rr: e.matmul(pb_r[:], ones_row_f[:], rr, start=True, stop=True))
        for c in range(CH):
            hsl = H[:, c * T + t0: c * T + t0 + 512]
            p.op("vector", lambda e, hsl=hsl, c=c, t0=t0: e.tensor_mul(
                R[:, c * T + t0: c * T + t0 + 512], hsl, pb_r[:]))
    W1 = p.sb(pref + "t3_W1", [128, CH * T], BF16)
    W2 = p.sb(pref + "t3_W2", [128, NT * D], BF16)
    b1r = p.sb(pref + "t3_b1", [1, 4096], F32)
    b2r = p.sb(pref + "t3_b2", [1, D], F32)
    h1 = p.sb(pref + "t3_h1", [128, NT * 512], BF16)
    w1r = ins["L0_attn_mlp_w1"].rearrange("(c p) f -> p c f", p=128)
    w2r = ins["L0_attn_mlp_w2"].rearrange("(c p) f -> p c f", p=128)
    p.op("sync", lambda e: e.dma_start(
        out=b1r[:], in_=ins["L0_attn_mlp_b1"].rearrange("(u f) -> u f", u=1)), dma=True)
    p.op("sync", lambda e: e.dma_start(
        out=b2r[:], in_=ins["L0_attn_mlp_b2"].rearrange("(u f) -> u f", u=1)), dma=True)
    mlpm = {}
    for fh in range(2):
        p.op("gpsimd", lambda e, fh=fh: e.dma_start(
            out=W1[:].rearrange("p (c f) -> p c f", f=T),
            in_=w1r[:, :, fh * T:(fh + 1) * T]), dma=True)
        for c in range(CH):
            p.op("vector", lambda e, c=c: e.tensor_scalar(
                W1[:, c * T:(c + 1) * T], W1[:, c * T:(c + 1) * T],
                n2w[:, c:c + 1], None, ALU.mult))
        p.op("gpsimd", lambda e, fh=fh: e.dma_start(
            out=W2[:].rearrange("p (c f) -> p c f", f=D),
            in_=w2r[:, fh * NT:(fh + 1) * NT, :]), dma=True)
        for tb in range(4):
            t0 = tb * 512
            for oc in range(NT):
                ph = p.pf[oc % 2]
                p.op("tensor", lambda e, ph=ph, oc=oc, fh=fh: e.matmul(
                    ph[:], b1r[:, fh * T + oc * 128: fh * T + oc * 128 + 128],
                    ones512_f[:], start=True, stop=False),
                    wait=(None if oc < 2 else mlpm.get(oc % 2)), inc=False)
                for c in range(CH):
                    p.op("tensor", lambda e, ph=ph, oc=oc, c=c, t0=t0: e.matmul(
                        ph[:], W1[:, c * T + oc * 128: c * T + oc * 128 + 128],
                        R[:, c * T + t0: c * T + t0 + 512], start=False, stop=(c == CH - 1)),
                        wait=False, inc=(c == CH - 1))
                p.op("scalar", lambda e, ph=ph, oc=oc: e.activation(
                    h1[:, oc * 512:(oc + 1) * 512], ph[:], AF.Gelu_apprx_tanh))
                mlpm[oc % 2] = p.mark()
            for oc in range(CH):
                py = p.pf[2 + (oc % 2)]
                first = True
                if fh == 0:
                    p.op("tensor", lambda e, py=py, oc=oc: e.matmul(
                        py[:], b2r[:, oc * 128:(oc + 1) * 128], ones512_f[:],
                        start=True, stop=False),
                        wait=(None if oc < 2 else mlpm.get(2 + oc % 2)), inc=False)
                    first = False
                for fc in range(NT):
                    p.op("tensor", lambda e, py=py, oc=oc, fc=fc, fh=fh: e.matmul(
                        py[:], W2[:, fc * D + oc * 128: fc * D + (oc + 1) * 128],
                        h1[:, fc * 512:(fc + 1) * 512],
                        start=(fh == 1 and fc == 0), stop=(fc == NT - 1)),
                        wait=((None if oc < 2 else mlpm.get(2 + oc % 2)) if (first and fc == 0) else False),
                        inc=(fc == NT - 1))
                hsl = H[:, oc * T + t0: oc * T + t0 + 512]
                p.op("vector", lambda e, py=py, hsl=hsl: e.tensor_add(hsl, hsl, py[:]))
                mlpm[2 + oc % 2] = p.mark()
    p.free_prefix(pref + "t3")

    xsl = p.sb(pref + "t4_xsl", [128, 512], F32)
    for oc in range(CH):
        for tb in range(4):
            t0 = tb * 512
            p.op("sync", lambda e, oc=oc, t0=t0: e.dma_start(
                out=xsl[:], in_=xin_d[oc * 128:(oc + 1) * 128, t0:t0 + 512]), dma=True)
            p.op("vector", lambda e, oc=oc, t0=t0: e.tensor_add(
                xsl[:], xsl[:], H[:, oc * T + t0: oc * T + t0 + 512]))
            p.op("sync", lambda e, oc=oc, t0=t0: e.dma_start(
                out=xout_d[oc * 128:(oc + 1) * 128, t0:t0 + 512], in_=xsl[:]), dma=True)
    p.free_prefix(pref)


def sl_L0_attn(cm):
    emit_L0_attn(cm)


def sl_L2_attn(cm):
    emit_L2_attn(cm)


def sl_L3_attn(cm):
    std_prologue(cm, lnx_first=True)
    cm.p.free_prefix("z3_XS")
    emit_L3_attn(cm)
    if DBG.get("l3_stop"):
        cm.p.op("sync", lambda e: e.dma_start(out=cm.x_out[:, :], in_=cm.x_in[:, :]), dma=True)
    cm.p.free_prefix("z")


def sl_L1_attn(cm):
    std_prologue(cm)
    emit_L1_attn(cm)
    std_epilogue(cm)


def sl_L1_ffn(cm):
    std_prologue(cm)
    emit_L1_ffn(cm)
    std_epilogue(cm)


def sl_L2_ffn(cm):
    std_prologue(cm)
    emit_L2_ffn(cm)
    std_epilogue(cm)


def sl_L3_ffn(cm):
    std_prologue(cm)
    emit_L3_ffn(cm)
    std_epilogue(cm)


def sl_L0_ffn(cm):
    std_prologue(cm)
    emit_L0_ffn(cm)
    std_epilogue(cm)


def shard_x(x):
    out = []
    for core in range(8):
        b, h = core // 2, core % 2
        xc = x[b].reshape(32, 128, D)[h::2]
        out.append(np.ascontiguousarray(xc.transpose(2, 0, 1).reshape(D, T)))
    return out


def unshard_x(parts):
    x = np.zeros((B, S, D), dtype=np.float32)
    for core in range(8):
        b, h = core // 2, core % 2
        xc = parts[core].reshape(D, NT, 128).transpose(1, 2, 0)
        x[b].reshape(32, 128, D)[h::2] = xc
    return x


def derive_inputs(inp):
    d = {}
    d["route_scale_col"] = np.ascontiguousarray(
        np.broadcast_to(np.asarray(inp["L3_ffn_route_scale"], dtype=np.float32), (128, 1)))
    d["L1_ffn_WgT_host"] = np.ascontiguousarray(np.asarray(inp["L1_ffn_Wg"], dtype=np.float32).T)
    d["L1_ffn_BT_host"] = np.ascontiguousarray(
        np.asarray(inp["L1_ffn_B"], dtype=np.float32).transpose(0, 2, 1))
    d["L1_ffn_AT_host"] = np.ascontiguousarray(
        np.asarray(inp["L1_ffn_A"], dtype=np.float32).transpose(2, 0, 1).reshape(D, 16))
    d["L1_attn_W_inT_host"] = np.ascontiguousarray(np.asarray(inp["L1_attn_W_in"], dtype=np.float32).T)
    d["conv_b_col"] = np.ascontiguousarray(np.asarray(inp["L1_attn_conv_b"], dtype=np.float32).reshape(16, 128).T)
    d["D_col"] = np.ascontiguousarray(np.asarray(inp["L1_attn_D"], dtype=np.float32).reshape(16, 128).T)
    d["L1_attn_W_xprojT_host"] = np.ascontiguousarray(np.asarray(inp["L1_attn_W_xproj"], dtype=np.float32).T)
    d["L1_attn_W_outT_host"] = np.ascontiguousarray(np.asarray(inp["L1_attn_W_out"], dtype=np.float32).T)
    d["qkv_b_row"] = np.ascontiguousarray(np.asarray(inp["L0_attn_qkv_b"], dtype=np.float32).reshape(1, 3 * D))
    d["n1w_col"] = np.ascontiguousarray(np.asarray(inp["L0_attn_norm1_w"], dtype=np.float32).reshape(CH, 128).T)
    d["n2w_col"] = np.ascontiguousarray(np.asarray(inp["L0_attn_norm2_w"], dtype=np.float32).reshape(CH, 128).T)
    wqkv = np.asarray(inp["L2_attn_w_qkv"], dtype=np.float32)
    d["L2_wqT_host"] = np.ascontiguousarray(wqkv[0:D].T)
    d["L2_wkT_host"] = np.ascontiguousarray(wqkv[D:2 * D].T)
    d["L2_wvT_host"] = np.ascontiguousarray(wqkv[2 * D:3 * D].T)
    d["L2_woT_host"] = np.ascontiguousarray(np.asarray(inp["L2_attn_w_o"], dtype=np.float32).T)
    d["L2_wiqT_host"] = np.ascontiguousarray(np.asarray(inp["L2_attn_w_iq"], dtype=np.float32).T)
    d["L2_wikT_host"] = np.ascontiguousarray(np.asarray(inp["L2_attn_w_ik"], dtype=np.float32).T)
    d["L2_wiwT_host"] = np.ascontiguousarray(np.asarray(inp["L2_attn_w_iw"], dtype=np.float32).T)
    bqkv = np.asarray(inp["L2_attn_b_qkv"], dtype=np.float32)
    d["b_q_row"] = np.ascontiguousarray(bqkv[0:D].reshape(1, D))
    d["b_k_row"] = np.ascontiguousarray(bqkv[D:2 * D].reshape(1, D))
    d["b_v_row"] = np.ascontiguousarray(bqkv[2 * D:3 * D].reshape(1, D))
    d["b_o_row"] = np.ascontiguousarray(np.asarray(inp["L2_attn_b_o"], dtype=np.float32).reshape(1, D))
    d["b_iq_row"] = np.ascontiguousarray(np.asarray(inp["L2_attn_b_iq"], dtype=np.float32).reshape(1, 256))
    d["b_ik_row"] = np.ascontiguousarray(np.asarray(inp["L2_attn_b_ik"], dtype=np.float32).reshape(1, 64))
    d["b_iw_row"] = np.ascontiguousarray(np.asarray(inp["L2_attn_b_iw"], dtype=np.float32).reshape(1, 4))
    for w in ("Wq", "Wk", "Wv", "Wo"):
        d[f"L3_attn_{w}T_host"] = np.ascontiguousarray(
            np.asarray(inp[f"L3_attn_{w}"], dtype=np.float32).T)
    d["L1_ffn_WiT_host"] = np.ascontiguousarray(np.asarray(inp["L1_ffn_Wi"], dtype=np.float32).T)
    d["L1_ffn_WoT_host"] = np.ascontiguousarray(np.asarray(inp["L1_ffn_Wo"], dtype=np.float32).T)
    return d


BF16_INPUTS = {"c_l3cos", "c_l3sin", "c_l3m_a4", "c_l3m_b4",
               "c_l2cos", "c_l2sin", "c_l0cs", "c_l0tri"}

SUBLAYER_EMITTERS = {
    "L0_ffn": sl_L0_ffn,
    "L3_ffn": sl_L3_ffn,
    "L2_ffn": sl_L2_ffn,
    "L1_ffn": sl_L1_ffn,
    "L1_attn": sl_L1_attn,
    "L3_attn": sl_L3_attn,
    "L2_attn": sl_L2_attn,
    "L0_attn": sl_L0_attn,
}

SUBLAYER_WEIGHTS = {
    "L0_attn": {
        "L0_attn_qkv_w": (D, 3 * D), "qkv_b_row": (1, 3 * D),
        "L0_attn_out_w": (D, D), "L0_attn_out_b": (D,),
        "n1w_col": (128, CH), "n2w_col": (128, CH),
        "L0_attn_mlp_w1": (D, 4096), "L0_attn_mlp_b1": (4096,),
        "L0_attn_mlp_w2": (4096, D), "L0_attn_mlp_b2": (D,),
        "c_l0cs": (128, T), "c_l0tri": (128, 128),
        "c_mW0": (128, 1), "c_mW1": (128, 1),
    },
    "L2_attn": {
        "L2_wqT_host": (D, D), "L2_wkT_host": (D, D), "L2_wvT_host": (D, D),
        "L2_woT_host": (D, D), "L2_wiqT_host": (D, 256), "L2_wikT_host": (D, 64),
        "L2_wiwT_host": (D, 4),
        "b_q_row": (1, D), "b_k_row": (1, D), "b_v_row": (1, D), "b_o_row": (1, D),
        "b_iq_row": (1, 256), "b_ik_row": (1, 64), "b_iw_row": (1, 4),
        "c_l2cos": (128, 4 * T), "c_l2sin": (128, 4 * T),
    },
    "L3_attn": {
        "L3_attn_WqT_host": (D, D), "L3_attn_bq": (D,),
        "L3_attn_WkT_host": (D, D), "L3_attn_bk": (D,),
        "L3_attn_WvT_host": (D, D), "L3_attn_bv": (D,),
        "L3_attn_WoT_host": (D, D), "L3_attn_bo": (D,),
        "c_l3cos": (128, T), "c_l3sin": (128, T),
        "c_l3m_a4": (128, 512), "c_l3m_b4": (128, 512),
    },
    "L1_attn": {
        "L1_attn_W_inT_host": (D, 4096), "L1_attn_conv_w": (2048, 4),
        "conv_b_col": (128, 16), "D_col": (128, 16),
        "L1_attn_W_xprojT_host": (2048, 2080), "L1_attn_W_outT_host": (2048, D),
        "c_mW0": (128, 1), "c_mW1": (128, 1),
    },
    "L1_ffn": {
        "L1_ffn_WgT_host": (D, 4), "L1_ffn_bg": (4,),
        "L1_ffn_WiT_host": (D, 4096), "L1_ffn_WoT_host": (4096, D),
        "L1_ffn_AT_host": (D, 16), "L1_ffn_BT_host": (4, 4, 4096),
    },
    "L2_ffn": {
        "L2_ffn_gate_w": (D, 8),
        "L2_ffn_ew1": (8, D, 2048), "L2_ffn_ew2": (8, D, 2048), "L2_ffn_ew3": (8, 2048, D),
        "L2_ffn_sw1": (D, 2048), "L2_ffn_sw2": (D, 2048), "L2_ffn_sw3": (2048, D),
    },
    "L3_ffn": {
        "L3_ffn_gate_W": (D, 8),
        "L3_ffn_W1": (8, D, 512), "L3_ffn_b1": (8, 512),
        "L3_ffn_W2": (8, 512, D), "L3_ffn_b2": (8, D),
        "L3_ffn_Ws1": (D, 512), "L3_ffn_bs1": (512,),
        "L3_ffn_Ws2": (512, D), "L3_ffn_bs2": (D,),
        "route_scale_col": (128, 1),
    },
    "L0_ffn": {
        "L0_ffn_gate_w": (D, 8), "L0_ffn_gate_b": (8,),
        "L0_ffn_w1": (8, D, 2048), "L0_ffn_b1": (8, 2048),
        "L0_ffn_w2": (8, 2048, D), "L0_ffn_b2": (8, D),
    },
}


def std_prologue(cm, lnx_first=False):
    p = cm.p
    if lnx_first:
        cm.LNX = p.sb("z2_LNX", [128, CH * T], BF16)
        cm.XS = p.sb("z3_XS", [128, CH * T], F32)
    else:
        cm.XS = p.sb("z0_XS", [128, CH * T], F32)
        cm.LNX = p.sb("z1_LNX", [128, CH * T], BF16)
    cm.load_X(cm.x_in)
    cm.layernorm()


def std_epilogue(cm):
    cm.store_X(cm.x_out)
    cm.p.free_prefix("z")


def build_program(sublayers, weight_names):
    p = Prog()
    ins = {}
    ins["x_in"] = p.dram("x_in", [D, T], F32, kind="ExternalInput")
    x_out = p.dram("x_out", [D, T], F32, kind="ExternalOutput")
    for n, (shape, dt) in const_decls().items():
        ins[n] = p.dram(n, list(shape), dt, kind="ExternalInput")
    for n, shape in weight_names.items():
        dt = BF16 if n in BF16_INPUTS else F32
        ins[n] = p.dram(n, list(shape), dt, kind="ExternalInput")
    cm = Common(p, ins)
    chain = [ins["x_in"]]
    for j in range(len(sublayers) - 1):
        chain.append(p.dram(f"x_mid{j}", [D, T], F32))
    chain.append(x_out)
    for j, sl in enumerate(sublayers):
        cm.x_in = chain[j]
        cm.x_out = chain[j + 1]
        p.relax = sl in RELAXED
        SUBLAYER_EMITTERS[sl](cm)
        p.relax = True
    p.final_wait()
    nc = p.finalize()
    return nc


RELAXED = {"L0_ffn", "L1_ffn", "L2_ffn", "L3_ffn", "L3_attn", "L0_attn"}

SUBLAYER_ORDER = ["L0_attn", "L0_ffn", "L1_attn", "L1_ffn",
                  "L2_attn", "L2_ffn", "L3_attn", "L3_ffn"]

LAUNCH_PLAN = [SUBLAYER_ORDER]


def _core_in_maps(groups_weights, inp, der, xs_parts):
    allowed = set(const_decls()) | set(groups_weights) | {"x_in"}
    in_maps = []
    for core in range(8):
        m = dict(host_constants(core % 2))
        m["x_in"] = np.ascontiguousarray(xs_parts[core], dtype=np.float32)
        for n in groups_weights:
            if n in der:
                m[n] = der[n]
            elif n in m:
                pass
            else:
                m[n] = np.ascontiguousarray(np.asarray(inp[n], dtype=np.float32))
        m = {k: v for k, v in m.items() if k in allowed}
        in_maps.append(m)
    return in_maps


def kernel(**inputs):
    inp = inputs
    der = derive_inputs(inp)
    x = np.ascontiguousarray(np.asarray(inp["L0_attn_x"], dtype=np.float32))
    xs_parts = shard_x(x)
    for group in LAUNCH_PLAN:
        wn = {}
        for s in group:
            wn.update(SUBLAYER_WEIGHTS[s])
        nc = build_program(group, wn)
        in_maps = _core_in_maps(wn, inp, der, xs_parts)
        res = run_bass_kernel_spmd(nc, in_maps, core_ids=list(range(8)))
        xs_parts = [np.asarray(res.results[c]["x_out"]) for c in range(8)]
    return unshard_x(xs_parts)
